# version 04
# baseline (speedup 1.0000x reference)
import contextlib

import numpy as np
import ml_dtypes

import concourse.bass as bass
import concourse.mybir as mybir
from concourse.bass_utils import run_bass_kernel_spmd

F32 = mybir.dt.float32
BF16 = mybir.dt.bfloat16
AF = mybir.ActivationFunctionType
ALU = mybir.AluOpType
AX = mybir.AxisListType
bfloat16 = ml_dtypes.bfloat16

B, S, D = 4, 4096, 1024
HALF = S // 2
NT = HALF // 128
NTB = S // 128
DC = D // 128
NHP = 4
RG = [[0, 1], [2, 3], [4, 5], [6, 7]]
BIGB = 30.0
MAGIC = 12582912.0
N_PHASES = 4

IDH = 64
INNER = 512
WINDOW = 512
HID = 2730
QK_SCALE = 8.0


def cdiv(a, b):
    return (a + b - 1) // b


def _rotary_tables():
    half_idx = np.arange(0, IDH, 2, dtype=np.float64)
    inv_freq = 1.0 / 10000.0 ** (half_idx / IDH)
    t = np.arange(S, dtype=np.float64)
    freqs = np.concatenate([t[:, None] * inv_freq[None, :]] * 2, axis=1)
    base = (half_idx + 0.4 * IDH) / (1.4 * IDH)
    power = (t - S // 2) / float(WINDOW // 2)
    scale = np.concatenate([base[None, :] ** power[:, None]] * 2, axis=1)
    cos, sin = np.cos(freqs), np.sin(freqs)
    return ((cos * scale).T.astype(np.float32), (sin * scale).T.astype(np.float32),
            (cos / scale).T.astype(np.float32), (sin / scale).T.astype(np.float32))


def _bitffn_perm():
    perm = []
    i = 0
    while i < HID:
        j = min(i + 128, HID)
        perm.extend(range(i, j))
        perm.extend(range(HID + i, HID + j))
        i = j
    return np.array(perm, dtype=np.int64)


def _stage_inputs(inp):
    per_core = []
    x = np.ascontiguousarray(np.asarray(inp["L0_attn_x"], dtype=np.float32))
    eye = np.eye(128, dtype=np.float32)
    tri_keep = np.triu(np.ones((128, 128), dtype=np.float32)).astype(bfloat16)
    tril_keep = np.tril(np.ones((128, 128), dtype=np.float32)).astype(bfloat16)
    if N_PHASES >= 4:
        Aq, Bq, Ak, Bk = _rotary_tables()
        w1perm = _bitffn_perm()
        rsw = np.zeros((128, 128), dtype=np.float32)
        for blk0 in (0, 64):
            for i in range(32):
                rsw[blk0 + i + 32, blk0 + i] = 1.0
                rsw[blk0 + i, blk0 + i + 32] = 1.0
        rsw = rsw.astype(bfloat16)

    for c in range(8):
        b, h = c // 2, c % 2
        d = {}
        d["x_own"] = np.ascontiguousarray(x[b, HALF * h:HALF * h + HALF])
        d["xb"] = x[b]
        hs = slice(512 * h, 512 * h + 512)
        d["l0_wq"] = np.ascontiguousarray(np.asarray(inp["L0_attn_WQ"], np.float32)[:, hs])
        d["l0_wk"] = np.ascontiguousarray(np.asarray(inp["L0_attn_WK"], np.float32)[:, hs])
        d["l0_wv"] = np.ascontiguousarray(np.asarray(inp["L0_attn_WV"], np.float32)[:, hs])
        d["l0_wo"] = np.asarray(inp["L0_attn_WO"], np.float32)
        d["ident"] = eye
        d["ident_bf"] = eye.astype(bfloat16)
        d["tri_pt"] = tri_keep
        d["tril_pt"] = tril_keep
        d["own_half"] = np.array([h], dtype=np.int32)
        if N_PHASES >= 2:
            d["l0_w1"] = np.asarray(inp["L0_ffn_w1"], np.float32)
            d["l0_w2"] = np.asarray(inp["L0_ffn_w2"], np.float32)
        if N_PHASES >= 3:
            rs = slice(512 * h, 512 * h + 512)
            d["l1_wq"] = np.ascontiguousarray(np.asarray(inp["L1_attn_W_query"], np.float32)[rs])
            d["l1_wk"] = np.ascontiguousarray(np.asarray(inp["L1_attn_W_key"], np.float32)[rs])
            d["l1_wv"] = np.ascontiguousarray(np.asarray(inp["L1_attn_W_value"], np.float32)[rs])
            d["l1_wp"] = np.asarray(inp["L1_attn_W_proj"], np.float32)
        if N_PHASES >= 4:
            pos = np.clip(np.arange(HALF * h - 513, HALF * h + HALF), 0, S - 1)
            posq = pos[513:]
            sgn = np.concatenate([-np.ones(32, np.float32), np.ones(32, np.float32)])
            d["rot_aq"] = np.ascontiguousarray(np.tile(Aq[:, posq], (2, 1)).astype(bfloat16))
            d["rot_bq"] = np.ascontiguousarray(np.tile(sgn[:, None] * Bq[:, posq], (2, 1)).astype(bfloat16))
            d["rot_ak"] = np.ascontiguousarray(np.tile(Ak[:, pos], (2, 1)).astype(bfloat16))
            d["rot_bk"] = np.ascontiguousarray(np.tile(sgn[:, None] * Bk[:, pos], (2, 1)).astype(bfloat16))
            d["hv128"] = np.full((128, 1), float(h), dtype=np.float32)
            d["rswap"] = rsw
            hs = np.zeros((2, 128), dtype=np.float32)
            hs[0, 0:64] = 1.0
            hs[1, 64:128] = 1.0
            d["hsel_f"] = hs
            d["hsel_bf"] = hs.astype(bfloat16)
            d["i_wqkv"] = np.asarray(inp["L1_ffn_wqkv"], np.float32)
            d["i_qs"] = np.asarray(inp["L1_ffn_q_scale"], np.float32)
            d["i_ks"] = np.asarray(inp["L1_ffn_k_scale"], np.float32)
            d["i_wo"] = np.asarray(inp["L1_ffn_wo"], np.float32)
            d["i_lng"] = np.asarray(inp["L1_ffn_ln_g"], np.float32)
            d["i_lnb"] = np.asarray(inp["L1_ffn_ln_b"], np.float32)
            d["i_w1"] = np.ascontiguousarray(np.asarray(inp["L1_ffn_w1"], np.float32)[:, :, w1perm])
            d["i_b1"] = np.ascontiguousarray(np.asarray(inp["L1_ffn_b1"], np.float32)[:, w1perm])
            d["i_w2"] = np.asarray(inp["L1_ffn_w2"], np.float32)
            d["i_b2"] = np.asarray(inp["L1_ffn_b2"], np.float32)
            d["halo_valid"] = np.array([float(h)], dtype=np.float32)
        per_core.append(d)
    return per_core


class RSem:
    __slots__ = ("h", "base", "ctr", "role")

    def __init__(self, h, role):
        self.h, self.base, self.ctr, self.role = h, 0, 0, role

    def tick(self, k=1):
        self.ctr += k
        return (self.h, k)


class Prog:
    def __init__(self):
        self.nc = bass.Bass("TRN2", target_bir_lowering=False)
        self.ap = {}
        self.est = contextlib.ExitStack()
        self._barr = None
        self._barr_n = 0
        self.rsems = {}

    ROLE_ALIAS = {
        "in_xbA": "fmb_ldA", "in_xbB": "fmb_ldB", "in_x3": "ga_dcc",
        "in_xv": "fmb_vx",
        "f4_w1lA": "p2w_ldA", "f4_w1lB": "p2w_ldB",
        "f4_w2lA": "p2w_ld2A", "f4_w2lB": "p2w_ld2B",
        "f4_ld": "p2_wbA", "f4_sh1": "p2_wbB", "f4_hrb": "ga_din1",
        "f4_dshA": "ga_dw", "f4_dshB": "p2w_dw",
        "p4_tld": "ga_ldA", "p4_tl2": "ga_ldB",
    }

    def rs(self, role):
        role = self.ROLE_ALIAS.get(role, role)
        r = self.rsems.get(role)
        if r is None:
            r = RSem(self.est.enter_context(self.nc.semaphore(role)), role)
            self.rsems[role] = r
        r.base = r.ctr
        return r

    def din(self, name, shape, dt=F32):
        self.ap[name] = self.nc.dram_tensor(name, shape, dt, kind="ExternalInput").ap()
        return self.ap[name]

    def dint(self, name, shape, dt=F32):
        self.ap[name] = self.nc.dram_tensor(name, shape, dt, kind="Internal").ap()
        return self.ap[name]

    def sb(self, name, shape, dt, scope=None):
        return (scope or self.est).enter_context(self.nc.sbuf_tensor(name, shape, dt))

    def sem(self, name):
        return self.est.enter_context(self.nc.semaphore(name))

    def barrier(self, blk):
        if self._barr is None:
            self._barr = {e: self.est.enter_context(self.nc.semaphore(f"barr_{e}"))
                          for e in ("sync", "tensor", "vector", "scalar", "gpsimd")}
        self._barr_n += 1
        k = self._barr_n
        sems = self._barr

        def arm(name):
            def emit(eng):
                eng.wait_ge(sems[name], k - 1).then_inc(sems[name], 1)
                for other in sems:
                    if other != name:
                        eng.wait_ge(sems[other], k)
            return emit

        blk.sync(arm("sync"))
        blk.tensor(arm("tensor"))
        blk.vector(arm("vector"))
        blk.scalar(arm("scalar"))
        blk.gpsimd(arm("gpsimd"))


def build_program():
    p = Prog()
    nc = p.nc

    p.din("x_own", [HALF, D])
    p.din("xb", [S, D])
    p.din("l0_wq", [D, 512]); p.din("l0_wk", [D, 512]); p.din("l0_wv", [D, 512])
    p.din("l0_wo", [D, D])
    p.din("ident", [128, 128])
    p.din("ident_bf", [128, 128], BF16)
    p.din("tri_pt", [128, 128], BF16)
    p.din("tril_pt", [128, 128], BF16)
    p.din("own_half", [1], mybir.dt.int32)
    if N_PHASES >= 2:
        p.din("l0_w1", [4096, D]); p.din("l0_w2", [D, 4096])
        p.dint("w1qT_d", [D, 4096], BF16)
        p.dint("w2qT_d", [4096, D], BF16)
    if N_PHASES >= 3:
        p.din("l1_wq", [512, D]); p.din("l1_wk", [512, D]); p.din("l1_wv", [512, D])
        p.din("l1_wp", [D, D])
        p.dint("l1_wqT_d", [D, 512], F32)
        p.dint("l1_wkT_d", [D, 512], F32)
        p.dint("l1_wvT_d", [D, 512], F32)
        p.dint("l1_wpT_d", [D, D], F32)
    if N_PHASES >= 4:
        PW = HALF + 513
        p.din("rot_aq", [128, HALF], BF16); p.din("rot_bq", [128, HALF], BF16)
        p.din("rot_ak", [128, PW], BF16); p.din("rot_bk", [128, PW], BF16)
        p.din("hv128", [128, 1])
        p.din("rswap", [128, 128], BF16)
        p.din("hsel_f", [2, 128], F32)
        p.din("hsel_bf", [2, 128], BF16)
        p.din("i_wqkv", [2, D, 3 * INNER])
        p.din("i_qs", [2, IDH]); p.din("i_ks", [2, IDH])
        p.din("i_wo", [2, INNER, D])
        p.din("i_lng", [2, D]); p.din("i_lnb", [2, D])
        p.din("i_w1", [2, D, 2 * HID]); p.din("i_b1", [2, 2 * HID])
        p.din("i_w2", [2, HID, D]); p.din("i_b2", [2, D])
        p.din("halo_valid", [1])
        p.dint("x3_d", [HALF, D])

    out_ap = nc.dram_tensor("out", [HALF, D], F32, kind="ExternalOutput").ap()

    for tag, nph in (("p1", 1), ("p3", 3)):
        if N_PHASES >= nph:
            for hp in range(NHP):
                for half in range(2):
                    p.dint(f"{tag}cc_in_{hp}_{half}", [128, HALF], BF16)
                    p.dint(f"{tag}cc_out_{hp}_{half}", [2, 128, HALF], BF16)
    if N_PHASES >= 3:
        p.dint("x2cc_in", [D, HALF], BF16)
        p.dint("x2cc_out", [2, D, HALF], BF16)
    if N_PHASES >= 4:
        p.dint("shcc_in", [D, 513], BF16)
        p.dint("shcc_out", [2, D, 513], BF16)
        p.dint("sh1_in", [1, D], F32)
        p.dint("sh1_out", [2, 1, D], F32)

    with p.est:
        _emit_all(p, out_ap)
    return nc


def _emit_all(p, out_ap):
    nc = p.nc
    x_res = p.sb("x_res", [128, NT, D], F32)
    cst = p.sb("cst", [128, 8], F32)
    identt = p.sb("identt", [128, 128], F32)
    ident_bf = p.sb("ident_bf_s", [128, 128], BF16)
    tri_pt = p.sb("tri_pt_s", [128, 128], BF16)
    tril_pt = p.sb("tril_pt_s", [128, 128], BF16)
    if N_PHASES >= 4:
        p.hv = p.sb("hv_s", [128, 1], F32)
        p.rsw = p.sb("rsw_s", [128, 128], BF16)

    with nc.Block() as blk:
        s_ld0 = p.rs("s_ld0")
        s_ldx = p.rs("s_ldx")
        s_cst = p.rs("s_cst")
        p.s_ldx = s_ldx
        p.s_ld0 = s_ld0
        p.s_cst = s_cst
        if N_PHASES >= 4:
            s_hv0 = p.rs("s_hv0")
            p.s_hv0 = s_hv0
            s_rsw = p.rs("s_rsw")
            p.s_rsw = s_rsw

        @blk.sync
        def _(sync):
            sync.dma_start(out=identt[:], in_=p.ap["ident"][:]).then_inc(*(s_ld0).tick(16))
            sync.dma_start(out=ident_bf[:], in_=p.ap["ident_bf"][:]).then_inc(*(s_ld0).tick(16))
            sync.dma_start(out=tri_pt[:], in_=p.ap["tri_pt"][:]).then_inc(*(s_ld0).tick(16))
            sync.dma_start(out=tril_pt[:], in_=p.ap["tril_pt"][:]).then_inc(*(s_ld0).tick(16))
            xo = p.ap["x_own"].rearrange("(t p) d -> p t d", p=128)
            for tt in range(NT):
                sync.dma_start(out=x_res[:, tt, :], in_=xo[:, tt, :]).then_inc(*(s_ldx).tick(16))
            if N_PHASES >= 4:
                sync.dma_start(out=p.hv[:], in_=p.ap["hv128"][:]).then_inc(*(s_hv0).tick(16))
                sync.dma_start(out=p.rsw[:], in_=p.ap["rswap"][:]).then_inc(*(s_rsw).tick(16))

        @blk.gpsimd
        def _(gp):
            gp.memset(cst[:, 0:1], -BIGB)
            gp.memset(cst[:, 1:2], 1e-5).then_inc(*(s_cst).tick(1))

        with contextlib.ExitStack() as p1scope:
            x_fm = p.sb("p1_x_fm", [128, DC, S], BF16, p1scope)
            p.s_fmdone = _emit_fm_build(p, blk, x_fm, identt, "xb", NTB,
                                        ident_ld=(s_ld0, 64))
            p.barrier(blk)
            _emit_global_attn(p, blk, tag="p1", x_fm=x_fm, x_res=x_res,
                              identt=ident_bf, tri_pt=tri_pt, cst=cst,
                              wq="l0_wq", wk="l0_wk", wv="l0_wv", wo="l0_wo",
                              score_scale=0.125, proj_scale=1.0)
            p.barrier(blk)
        s_lnv = _emit_ln(p, blk, x_res, cst, tag="ln1")
        p.barrier(blk)

        if N_PHASES >= 2:
            _emit_bitffn(p, blk, x_res, ident_bf, cst)
            p.barrier(blk)
            s_lnv = _emit_ln(p, blk, x_res, cst, tag="ln2")
            p.barrier(blk)

        if N_PHASES >= 3:
            wts = []
            for args in (("l1_wq", "l1_wqT_d", 4, "wtq"), ("l1_wk", "l1_wkT_d", 4, "wtk"),
                         ("l1_wv", "l1_wvT_d", 4, "wtv"), ("l1_wp", "l1_wpT_d", 8, "wtp")):
                wts.append(_emit_wT(p, blk, args[0], args[1], args[2], D, identt, args[3]))
                p.barrier(blk)

            @blk.sync
            def _(sync):
                for sdw_, tot_ in wts:
                    sync.wait_ge(sdw_.h, tot_)
            p.barrier(blk)
            with contextlib.ExitStack() as p3scope:
                x_fm3 = p.sb("p3_x_fm", [128, DC, S], BF16, p3scope)
                _emit_x2gather(p, blk, x_res, x_fm3, identt)
                p.barrier(blk)
                _emit_global_attn(p, blk, tag="p3", x_fm=x_fm3, x_res=x_res,
                                  identt=ident_bf, tri_pt=tri_pt, cst=cst,
                                  wq="l1_wqT_d", wk="l1_wkT_d", wv="l1_wvT_d",
                                  wo="l1_wpT_d",
                                  score_scale=1.0 / 8192.0,
                                  proj_scale=1.0 / 1024.0)
                p.barrier(blk)
            s_lnv = _emit_ln(p, blk, x_res, cst, tag="ln3")
            p.barrier(blk)

        if N_PHASES >= 4:
            _emit_inner(p, blk, x_res, identt, ident_bf, tri_pt, tril_pt, cst)
            p.barrier(blk)
            s_lnv = _emit_ln(p, blk, x_res, cst, tag="ln4")
            p.barrier(blk)

        s_st = p.rs("s_st")

        @blk.sync
        def _(sync):
            sync.wait_ge((s_lnv).h, (s_lnv).base + (9 * NT))
            xo = out_ap.rearrange("(t p) d -> p t d", p=128)
            for tt in range(NT):
                sync.dma_start(out=xo[:, tt, :], in_=x_res[:, tt, :]).then_inc(*(s_st).tick(16))
            sync.wait_ge((s_st).h, (s_st).base + (16 * NT))


def _emit_fm_build(p, blk, dst_fm, identt, src_name, ntiles, ident_ld):
    nc = p.nc
    s_ld = [p.rs("fmb_ldA"), p.rs("fmb_ldB")]
    s_tx = p.rs("fmb_tx")
    s_vx = p.rs("fmb_vx")
    with contextlib.ExitStack() as scope:
        xstage = [p.sb(f"fmb_{src_name}_st{i}", [128, D], F32, scope) for i in range(2)]
        ps_tp = [scope.enter_context(nc.psum_tensor(f"fmb_{src_name}_ps{i}", [128, 1024], F32))
                 for i in range(2)]

        @blk.sync
        def _(sync):
            src = p.ap[src_name].rearrange("(t p) d -> p t d", p=128)
            for tt in range(ntiles):
                if tt >= 2:
                    sync.wait_ge((s_tx).h, (s_tx).base + (tt - 1))
                sync.dma_start(out=xstage[tt % 2][:], in_=src[:, tt, :]).then_inc(*(s_ld[tt % 2]).tick(16))

        @blk.tensor
        def _(tensor):
            tensor.wait_ge((ident_ld[0]).h, (ident_ld[0]).base + (ident_ld[1]))
            for tt in range(ntiles):
                tensor.wait_ge((s_ld[tt % 2]).h, (s_ld[tt % 2]).base + (16 * (tt // 2 + 1)))
                if tt >= 2:
                    tensor.wait_ge((s_vx).h, (s_vx).base + (tt - 1))
                pst = ps_tp[tt % 2]
                for c in range(DC):
                    ins = nc.tensor.transpose(
                        pst[:, 128 * c:128 * c + 128],
                        xstage[tt % 2][:, 128 * c:128 * c + 128], identt[:])
                ins.then_inc(*(s_tx).tick(1))

        @blk.vector
        def _(vector):
            for tt in range(ntiles):
                vector.wait_ge((s_tx).h, (s_tx).base + (tt + 1))
                src = ps_tp[tt % 2][:].rearrange("p (c k) -> p c k", c=DC)
                vector.tensor_copy(dst_fm[:, :, 128 * tt:128 * tt + 128], src).then_inc(*(s_vx).tick(1))

        s_done = p.rs("fmb_done")

        @blk.tensor
        def _(tensor):
            tensor.wait_ge((s_vx).h, (s_vx).base + ntiles).then_inc(*(s_done).tick(1))
    return s_done


def _emit_ln(p, blk, x_res, cst, tag, ntiles=NT):
    nc = p.nc
    lscope = contextlib.ExitStack()
    sq_t = p.sb(f"{tag}_sq", [128, D], F32, lscope)
    st_t = p.sb(f"{tag}_st", [128, 8], F32, lscope)
    s_v = p.rs("ga_v")
    s_s = p.rs("ga_s")

    @blk.vector
    def _(vector):
        rr = p.rs("fmb_vx")
        vector.wait_ge(rr.h, rr.base + 0)
        n = [0]

        def step(ins):
            n[0] += 1
            ins.then_inc(*(s_v).tick(1))
            vector.wait_ge((s_v).h, (s_v).base + (n[0]))

        for tt in range(ntiles):
            xt = x_res[:, tt, :]
            step(vector.tensor_tensor(sq_t[:], xt, xt, ALU.mult))
            step(vector.reduce_sum(st_t[:, 0:1], xt, AX.X))
            step(vector.tensor_scalar(st_t[:, 2:3], st_t[:, 0:1], 1.0 / D, None, ALU.mult))
            step(vector.reduce_sum(st_t[:, 1:2], sq_t[:], AX.X))
            step(vector.tensor_scalar(st_t[:, 3:4], st_t[:, 1:2], 1.0 / D, None, ALU.mult))
            step(vector.tensor_tensor(st_t[:, 4:5], st_t[:, 2:3], st_t[:, 2:3], ALU.mult))
            step(vector.tensor_tensor(st_t[:, 5:6], st_t[:, 3:4], st_t[:, 4:5], ALU.subtract))
            vector.wait_ge((s_s).h, (s_s).base + (tt + 1))
            step(vector.reciprocal(st_t[:, 7:8], st_t[:, 6:7]))
            step(vector.tensor_scalar(xt, xt, st_t[:, 2:3], st_t[:, 7:8],
                                      ALU.subtract, ALU.mult))

    @blk.scalar
    def _(scalar):
        scalar.wait_ge((p.s_cst).h, (p.s_cst).base + (1))
        for tt in range(ntiles):
            scalar.wait_ge((s_v).h, (s_v).base + (9 * tt + 7))
            scalar.activation(out=st_t[:, 6:7], in_=st_t[:, 5:6], func=AF.Sqrt,
                              bias=cst[:, 1:2]).then_inc(*(s_s).tick(1))
    lscope.close()
    return s_v


def _emit_global_attn(p, blk, tag, x_fm, x_res, identt, tri_pt, cst,
                      wq, wk, wv, wo, score_scale, proj_scale,
                      w_pre_bf16=False):
    nc = p.nc

    s_wld = p.rs("ga_wld")
    s_vw = p.rs("ga_vw")
    s_tq = p.rs("ga_tq")
    s_vq = p.rs("ga_vq")
    s_tv = p.rs("ga_tv")
    s_vv = p.rs("ga_vv")
    s_ts = p.rs("ga_ts")
    s_se = p.rs("ga_se")
    s_vm = p.rs("ga_vm")
    s_tav = p.rs("ga_tav")
    s_vo = p.rs("ga_vo")
    s_gl = p.rs("ga_gl")
    s_vrc = p.rs("ga_vrc")
    s_vn = p.rs("ga_vn")
    s_dcc = p.rs("ga_dcc")
    s_gc = p.rs("ga_gc")
    s_drb = p.rs("ga_drb")
    s_tw = p.rs("ga_tw")
    s_vwo = p.rs("ga_vwo")

    bscope = contextlib.ExitStack()
    wstage = p.sb(f"{tag}_wstage", [128, DC, 128], F32, bscope)
    w_bf = [p.sb(f"{tag}_wbf{i}", [128, DC, 128], BF16, bscope) for i in range(3)]
    q_fm = p.sb(f"{tag}_qfm", [128, S], BF16, bscope)
    k_fm = p.sb(f"{tag}_kfm", [128, S], BF16, bscope)
    v_bf = p.sb(f"{tag}_vbf", [128, S], BF16, bscope)
    v_aug = p.sb(f"{tag}_vaug", [128, NTB, 130], BF16, bscope)
    pt_buf = [p.sb(f"{tag}_pt{i}", [128, 1536], BF16, bscope) for i in range(2)]
    ot_un = [p.sb(f"{tag}_otun{i}", [65, S], BF16, bscope) for i in range(2)]
    l2_t = [p.sb(f"{tag}_l2{i}", [1, HALF], BF16, bscope) for i in range(2)]
    rlb = p.sb(f"{tag}_rlb", [64, HALF], BF16, bscope)
    ones64 = p.sb(f"{tag}_ones64", [1, 64], BF16, bscope)
    s_tb = p.rs("ga_tb")
    s_vb = p.rs("ga_vb")

    def qb_supers(qb):
        slots = [(kt, h) for kt in range(4 * qb + 4) for h in range(2)]
        return [slots[i:i + 3] for i in range(0, len(slots), 3)]

    SUP = []
    for hp in range(NHP):
        for qb in range(8):
            sups = qb_supers(qb)
            for i, sl in enumerate(sups):
                diag = [(si, kt, h) for si, (kt, h) in enumerate(sl) if kt >= 4 * qb]
                SUP.append(dict(hp=hp, qb=qb, slots=sl, diag=diag, first=(i == 0)))
    n_sup = len(SUP)
    n_sup_hp = n_sup // NHP
    cum_diag = np.cumsum([len(s["diag"]) for s in SUP]).tolist()

    for hp in range(NHP):
        with contextlib.ExitStack() as scope:
            ps_qkv = scope.enter_context(nc.psum_tensor(f"{tag}_psqkv{hp}", [128, 1024], F32))
            ps_vt = [scope.enter_context(nc.psum_tensor(f"{tag}_psvt{hp}{i}", [128, 512], F32))
                     for i in range(2)]

            @blk.sync
            def _(sync, hp=hp):
                if hp == 0:
                    sync.wait_ge((p.s_fmdone).h, (p.s_fmdone).base + (1))
                for wi, wn in enumerate((wq, wk, wv)):
                    idx = hp * 3 + wi
                    if idx > 0:
                        sync.wait_ge((s_vw).h, (s_vw).base + (idx))
                    src = p.ap[wn].rearrange("(c p) o -> p c o", p=128)[:, :, 128 * hp:128 * hp + 128]
                    for c in range(DC):
                        sync.dma_start(out=wstage[:, c, :], in_=src[:, c, :]).then_inc(*(s_wld).tick(16))

            @blk.vector
            def _(vector, hp=hp):
                if hp == 0:
                    vector.wait_ge((p.s_ld0).h, (p.s_ld0).base + (64))
                    va = v_aug[:].rearrange("p t (h c) -> p t h c", c=65)
                    vector.memset(va[:, :, :, 64:65], 1.0)
                    vector.memset(ones64[:], 1.0)
                for wi in range(3):
                    idx = hp * 3 + wi
                    vector.wait_ge((s_wld).h, (s_wld).base + (128 * (idx + 1)))
                    vector.tensor_copy(w_bf[wi][:], wstage[:]).then_inc(*(s_vw).tick(1))
                for wi in range(3):
                    for qb in range(8):
                        e = (hp * 3 + wi) * 8 + qb
                        vector.wait_ge((s_tq).h, (s_tq).base + (e + 1))
                        src = ps_qkv[:, 512 * (qb % 2):512 * (qb % 2) + 512]
                        dst = (q_fm, k_fm, v_bf)[wi][:, 512 * qb:512 * qb + 512]
                        vector.tensor_copy(dst, src).then_inc(*(s_vq).tick(1))
                for g in range(8):
                    vector.wait_ge((s_tv).h, (s_tv).base + (hp * 8 + g + 1))
                    src = ps_vt[g % 2][:].rearrange("p (t h c) -> p t h c", t=4, h=2)
                    dst = v_aug[:, 4 * g:4 * g + 4, :].rearrange(
                        "p t (h c) -> p t h c", c=65)[:, :, :, 0:64]
                    vector.tensor_copy(dst, src).then_inc(*(s_vv).tick(1))

            @blk.tensor
            def _(tensor, hp=hp):
                if hp > 0:
                    tensor.wait_ge((s_se).h, (s_se).base + (hp * n_sup_hp))
                    tensor.wait_ge((s_vo).h, (s_vo).base + (16 * hp))
                    tensor.wait_ge((s_vb).h, (s_vb).base + (16 * hp))
                for wi in range(3):
                    tensor.wait_ge((s_vw).h, (s_vw).base + (hp * 3 + wi + 1))
                    for qb in range(8):
                        e = (hp * 3 + wi) * 8 + qb
                        if e >= 2:
                            tensor.wait_ge((s_vq).h, (s_vq).base + (e - 1))
                        dst = ps_qkv[:, 512 * (qb % 2):512 * (qb % 2) + 512]
                        for c in range(DC):
                            ins = nc.tensor.matmul(
                                dst, w_bf[wi][:, c, :],
                                x_fm[:, c, 512 * qb:512 * qb + 512],
                                start=(c == 0), stop=(c == DC - 1))
                        ins.then_inc(*(s_tq).tick(1))
                tensor.wait_ge((s_vq).h, (s_vq).base + ((hp * 3 + 3) * 8))
                for g in range(8):
                    if g >= 2:
                        tensor.wait_ge((s_vv).h, (s_vv).base + (hp * 8 + g - 1))
                    for t4 in range(4):
                        tt = 4 * g + t4
                        ins = nc.tensor.matmul(
                            ps_vt[g % 2][:, 128 * t4:128 * t4 + 128],
                            v_bf[:, 128 * tt:128 * tt + 128], identt[:],
                            start=True, stop=True, skip_group_check=True)
                    ins.then_inc(*(s_tv).tick(1))

        with contextlib.ExitStack() as scope:
            ps_s = [scope.enter_context(nc.psum_tensor(f"{tag}_pss{hp}{i}", [128, 1536], F32))
                    for i in range(2)]
            ps_o = [scope.enter_context(nc.psum_tensor(f"{tag}_pso{hp}{i}", [65, 512], F32))
                    for i in range(2)]
            sup_lo = hp * n_sup_hp
            sup_hi = (hp + 1) * n_sup_hp

            @blk.tensor
            def _(tensor, hp=hp, sup_lo=sup_lo, sup_hi=sup_hi):
                tensor.wait_ge((s_vv).h, (s_vv).base + (8 * (hp + 1)))

                def emit_S(sg):
                    sup = SUP[sg]
                    qb = sup["qb"]
                    if sg - sup_lo >= 2:
                        tensor.wait_ge((s_se).h, (s_se).base + (sg - 1))
                    for si, (kt, h) in enumerate(sup["slots"]):
                        ins = nc.tensor.matmul(
                            ps_s[sg % 2][:, 512 * si:512 * si + 512],
                            k_fm[64 * h:64 * h + 64, 128 * kt:128 * kt + 128],
                            q_fm[64 * h:64 * h + 64, 512 * qb:512 * qb + 512],
                            start=True, stop=True, tile_position=(64 * h, 0),
                            skip_group_check=True)
                    ins.then_inc(*(s_ts).tick(1))

                def emit_AV(sg):
                    sup = SUP[sg]
                    qb = sup["qb"]
                    tensor.wait_ge((s_se).h, (s_se).base + (sg + 1))
                    tensor.wait_ge((s_vv).h, (s_vv).base + (hp * 8 + sup["slots"][-1][0] // 4 + 1))
                    if cum_diag[sg] > 0:
                        tensor.wait_ge((s_vm).h, (s_vm).base + (cum_diag[sg]))
                    if sup["first"]:
                        tensor.wait_ge((s_vo).h, (s_vo).base + (2 * (hp * 8 + qb)))
                    for si, (kt, h) in enumerate(sup["slots"]):
                        j = kt - 4 * qb
                        off = 128 * j if j > 0 else 0
                        ins = nc.tensor.matmul(
                            ps_o[h][:, off:512],
                            v_aug[:, kt, 65 * h:65 * h + 65],
                            pt_buf[sg % 2][:, 512 * si + off:512 * si + 512],
                            start=(kt == 0), stop=(kt == 4 * qb + 3),
                            skip_group_check=True)
                    ins.then_inc(*(s_tav).tick(1))

                emit_S(sup_lo)
                for sg in range(sup_lo, sup_hi - 1):
                    emit_S(sg + 1)
                    emit_AV(sg)
                emit_AV(sup_hi - 1)
                for half in range(2):
                    tensor.wait_ge((s_vrc).h, (s_vrc).base + (2 * hp + half + 1))
                    for head in range(2):
                        for c4 in range(4):
                            idx = 16 * hp + 8 * half + 4 * head + c4
                            if idx >= 3:
                                tensor.wait_ge((s_vb).h, (s_vb).base + (idx - 2))
                            nc.tensor.matmul(
                                ps_s[0][0:64, 512 * (idx % 3):512 * (idx % 3) + 512],
                                ones64[:], l2_t[head][:, 512 * c4:512 * c4 + 512],
                                start=True, stop=True,
                                skip_group_check=True).then_inc(*(s_tb).tick(1))

            @blk.scalar
            def _(scalar, hp=hp, sup_lo=sup_lo, sup_hi=sup_hi):
                if hp == 0:
                    scalar.wait_ge((p.s_cst).h, (p.s_cst).base + (1))
                for sg in range(sup_lo, sup_hi):
                    n = len(SUP[sg]["slots"])
                    scalar.wait_ge((s_ts).h, (s_ts).base + (sg + 1))
                    if sg - sup_lo >= 2:
                        scalar.wait_ge((s_tav).h, (s_tav).base + (sg - 1))
                    scalar.activation(
                        out=pt_buf[sg % 2][:, 0:512 * n],
                        in_=ps_s[sg % 2][:, 0:512 * n],
                        func=AF.Exp, bias=cst[:, 0:1], scale=score_scale,
                    ).then_inc(*(s_se).tick(1))

            @blk.vector
            def _(vector, hp=hp, sup_lo=sup_lo, sup_hi=sup_hi):
                if hp > 0:
                    vector.wait_ge((s_gl).h, (s_gl).base + (64 * hp))
                    vector.wait_ge((s_dcc).h, (s_dcc).base + (64 * hp))
                for sg in range(sup_lo, sup_hi):
                    sup = SUP[sg]
                    if sup["diag"]:
                        vector.wait_ge((s_se).h, (s_se).base + (sg + 1))
                        for (si, kt, h) in sup["diag"]:
                            j = kt - 4 * sup["qb"]
                            sl = pt_buf[sg % 2][:, 512 * si + 128 * j:512 * si + 128 * j + 128]
                            vector.tensor_tensor(sl, sl, tri_pt[:], ALU.mult).then_inc(*(s_vm).tick(1))
                    if (sg + 1 == sup_hi) or SUP[sg + 1]["first"]:
                        qb = sup["qb"]
                        vector.wait_ge((s_tav).h, (s_tav).base + (sg + 1))
                        for head in range(2):
                            vector.tensor_copy(
                                ot_un[head][0:65, 512 * qb:512 * qb + 512],
                                ps_o[head][0:65, :]).then_inc(*(s_vo).tick(1))
                for half in range(2):
                    vector.wait_ge((s_gl).h, (s_gl).base + (64 * hp + 32 * (half + 1)))
                    with nc.allow_low_precision(reason="softmax denom tolerates bf16"):
                        vector.reciprocal(l2_t[0][:], l2_t[0][:])
                        vector.reciprocal(l2_t[1][:], l2_t[1][:]).then_inc(*(s_vrc).tick(1))
                    for head in range(2):
                        g = 4 * hp + 2 * half + head
                        if g > 0:
                            vector.wait_ge((s_vn).h, (s_vn).base + (g))
                        for c4 in range(4):
                            idx = 16 * hp + 8 * half + 4 * head + c4
                            vector.wait_ge((s_tb).h, (s_tb).base + (idx + 1))
                            vector.tensor_copy(
                                rlb[:, 512 * c4:512 * c4 + 512],
                                ps_s[0][0:64, 512 * (idx % 3):512 * (idx % 3) + 512],
                            ).then_inc(*(s_vb).tick(1))
                        vector.wait_ge((s_vb).h, (s_vb).base + (16 * hp + 8 * half + 4 * head + 4))
                        sl = ot_un[head][0:64, HALF * half:HALF * half + HALF]
                        vector.tensor_tensor(sl, sl, rlb[:, :], ALU.mult).then_inc(*(s_vn).tick(1))

            @blk.sync
            def _(sync, hp=hp):
                sync.wait_ge((s_vo).h, (s_vo).base + (16 * (hp + 1)))
                for half in range(2):
                    if hp + half > 0:
                        sync.wait_ge((s_tb).h, (s_tb).base + (16 * hp + 8 * half))
                    for head in range(2):
                        sync.dma_start(
                            out=l2_t[head][:],
                            in_=ot_un[head][64:65, HALF * half:HALF * half + HALF],
                        ).then_inc(*(s_gl).tick(16))
                sync.wait_ge((s_vn).h, (s_vn).base + (4 * (hp + 1)))
                for half in range(2):
                    ccin = p.ap[f"{tag}cc_in_{hp}_{half}"]
                    for head in range(2):
                        sync.dma_start(
                            out=ccin[64 * head:64 * head + 64, :],
                            in_=ot_un[head][0:64, HALF * half:HALF * half + HALF],
                        ).then_inc(*(s_dcc).tick(16))

            @blk.gpsimd
            def _(gp, hp=hp):
                gp.wait_ge((s_dcc).h, (s_dcc).base + (64 * (hp + 1)))
                for half in range(2):
                    gp.collective_compute(
                        "AllGather", ALU.bypass, replica_groups=RG,
                        ins=[p.ap[f"{tag}cc_in_{hp}_{half}"][:]],
                        outs=[p.ap[f"{tag}cc_out_{hp}_{half}"][:]],
                    ).then_inc(*(s_gc).tick(1))

    bscope.close()
    with contextlib.ExitStack() as scope:
        ps_wo = scope.enter_context(nc.psum_tensor(f"{tag}_pswo", [128, 1024], F32))
        wstage_c = p.sb(f"{tag}_wstc", [128, DC, 128], F32, scope)
        wo_bf = p.sb(f"{tag}_wobf", [128, DC, 512], BF16, scope)

        @blk.sync
        def _(sync):
            sync.wait_ge((s_tq).h, (s_tq).base + (12 * 8))
            for hp in range(NHP):
                sync.wait_ge((s_gc).h, (s_gc).base + (2 * (hp + 1)))
                for arm in sync.switch_core_id(n=128):
                    my_half = arm.logical % 2
                    cout = p.ap[f"{tag}cc_out_{hp}_{my_half}"]
                    for rank in range(2):
                        sync.dma_start(
                            out=x_fm[:, 4 * rank + hp, 0:HALF],
                            in_=cout[rank],
                        ).then_inc(s_drb.h, 16)
                s_drb.ctr += 32
            sync.wait_ge((s_tav).h, (s_tav).base + (n_sup))
            sync.wait_ge((s_se).h, (s_se).base + (n_sup))
            sync.wait_ge((s_vo).h, (s_vo).base + (64))
            sync.wait_ge((s_vb).h, (s_vb).base + (64))
            sync.wait_ge((s_vn).h, (s_vn).base + (16))
            sync.wait_ge((s_gl).h, (s_gl).base + (64 * NHP))
            sync.wait_ge((s_dcc).h, (s_dcc).base + (64 * NHP))
            for qq in range(8):
                if w_pre_bf16 and qq % 2 == 0:
                    oh = qq // 4
                    srcw = p.ap[wo].rearrange("(c p) o -> p c o", p=128)[:, :, 512 * oh:512 * oh + 512]
                    if oh == 1:
                        sync.wait_ge((s_tw).h, (s_tw).base + (NT))
                    for c in range(DC):
                        sync.dma_start(out=wo_bf[:, c, :], in_=srcw[:, c, :]).then_inc(*(s_wld).tick(16))
                elif not w_pre_bf16:
                    sync.wait_ge((s_vw).h, (s_vw).base + (12 + qq))
                    srcw = p.ap[wo].rearrange("(c p) o -> p c o", p=128)[:, :, 128 * qq:128 * qq + 128]
                    for c in range(DC):
                        sync.dma_start(out=wstage_c[:, c, :], in_=srcw[:, c, :]).then_inc(*(s_wld).tick(16))


        @blk.tensor
        def _(tensor):
            tensor.wait_ge((s_drb).h, (s_drb).base + (16 * 8))
            for oh in range(2):
                if not w_pre_bf16:
                    tensor.wait_ge((s_vw).h, (s_vw).base + (12 + 4 * (oh + 1)))
                else:
                    tensor.wait_ge((s_wld).h, (s_wld).base + (128 * (12 + oh + 1)))
                for tt in range(NT):
                    e = NT * oh + tt
                    if e >= 2:
                        tensor.wait_ge((s_vwo).h, (s_vwo).base + (e - 1))
                    dst = ps_wo[:, 512 * (e % 2):512 * (e % 2) + 512]
                    for ci in range(DC):
                        ins = nc.tensor.matmul(
                            dst, x_fm[:, ci, 128 * tt:128 * tt + 128],
                            wo_bf[:, ci, 0:512],
                            start=(ci == 0), stop=(ci == DC - 1))
                    ins.then_inc(*(s_tw).tick(1))

        @blk.vector
        def _(vector):
            vector.wait_ge((p.s_ldx).h, (p.s_ldx).base + (256))
            for oh in range(2):
                if not w_pre_bf16:
                    for q4 in range(4):
                        qq = 4 * oh + q4
                        vector.wait_ge((s_wld).h, (s_wld).base + (128 * (12 + qq + 1)))
                        if oh == 1 and q4 == 0:
                            vector.wait_ge((s_tw).h, (s_tw).base + (NT))
                        vector.tensor_copy(
                            wo_bf[:, :, 128 * q4:128 * q4 + 128],
                            wstage_c[:]).then_inc(*(s_vw).tick(1))
                for tt in range(NT):
                    e = NT * oh + tt
                    vector.wait_ge((s_tw).h, (s_tw).base + (e + 1))
                    xt = x_res[:, tt, 512 * oh:512 * oh + 512]
                    vector.scalar_tensor_tensor(
                        xt, ps_wo[:, 512 * (e % 2):512 * (e % 2) + 512], proj_scale, xt,
                        ALU.mult, ALU.add).then_inc(*(s_vwo).tick(1))




def _emit_bitffn(p, blk, x_res, ident_bf, cst):
    nc = p.nc
    sc = contextlib.ExitStack()

    wstg = [p.sb("p2_wstg0", [128, 4096], F32, sc), p.sb("p2_wstg1", [128, 1024], F32, sc)]
    w1qT = p.sb("p2_w1qT", [128, DC, 4096], BF16, sc)
    tern = p.sb("p2_tern", [128, 4096], BF16, sc)
    tstg = p.sb("p2_tstg", [128, 512], BF16, sc)
    acc = p.sb("p2_acc", [128, 4], F32, sc)
    sbc = p.sb("p2_sbc", [128, 4], F32, sc)
    onesf = p.sb("p2_onesf", [128, 1], F32, sc)
    onesr = p.sb("p2_onesr", [1, 128], F32, sc)
    sval = p.sb("p2_sval", [1, 2], F32, sc)
    h_t = p.sb("p2_h", [128, 4096], F32, sc)
    xq_tm = p.sb("p2_xq", [128, D], BF16, sc)
    xq_fm = p.sb("p2_xqfm", [128, DC, 128], BF16, sc)
    hq_tm = p.sb("p2_hqtm", [128, 4096], BF16, sc)
    hq_fm = p.sb("p2_hqfm", [128, 32, 128], BF16, sc)
    w2b = [p.sb(f"p2_w2b{i}", [128, D], BF16, sc) for i in range(2)]
    stt = p.sb("p2_stt", [128, 8], F32, sc)

    ps_t = [sc.enter_context(nc.psum_tensor(f"p2_pst{i}", [128, 512], F32)) for i in range(2)]
    ps_m1 = [sc.enter_context(nc.psum_tensor(f"p2_psm1{i}", [128, 512], F32)) for i in range(2)]
    ps_m2 = [sc.enter_context(nc.psum_tensor(f"p2_psm2{i}", [128, 512], F32)) for i in range(2)]
    ps_s = sc.enter_context(nc.psum_tensor("p2_pss", [1, 128], F32))
    ps_b = sc.enter_context(nc.psum_tensor("p2_psb", [128, 128], F32))

    WCfg = [("l0_w1", 32, 1024, True), ("l0_w2", 8, 4096, False)]
    wprep = {}

    for wi, (wn, ntl, wf, res) in enumerate(WCfg):
        nblk = wf // 128
        ngrp = nblk // 4
        sldA = p.rs("p2w_ldA"); sldB = p.rs("p2w_ldB")
        svr = p.rs("p2w_vr")
        st1 = p.rs("p2w_t1")
        sv1 = p.rs("p2w_v1")
        sld2A = p.rs("p2w_ld2A"); sld2B = p.rs("p2w_ld2B")
        svq = p.rs("p2w_vq")
        s_tt = p.rs("p2w_tt")
        sve = p.rs("p2w_ve")
        sdw = p.rs("p2w_dw")

        wprep[wi] = dict(sve=sve, sdw=sdw, svq=svq, stt=s_tt,
                         sve_total=ntl * (wf // 512), nq=3 * ntl,
                         ntt=ntl * (wf // 128))

        @blk.sync
        def _(sync, wn=wn, ntl=ntl, wf=wf, res=res, sldA=sldA, sldB=sldB,
              svr=svr, sv1=sv1, svq=svq, sve=sve, sdw=sdw, nblk=nblk, ngrp=ngrp,
              sld2A=sld2A, sld2B=sld2B, wi=wi):
            srcw = p.ap[wn].rearrange("(t p) f -> p t f", p=128)
            nb = 2 if wf <= 1024 else 1
            if wi == 1:
                sync.wait_ge(wprep[0]["svq"].h, wprep[0]["svq"].base)
            for t in range(ntl):
                if t >= nb:
                    sync.wait_ge((svr).h, (svr).base + (2 * (t - nb + 1)))
                sync.dma_start(out=wstg[t % nb][:, 0:wf], in_=srcw[:, t, :]).then_inc(*((sldA, sldB)[t % nb]).tick(16))
            sync.wait_ge((sv1).h, (sv1).base + (4))
            for t in range(ntl):
                if t >= nb:
                    sync.wait_ge((svq).h, (svq).base + (3 * (t - nb + 1)))
                sync.dma_start(out=wstg[t % nb][:, 0:wf], in_=srcw[:, t, :]).then_inc(*((sld2A, sld2B)[t % nb]).tick(16))

        if not res:
            @blk.scalar
            def _(scalar, wn=wn, ntl=ntl, res=res, sve=sve, sdw=sdw, ngrp=ngrp):
                for t in range(ntl):
                    for g in range(ngrp):
                        gg = t * ngrp + g
                        scalar.wait_ge((sve).h, (sve).base + (gg + 1))
                        for j in range(4):
                            c = 4 * g + j
                            dst = p.ap["w2qT_d"][128 * c:128 * c + 128,
                                                 128 * t:128 * t + 128]
                            scalar.dma_start(out=dst, in_=tstg[:, 128 * j:128 * j + 128]
                                             ).then_inc(*(sdw).tick(16))

        @blk.vector
        def _(vector, wn=wn, ntl=ntl, wf=wf, res=res, sldA=sldA, sldB=sldB,
              svr=svr, st1=st1, sv1=sv1, svq=svq, s_tt=s_tt, sve=sve, sdw=sdw,
              nblk=nblk, ngrp=ngrp, sld2A=sld2A, sld2B=sld2B, wi=wi):
            if wi == 0:
                vector.memset(onesf[:], 1.0)
                vector.memset(onesr[:], 1.0)
            vector.memset(acc[:, 0:1], 0.0)
            nb = 2 if wf <= 1024 else 1
            for t in range(ntl):
                vector.wait_ge(((sldA, sldB)[t % nb]).h, ((sldA, sldB)[t % nb]).base + (16 * (t // nb + 1)))
                if t > 0:
                    vector.wait_ge((svr).h, (svr).base + (2 * t))
                vector.reduce_sum(acc[:, 2:3], wstg[t % nb][:, 0:wf], AX.X,
                                  apply_absolute_value=True).then_inc(*(svr).tick(1))
                vector.wait_ge((svr).h, (svr).base + (2 * t + 1))
                vector.tensor_tensor(acc[:, 0:1], acc[:, 0:1], acc[:, 2:3],
                                     ALU.add).then_inc(*(svr).tick(1))
            vector.wait_ge((st1).h, (st1).base + (1))
            vector.tensor_scalar(sval[:, wi:wi + 1], ps_s[:, 0:1],
                                 1.0 / (ntl * 128 * wf), None, ALU.mult).then_inc(*(sv1).tick(1))
            vector.wait_ge((st1).h, (st1).base + (2))
            vector.tensor_copy(sbc[:, 2 + wi:3 + wi], ps_b[:, 0:1]).then_inc(*(sv1).tick(1))
            vector.wait_ge((sv1).h, (sv1).base + (2))
            vector.reciprocal(sbc[:, wi:wi + 1], sbc[:, 2 + wi:3 + wi]).then_inc(*(sv1).tick(2))
            if wi == 1:
                vector.wait_ge(wprep[0]["stt"].h, wprep[0]["stt"].base)
            for t in range(ntl):
                vector.wait_ge(((sld2A, sld2B)[t % nb]).h, ((sld2A, sld2B)[t % nb]).base + (16 * (t // nb + 1)))
                if t > 0:
                    vector.wait_ge((s_tt).h, (s_tt).base + (nblk * t))
                wv = wstg[t % nb][:, 0:wf]
                vector.tensor_scalar(wv, wv, sbc[:, wi:wi + 1], MAGIC,
                                     ALU.mult, ALU.add).then_inc(*(svq).tick(1))
                vector.wait_ge((svq).h, (svq).base + (3 * t + 1))
                vector.tensor_scalar(wv, wv, MAGIC, -1.0,
                                     ALU.subtract, ALU.max).then_inc(*(svq).tick(1))
                vector.wait_ge((svq).h, (svq).base + (3 * t + 2))
                vector.tensor_scalar_min(tern[:, 0:wf], wv, 1.0).then_inc(*(svq).tick(1))
                for g in range(ngrp):
                    gg = t * ngrp + g
                    vector.wait_ge((s_tt).h, (s_tt).base + (nblk * t + 4 * (g + 1)))
                    srcp = ps_t[g % 2][:].rearrange("p (c k) -> p c k", c=4)
                    if res:
                        dstv = w1qT[:, 4 * g:4 * g + 4, 128 * t:128 * t + 128]
                        vector.tensor_copy(dstv, srcp).then_inc(*(sve).tick(1))
                    else:
                        if gg > 0:
                            vector.wait_ge((sdw).h, (sdw).base + (64 * gg))
                        vector.tensor_copy(tstg[:], ps_t[g % 2][:]).then_inc(*(sve).tick(1))

        @blk.tensor
        def _(tensor, wn=wn, ntl=ntl, wf=wf, res=res, svr=svr, st1=st1, sv1=sv1,
              svq=svq, s_tt=s_tt, sve=sve, nblk=nblk, ngrp=ngrp, wi=wi):
            if wi == 1:
                tensor.wait_ge(wprep[0]["sve"].h, wprep[0]["sve"].base)
            tensor.wait_ge((svr).h, (svr).base + (2 * ntl))
            nc.tensor.matmul(ps_s[:, 0:1], acc[:, 0:1], onesf[:],
                             start=True, stop=True, skip_group_check=True).then_inc(*(st1).tick(1))
            tensor.wait_ge((sv1).h, (sv1).base + (1))
            nc.tensor.matmul(ps_b[:, 0:1], onesr[:], sval[:, wi:wi + 1],
                             start=True, stop=True, skip_group_check=True).then_inc(*(st1).tick(1))
            for t in range(ntl):
                tensor.wait_ge((svq).h, (svq).base + (3 * (t + 1)))
                for c in range(nblk):
                    g, j = c // 4, c % 4
                    if t * ngrp + g >= 2:
                        tensor.wait_ge((sve).h, (sve).base + (t * ngrp + g - 1))
                    nc.tensor.matmul(
                        ps_t[g % 2][:, 128 * j:128 * j + 128],
                        tern[:, 128 * c:128 * c + 128], ident_bf[:],
                        start=True, stop=True, skip_group_check=True).then_inc(*(s_tt).tick(1))

    s_vx = p.rs("p2_vx")
    s_tx = p.rs("p2_tx")
    s_vxe = p.rs("p2_vxe")
    s_tm1 = p.rs("p2_tm1")
    s_sg = p.rs("p2_sg")
    s_vh = p.rs("p2_vh")
    s_th = p.rs("p2_th")
    s_vhq = p.rs("p2_vhq")
    s_tm2 = p.rs("p2_tm2")
    s_vr2 = p.rs("p2_vr2")
    s_wb = [p.rs("p2_wbA"), p.rs("p2_wbB")]
    V_X = 9

    @blk.sync
    def _(sync):
        sync.wait_ge((wprep[1]["sdw"]).h, (wprep[1]["sdw"]).base + (64 * 64 * 8 // 8))
        for tt in range(NT):
            for i2 in range(32):
                k = 32 * tt + i2
                if k >= 2:
                    sync.wait_ge((s_tm2).h, (s_tm2).base + (k - 1))
                src2 = p.ap["w2qT_d"][128 * i2:128 * i2 + 128, :]
                sync.dma_start(out=w2b[k % 2][:], in_=src2).then_inc(*(s_wb[k % 2]).tick(16))

    @blk.vector
    def _(vector):
        for tt in range(NT):
            xt = x_res[:, tt, :]
            n0 = V_X * tt

            def step(ins, k=[0]):
                ins.then_inc(*(s_vx).tick(1))
                k[0] += 1
                vector.wait_ge((s_vx).h, (s_vx).base + (n0 + k[0]))

            step(vector.reduce_max(stt[:, 0:1], xt, AX.X, apply_absolute_value=True))
            step(vector.tensor_scalar_max(stt[:, 0:1], stt[:, 0:1], 1e-5))
            step(vector.reciprocal(stt[:, 1:2], stt[:, 0:1]))
            step(vector.tensor_scalar(stt[:, 2:3], stt[:, 1:2], 127.0, None, ALU.mult))
            step(vector.tensor_scalar(stt[:, 3:4], stt[:, 0:1], 1.0 / 127.0, None, ALU.mult))
            step(vector.tensor_tensor(stt[:, 4:5], stt[:, 3:4], sbc[:, 2:3], ALU.mult))
            if tt > 0:
                vector.wait_ge((s_tx).h, (s_tx).base + (8 * tt))
                vector.wait_ge((s_sg).h, (s_sg).base + (8 * tt))
            qt = h_t[:, 0:D]
            step(vector.tensor_scalar(qt, xt, stt[:, 2:3], MAGIC, ALU.mult, ALU.add))
            step(vector.tensor_scalar(qt, qt, MAGIC, -128.0, ALU.subtract, ALU.max))
            step(vector.tensor_scalar_min(xq_tm[:], qt, 127.0))
            for g in range(2):
                vector.wait_ge((s_tx).h, (s_tx).base + (8 * tt + 4 * (g + 1)))
                srcp = ps_t[g % 2][:].rearrange("p (c k) -> p c k", c=4)
                vector.tensor_copy(xq_fm[:, 4 * g:4 * g + 4, :], srcp).then_inc(*(s_vxe).tick(1))
            m0 = 8 * tt

            def hstep(ins, k=[0]):
                ins.then_inc(*(s_vh).tick(1))
                k[0] += 1
                vector.wait_ge((s_vh).h, (s_vh).base + (m0 + k[0]))

            vector.wait_ge((s_sg).h, (s_sg).base + (8 * (tt + 1)))
            hstep(vector.reduce_max(stt[:, 5:6], h_t[:], AX.X, apply_absolute_value=True))
            hstep(vector.tensor_scalar_max(stt[:, 5:6], stt[:, 5:6], 1e-5))
            hstep(vector.reciprocal(stt[:, 6:7], stt[:, 5:6]))
            hstep(vector.tensor_scalar(stt[:, 7:8], stt[:, 6:7], 127.0, None, ALU.mult))
            if tt > 0:
                vector.wait_ge((s_th).h, (s_th).base + (32 * tt))
            hstep(vector.tensor_scalar(h_t[:], h_t[:], stt[:, 7:8], MAGIC, ALU.mult, ALU.add))
            hstep(vector.tensor_scalar(h_t[:], h_t[:], MAGIC, -128.0, ALU.subtract, ALU.max))
            hstep(vector.tensor_scalar_min(hq_tm[:], h_t[:], 127.0))
            hstep(vector.tensor_scalar(stt[:, 5:6], stt[:, 5:6], 1.0 / 127.0, None, ALU.mult))
            vector.wait_ge((s_vh).h, (s_vh).base + (m0 + 8))
            vector.tensor_tensor(stt[:, 6:7], stt[:, 5:6], sbc[:, 3:4], ALU.mult)
            for g in range(8):
                vector.wait_ge((s_th).h, (s_th).base + (32 * tt + 4 * (g + 1)))
                srcp = ps_t[g % 2][:].rearrange("p (c k) -> p c k", c=4)
                vector.tensor_copy(hq_fm[:, 4 * g:4 * g + 4, :], srcp).then_inc(*(s_vhq).tick(1))
            for oh in range(2):
                vector.wait_ge((s_tm2).h, (s_tm2).base + (32 * (tt + 1)))
                vector.scalar_tensor_tensor(
                    x_res[:, tt, 512 * oh:512 * oh + 512],
                    ps_m2[oh][:], stt[:, 6:7],
                    x_res[:, tt, 512 * oh:512 * oh + 512],
                    ALU.mult, ALU.add).then_inc(*(s_vr2).tick(1))

    @blk.scalar
    def _(scalar):
        for tt in range(NT):
            for oc in range(8):
                k = 8 * tt + oc
                scalar.wait_ge((s_tm1).h, (s_tm1).base + (k + 1))
                scalar.activation(
                    out=h_t[:, 512 * oc:512 * oc + 512],
                    in_=ps_m1[oc % 2][:], func=AF.Gelu,
                    scale=stt[:, 4:5]).then_inc(*(s_sg).tick(1))

    @blk.tensor
    def _(tensor):
        tensor.wait_ge((wprep[1]["sve"]).h, (wprep[1]["sve"]).base + (wprep[1]["sve_total"]))
        tensor.wait_ge((wprep[1]["sdw"]).h, (wprep[1]["sdw"]).base + (64 * wprep[1]["sve_total"]))
        for tt in range(NT):
            tensor.wait_ge((s_vx).h, (s_vx).base + (V_X * tt + 9))
            if tt == 0:
                pass
            for c in range(8):
                g, j = c // 4, c % 4
                gg = 2 * tt + g
                if gg >= 2:
                    tensor.wait_ge((s_vxe).h, (s_vxe).base + (gg - 1))
                nc.tensor.matmul(
                    ps_t[g % 2][:, 128 * j:128 * j + 128],
                    xq_tm[:, 128 * c:128 * c + 128], ident_bf[:],
                    start=True, stop=True, skip_group_check=True).then_inc(*(s_tx).tick(1))
            tensor.wait_ge((s_vxe).h, (s_vxe).base + (2 * tt + 2))
            for oc in range(8):
                k = 8 * tt + oc
                if k >= 2:
                    tensor.wait_ge((s_sg).h, (s_sg).base + (k - 1))
                for ci in range(DC):
                    ins = nc.tensor.matmul(
                        ps_m1[oc % 2][:], xq_fm[:, ci, :],
                        w1qT[:, ci, 512 * oc:512 * oc + 512],
                        start=(ci == 0), stop=(ci == DC - 1),
                        skip_group_check=True)
                ins.then_inc(*(s_tm1).tick(1))
            tensor.wait_ge((s_vh).h, (s_vh).base + (8 * tt + 7))
            for c in range(32):
                g, j = c // 4, c % 4
                gg = 8 * tt + g
                if gg >= 2:
                    tensor.wait_ge((s_vhq).h, (s_vhq).base + (gg - 1))
                nc.tensor.matmul(
                    ps_t[g % 2][:, 128 * j:128 * j + 128],
                    hq_tm[:, 128 * c:128 * c + 128], ident_bf[:],
                    start=True, stop=True, skip_group_check=True).then_inc(*(s_th).tick(1))
            tensor.wait_ge((s_vhq).h, (s_vhq).base + (8 * tt + 8))
            if tt > 0:
                tensor.wait_ge((s_vr2).h, (s_vr2).base + (2 * tt))
            for i2 in range(32):
                k = 32 * tt + i2
                tensor.wait_ge((s_wb[k % 2]).h, (s_wb[k % 2]).base + (16 * (k // 2 + 1)))
                for oh in range(2):
                    ins = nc.tensor.matmul(
                        ps_m2[oh][:], hq_fm[:, i2, :],
                        w2b[k % 2][:, 512 * oh:512 * oh + 512],
                        start=(i2 == 0), stop=(i2 == 31),
                        skip_group_check=True)
                ins.then_inc(*(s_tm2).tick(1))

    sc.close()


def _emit_wT(p, blk, srcname, dstname, ntl, C, identt, tag):
    nc = p.nc
    nblk = C // 128
    ngrp = nblk // 4
    sc = contextlib.ExitStack()
    wstg = [p.sb(f"{tag}_wstg{i}", [128, C], F32, sc) for i in range(2)]
    tstg = p.sb(f"{tag}_tstg", [128, 512], F32, sc)
    sldA = p.rs("ga_ldA"); sldB = p.rs("ga_ldB")
    s_tt = p.rs("ga_tt")
    sve = p.rs("ga_ve")
    sdw = p.rs("ga_dw")
    ps = [sc.enter_context(nc.psum_tensor(f"{tag}_ps{i}", [128, 512], F32))
          for i in range(2)]

    @blk.sync
    def _(sync):
        srcw = p.ap[srcname].rearrange("(t p) f -> p t f", p=128)
        for t in range(ntl):
            if t >= 2:
                sync.wait_ge((s_tt).h, (s_tt).base + (nblk * (t - 1)))
            sync.dma_start(out=wstg[t % 2][:], in_=srcw[:, t, :]).then_inc(*((sldA, sldB)[t % 2]).tick(16))

    @blk.tensor
    def _(tensor):
        for t in range(ntl):
            tensor.wait_ge(((sldA, sldB)[t % 2]).h, ((sldA, sldB)[t % 2]).base + (16 * (t // 2 + 1)))
            for c in range(nblk):
                g, j = c // 4, c % 4
                gg = t * ngrp + g
                if gg >= 2:
                    tensor.wait_ge((sve).h, (sve).base + (gg - 1))
                nc.tensor.transpose(
                    ps[g % 2][:, 128 * j:128 * j + 128],
                    wstg[t % 2][:, 128 * c:128 * c + 128], identt[:]
                ).then_inc(*(s_tt).tick(1))

    @blk.vector
    def _(vector):
        for t in range(ntl):
            for g in range(ngrp):
                gg = t * ngrp + g
                vector.wait_ge((s_tt).h, (s_tt).base + (nblk * t + 4 * (g + 1)))
                vector.wait_ge((sdw).h, (sdw).base + (64 * gg))
                vector.tensor_copy(tstg[:], ps[g % 2][:]).then_inc(*(sve).tick(1))

    @blk.scalar
    def _(scalar):
        for t in range(ntl):
            for g in range(ngrp):
                gg = t * ngrp + g
                scalar.wait_ge((sve).h, (sve).base + (gg + 1))
                for j in range(4):
                    c = 4 * g + j
                    dst = p.ap[dstname][128 * c:128 * c + 128, 128 * t:128 * t + 128]
                    scalar.dma_start(out=dst, in_=tstg[:, 128 * j:128 * j + 128]
                                     ).then_inc(*(sdw).tick(16))
    sc.close()
    return sdw, sdw.ctr


def _emit_x2gather(p, blk, x_res, x_fm3, identt, tag="x2g"):
    nc = p.nc
    sc = contextlib.ExitStack()
    fms = [p.sb(f"{tag}_fms{i}", [128, DC, 128], BF16, sc) for i in range(2)]
    stx = p.rs("ga_tx")
    sve = p.rs("ga_ve")
    sdin = [p.rs("ga_din0"), p.rs("ga_din1")]
    sgc = p.rs("ga_gc")
    sdrb = p.rs("ga_drb")
    ps = [sc.enter_context(nc.psum_tensor(f"{tag}_ps{i}", [128, 1024], F32))
          for i in range(2)]

    @blk.tensor
    def _(tensor):
        for tt in range(NT):
            if tt >= 2:
                tensor.wait_ge((sve).h, (sve).base + (tt - 1))
            for c in range(DC):
                ins = nc.tensor.transpose(
                    ps[tt % 2][:, 128 * c:128 * c + 128],
                    x_res[:, tt, 128 * c:128 * c + 128], identt[:])
            ins.then_inc(*(stx).tick(1))

    @blk.vector
    def _(vector):
        for tt in range(NT):
            vector.wait_ge((stx).h, (stx).base + (tt + 1))
            if tt >= 2:
                vector.wait_ge((sdin[tt % 2]).h, (sdin[tt % 2]).base + (128 * (tt // 2)))
            srcp = ps[tt % 2][:].rearrange("p (c k) -> p c k", c=DC)
            vector.tensor_copy(fms[tt % 2][:], srcp).then_inc(*(sve).tick(1))

    @blk.sync
    def _(sync):
        for tt in range(NT):
            sync.wait_ge((sve).h, (sve).base + (tt + 1))
            for c in range(DC):
                dst = p.ap["x2cc_in"][128 * c:128 * c + 128, 128 * tt:128 * tt + 128]
                sync.dma_start(out=dst, in_=fms[tt % 2][:, c, :]).then_inc(*(sdin[tt % 2]).tick(16))

    @blk.gpsimd
    def _(gp):
        gp.wait_ge((sdin[0]).h, (sdin[0]).base + (128 * NT // 2))
        gp.wait_ge((sdin[1]).h, (sdin[1]).base + (128 * NT // 2))
        gp.collective_compute(
            "AllGather", ALU.bypass, replica_groups=RG,
            ins=[p.ap["x2cc_in"][:]], outs=[p.ap["x2cc_out"][:]],
        ).then_inc(*(sgc).tick(1))

    @blk.sync
    def _(sync):
        sync.wait_ge((sgc).h, (sgc).base + (1))
        for rank in range(2):
            for c in range(DC):
                srcd = p.ap["x2cc_out"][rank][128 * c:128 * c + 128, :]
                sync.dma_start(out=x_fm3[:, c, HALF * rank:HALF * rank + HALF],
                               in_=srcd).then_inc(*(sdrb).tick(16))
        sync.wait_ge((sdrb).h, (sdrb).base + (256))
    sc.close()


PW = HALF + 513


def _emit_inner(p, blk, x_res, identt, ident_bf, tri_pt, tril_pt, cst):
    nc = p.nc
    s_x3 = p.rs("in_x3")

    s_lnc = p.rs("ga_v")

    @blk.sync
    def _(sync):
        sync.wait_ge((s_lnc).h, (s_lnc).base + (0))
        x3d = p.ap["x3_d"].rearrange("(t p) d -> p t d", p=128)
        for tt in range(NT):
            sync.dma_start(out=x3d[:, tt, :], in_=x_res[:, tt, :]).then_inc(*(s_x3).tick(16))
        sync.wait_ge((s_x3).h, (s_x3).base + (16 * NT))
    p.barrier(blk)

    for il in range(2):
        _emit_inner_attn(p, blk, x_res, identt, ident_bf, tri_pt, tril_pt, cst, il)
        p.barrier(blk)
        _emit_inner_ffn(p, blk, x_res, identt, ident_bf, cst, il)
        p.barrier(blk)

    sc = contextlib.ExitStack()
    x3s = [p.sb(f"in_x3s{i}", [128, D], F32, sc) for i in range(2)]
    s_xb = [p.rs("in_xbA"), p.rs("in_xbB")]
    s_xv = p.rs("in_xv")

    @blk.sync
    def _(sync):
        x3d = p.ap["x3_d"].rearrange("(t p) d -> p t d", p=128)
        for tt in range(NT):
            if tt >= 2:
                sync.wait_ge((s_xv).h, (s_xv).base + (tt - 1))
            sync.dma_start(out=x3s[tt % 2][:], in_=x3d[:, tt, :]).then_inc(
                *(s_xb[tt % 2]).tick(16))

    s_frc = p.rs("f4_vr")

    @blk.vector
    def _(vector):
        vector.wait_ge((s_frc).h, (s_frc).base + (0))
        for tt in range(NT):
            vector.wait_ge((s_xb[tt % 2]).h, (s_xb[tt % 2]).base + (16 * (tt // 2 + 1)))
            vector.tensor_tensor(x_res[:, tt, :], x_res[:, tt, :], x3s[tt % 2][:],
                                 ALU.add).then_inc(*(s_xv).tick(1))
    sc.close()


def _norm_rot(p, blk, buf, width, toff, ta, tb, sccol, B, ready):
    nc = p.nc
    s_sq = p.rs("nr_sq")
    s_tn = p.rs("nr_tn")
    s_vn = p.rs("nr_vn")
    s_tb = p.rs("nr_tb")
    s_vtr = p.rs("p4_tl2")
    nch = cdiv(width, 512)
    n2, nrb, rt1, rt2, sqs = B["n2"], B["nrb"], B["rt1"], B["rt2"], B["sqs"]
    bo33, hsel, qs8 = B["bo33"], B["hself"], B["qs8"]
    ps_n = B["ps_n"]
    ps_b = B["ps_b"]

    @blk.scalar
    def _(scalar):
        scalar.wait_ge(ready[0], ready[1])
        scalar.wait_ge((s_tn).h, (s_tn).base + (0))
        scalar.wait_ge((s_vn).h, (s_vn).base + (0))
        scalar.activation(out=sqs[:, 0:width], in_=buf[:, 0:width],
                          func=AF.Square).then_inc(*(s_sq).tick(1))
        scalar.wait_ge((s_vn).h, (s_vn).base + (nch))
        scalar.activation(out=n2[:, 0:width], in_=n2[:, 0:width],
                          func=AF.Sqrt).then_inc(*(s_sq).tick(1))

    @blk.tensor
    def _(tensor):
        tensor.wait_ge((s_sq).h, (s_sq).base + (1))
        for cch in range(nch):
            w = min(512, width - 512 * cch)
            if cch >= 1:
                tensor.wait_ge((s_vn).h, (s_vn).base + (cch))
            nc.tensor.matmul(ps_n[0:2, 0:w], bo33[:],
                             sqs[:, 512 * cch:512 * cch + w],
                             start=True, stop=True,
                             skip_group_check=True).then_inc(*(s_tn).tick(1))
        tensor.wait_ge((s_vn).h, (s_vn).base + (nch + 2))
        for cch in range(nch):
            w = min(512, width - 512 * cch)
            if cch >= 1:
                tensor.wait_ge((s_vn).h, (s_vn).base + (nch + 2 + cch))
            nc.tensor.matmul(ps_b[:, 0:w], hsel[:],
                             n2[:, 512 * cch:512 * cch + w],
                             start=True, stop=True,
                             skip_group_check=True).then_inc(*(s_tb).tick(1))
        tensor.wait_ge(p.s_rsw.h, 16)
        base2 = 2 * nch + 2
        for cch in range(nch):
            w = min(512, width - 512 * cch)
            tensor.wait_ge((s_vn).h, (s_vn).base + (base2 + 2 if cch == 0
                                                    else base2 + 3 + cch))
            nc.tensor.matmul(ps_b[:, 0:w], p.rsw[:],
                             buf[:, 512 * cch:512 * cch + w],
                             start=True, stop=True,
                             skip_group_check=True).then_inc(*(s_tb).tick(1))

    @blk.vector
    def _(vector):
        vector.wait_ge((s_vtr).h, (s_vtr).base + (0))
        vector.wait_ge(p.p4_qs_ready[0], p.p4_qs_ready[1])
        for cch in range(nch):
            w = min(512, width - 512 * cch)
            vector.wait_ge((s_tn).h, (s_tn).base + (cch + 1))
            vector.tensor_copy(n2[:, 512 * cch:512 * cch + w],
                               ps_n[0:2, 0:w]).then_inc(*(s_vn).tick(1))
        vector.wait_ge((s_sq).h, (s_sq).base + (2))
        vector.tensor_scalar(n2[:, 0:width], n2[:, 0:width], 1e-12, None,
                             ALU.max).then_inc(*(s_vn).tick(1))
        vector.wait_ge((s_vn).h, (s_vn).base + (nch + 1))
        with nc.allow_low_precision(reason="l2 norms tolerate reduced precision"):
            vector.reciprocal(n2[:, 0:width], n2[:, 0:width]).then_inc(*(s_vn).tick(1))
        for cch in range(nch):
            w = min(512, width - 512 * cch)
            vector.wait_ge((s_tb).h, (s_tb).base + (cch + 1))
            vector.tensor_copy(nrb[:, 512 * cch:512 * cch + w],
                               ps_b[:, 0:w]).then_inc(*(s_vn).tick(1))
        bufv = buf[:, 0:width]
        base = 2 * nch + 2
        vector.wait_ge((s_vn).h, (s_vn).base + (base))
        vector.tensor_tensor(bufv, bufv, nrb[:, 0:width], ALU.mult).then_inc(*(s_vn).tick(1))
        vector.wait_ge((s_vn).h, (s_vn).base + (base + 1))
        vector.tensor_scalar(bufv, bufv, qs8[:, sccol:sccol + 1], None,
                             ALU.mult).then_inc(*(s_vn).tick(1))
        vector.wait_ge((s_vn).h, (s_vn).base + (base + 2))
        vector.tensor_tensor(rt1[:, 0:width], bufv, ta[:, toff:toff + width],
                             ALU.mult).then_inc(*(s_vn).tick(1))
        for cch in range(nch):
            w = min(512, width - 512 * cch)
            vector.wait_ge((s_tb).h, (s_tb).base + (nch + cch + 1))
            vector.tensor_tensor(rt2[:, 512 * cch:512 * cch + w],
                                 ps_b[:, 0:w],
                                 tb[:, toff + 512 * cch:toff + 512 * cch + w],
                                 ALU.mult).then_inc(*(s_vn).tick(1))
        vector.wait_ge((s_vn).h, (s_vn).base + (base + 3 + nch))
        vector.tensor_tensor(bufv, rt1[:, 0:width], rt2[:, 0:width],
                             ALU.add).then_inc(*(s_vn).tick(1))
    return s_vn




def _emit_inner_attn(p, blk, x_res, identt, ident_bf, tri_pt, tril_pt, cst, il):
    nc = p.nc
    sc = contextlib.ExitStack()

    x_fm4 = p.sb(f"p4{il}_xfm", [128, DC, 2 + PW], BF16, sc)
    q4 = p.sb(f"p4{il}_q", [128, HALF], BF16, sc)
    k4 = p.sb(f"p4{il}_k", [128, WINDOW + HALF], BF16, sc)
    v4 = p.sb(f"p4{il}_v", [128, 20 * 128], BF16, sc)
    wst = p.sb(f"p4{il}_wst", [128, DC, 128], F32, sc)
    wbf = [p.sb(f"p4{il}_wbf{i}", [128, DC, 128], BF16, sc) for i in range(3)]
    tA = {nm: p.sb(f"p4{il}_t{nm}", [128, HALF if nm in ("aq", "bq") else PW],
                   BF16, sc)
          for nm in ("aq", "bq", "ak", "bk")}
    qs8 = p.sb(f"p4{il}_qs8", [128, 2], F32, sc)
    n2 = p.sb(f"p4{il}_n2", [2, PW], F32, sc)
    nrb = p.sb(f"p4{il}_nrb", [128, PW], BF16, sc)
    rt1 = p.sb(f"p4{il}_rt1", [128, PW], BF16, sc)
    rt2 = p.sb(f"p4{il}_rt2", [128, PW], BF16, sc)
    sqs = rt1
    bo33 = p.sb(f"p4{il}_bo33", [128, 2], BF16, sc)
    hsel = p.sb(f"p4{il}_hsel", [2, 128], BF16, sc)
    hself = p.sb(f"p4{il}_hself", [2, 128], F32, sc)
    or64 = p.sb(f"p4{il}_or64", [1, 64], BF16, sc)
    orf = p.sb(f"p4{il}_orf", [1, 64], F32, sc)
    oc1 = p.sb(f"p4{il}_oc1", [128, 1], BF16, sc)
    p_f = p.sb(f"p4{il}_pf", [128, 640], F32, sc)
    p_bf = [p.sb(f"p4{il}_pbf{i}", [128, 5, 128], BF16, sc) for i in range(2)]
    negm = p.sb(f"p4{il}_negm", [128, 2], F32, sc)
    ot4 = p.sb(f"p4{il}_ot4", [128, NHP, HALF], BF16, sc)
    l_sb = p.sb(f"p4{il}_lsb", [33, HALF], BF16, sc)
    bc_t = p.sb(f"p4{il}_bct", [128, 512], BF16, sc)
    wo4 = p.sb(f"p4{il}_wo4", [128, 4, 512], BF16, sc)

    ps_sa = sc.enter_context(nc.psum_tensor(f"p4a{il}_sa", [128, 1024], F32))
    ps_sb = sc.enter_context(nc.psum_tensor(f"p4a{il}_sb", [128, 1024], F32))
    ps_pt = sc.enter_context(nc.psum_tensor(f"p4a{il}_pt", [128, 640], F32))
    ps_ot = sc.enter_context(nc.psum_tensor(f"p4a{il}_ot", [128, 512], F32))
    ps_l = sc.enter_context(nc.psum_tensor(f"p4a{il}_l", [33, 512], F32))

    B = dict(n2=n2, nrb=nrb, rt1=rt1, rt2=rt2, sqs=sqs, bo33=bo33,
             hsel=hsel, hself=hself, qs8=qs8, ps_n=ps_l, ps_b=ps_ot)
    p.p4_qs_ready = None

    s_tx = p.rs("ga_tx"); s_vx = p.rs("ga_vx")

    @blk.tensor
    def _(tensor):
        for role in ("ga_vwo", "ga_v", "f4_vr"):
            rr = p.rs(role)
            tensor.wait_ge(rr.h, rr.base + 0)
        for tt in range(NT):
            if tt >= 2:
                tensor.wait_ge((s_vx).h, (s_vx).base + (tt - 1))
            pst = ps_sa if tt % 2 == 0 else ps_sb
            for c in range(DC):
                ins = nc.tensor.transpose(
                    pst[:, 128 * c:128 * c + 128],
                    x_res[:, tt, 128 * c:128 * c + 128], identt[:])
            ins.then_inc(*(s_tx).tick(1))

    @blk.vector
    def _(vector):
        vector.memset(x_fm4[:, :, 0:2], 0.0)
        vector.memset(bo33[0:64, 0:1], 1.0)
        vector.memset(bo33[64:128, 0:1], 0.0)
        vector.memset(bo33[0:64, 1:2], 0.0)
        vector.memset(bo33[64:128, 1:2], 1.0)
        vector.memset(or64[:], 1.0)
        vector.memset(orf[:], 1.0)
        vector.memset(oc1[:], 1.0)
        for tt in range(NT):
            vector.wait_ge((s_tx).h, (s_tx).base + (tt + 1))
            srcp = (ps_sa if tt % 2 == 0 else ps_sb)[:].rearrange("p (c k) -> p c k", c=DC)
            vector.tensor_copy(x_fm4[:, :, 515 + 128 * tt:515 + 128 * tt + 128],
                               srcp).then_inc(*(s_vx).tick(1))

    s_din = p.rs("ga_din0"); s_gc = p.rs("ga_gc"); s_drb = p.rs("ga_drb")
    s_vh = p.rs("p4_vh"); s_tld = p.rs("p4_tld")
    s_tl2 = p.rs("p4_tl2")
    p.p4_qs_ready = (s_vh.h, s_vh.base + 2)

    @blk.sync
    def _(sync):
        sync.wait_ge((s_vx).h, (s_vx).base + (NT))
        for c in range(DC):
            dst = p.ap["shcc_in"][128 * c:128 * c + 128, :]
            sync.dma_start(out=dst, in_=x_fm4[:, c, 2050:2563]).then_inc(*(s_din).tick(16))

    @blk.gpsimd
    def _(gp):
        gp.wait_ge((s_din).h, (s_din).base + (16 * DC))
        gp.collective_compute(
            "AllGather", ALU.bypass, replica_groups=RG,
            ins=[p.ap["shcc_in"][:]], outs=[p.ap["shcc_out"][:]],
        ).then_inc(*(s_gc).tick(1))

    @blk.sync
    def _(sync):
        sync.wait_ge((s_gc).h, (s_gc).base + (1))
        for arm in sync.switch_core_id(n=128):
            pr = 1 - (arm.logical % 2)
            srcd = p.ap["shcc_out"][pr]
            for c in range(DC):
                sync.dma_start(out=x_fm4[:, c, 2:515],
                               in_=srcd[128 * c:128 * c + 128, :]).then_inc(s_drb.h, 16)
        s_drb.ctr += 16 * DC
        for half128 in range(2):
            b = 64 * half128
            sync.dma_start(out=qs8[b:b + 64, 0:1],
                           in_=p.ap["i_qs"][il][:, None]).then_inc(*(s_tld).tick(16))
            sync.dma_start(out=qs8[b:b + 64, 1:2],
                           in_=p.ap["i_ks"][il][:, None]).then_inc(*(s_tld).tick(16))
        for nm in ("aq", "bq", "ak", "bk"):
            sync.dma_start(out=tA[nm][:], in_=p.ap[f"rot_{nm}"][:]).then_inc(*(s_tl2).tick(16))
        sync.dma_start(out=hsel[:], in_=p.ap["hsel_bf"][:]).then_inc(*(s_tl2).tick(16))
        sync.dma_start(out=hself[:], in_=p.ap["hsel_f"][:]).then_inc(*(s_tl2).tick(16))

    @blk.vector
    def _(vector):
        vector.wait_ge((s_drb).h, (s_drb).base + (16 * DC))
        vector.wait_ge(p.s_hv0.h, 16)
        vector.tensor_scalar(x_fm4[:, :, 2:515], x_fm4[:, :, 2:515],
                             p.hv[:, 0:1], None, ALU.mult).then_inc(*(s_vh).tick(1))
        vector.wait_ge((s_tld).h, (s_tld).base + (64))
        vector.tensor_scalar(qs8[:, 0:1], qs8[:, 0:1], QK_SCALE, None,
                             ALU.mult).then_inc(*(s_vh).tick(1))

    for hp in range(NHP):
        _inner_attn_hp(p, blk, hp, il, x_res, x_fm4, q4, k4, v4, wst, wbf,
                       tA, B, p_f, p_bf, negm, ot4, l_sb, bc_t, oc1, or64,
                       tri_pt, tril_pt, cst, identt, ident_bf,
                       ps_sa, ps_sb, ps_pt, ps_ot, ps_l,
                       s_vh)

    s_wld = p.rs("ga_wld"); s_vw = p.rs("ga_vw")
    s_tw = p.rs("ga_tw"); s_vwo = p.rs("ga_vwo")
    s_ta = p.rs("ga_tav"); s_vo = p.rs("ga_vo"); s_vb = p.rs("ga_vb")

    @blk.sync
    def _(sync):
        sync.wait_ge((s_vb).h, (s_vb).base + (0))
        srcw = p.ap["i_wo"][il].rearrange("(t p) o -> p t o", p=128)
        for oh in range(2):
            for t4 in range(4):
                e = 4 * oh + t4
                if e > 0:
                    sync.wait_ge((s_vw).h, (s_vw).base + (e))
                for j in range(4):
                    sync.dma_start(
                        out=wst[:, j, :],
                        in_=srcw[:, t4, 512 * oh + 128 * j:512 * oh + 128 * j + 128],
                    ).then_inc(*(s_wld).tick(16))

    @blk.vector
    def _(vector):
        for oh in range(2):
            for t4 in range(4):
                e = 4 * oh + t4
                vector.wait_ge((s_wld).h, (s_wld).base + (64 * (e + 1)))
                if oh > 0:
                    vector.wait_ge((s_tw).h, (s_tw).base + (16 * oh))
                vector.tensor_copy(
                    wo4[:, t4, :],
                    wst[:, 0:4, :].rearrange("p c k -> p (c k)")).then_inc(*(s_vw).tick(1))
            for tt in range(NT):
                g = 16 * oh + tt
                vector.wait_ge((s_tw).h, (s_tw).base + (g + 1))
                xt = x_res[:, tt, 512 * oh:512 * oh + 512]
                pst = (ps_sa if g % 2 == 0 else ps_sb)[:, 0:512]
                vector.tensor_tensor(xt, pst, xt, ALU.add).then_inc(*(s_vwo).tick(1))

    @blk.tensor
    def _(tensor):
        tensor.wait_ge((s_vo).h, (s_vo).base + (0))
        tensor.wait_ge((s_vb).h, (s_vb).base + (0))
        for oh in range(2):
            tensor.wait_ge((s_vw).h, (s_vw).base + (4 * (oh + 1)))
            for tt in range(NT):
                g = 16 * oh + tt
                if g >= 2:
                    tensor.wait_ge((s_vwo).h, (s_vwo).base + (g - 1))
                dst = (ps_sa if g % 2 == 0 else ps_sb)[:, 0:512]
                for c4 in range(4):
                    ins = nc.tensor.matmul(
                        dst, ot4[:, c4, 128 * tt:128 * tt + 128],
                        wo4[:, c4, :],
                        start=(c4 == 0), stop=(c4 == 3), skip_group_check=True)
                ins.then_inc(*(s_tw).tick(1))
    sc.close()


def _inner_attn_hp(p, blk, hp, il, x_res, x_fm4, q4, k4, v4, wst, wbf,
                   tA, B, p_f, p_bf, negm, ot4, l_sb, bc_t, oc1, or64,
                   tri_pt, tril_pt, cst, identt, ident_bf,
                   ps_sa, ps_sb, ps_pt, ps_ot, ps_l, s_vh):
    nc = p.nc
    KW = WINDOW + HALF
    hsel2 = B["hsel"]
    s_wld = p.rs("ga_wld"); s_vw = p.rs("ga_vw")
    s_tq = p.rs("ga_tq"); s_vq = p.rs("ga_vq")
    s_tv = p.rs("ga_tv"); s_vv = p.rs("ga_vv")

    @blk.sync
    def _(sync, hp=hp):
        for wi in range(3):
            sync.wait_ge((s_vw).h, (s_vw).base + (wi))
            col = (0, 512, 1024)[wi] + 128 * hp
            srcw = p.ap["i_wqkv"][il].rearrange("(c p) o -> p c o", p=128)
            for c in range(DC):
                sync.dma_start(out=wst[:, c, :],
                               in_=srcw[:, c, col:col + 128]).then_inc(*(s_wld).tick(16))

    @blk.vector
    def _(vector, hp=hp):
        for wi in range(3):
            vector.wait_ge((s_wld).h, (s_wld).base + (128 * (wi + 1)))
            vector.wait_ge((s_tq).h, (s_tq).base + (0))
            vector.tensor_copy(wbf[wi][:], wst[:]).then_inc(*(s_vw).tick(1))
        nq = (cdiv(HALF, 512), cdiv(KW, 512), cdiv(KW, 512))
        e = 0
        for wi in range(3):
            for qb in range(nq[wi]):
                vector.wait_ge((s_tq).h, (s_tq).base + (e + 1))
                w = min(512, (HALF if wi == 0 else KW) - 512 * qb)
                srcps = (ps_sa if e % 2 == 0 else ps_sb)[:, 0:w]
                if wi == 0:
                    vector.tensor_copy(q4[:, 512 * qb:512 * qb + w], srcps).then_inc(*(s_vq).tick(1))
                elif wi == 1:
                    vector.tensor_copy(k4[:, 512 * qb:512 * qb + w], srcps).then_inc(*(s_vq).tick(1))
                else:
                    vector.tensor_copy(v4[:, 512 * qb:512 * qb + w], srcps).then_inc(*(s_vq).tick(1))
                e += 1
        for g in range(5):
            vector.wait_ge((s_tv).h, (s_tv).base + (g + 1))
            n4 = 4
            srcp = (ps_pt if g % 2 == 0 else ps_ot)[:, 0:512]
            vector.tensor_copy(v4[:, 512 * g:512 * g + 512], srcp).then_inc(*(s_vv).tick(1))

    @blk.tensor
    def _(tensor, hp=hp):
        if hp == 0:
            tensor.wait_ge((s_vh).h, (s_vh).base + (1))
        nq = (cdiv(HALF, 512), cdiv(KW, 512), cdiv(KW, 512))
        offs = (515, 3, 3)
        e = 0
        for wi in range(3):
            tensor.wait_ge((s_vw).h, (s_vw).base + (wi + 1))
            for qb in range(nq[wi]):
                if e >= 2:
                    tensor.wait_ge((s_vq).h, (s_vq).base + (e - 1))
                w = min(512, (HALF if wi == 0 else KW) - 512 * qb)
                dst = (ps_sa if e % 2 == 0 else ps_sb)[:, 0:w]
                base = offs[wi] + 512 * qb
                for c in range(DC):
                    off = base if c < 4 else base - 1
                    ins = nc.tensor.matmul(
                        dst, wbf[wi][:, c, :], x_fm4[:, c, off:off + w],
                        start=(c == 0), stop=(c == DC - 1), skip_group_check=True)
                ins.then_inc(*(s_tq).tick(1))
                e += 1
        tensor.wait_ge((s_vq).h, (s_vq).base + (14))
        for g in range(5):
            if g >= 2:
                tensor.wait_ge((s_vv).h, (s_vv).base + (g - 1))
            for t4 in range(4):
                tt = 4 * g + t4
                ins = nc.tensor.matmul(
                    (ps_pt if g % 2 == 0 else ps_ot)[:, 128 * t4:128 * t4 + 128],
                    v4[:, 128 * tt:128 * tt + 128], ident_bf[:],
                    start=True, stop=True, skip_group_check=True)
            ins.then_inc(*(s_tv).tick(1))


    _norm_rot(p, blk, q4, HALF, 0, tA["aq"], tA["bq"], 0, B,
              ready=(s_vq.h, s_vq.base + 4))
    _norm_rot(p, blk, k4, KW, 1, tA["ak"], tA["bk"], 1, B,
              ready=(s_vq.h, s_vq.base + 9))

    s_ts = p.rs("ga_ts"); s_vm = p.rs("ga_vm"); s_se = p.rs("ga_se")
    s_tp = p.rs("p4_tp"); s_vp = p.rs("p4_vp")
    s_ta = p.rs("ga_tav"); s_vo = p.rs("ga_vo")
    s_vl = p.rs("p4_vl"); s_tb = p.rs("ga_tb"); s_vb = p.rs("ga_vb")
    s_vnr = p.rs("nr_vn")

    nunit = 2 * NT

    @blk.tensor
    def _(tensor, hp=hp):
        tensor.wait_ge((s_vv).h, (s_vv).base + (5))
        tensor.wait_ge((s_vnr).h, (s_vnr).base + (0))
        for t in range(NT):
            for h in range(2):
                u = 2 * t + h
                pss = ps_sa if u % 2 == 0 else ps_sb
                if u >= 2:
                    tensor.wait_ge((s_se).h, (s_se).base + (u - 1))
                for kk in range(2):
                    w = 512 if kk == 0 else 128
                    ins = nc.tensor.matmul(
                        pss[:, 512 * kk:512 * kk + w],
                        q4[64 * h:64 * h + 64, 128 * t:128 * t + 128],
                        k4[64 * h:64 * h + 64, 128 * t + 512 * kk:128 * t + 512 * kk + w],
                        start=True, stop=True, tile_position=(64 * h, 0),
                        skip_group_check=True)
                ins.then_inc(*(s_ts).tick(1))
                tensor.wait_ge((s_se).h, (s_se).base + (u + 1))
                if u >= 1:
                    tensor.wait_ge((s_vp).h, (s_vp).base + (2 * u - 1))
                for j in range(5):
                    ins = nc.tensor.transpose(
                        ps_pt[:, 128 * j:128 * j + 128],
                        p_f[:, 128 * j:128 * j + 128], identt[:])
                ins.then_inc(*(s_tp).tick(1))
                tensor.wait_ge((s_vp).h, (s_vp).base + (2 * u + 2))
                if t % 4 == 0 and h == 0 and t > 0:
                    tensor.wait_ge((s_vo).h, (s_vo).base + (2 * (t // 4)))
                for j in range(5):
                    nc.tensor.matmul(
                        ps_l[32 * h:32 * h + 1, 128 * (t % 4):128 * (t % 4) + 128],
                        oc1[:], p_bf[u % 2][:, j, :],
                        start=(j == 0), stop=(j == 4), tile_position=(0, 32 * h),
                        skip_group_check=True)
                for j in range(5):
                    ins = nc.tensor.matmul(
                        ps_ot[64 * h:64 * h + 64, 128 * (t % 4):128 * (t % 4) + 128],
                        v4[:].rearrange("p (n k) -> p n k", k=128)[:, t + j, 64 * h:64 * h + 64],
                        p_bf[u % 2][:, j, :],
                        start=(j == 0), stop=(j == 4), tile_position=(0, 64 * h),
                        skip_group_check=True)
                ins.then_inc(*(s_ta).tick(1))

    @blk.vector
    def _(vector, hp=hp):
        for t in range(NT):
            for h in range(2):
                u = 2 * t + h
                pss = ps_sa if u % 2 == 0 else ps_sb
                vector.wait_ge((s_ts).h, (s_ts).base + (u + 1))
                vector.reduce_max(negm[:, 0:1], pss[:, 0:640], AX.X).then_inc(*(s_vm).tick(1))
                vector.wait_ge((s_vm).h, (s_vm).base + (2 * u + 1))
                if u >= 1:
                    vector.wait_ge((s_se).h, (s_se).base + (u))
                vector.tensor_scalar(negm[:, 1:2], negm[:, 0:1], -1.0, None,
                                     ALU.mult).then_inc(*(s_vm).tick(1))
                vector.wait_ge((s_tp).h, (s_tp).base + (u + 1))
                if u >= 2:
                    vector.wait_ge((s_ta).h, (s_ta).base + (u - 1))
                vector.tensor_copy(p_bf[u % 2][:].rearrange("p j k -> p (j k)"),
                                   ps_pt[:, 0:640]).then_inc(*(s_vp).tick(1))
                vector.wait_ge((s_vp).h, (s_vp).base + (2 * u + 1))
                m0 = p_bf[u % 2][:, 0, :]
                m4 = p_bf[u % 2][:, 4, :]
                if t < 4:
                    vector.scalar_tensor_tensor(m0, m0, p.hv[:, 0:1], tril_pt[:],
                                                ALU.mult, ALU.mult)
                    for j in range(1, 4 - t):
                        mj = p_bf[u % 2][:, j, :]
                        vector.tensor_scalar(mj, mj, p.hv[:, 0:1], None, ALU.mult)
                else:
                    vector.tensor_tensor(m0, m0, tril_pt[:], ALU.mult)
                vector.tensor_tensor(m4, m4, tri_pt[:], ALU.mult)
                vector.tensor_copy(negm[0:1, 0:1], negm[0:1, 0:1]).then_inc(*(s_vp).tick(1))
            if t % 4 == 3:
                g = t // 4
                vector.wait_ge((s_ta).h, (s_ta).base + (2 * t + 2))
                vector.tensor_copy(ot4[:, hp, 512 * g:512 * g + 512],
                                   ps_ot[:, 0:512]).then_inc(*(s_vo).tick(1))
                vector.tensor_copy(l_sb[0:1, 512 * g:512 * g + 512],
                                   ps_l[0:1, 0:512])
                vector.tensor_copy(l_sb[32:33, 512 * g:512 * g + 512],
                                   ps_l[32:33, 0:512]).then_inc(*(s_vo).tick(1))
        vector.wait_ge((s_vo).h, (s_vo).base + (8))
        with nc.allow_low_precision(reason="softmax denom tolerates bf16-grade"):
            vector.reciprocal(l_sb[0:1, :], l_sb[0:1, :])
            vector.reciprocal(l_sb[32:33, :], l_sb[32:33, :]).then_inc(*(s_vl).tick(1))

    s_gl = p.rs("ga_gl")

    @blk.sync
    def _(sync, hp=hp):
        sync.wait_ge((s_vl).h, (s_vl).base + (1))
        sync.dma_start(out=l_sb[1:2, :], in_=l_sb[32:33, :]).then_inc(*(s_gl).tick(16))

    @blk.vector
    def _(vector, hp=hp):
        for cch in range(4):
            vector.wait_ge((s_tb).h, (s_tb).base + (cch + 1))
            vector.tensor_copy(bc_t[:], ps_ot[:, 0:512]).then_inc(*(s_vb).tick(1))
            vector.wait_ge((s_vb).h, (s_vb).base + (2 * cch + 1))
            sl = ot4[:, hp, 512 * cch:512 * cch + 512]
            vector.tensor_tensor(sl, sl, bc_t[:], ALU.mult).then_inc(*(s_vb).tick(1))

    @blk.scalar
    def _(scalar, hp=hp):
        for t in range(NT):
            for h in range(2):
                u = 2 * t + h
                pss = ps_sa if u % 2 == 0 else ps_sb
                scalar.wait_ge((s_vm).h, (s_vm).base + (2 * u + 2))
                if u >= 1:
                    scalar.wait_ge((s_tp).h, (s_tp).base + (u))
                scalar.activation(out=p_f[:], in_=pss[:, 0:640], func=AF.Exp,
                                  bias=negm[:, 1:2]).then_inc(*(s_se).tick(1))

    @blk.tensor
    def _(tensor, hp=hp):
        s_tl2r = p.rs("p4_tl2")
        tensor.wait_ge((s_tl2r).h, (s_tl2r).base + (0))
        tensor.wait_ge((s_vl).h, (s_vl).base + (1))
        tensor.wait_ge((s_gl).h, (s_gl).base + (16))
        for cch in range(4):
            if cch >= 1:
                tensor.wait_ge((s_vb).h, (s_vb).base + (2 * cch))
            nc.tensor.matmul(ps_ot[:, 0:512], hsel2[:],
                             l_sb[0:2, 512 * cch:512 * cch + 512],
                             start=True, stop=True,
                             skip_group_check=True).then_inc(*(s_tb).tick(1))


def _emit_inner_ffn(p, blk, x_res, identt, ident_bf, cst, il):
    nc = p.nc
    sc = contextlib.ExitStack()
    NHC = 22

    sh = p.sb(f"f4{il}_sh", [128, D], F32, sc)
    hn = p.sb(f"f4{il}_hn", [128, D], F32, sc)
    hfm = p.sb(f"f4{il}_hfm", [128, DC, HALF], BF16, sc)
    hrow = p.sb(f"f4{il}_hrow", [1, D], F32, sc)
    g_fm = p.sb(f"f4{il}_gfm", [128, NHC, 512], BF16, sc)
    w1stg = [p.sb(f"f4{il}_w1s{i}", [128, DC, 256], F32, sc) for i in range(2)]
    w1bf = [p.sb(f"f4{il}_w1b{i}", [128, DC, 256], BF16, sc) for i in range(2)]
    w2stg = [p.sb(f"f4{il}_w2s{i}", [128, D], F32, sc) for i in range(2)]
    w2bf = [p.sb(f"f4{il}_w2b{i}", [128, D], BF16, sc) for i in range(2)]
    lgro = p.sb(f"f4{il}_lgro", [1, D], F32, sc)
    lbro = p.sb(f"f4{il}_lbro", [1, D], F32, sc)
    b2ro = p.sb(f"f4{il}_b2ro", [1, D], F32, sc)
    lg_b = p.sb(f"f4{il}_lgb", [128, D], F32, sc)
    lb_b = p.sb(f"f4{il}_lbb", [128, D], F32, sc)
    b2_b = p.sb(f"f4{il}_b2b", [128, D], F32, sc)
    b1c = p.sb(f"f4{il}_b1c", [128, 2 * NHC], F32, sc)
    gtmp = p.sb(f"f4{il}_gtmp", [128, 512], F32, sc)
    stt = p.sb(f"f4{il}_stt", [128, 8], F32, sc)
    or128 = p.sb(f"f4{il}_or128", [1, 128], F32, sc)

    ps8 = [sc.enter_context(nc.psum_tensor(f"f4{il}_ps{i}", [128, 512], F32))
           for i in range(8)]
    ps_a, ps_g, ps_t = ps8[0], ps8[1], ps8[2]

    s_sh1 = p.rs("f4_sh1")
    s_gc = p.rs("ga_gc")
    s_hrb = p.rs("f4_hrb")
    s_ld = p.rs("f4_ld")
    s_tbb = p.rs("ga_tb")
    s_vbb = p.rs("ga_vb")

    s_vwoF = p.rs("ga_vwo")

    @blk.sync
    def _(sync):
        sync.wait_ge((s_vwoF).h, (s_vwoF).base + (0))
        sync.dma_start(out=p.ap["sh1_in"][:], in_=x_res[127:128, NT - 1, :]).then_inc(
            *(s_sh1).tick(16))

    @blk.gpsimd
    def _(gp):
        gp.wait_ge((s_sh1).h, (s_sh1).base + (16))
        gp.collective_compute(
            "AllGather", ALU.bypass, replica_groups=RG,
            ins=[p.ap["sh1_in"][:]], outs=[p.ap["sh1_out"][:]],
        ).then_inc(*(s_gc).tick(1))

    @blk.sync
    def _(sync):
        sync.wait_ge((s_gc).h, (s_gc).base + (1))
        for arm in sync.switch_core_id(n=128):
            pr = 1 - (arm.logical % 2)
            sync.dma_start(out=hrow[:], in_=p.ap["sh1_out"][pr]).then_inc(s_hrb.h, 16)
        s_hrb.ctr += 16
        sync.dma_start(out=lgro[:], in_=p.ap["i_lng"][il][None, :]).then_inc(*(s_ld).tick(16))
        sync.dma_start(out=lbro[:], in_=p.ap["i_lnb"][il][None, :]).then_inc(*(s_ld).tick(16))
        sync.dma_start(out=b2ro[:], in_=p.ap["i_b2"][il][None, :]).then_inc(*(s_ld).tick(16))
        for hc in range(NHC):
            n = 128 if hc < NHC - 1 else 42
            for ag in range(2):
                srcb = p.ap["i_b1"][il][256 * hc + n * ag:256 * hc + n * ag + n, None]
                sync.dma_start(out=b1c[0:n, 2 * hc + ag:2 * hc + ag + 1],
                               in_=srcb).then_inc(*(s_ld).tick(16))

    @blk.vector
    def _(vector):
        vector.memset(or128[:], 1.0).then_inc(*(s_vbb).tick(1))
        vector.wait_ge((s_hrb).h, (s_hrb).base + (16))
        vector.tensor_scalar(hrow[:], hrow[:], p.hv[0:1, 0:1], None,
                             ALU.mult).then_inc(*(s_vbb).tick(1))
        for bi in range(3):
            dstb = (lg_b, lb_b, b2_b)[bi]
            for oh in range(2):
                vector.wait_ge((s_tbb).h, (s_tbb).base + (2 * bi + oh + 1))
                vector.tensor_copy(dstb[:, 512 * oh:512 * oh + 512],
                                   ps_t[:, 0:512]).then_inc(*(s_vbb).tick(1))

    @blk.tensor
    def _(tensor):
        tensor.wait_ge((s_ld).h, (s_ld).base + (16 * (3 + 2 * NHC)))
        for bi in range(3):
            rowb = (lgro, lbro, b2ro)[bi]
            for oh in range(2):
                k = 2 * bi + oh
                tensor.wait_ge((s_vbb).h, (s_vbb).base + (1 if k == 0 else k + 2))
                nc.tensor.matmul(ps_t[:, 0:512], or128[:],
                                 rowb[:, 512 * oh:512 * oh + 512],
                                 start=True, stop=True,
                                 skip_group_check=True).then_inc(*(s_tbb).tick(1))

    s_dsh = [p.rs("f4_dshA"), p.rs("f4_dshB")]
    s_vln = p.rs("f4_vln")
    s_sln = p.rs("f4_sln")
    s_th = p.rs("f4_th")
    s_vh2 = p.rs("f4_vh2")

    @blk.scalar
    def _(scalar):
        for tt in range(NT):
            scalar.wait_ge((s_vln).h, (s_vln).base + (12 * tt))
            if tt == 0:
                scalar.wait_ge((s_vbb).h, (s_vbb).base + (2))
            scalar.dma_start(out=sh[1:128, 512:1024],
                             in_=x_res[0:127, tt, 512:1024]).then_inc(*(s_dsh[0]).tick(16))
            if tt > 0:
                scalar.dma_start(out=sh[0:1, 512:1024],
                                 in_=x_res[127:128, tt - 1, 512:1024]).then_inc(
                    *(s_dsh[1]).tick(16))
            else:
                scalar.dma_start(out=sh[0:1, 512:1024],
                                 in_=hrow[:, 512:1024]).then_inc(*(s_dsh[1]).tick(16))
            scalar.wait_ge((s_vln).h, (s_vln).base + (12 * tt + 8))
            scalar.activation(out=stt[:, 6:7], in_=stt[:, 5:6], func=AF.Sqrt,
                              bias=cst[:, 1:2]).then_inc(*(s_sln).tick(1))

    @blk.vector
    def _(vector):
        for tt in range(NT):
            n0 = 12 * tt
            k = [0]

            def step(ins):
                ins.then_inc(*(s_vln).tick(1))
                k[0] += 1
                vector.wait_ge((s_vln).h, (s_vln).base + (n0 + k[0]))

            if tt >= 1:
                vector.wait_ge((s_th).h, (s_th).base + (2 * tt))
            vector.wait_ge((s_dsh[0]).h, (s_dsh[0]).base + (16 * (tt + 1)))
            vector.wait_ge((s_dsh[1]).h, (s_dsh[1]).base + (16 * (tt + 1)))
            if tt == 0:
                vector.wait_ge((s_vbb).h, (s_vbb).base + (8))
            step(vector.tensor_copy(sh[:, 0:512], x_res[:, tt, 0:512]))
            step(vector.tensor_tensor(hn[:], sh[:], sh[:], ALU.mult))
            step(vector.reduce_sum(stt[:, 0:1], sh[:], AX.X))
            step(vector.tensor_scalar(stt[:, 2:3], stt[:, 0:1], 1.0 / D, None, ALU.mult))
            step(vector.reduce_sum(stt[:, 1:2], hn[:], AX.X))
            step(vector.tensor_scalar(stt[:, 3:4], stt[:, 1:2], 1.0 / D, None, ALU.mult))
            step(vector.tensor_tensor(stt[:, 4:5], stt[:, 2:3], stt[:, 2:3], ALU.mult))
            step(vector.tensor_tensor(stt[:, 5:6], stt[:, 3:4], stt[:, 4:5], ALU.subtract))
            vector.wait_ge((s_sln).h, (s_sln).base + (tt + 1))
            step(vector.reciprocal(stt[:, 7:8], stt[:, 6:7]))
            step(vector.tensor_scalar(hn[:], sh[:], stt[:, 2:3], stt[:, 7:8],
                                      ALU.subtract, ALU.mult))
            step(vector.tensor_tensor(hn[:], hn[:], lg_b[:], ALU.mult))
            step(vector.tensor_tensor(hn[:], hn[:], lb_b[:], ALU.add))
            for g in range(2):
                vector.wait_ge((s_th).h, (s_th).base + (2 * tt + g + 1))
                srcp = (ps_t if g == 0 else ps_a)[:, 0:512].rearrange("p (c k) -> p c k", c=4)
                vector.tensor_copy(hfm[:, 4 * g:4 * g + 4, 128 * tt:128 * tt + 128],
                                   srcp).then_inc(*(s_vh2).tick(1))

    @blk.tensor
    def _(tensor):
        for tt in range(NT):
            tensor.wait_ge((s_vln).h, (s_vln).base + (12 * tt + 12))
            for c in range(DC):
                g = c // 4
                if 2 * tt + g >= 2:
                    tensor.wait_ge((s_vh2).h, (s_vh2).base + (2 * tt + g - 1))
                ins = nc.tensor.transpose(
                    (ps_t if g == 0 else ps_a)[:, 128 * (c % 4):128 * (c % 4) + 128],
                    hn[:, 128 * c:128 * c + 128], identt[:])
                if c % 4 == 3:
                    ins.then_inc(*(s_th).tick(1))

    for ts in range(4):
        s_w1l = [p.rs("f4_w1lA"), p.rs("f4_w1lB")]
        s_vw1 = p.rs("f4_vw1")
        s_tm1 = p.rs("f4_tm1")
        s_sg = p.rs("f4_sg")
        s_vg = p.rs("f4_vg")
        s_w2l = [p.rs("f4_w2lA"), p.rs("f4_w2lB")]
        s_vw2 = p.rs("f4_vw2")
        s_tm2 = p.rs("f4_tm2")
        s_vr = p.rs("f4_vr")
        s_vh2b = p.rs("f4_vh2")
        s_vlnF = p.rs("f4_vln")

        @blk.sync
        def _(sync, ts=ts):
            sync.wait_ge((s_vw1).h, (s_vw1).base + (0))
            sync.wait_ge((s_vw2).h, (s_vw2).base + (0))
            for hc in range(NHC):
                if hc >= 2:
                    sync.wait_ge((s_vw1).h, (s_vw1).base + (hc - 1))
                ncol = 256 if hc < NHC - 1 else 84
                srcw = p.ap["i_w1"][il].rearrange("(c p) f -> p c f", p=128)
                for c in range(DC):
                    sync.dma_start(out=w1stg[hc % 2][:, c, 0:ncol],
                                   in_=srcw[:, c, 256 * hc:256 * hc + ncol]).then_inc(
                        *(s_w1l[hc % 2]).tick(16))
            for hc in range(NHC):
                if hc >= 2:
                    sync.wait_ge((s_vw2).h, (s_vw2).base + (hc - 1))
                n = 128 if hc < NHC - 1 else 42
                sync.dma_start(out=w2stg[hc % 2][0:n, :],
                               in_=p.ap["i_w2"][il][128 * hc:128 * hc + n, :]).then_inc(
                    *(s_w2l[hc % 2]).tick(16))

        @blk.vector
        def _(vector, ts=ts):
            vector.wait_ge((s_vlnF).h, (s_vlnF).base + (0))
            vector.wait_ge((s_tm1).h, (s_tm1).base + (0))
            vector.wait_ge((s_tm2).h, (s_tm2).base + (0))
            for hc in range(NHC):
                ncol = 256 if hc < NHC - 1 else 84
                vector.wait_ge((s_w1l[hc % 2]).h, (s_w1l[hc % 2]).base + (128 * (hc // 2 + 1)))
                if hc >= 2:
                    vector.wait_ge((s_tm1).h, (s_tm1).base + (2 * (hc - 1)))
                vector.tensor_copy(w1bf[hc % 2][:, :, 0:ncol],
                                   w1stg[hc % 2][:, :, 0:ncol]).then_inc(*(s_vw1).tick(1))
                n = 128 if hc < NHC - 1 else 42
                vector.wait_ge((s_sg).h, (s_sg).base + (hc + 1))
                vector.scalar_tensor_tensor(
                    g_fm[0:n, hc, :], ps_a[0:n, :], b1c[0:n, 2 * hc:2 * hc + 1],
                    gtmp[0:n, :], ALU.add, ALU.mult).then_inc(*(s_vg).tick(1))
            for hc in range(NHC):
                n = 128 if hc < NHC - 1 else 42
                vector.wait_ge((s_w2l[hc % 2]).h, (s_w2l[hc % 2]).base + (16 * (hc // 2 + 1)))
                if hc >= 2:
                    vector.wait_ge((s_tm2).h, (s_tm2).base + (8 * (hc - 1)))
                vector.tensor_copy(w2bf[hc % 2][0:n, :],
                                   w2stg[hc % 2][0:n, :]).then_inc(*(s_vw2).tick(1))
            for e in range(8):
                t4, oh = e // 2, e % 2
                tt = 4 * ts + t4
                vector.wait_ge((s_tm2).h, (s_tm2).base + (8 * NHC))
                if e >= 1:
                    vector.wait_ge((s_vr).h, (s_vr).base + (2 * e))
                xt = x_res[:, tt, 512 * oh:512 * oh + 512]
                vector.tensor_tensor(gtmp[:], ps8[e][:],
                                     b2_b[:, 512 * oh:512 * oh + 512],
                                     ALU.add).then_inc(*(s_vr).tick(1))
                vector.wait_ge((s_vr).h, (s_vr).base + (2 * e + 1))
                vector.tensor_tensor(xt, gtmp[:], xt, ALU.add).then_inc(*(s_vr).tick(1))

        @blk.scalar
        def _(scalar, ts=ts):
            for hc in range(NHC):
                n = 128 if hc < NHC - 1 else 42
                if hc >= 1:
                    scalar.wait_ge((s_vg).h, (s_vg).base + (hc))
                scalar.wait_ge((s_tm1).h, (s_tm1).base + (2 * hc + 2))
                scalar.activation(out=gtmp[0:n, :], in_=ps_g[0:n, :], func=AF.Gelu,
                                  bias=b1c[0:n, 2 * hc + 1:2 * hc + 2]).then_inc(
                    *(s_sg).tick(1))

        @blk.tensor
        def _(tensor, ts=ts):
            tensor.wait_ge((s_vh2b).h, (s_vh2b).base + (0))
            tensor.wait_ge((s_vr).h, (s_vr).base + (0))
            for hc in range(NHC):
                n = 128 if hc < NHC - 1 else 42
                gcol0 = 128 if hc < NHC - 1 else 42
                tensor.wait_ge((s_vw1).h, (s_vw1).base + (hc + 1))
                if hc >= 1:
                    tensor.wait_ge((s_vg).h, (s_vg).base + (hc))
                if hc >= 2:
                    tensor.wait_ge((s_sg).h, (s_sg).base + (hc - 1))
                for half in range(2):
                    pdst = (ps_a, ps_g)[half]
                    col0 = 0 if half == 0 else gcol0
                    for c in range(DC):
                        ins = nc.tensor.matmul(
                            pdst[0:n, 0:512],
                            w1bf[hc % 2][:, c, col0:col0 + n],
                            hfm[:, c, 512 * ts:512 * ts + 512],
                            start=(c == 0), stop=(c == DC - 1), skip_group_check=True)
                    if half == 1:
                        ins.then_inc(*(s_tm1).tick(2))
            tensor.wait_ge((s_vg).h, (s_vg).base + (NHC))
            for hc in range(NHC):
                n = 128 if hc < NHC - 1 else 42
                tensor.wait_ge((s_vw2).h, (s_vw2).base + (hc + 1))
                for e in range(8):
                    t4, oh = e // 2, e % 2
                    ins = nc.tensor.matmul(
                        ps8[e][:],
                        g_fm[0:n, hc, 128 * t4:128 * t4 + 128],
                        w2bf[hc % 2][0:n, 512 * oh:512 * oh + 512],
                        start=(hc == 0), stop=(hc == NHC - 1), skip_group_check=True)
                    if e == 7:
                        ins.then_inc(*(s_tm2).tick(8))
    sc.close()


_cache = {}


def kernel(**inputs):
    per_core = _stage_inputs(inputs)
    if "prog" not in _cache:
        _cache["prog"] = build_program()
    nc = _cache["prog"]
    res = run_bass_kernel_spmd(nc, per_core, core_ids=list(range(8)))
    shards = [np.asarray(res.results[c]["out"]) for c in range(8)]
    return np.concatenate(shards, axis=0).reshape(B, S, D).astype(np.float32)


# version 05
# speedup vs baseline: 1.0108x; 1.0108x over previous
import contextlib

import numpy as np
import ml_dtypes

import concourse.bass as bass
import concourse.mybir as mybir
from concourse.bass_utils import run_bass_kernel_spmd

F32 = mybir.dt.float32
BF16 = mybir.dt.bfloat16
AF = mybir.ActivationFunctionType
ALU = mybir.AluOpType
AX = mybir.AxisListType
bfloat16 = ml_dtypes.bfloat16

B, S, D = 4, 4096, 1024
HALF = S // 2
NT = HALF // 128
NTB = S // 128
DC = D // 128
NHP = 4
RG = [[0, 1], [2, 3], [4, 5], [6, 7]]
BIGB = 30.0
MAGIC = 12582912.0
N_PHASES = 4

IDH = 64
INNER = 512
WINDOW = 512
HID = 2730
QK_SCALE = 8.0


def cdiv(a, b):
    return (a + b - 1) // b


def _rotary_tables():
    half_idx = np.arange(0, IDH, 2, dtype=np.float64)
    inv_freq = 1.0 / 10000.0 ** (half_idx / IDH)
    t = np.arange(S, dtype=np.float64)
    freqs = np.concatenate([t[:, None] * inv_freq[None, :]] * 2, axis=1)
    base = (half_idx + 0.4 * IDH) / (1.4 * IDH)
    power = (t - S // 2) / float(WINDOW // 2)
    scale = np.concatenate([base[None, :] ** power[:, None]] * 2, axis=1)
    cos, sin = np.cos(freqs), np.sin(freqs)
    return ((cos * scale).T.astype(np.float32), (sin * scale).T.astype(np.float32),
            (cos / scale).T.astype(np.float32), (sin / scale).T.astype(np.float32))


def _bitffn_perm():
    perm = []
    i = 0
    while i < HID:
        j = min(i + 128, HID)
        perm.extend(range(i, j))
        perm.extend(range(HID + i, HID + j))
        i = j
    return np.array(perm, dtype=np.int64)


def _stage_inputs(inp):
    per_core = []
    x = np.ascontiguousarray(np.asarray(inp["L0_attn_x"], dtype=np.float32))
    eye = np.eye(128, dtype=np.float32)
    tri_keep = np.triu(np.ones((128, 128), dtype=np.float32)).astype(bfloat16)
    tril_keep = np.tril(np.ones((128, 128), dtype=np.float32)).astype(bfloat16)
    if N_PHASES >= 4:
        Aq, Bq, Ak, Bk = _rotary_tables()
        w1perm = _bitffn_perm()
        rsw = np.zeros((128, 128), dtype=np.float32)
        for blk0 in (0, 64):
            for i in range(32):
                rsw[blk0 + i + 32, blk0 + i] = 1.0
                rsw[blk0 + i, blk0 + i + 32] = 1.0
        rsw = rsw.astype(bfloat16)

    for c in range(8):
        b, h = c // 2, c % 2
        d = {}
        d["x_own"] = np.ascontiguousarray(x[b, HALF * h:HALF * h + HALF])
        d["xb"] = x[b]
        hs = slice(512 * h, 512 * h + 512)
        d["l0_wq"] = np.ascontiguousarray(np.asarray(inp["L0_attn_WQ"], np.float32)[:, hs])
        d["l0_wk"] = np.ascontiguousarray(np.asarray(inp["L0_attn_WK"], np.float32)[:, hs])
        d["l0_wv"] = np.ascontiguousarray(np.asarray(inp["L0_attn_WV"], np.float32)[:, hs])
        d["l0_wo"] = np.asarray(inp["L0_attn_WO"], np.float32)
        d["ident"] = eye
        d["ident_bf"] = eye.astype(bfloat16)
        d["tri_pt"] = tri_keep
        d["tril_pt"] = tril_keep
        d["own_half"] = np.array([h], dtype=np.int32)
        if N_PHASES >= 2:
            d["l0_w1"] = np.asarray(inp["L0_ffn_w1"], np.float32)
            d["l0_w2"] = np.asarray(inp["L0_ffn_w2"], np.float32)
        if N_PHASES >= 3:
            rs = slice(512 * h, 512 * h + 512)
            d["l1_wq"] = np.ascontiguousarray(np.asarray(inp["L1_attn_W_query"], np.float32)[rs])
            d["l1_wk"] = np.ascontiguousarray(np.asarray(inp["L1_attn_W_key"], np.float32)[rs])
            d["l1_wv"] = np.ascontiguousarray(np.asarray(inp["L1_attn_W_value"], np.float32)[rs])
            d["l1_wp"] = np.asarray(inp["L1_attn_W_proj"], np.float32)
        if N_PHASES >= 4:
            pos = np.clip(np.arange(HALF * h - 513, HALF * h + HALF), 0, S - 1)
            posq = pos[513:]
            sgn = np.concatenate([-np.ones(32, np.float32), np.ones(32, np.float32)])
            d["rot_aq"] = np.ascontiguousarray(np.tile(Aq[:, posq], (2, 1)).astype(bfloat16))
            d["rot_bq"] = np.ascontiguousarray(np.tile(sgn[:, None] * Bq[:, posq], (2, 1)).astype(bfloat16))
            d["rot_ak"] = np.ascontiguousarray(np.tile(Ak[:, pos], (2, 1)).astype(bfloat16))
            d["rot_bk"] = np.ascontiguousarray(np.tile(sgn[:, None] * Bk[:, pos], (2, 1)).astype(bfloat16))
            d["hv128"] = np.full((128, 1), float(h), dtype=np.float32)
            d["rswap"] = rsw
            d["tri_f"] = np.triu(np.ones((128, 128), dtype=np.float32))
            d["tril_f"] = np.tril(np.ones((128, 128), dtype=np.float32))
            hs = np.zeros((2, 128), dtype=np.float32)
            hs[0, 0:64] = 1.0
            hs[1, 64:128] = 1.0
            d["hsel_f"] = hs
            d["hsel_bf"] = hs.astype(bfloat16)
            d["i_wqkv"] = np.asarray(inp["L1_ffn_wqkv"], np.float32)
            d["i_qs"] = np.asarray(inp["L1_ffn_q_scale"], np.float32)
            d["i_ks"] = np.asarray(inp["L1_ffn_k_scale"], np.float32)
            d["i_wo"] = np.asarray(inp["L1_ffn_wo"], np.float32)
            d["i_lng"] = np.asarray(inp["L1_ffn_ln_g"], np.float32)
            d["i_lnb"] = np.asarray(inp["L1_ffn_ln_b"], np.float32)
            d["i_w1"] = np.ascontiguousarray(np.asarray(inp["L1_ffn_w1"], np.float32)[:, :, w1perm])
            d["i_b1"] = np.ascontiguousarray(np.asarray(inp["L1_ffn_b1"], np.float32)[:, w1perm])
            d["i_w2"] = np.asarray(inp["L1_ffn_w2"], np.float32)
            d["i_b2"] = np.asarray(inp["L1_ffn_b2"], np.float32)
            d["halo_valid"] = np.array([float(h)], dtype=np.float32)
        per_core.append(d)
    return per_core


class RSem:
    __slots__ = ("h", "base", "ctr", "role")

    def __init__(self, h, role):
        self.h, self.base, self.ctr, self.role = h, 0, 0, role

    def tick(self, k=1):
        self.ctr += k
        return (self.h, k)


class Prog:
    def __init__(self):
        self.nc = bass.Bass("TRN2", target_bir_lowering=False)
        self.ap = {}
        self.est = contextlib.ExitStack()
        self._barr = None
        self._barr_n = 0
        self.rsems = {}

    ROLE_ALIAS = {
        "in_xbA": "fmb_ldA", "in_xbB": "fmb_ldB", "in_x3": "ga_dcc",
        "in_xv": "fmb_vx",
        "f4_w1lA": "p2w_ldA", "f4_w1lB": "p2w_ldB",
        "f4_w2lA": "p2w_ld2A", "f4_w2lB": "p2w_ld2B",
        "f4_ld": "p2_wbA", "f4_sh1": "p2_wbB", "f4_hrb": "ga_din1",
        "f4_dshA": "ga_dw", "f4_dshB": "p2w_dw",
        "p4_tld": "ga_ldA", "p4_tl2": "ga_ldB",
    }

    def rs(self, role):
        role = self.ROLE_ALIAS.get(role, role)
        r = self.rsems.get(role)
        if r is None:
            r = RSem(self.est.enter_context(self.nc.semaphore(role)), role)
            self.rsems[role] = r
        r.base = r.ctr
        return r

    def din(self, name, shape, dt=F32):
        self.ap[name] = self.nc.dram_tensor(name, shape, dt, kind="ExternalInput").ap()
        return self.ap[name]

    def dint(self, name, shape, dt=F32):
        self.ap[name] = self.nc.dram_tensor(name, shape, dt, kind="Internal").ap()
        return self.ap[name]

    def sb(self, name, shape, dt, scope=None):
        return (scope or self.est).enter_context(self.nc.sbuf_tensor(name, shape, dt))

    def sem(self, name):
        return self.est.enter_context(self.nc.semaphore(name))

    def barrier(self, blk):
        if self._barr is None:
            self._barr = {e: self.est.enter_context(self.nc.semaphore(f"barr_{e}"))
                          for e in ("sync", "tensor", "vector", "scalar", "gpsimd")}
        self._barr_n += 1
        k = self._barr_n
        sems = self._barr

        def arm(name):
            def emit(eng):
                eng.wait_ge(sems[name], k - 1).then_inc(sems[name], 1)
                for other in sems:
                    if other != name:
                        eng.wait_ge(sems[other], k)
            return emit

        blk.sync(arm("sync"))
        blk.tensor(arm("tensor"))
        blk.vector(arm("vector"))
        blk.scalar(arm("scalar"))
        blk.gpsimd(arm("gpsimd"))


def build_program():
    p = Prog()
    nc = p.nc

    p.din("x_own", [HALF, D])
    p.din("xb", [S, D])
    p.din("l0_wq", [D, 512]); p.din("l0_wk", [D, 512]); p.din("l0_wv", [D, 512])
    p.din("l0_wo", [D, D])
    p.din("ident", [128, 128])
    p.din("ident_bf", [128, 128], BF16)
    p.din("tri_pt", [128, 128], BF16)
    p.din("tril_pt", [128, 128], BF16)
    p.din("own_half", [1], mybir.dt.int32)
    if N_PHASES >= 2:
        p.din("l0_w1", [4096, D]); p.din("l0_w2", [D, 4096])
        p.dint("w1qT_d", [D, 4096], BF16)
        p.dint("w2qT_d", [4096, D], BF16)
    if N_PHASES >= 3:
        p.din("l1_wq", [512, D]); p.din("l1_wk", [512, D]); p.din("l1_wv", [512, D])
        p.din("l1_wp", [D, D])
        p.dint("l1_wqT_d", [D, 512], F32)
        p.dint("l1_wkT_d", [D, 512], F32)
        p.dint("l1_wvT_d", [D, 512], F32)
        p.dint("l1_wpT_d", [D, D], F32)
    if N_PHASES >= 4:
        PW = HALF + 513
        p.din("rot_aq", [128, HALF], BF16); p.din("rot_bq", [128, HALF], BF16)
        p.din("rot_ak", [128, PW], BF16); p.din("rot_bk", [128, PW], BF16)
        p.din("hv128", [128, 1])
        p.din("rswap", [128, 128], BF16)
        p.din("hsel_f", [2, 128], F32)
        p.din("tri_f", [128, 128]); p.din("tril_f", [128, 128])
        p.din("hsel_bf", [2, 128], BF16)
        p.din("i_wqkv", [2, D, 3 * INNER])
        p.din("i_qs", [2, IDH]); p.din("i_ks", [2, IDH])
        p.din("i_wo", [2, INNER, D])
        p.din("i_lng", [2, D]); p.din("i_lnb", [2, D])
        p.din("i_w1", [2, D, 2 * HID]); p.din("i_b1", [2, 2 * HID])
        p.din("i_w2", [2, HID, D]); p.din("i_b2", [2, D])
        p.din("halo_valid", [1])
        p.dint("x3_d", [HALF, D])

    out_ap = nc.dram_tensor("out", [HALF, D], F32, kind="ExternalOutput").ap()

    for tag, nph in (("p1", 1), ("p3", 3)):
        if N_PHASES >= nph:
            for hp in range(NHP):
                for half in range(2):
                    p.dint(f"{tag}cc_in_{hp}_{half}", [128, HALF], BF16)
                    p.dint(f"{tag}cc_out_{hp}_{half}", [2, 128, HALF], BF16)
    if N_PHASES >= 3:
        p.dint("x2cc_in", [D, HALF], BF16)
        p.dint("x2cc_out", [2, D, HALF], BF16)
    if N_PHASES >= 4:
        p.dint("shcc_in", [D, 513], BF16)
        p.dint("shcc_out", [2, D, 513], BF16)
        p.dint("sh1_in", [1, D], F32)
        p.dint("sh1_out", [2, 1, D], F32)

    with p.est:
        _emit_all(p, out_ap)
    return nc


def _emit_all(p, out_ap):
    nc = p.nc
    x_res = p.sb("x_res", [128, NT, D], F32)
    cst = p.sb("cst", [128, 8], F32)
    identt = p.sb("identt", [128, 128], F32)
    ident_bf = p.sb("ident_bf_s", [128, 128], BF16)
    tri_pt = p.sb("tri_pt_s", [128, 128], BF16)
    tril_pt = p.sb("tril_pt_s", [128, 128], BF16)
    if N_PHASES >= 4:
        p.hv = p.sb("hv_s", [128, 1], F32)
        p.rsw = p.sb("rsw_s", [128, 128], BF16)
        p.trif = p.sb("trif_s", [128, 128], F32)
        p.trilf = p.sb("trilf_s", [128, 128], F32)

    with nc.Block() as blk:
        s_ld0 = p.rs("s_ld0")
        s_ldx = p.rs("s_ldx")
        s_cst = p.rs("s_cst")
        p.s_ldx = s_ldx
        p.s_ld0 = s_ld0
        p.s_cst = s_cst
        if N_PHASES >= 4:
            s_hv0 = p.rs("s_hv0")
            p.s_hv0 = s_hv0
            s_rsw = p.rs("s_rsw")
            p.s_rsw = s_rsw

        @blk.sync
        def _(sync):
            sync.dma_start(out=identt[:], in_=p.ap["ident"][:]).then_inc(*(s_ld0).tick(16))
            sync.dma_start(out=ident_bf[:], in_=p.ap["ident_bf"][:]).then_inc(*(s_ld0).tick(16))
            sync.dma_start(out=tri_pt[:], in_=p.ap["tri_pt"][:]).then_inc(*(s_ld0).tick(16))
            sync.dma_start(out=tril_pt[:], in_=p.ap["tril_pt"][:]).then_inc(*(s_ld0).tick(16))
            xo = p.ap["x_own"].rearrange("(t p) d -> p t d", p=128)
            for tt in range(NT):
                sync.dma_start(out=x_res[:, tt, :], in_=xo[:, tt, :]).then_inc(*(s_ldx).tick(16))
            if N_PHASES >= 4:
                sync.dma_start(out=p.hv[:], in_=p.ap["hv128"][:]).then_inc(*(s_hv0).tick(16))
                sync.dma_start(out=p.rsw[:], in_=p.ap["rswap"][:]).then_inc(*(s_rsw).tick(16))
                sync.dma_start(out=p.trif[:], in_=p.ap["tri_f"][:]).then_inc(*(s_rsw).tick(16))
                sync.dma_start(out=p.trilf[:], in_=p.ap["tril_f"][:]).then_inc(*(s_rsw).tick(16))

        @blk.gpsimd
        def _(gp):
            gp.memset(cst[:, 0:1], -BIGB)
            gp.memset(cst[:, 1:2], 1e-5).then_inc(*(s_cst).tick(1))

        with contextlib.ExitStack() as p1scope:
            x_fm = p.sb("p1_x_fm", [128, DC, S], BF16, p1scope)
            p.s_fmdone = _emit_fm_build(p, blk, x_fm, identt, "xb", NTB,
                                        ident_ld=(s_ld0, 64))
            p.barrier(blk)
            _emit_global_attn(p, blk, tag="p1", x_fm=x_fm, x_res=x_res,
                              identt=ident_bf, tri_pt=tri_pt, cst=cst,
                              wq="l0_wq", wk="l0_wk", wv="l0_wv", wo="l0_wo",
                              score_scale=0.125, proj_scale=1.0)
            p.barrier(blk)
        s_lnv = _emit_ln(p, blk, x_res, cst, tag="ln1")
        p.barrier(blk)

        if N_PHASES >= 2:
            _emit_bitffn(p, blk, x_res, ident_bf, cst)
            p.barrier(blk)
            s_lnv = _emit_ln(p, blk, x_res, cst, tag="ln2")
            p.barrier(blk)

        if N_PHASES >= 3:
            wts = []
            for args in (("l1_wq", "l1_wqT_d", 4, "wtq"), ("l1_wk", "l1_wkT_d", 4, "wtk"),
                         ("l1_wv", "l1_wvT_d", 4, "wtv"), ("l1_wp", "l1_wpT_d", 8, "wtp")):
                wts.append(_emit_wT(p, blk, args[0], args[1], args[2], D, identt, args[3]))
                p.barrier(blk)

            @blk.sync
            def _(sync):
                for sdw_, tot_ in wts:
                    sync.wait_ge(sdw_.h, tot_)
            p.barrier(blk)
            with contextlib.ExitStack() as p3scope:
                x_fm3 = p.sb("p3_x_fm", [128, DC, S], BF16, p3scope)
                _emit_x2gather(p, blk, x_res, x_fm3, identt)
                p.barrier(blk)
                _emit_global_attn(p, blk, tag="p3", x_fm=x_fm3, x_res=x_res,
                                  identt=ident_bf, tri_pt=tri_pt, cst=cst,
                                  wq="l1_wqT_d", wk="l1_wkT_d", wv="l1_wvT_d",
                                  wo="l1_wpT_d",
                                  score_scale=1.0 / 8192.0,
                                  proj_scale=1.0 / 1024.0)
                p.barrier(blk)
            s_lnv = _emit_ln(p, blk, x_res, cst, tag="ln3")
            p.barrier(blk)

        if N_PHASES >= 4:
            _emit_inner(p, blk, x_res, identt, ident_bf, tri_pt, tril_pt, cst)
            p.barrier(blk)
            s_lnv = _emit_ln(p, blk, x_res, cst, tag="ln4")
            p.barrier(blk)

        s_st = p.rs("s_st")

        @blk.sync
        def _(sync):
            sync.wait_ge((s_lnv).h, (s_lnv).base + (9 * NT))
            xo = out_ap.rearrange("(t p) d -> p t d", p=128)
            for tt in range(NT):
                sync.dma_start(out=xo[:, tt, :], in_=x_res[:, tt, :]).then_inc(*(s_st).tick(16))
            sync.wait_ge((s_st).h, (s_st).base + (16 * NT))


def _emit_fm_build(p, blk, dst_fm, identt, src_name, ntiles, ident_ld):
    nc = p.nc
    s_ld = [p.rs("fmb_ldA"), p.rs("fmb_ldB")]
    s_tx = p.rs("fmb_tx")
    s_vx = p.rs("fmb_vx")
    with contextlib.ExitStack() as scope:
        xstage = [p.sb(f"fmb_{src_name}_st{i}", [128, D], F32, scope) for i in range(2)]
        ps_tp = [scope.enter_context(nc.psum_tensor(f"fmb_{src_name}_ps{i}", [128, 1024], F32))
                 for i in range(2)]

        @blk.sync
        def _(sync):
            src = p.ap[src_name].rearrange("(t p) d -> p t d", p=128)
            for tt in range(ntiles):
                if tt >= 2:
                    sync.wait_ge((s_tx).h, (s_tx).base + (tt - 1))
                sync.dma_start(out=xstage[tt % 2][:], in_=src[:, tt, :]).then_inc(*(s_ld[tt % 2]).tick(16))

        @blk.tensor
        def _(tensor):
            tensor.wait_ge((ident_ld[0]).h, (ident_ld[0]).base + (ident_ld[1]))
            for tt in range(ntiles):
                tensor.wait_ge((s_ld[tt % 2]).h, (s_ld[tt % 2]).base + (16 * (tt // 2 + 1)))
                if tt >= 2:
                    tensor.wait_ge((s_vx).h, (s_vx).base + (tt - 1))
                pst = ps_tp[tt % 2]
                for c in range(DC):
                    ins = nc.tensor.transpose(
                        pst[:, 128 * c:128 * c + 128],
                        xstage[tt % 2][:, 128 * c:128 * c + 128], identt[:])
                ins.then_inc(*(s_tx).tick(1))

        @blk.vector
        def _(vector):
            for tt in range(ntiles):
                vector.wait_ge((s_tx).h, (s_tx).base + (tt + 1))
                src = ps_tp[tt % 2][:].rearrange("p (c k) -> p c k", c=DC)
                vector.tensor_copy(dst_fm[:, :, 128 * tt:128 * tt + 128], src).then_inc(*(s_vx).tick(1))

        s_done = p.rs("fmb_done")

        @blk.tensor
        def _(tensor):
            tensor.wait_ge((s_vx).h, (s_vx).base + ntiles).then_inc(*(s_done).tick(1))
    return s_done


def _emit_ln(p, blk, x_res, cst, tag, ntiles=NT):
    nc = p.nc
    lscope = contextlib.ExitStack()
    sq_t = p.sb(f"{tag}_sq", [128, D], F32, lscope)
    st_t = p.sb(f"{tag}_st", [128, 8], F32, lscope)
    s_v = p.rs("ga_v")
    s_s = p.rs("ga_s")

    @blk.vector
    def _(vector):
        rr = p.rs("fmb_vx")
        vector.wait_ge(rr.h, rr.base + 0)
        n = [0]

        def step(ins):
            n[0] += 1
            ins.then_inc(*(s_v).tick(1))
            vector.wait_ge((s_v).h, (s_v).base + (n[0]))

        for tt in range(ntiles):
            xt = x_res[:, tt, :]
            step(vector.tensor_tensor(sq_t[:], xt, xt, ALU.mult))
            step(vector.reduce_sum(st_t[:, 0:1], xt, AX.X))
            step(vector.tensor_scalar(st_t[:, 2:3], st_t[:, 0:1], 1.0 / D, None, ALU.mult))
            step(vector.reduce_sum(st_t[:, 1:2], sq_t[:], AX.X))
            step(vector.tensor_scalar(st_t[:, 3:4], st_t[:, 1:2], 1.0 / D, None, ALU.mult))
            step(vector.tensor_tensor(st_t[:, 4:5], st_t[:, 2:3], st_t[:, 2:3], ALU.mult))
            step(vector.tensor_tensor(st_t[:, 5:6], st_t[:, 3:4], st_t[:, 4:5], ALU.subtract))
            vector.wait_ge((s_s).h, (s_s).base + (tt + 1))
            step(vector.reciprocal(st_t[:, 7:8], st_t[:, 6:7]))
            step(vector.tensor_scalar(xt, xt, st_t[:, 2:3], st_t[:, 7:8],
                                      ALU.subtract, ALU.mult))

    @blk.scalar
    def _(scalar):
        scalar.wait_ge((p.s_cst).h, (p.s_cst).base + (1))
        for tt in range(ntiles):
            scalar.wait_ge((s_v).h, (s_v).base + (9 * tt + 7))
            scalar.activation(out=st_t[:, 6:7], in_=st_t[:, 5:6], func=AF.Sqrt,
                              bias=cst[:, 1:2]).then_inc(*(s_s).tick(1))
    lscope.close()
    return s_v


def _emit_global_attn(p, blk, tag, x_fm, x_res, identt, tri_pt, cst,
                      wq, wk, wv, wo, score_scale, proj_scale,
                      w_pre_bf16=False):
    nc = p.nc

    s_wld = p.rs("ga_wld")
    s_vw = p.rs("ga_vw")
    s_tq = p.rs("ga_tq")
    s_vq = p.rs("ga_vq")
    s_tv = p.rs("ga_tv")
    s_vv = p.rs("ga_vv")
    s_ts = p.rs("ga_ts")
    s_se = p.rs("ga_se")
    s_vm = p.rs("ga_vm")
    s_tav = p.rs("ga_tav")
    s_vo = p.rs("ga_vo")
    s_gl = p.rs("ga_gl")
    s_vrc = p.rs("ga_vrc")
    s_vn = p.rs("ga_vn")
    s_dcc = p.rs("ga_dcc")
    s_gc = p.rs("ga_gc")
    s_drb = p.rs("ga_drb")
    s_tw = p.rs("ga_tw")
    s_vwo = p.rs("ga_vwo")

    bscope = contextlib.ExitStack()
    wstage = p.sb(f"{tag}_wstage", [128, DC, 128], F32, bscope)
    w_bf = [p.sb(f"{tag}_wbf{i}", [128, DC, 128], BF16, bscope) for i in range(3)]
    q_fm = p.sb(f"{tag}_qfm", [128, S], BF16, bscope)
    k_fm = p.sb(f"{tag}_kfm", [128, S], BF16, bscope)
    v_bf = p.sb(f"{tag}_vbf", [128, S], BF16, bscope)
    v_aug = p.sb(f"{tag}_vaug", [128, NTB, 130], BF16, bscope)
    pt_buf = [p.sb(f"{tag}_pt{i}", [128, 1536], BF16, bscope) for i in range(2)]
    ot_un = [p.sb(f"{tag}_otun{i}", [65, S], BF16, bscope) for i in range(2)]
    l2_t = [p.sb(f"{tag}_l2{i}", [1, HALF], BF16, bscope) for i in range(2)]
    rlb = p.sb(f"{tag}_rlb", [64, HALF], BF16, bscope)
    ones64 = p.sb(f"{tag}_ones64", [1, 64], BF16, bscope)
    s_tb = p.rs("ga_tb")
    s_vb = p.rs("ga_vb")

    def qb_supers(qb):
        slots = [(kt, h) for kt in range(4 * qb + 4) for h in range(2)]
        return [slots[i:i + 3] for i in range(0, len(slots), 3)]

    SUP = []
    for hp in range(NHP):
        for qb in range(8):
            sups = qb_supers(qb)
            for i, sl in enumerate(sups):
                diag = [(si, kt, h) for si, (kt, h) in enumerate(sl) if kt >= 4 * qb]
                SUP.append(dict(hp=hp, qb=qb, slots=sl, diag=diag, first=(i == 0)))
    n_sup = len(SUP)
    n_sup_hp = n_sup // NHP
    cum_diag = np.cumsum([len(s["diag"]) for s in SUP]).tolist()

    for hp in range(NHP):
        with contextlib.ExitStack() as scope:
            ps_qkv = scope.enter_context(nc.psum_tensor(f"{tag}_psqkv{hp}", [128, 1024], F32))
            ps_vt = [scope.enter_context(nc.psum_tensor(f"{tag}_psvt{hp}{i}", [128, 512], F32))
                     for i in range(2)]

            @blk.sync
            def _(sync, hp=hp):
                if hp == 0:
                    sync.wait_ge((p.s_fmdone).h, (p.s_fmdone).base + (1))
                for wi, wn in enumerate((wq, wk, wv)):
                    idx = hp * 3 + wi
                    if idx > 0:
                        sync.wait_ge((s_vw).h, (s_vw).base + (idx))
                    src = p.ap[wn].rearrange("(c p) o -> p c o", p=128)[:, :, 128 * hp:128 * hp + 128]
                    for c in range(DC):
                        sync.dma_start(out=wstage[:, c, :], in_=src[:, c, :]).then_inc(*(s_wld).tick(16))

            @blk.vector
            def _(vector, hp=hp):
                if hp == 0:
                    vector.wait_ge((p.s_ld0).h, (p.s_ld0).base + (64))
                    va = v_aug[:].rearrange("p t (h c) -> p t h c", c=65)
                    vector.memset(va[:, :, :, 64:65], 1.0)
                    vector.memset(ones64[:], 1.0)
                for wi in range(3):
                    idx = hp * 3 + wi
                    vector.wait_ge((s_wld).h, (s_wld).base + (128 * (idx + 1)))
                    vector.tensor_copy(w_bf[wi][:], wstage[:]).then_inc(*(s_vw).tick(1))
                for wi in range(3):
                    for qb in range(8):
                        e = (hp * 3 + wi) * 8 + qb
                        vector.wait_ge((s_tq).h, (s_tq).base + (e + 1))
                        src = ps_qkv[:, 512 * (qb % 2):512 * (qb % 2) + 512]
                        dst = (q_fm, k_fm, v_bf)[wi][:, 512 * qb:512 * qb + 512]
                        vector.tensor_copy(dst, src).then_inc(*(s_vq).tick(1))
                for g in range(8):
                    vector.wait_ge((s_tv).h, (s_tv).base + (hp * 8 + g + 1))
                    src = ps_vt[g % 2][:].rearrange("p (t h c) -> p t h c", t=4, h=2)
                    dst = v_aug[:, 4 * g:4 * g + 4, :].rearrange(
                        "p t (h c) -> p t h c", c=65)[:, :, :, 0:64]
                    vector.tensor_copy(dst, src).then_inc(*(s_vv).tick(1))

            @blk.tensor
            def _(tensor, hp=hp):
                if hp > 0:
                    tensor.wait_ge((s_se).h, (s_se).base + (hp * n_sup_hp))
                    tensor.wait_ge((s_vo).h, (s_vo).base + (16 * hp))
                    tensor.wait_ge((s_vb).h, (s_vb).base + (16 * hp))
                for wi in range(3):
                    tensor.wait_ge((s_vw).h, (s_vw).base + (hp * 3 + wi + 1))
                    for qb in range(8):
                        e = (hp * 3 + wi) * 8 + qb
                        if e >= 2:
                            tensor.wait_ge((s_vq).h, (s_vq).base + (e - 1))
                        dst = ps_qkv[:, 512 * (qb % 2):512 * (qb % 2) + 512]
                        for c in range(DC):
                            ins = nc.tensor.matmul(
                                dst, w_bf[wi][:, c, :],
                                x_fm[:, c, 512 * qb:512 * qb + 512],
                                start=(c == 0), stop=(c == DC - 1))
                        ins.then_inc(*(s_tq).tick(1))
                tensor.wait_ge((s_vq).h, (s_vq).base + ((hp * 3 + 3) * 8))
                for g in range(8):
                    if g >= 2:
                        tensor.wait_ge((s_vv).h, (s_vv).base + (hp * 8 + g - 1))
                    for t4 in range(4):
                        tt = 4 * g + t4
                        ins = nc.tensor.matmul(
                            ps_vt[g % 2][:, 128 * t4:128 * t4 + 128],
                            v_bf[:, 128 * tt:128 * tt + 128], identt[:],
                            start=True, stop=True, skip_group_check=True)
                    ins.then_inc(*(s_tv).tick(1))

        with contextlib.ExitStack() as scope:
            ps_s = [scope.enter_context(nc.psum_tensor(f"{tag}_pss{hp}{i}", [128, 1536], F32))
                    for i in range(2)]
            ps_o = [scope.enter_context(nc.psum_tensor(f"{tag}_pso{hp}{i}", [65, 512], F32))
                    for i in range(2)]
            sup_lo = hp * n_sup_hp
            sup_hi = (hp + 1) * n_sup_hp

            @blk.tensor
            def _(tensor, hp=hp, sup_lo=sup_lo, sup_hi=sup_hi):
                tensor.wait_ge((s_vv).h, (s_vv).base + (8 * (hp + 1)))

                def emit_S(sg):
                    sup = SUP[sg]
                    qb = sup["qb"]
                    if sg - sup_lo >= 2:
                        tensor.wait_ge((s_se).h, (s_se).base + (sg - 1))
                    for si, (kt, h) in enumerate(sup["slots"]):
                        ins = nc.tensor.matmul(
                            ps_s[sg % 2][:, 512 * si:512 * si + 512],
                            k_fm[64 * h:64 * h + 64, 128 * kt:128 * kt + 128],
                            q_fm[64 * h:64 * h + 64, 512 * qb:512 * qb + 512],
                            start=True, stop=True, tile_position=(64 * h, 0),
                            skip_group_check=True)
                    ins.then_inc(*(s_ts).tick(1))

                def emit_AV(sg):
                    sup = SUP[sg]
                    qb = sup["qb"]
                    tensor.wait_ge((s_se).h, (s_se).base + (sg + 1))
                    tensor.wait_ge((s_vv).h, (s_vv).base + (hp * 8 + sup["slots"][-1][0] // 4 + 1))
                    if cum_diag[sg] > 0:
                        tensor.wait_ge((s_vm).h, (s_vm).base + (cum_diag[sg]))
                    if sup["first"]:
                        tensor.wait_ge((s_vo).h, (s_vo).base + (2 * (hp * 8 + qb)))
                    for si, (kt, h) in enumerate(sup["slots"]):
                        j = kt - 4 * qb
                        off = 128 * j if j > 0 else 0
                        ins = nc.tensor.matmul(
                            ps_o[h][:, off:512],
                            v_aug[:, kt, 65 * h:65 * h + 65],
                            pt_buf[sg % 2][:, 512 * si + off:512 * si + 512],
                            start=(kt == 0), stop=(kt == 4 * qb + 3),
                            skip_group_check=True)
                    ins.then_inc(*(s_tav).tick(1))

                emit_S(sup_lo)
                for sg in range(sup_lo, sup_hi - 1):
                    emit_S(sg + 1)
                    emit_AV(sg)
                emit_AV(sup_hi - 1)
                for half in range(2):
                    tensor.wait_ge((s_vrc).h, (s_vrc).base + (2 * hp + half + 1))
                    for head in range(2):
                        for c4 in range(4):
                            idx = 16 * hp + 8 * half + 4 * head + c4
                            if idx >= 3:
                                tensor.wait_ge((s_vb).h, (s_vb).base + (idx - 2))
                            nc.tensor.matmul(
                                ps_s[0][0:64, 512 * (idx % 3):512 * (idx % 3) + 512],
                                ones64[:], l2_t[head][:, 512 * c4:512 * c4 + 512],
                                start=True, stop=True,
                                skip_group_check=True).then_inc(*(s_tb).tick(1))

            @blk.scalar
            def _(scalar, hp=hp, sup_lo=sup_lo, sup_hi=sup_hi):
                if hp == 0:
                    scalar.wait_ge((p.s_cst).h, (p.s_cst).base + (1))
                for sg in range(sup_lo, sup_hi):
                    n = len(SUP[sg]["slots"])
                    scalar.wait_ge((s_ts).h, (s_ts).base + (sg + 1))
                    if sg - sup_lo >= 2:
                        scalar.wait_ge((s_tav).h, (s_tav).base + (sg - 1))
                    scalar.activation(
                        out=pt_buf[sg % 2][:, 0:512 * n],
                        in_=ps_s[sg % 2][:, 0:512 * n],
                        func=AF.Exp, bias=cst[:, 0:1], scale=score_scale,
                    ).then_inc(*(s_se).tick(1))

            @blk.vector
            def _(vector, hp=hp, sup_lo=sup_lo, sup_hi=sup_hi):
                if hp > 0:
                    vector.wait_ge((s_gl).h, (s_gl).base + (64 * hp))
                    vector.wait_ge((s_dcc).h, (s_dcc).base + (64 * hp))
                for sg in range(sup_lo, sup_hi):
                    sup = SUP[sg]
                    if sup["diag"]:
                        vector.wait_ge((s_se).h, (s_se).base + (sg + 1))
                        for (si, kt, h) in sup["diag"]:
                            j = kt - 4 * sup["qb"]
                            sl = pt_buf[sg % 2][:, 512 * si + 128 * j:512 * si + 128 * j + 128]
                            vector.tensor_tensor(sl, sl, tri_pt[:], ALU.mult).then_inc(*(s_vm).tick(1))
                    if (sg + 1 == sup_hi) or SUP[sg + 1]["first"]:
                        qb = sup["qb"]
                        vector.wait_ge((s_tav).h, (s_tav).base + (sg + 1))
                        for head in range(2):
                            vector.tensor_copy(
                                ot_un[head][0:65, 512 * qb:512 * qb + 512],
                                ps_o[head][0:65, :]).then_inc(*(s_vo).tick(1))
                for half in range(2):
                    vector.wait_ge((s_gl).h, (s_gl).base + (64 * hp + 32 * (half + 1)))
                    with nc.allow_low_precision(reason="softmax denom tolerates bf16"):
                        vector.reciprocal(l2_t[0][:], l2_t[0][:])
                        vector.reciprocal(l2_t[1][:], l2_t[1][:]).then_inc(*(s_vrc).tick(1))
                    for head in range(2):
                        g = 4 * hp + 2 * half + head
                        if g > 0:
                            vector.wait_ge((s_vn).h, (s_vn).base + (g))
                        for c4 in range(4):
                            idx = 16 * hp + 8 * half + 4 * head + c4
                            vector.wait_ge((s_tb).h, (s_tb).base + (idx + 1))
                            vector.tensor_copy(
                                rlb[:, 512 * c4:512 * c4 + 512],
                                ps_s[0][0:64, 512 * (idx % 3):512 * (idx % 3) + 512],
                            ).then_inc(*(s_vb).tick(1))
                        vector.wait_ge((s_vb).h, (s_vb).base + (16 * hp + 8 * half + 4 * head + 4))
                        sl = ot_un[head][0:64, HALF * half:HALF * half + HALF]
                        vector.tensor_tensor(sl, sl, rlb[:, :], ALU.mult).then_inc(*(s_vn).tick(1))

            @blk.sync
            def _(sync, hp=hp):
                sync.wait_ge((s_vo).h, (s_vo).base + (16 * (hp + 1)))
                for half in range(2):
                    if hp + half > 0:
                        sync.wait_ge((s_tb).h, (s_tb).base + (16 * hp + 8 * half))
                    for head in range(2):
                        sync.dma_start(
                            out=l2_t[head][:],
                            in_=ot_un[head][64:65, HALF * half:HALF * half + HALF],
                        ).then_inc(*(s_gl).tick(16))
                sync.wait_ge((s_vn).h, (s_vn).base + (4 * (hp + 1)))
                for half in range(2):
                    ccin = p.ap[f"{tag}cc_in_{hp}_{half}"]
                    for head in range(2):
                        sync.dma_start(
                            out=ccin[64 * head:64 * head + 64, :],
                            in_=ot_un[head][0:64, HALF * half:HALF * half + HALF],
                        ).then_inc(*(s_dcc).tick(16))

            @blk.gpsimd
            def _(gp, hp=hp):
                gp.wait_ge((s_dcc).h, (s_dcc).base + (64 * (hp + 1)))
                for half in range(2):
                    gp.collective_compute(
                        "AllGather", ALU.bypass, replica_groups=RG,
                        ins=[p.ap[f"{tag}cc_in_{hp}_{half}"][:]],
                        outs=[p.ap[f"{tag}cc_out_{hp}_{half}"][:]],
                    ).then_inc(*(s_gc).tick(1))

    bscope.close()
    with contextlib.ExitStack() as scope:
        ps_wo = scope.enter_context(nc.psum_tensor(f"{tag}_pswo", [128, 1024], F32))
        wstage_c = p.sb(f"{tag}_wstc", [128, DC, 128], F32, scope)
        wo_bf = p.sb(f"{tag}_wobf", [128, DC, 512], BF16, scope)

        @blk.sync
        def _(sync):
            sync.wait_ge((s_tq).h, (s_tq).base + (12 * 8))
            for hp in range(NHP):
                sync.wait_ge((s_gc).h, (s_gc).base + (2 * (hp + 1)))
                for arm in sync.switch_core_id(n=128):
                    my_half = arm.logical % 2
                    cout = p.ap[f"{tag}cc_out_{hp}_{my_half}"]
                    for rank in range(2):
                        sync.dma_start(
                            out=x_fm[:, 4 * rank + hp, 0:HALF],
                            in_=cout[rank],
                        ).then_inc(s_drb.h, 16)
                s_drb.ctr += 32
            sync.wait_ge((s_tav).h, (s_tav).base + (n_sup))
            sync.wait_ge((s_se).h, (s_se).base + (n_sup))
            sync.wait_ge((s_vo).h, (s_vo).base + (64))
            sync.wait_ge((s_vb).h, (s_vb).base + (64))
            sync.wait_ge((s_vn).h, (s_vn).base + (16))
            sync.wait_ge((s_gl).h, (s_gl).base + (64 * NHP))
            sync.wait_ge((s_dcc).h, (s_dcc).base + (64 * NHP))
            for qq in range(8):
                if w_pre_bf16 and qq % 2 == 0:
                    oh = qq // 4
                    srcw = p.ap[wo].rearrange("(c p) o -> p c o", p=128)[:, :, 512 * oh:512 * oh + 512]
                    if oh == 1:
                        sync.wait_ge((s_tw).h, (s_tw).base + (NT))
                    for c in range(DC):
                        sync.dma_start(out=wo_bf[:, c, :], in_=srcw[:, c, :]).then_inc(*(s_wld).tick(16))
                elif not w_pre_bf16:
                    sync.wait_ge((s_vw).h, (s_vw).base + (12 + qq))
                    srcw = p.ap[wo].rearrange("(c p) o -> p c o", p=128)[:, :, 128 * qq:128 * qq + 128]
                    for c in range(DC):
                        sync.dma_start(out=wstage_c[:, c, :], in_=srcw[:, c, :]).then_inc(*(s_wld).tick(16))


        @blk.tensor
        def _(tensor):
            tensor.wait_ge((s_drb).h, (s_drb).base + (16 * 8))
            for oh in range(2):
                if not w_pre_bf16:
                    tensor.wait_ge((s_vw).h, (s_vw).base + (12 + 4 * (oh + 1)))
                else:
                    tensor.wait_ge((s_wld).h, (s_wld).base + (128 * (12 + oh + 1)))
                for tt in range(NT):
                    e = NT * oh + tt
                    if e >= 2:
                        tensor.wait_ge((s_vwo).h, (s_vwo).base + (e - 1))
                    dst = ps_wo[:, 512 * (e % 2):512 * (e % 2) + 512]
                    for ci in range(DC):
                        ins = nc.tensor.matmul(
                            dst, x_fm[:, ci, 128 * tt:128 * tt + 128],
                            wo_bf[:, ci, 0:512],
                            start=(ci == 0), stop=(ci == DC - 1))
                    ins.then_inc(*(s_tw).tick(1))

        @blk.vector
        def _(vector):
            vector.wait_ge((p.s_ldx).h, (p.s_ldx).base + (256))
            for oh in range(2):
                if not w_pre_bf16:
                    for q4 in range(4):
                        qq = 4 * oh + q4
                        vector.wait_ge((s_wld).h, (s_wld).base + (128 * (12 + qq + 1)))
                        if oh == 1 and q4 == 0:
                            vector.wait_ge((s_tw).h, (s_tw).base + (NT))
                        vector.tensor_copy(
                            wo_bf[:, :, 128 * q4:128 * q4 + 128],
                            wstage_c[:]).then_inc(*(s_vw).tick(1))
                for tt in range(NT):
                    e = NT * oh + tt
                    vector.wait_ge((s_tw).h, (s_tw).base + (e + 1))
                    xt = x_res[:, tt, 512 * oh:512 * oh + 512]
                    vector.scalar_tensor_tensor(
                        xt, ps_wo[:, 512 * (e % 2):512 * (e % 2) + 512], proj_scale, xt,
                        ALU.mult, ALU.add).then_inc(*(s_vwo).tick(1))




def _emit_bitffn(p, blk, x_res, ident_bf, cst):
    nc = p.nc
    sc = contextlib.ExitStack()

    wstg = [p.sb("p2_wstg0", [128, 4096], F32, sc), p.sb("p2_wstg1", [128, 1024], F32, sc)]
    w1qT = p.sb("p2_w1qT", [128, DC, 4096], BF16, sc)
    tern = p.sb("p2_tern", [128, 4096], BF16, sc)
    tstg = p.sb("p2_tstg", [128, 512], BF16, sc)
    acc = p.sb("p2_acc", [128, 4], F32, sc)
    sbc = p.sb("p2_sbc", [128, 4], F32, sc)
    onesf = p.sb("p2_onesf", [128, 1], F32, sc)
    onesr = p.sb("p2_onesr", [1, 128], F32, sc)
    sval = p.sb("p2_sval", [1, 2], F32, sc)
    h_t = p.sb("p2_h", [128, 4096], F32, sc)
    xq_tm = p.sb("p2_xq", [128, D], BF16, sc)
    xq_fm = p.sb("p2_xqfm", [128, DC, 128], BF16, sc)
    hq_tm = p.sb("p2_hqtm", [128, 4096], BF16, sc)
    hq_fm = p.sb("p2_hqfm", [128, 32, 128], BF16, sc)
    w2b = [p.sb(f"p2_w2b{i}", [128, D], BF16, sc) for i in range(2)]
    stt = p.sb("p2_stt", [128, 8], F32, sc)

    ps_t = [sc.enter_context(nc.psum_tensor(f"p2_pst{i}", [128, 512], F32)) for i in range(2)]
    ps_m1 = [sc.enter_context(nc.psum_tensor(f"p2_psm1{i}", [128, 512], F32)) for i in range(2)]
    ps_m2 = [sc.enter_context(nc.psum_tensor(f"p2_psm2{i}", [128, 512], F32)) for i in range(2)]
    ps_s = sc.enter_context(nc.psum_tensor("p2_pss", [1, 128], F32))
    ps_b = sc.enter_context(nc.psum_tensor("p2_psb", [128, 128], F32))

    WCfg = [("l0_w1", 32, 1024, True), ("l0_w2", 8, 4096, False)]
    wprep = {}

    for wi, (wn, ntl, wf, res) in enumerate(WCfg):
        nblk = wf // 128
        ngrp = nblk // 4
        sldA = p.rs("p2w_ldA"); sldB = p.rs("p2w_ldB")
        svr = p.rs("p2w_vr")
        st1 = p.rs("p2w_t1")
        sv1 = p.rs("p2w_v1")
        sld2A = p.rs("p2w_ld2A"); sld2B = p.rs("p2w_ld2B")
        svq = p.rs("p2w_vq")
        s_tt = p.rs("p2w_tt")
        sve = p.rs("p2w_ve")
        sdw = p.rs("p2w_dw")

        wprep[wi] = dict(sve=sve, sdw=sdw, svq=svq, stt=s_tt,
                         sve_total=ntl * (wf // 512), nq=3 * ntl,
                         ntt=ntl * (wf // 128))

        @blk.sync
        def _(sync, wn=wn, ntl=ntl, wf=wf, res=res, sldA=sldA, sldB=sldB,
              svr=svr, sv1=sv1, svq=svq, sve=sve, sdw=sdw, nblk=nblk, ngrp=ngrp,
              sld2A=sld2A, sld2B=sld2B, wi=wi):
            srcw = p.ap[wn].rearrange("(t p) f -> p t f", p=128)
            nb = 2 if wf <= 1024 else 1
            if wi == 1:
                sync.wait_ge(wprep[0]["svq"].h, wprep[0]["svq"].base)
            for t in range(ntl):
                if t >= nb:
                    sync.wait_ge((svr).h, (svr).base + (2 * (t - nb + 1)))
                sync.dma_start(out=wstg[t % nb][:, 0:wf], in_=srcw[:, t, :]).then_inc(*((sldA, sldB)[t % nb]).tick(16))
            sync.wait_ge((sv1).h, (sv1).base + (4))
            for t in range(ntl):
                if t >= nb:
                    sync.wait_ge((svq).h, (svq).base + (3 * (t - nb + 1)))
                sync.dma_start(out=wstg[t % nb][:, 0:wf], in_=srcw[:, t, :]).then_inc(*((sld2A, sld2B)[t % nb]).tick(16))

        if not res:
            @blk.scalar
            def _(scalar, wn=wn, ntl=ntl, res=res, sve=sve, sdw=sdw, ngrp=ngrp):
                for t in range(ntl):
                    for g in range(ngrp):
                        gg = t * ngrp + g
                        scalar.wait_ge((sve).h, (sve).base + (gg + 1))
                        for j in range(4):
                            c = 4 * g + j
                            dst = p.ap["w2qT_d"][128 * c:128 * c + 128,
                                                 128 * t:128 * t + 128]
                            scalar.dma_start(out=dst, in_=tstg[:, 128 * j:128 * j + 128]
                                             ).then_inc(*(sdw).tick(16))

        @blk.vector
        def _(vector, wn=wn, ntl=ntl, wf=wf, res=res, sldA=sldA, sldB=sldB,
              svr=svr, st1=st1, sv1=sv1, svq=svq, s_tt=s_tt, sve=sve, sdw=sdw,
              nblk=nblk, ngrp=ngrp, sld2A=sld2A, sld2B=sld2B, wi=wi):
            if wi == 0:
                vector.memset(onesf[:], 1.0)
                vector.memset(onesr[:], 1.0)
            vector.memset(acc[:, 0:1], 0.0)
            nb = 2 if wf <= 1024 else 1
            for t in range(ntl):
                vector.wait_ge(((sldA, sldB)[t % nb]).h, ((sldA, sldB)[t % nb]).base + (16 * (t // nb + 1)))
                if t > 0:
                    vector.wait_ge((svr).h, (svr).base + (2 * t))
                vector.reduce_sum(acc[:, 2:3], wstg[t % nb][:, 0:wf], AX.X,
                                  apply_absolute_value=True).then_inc(*(svr).tick(1))
                vector.wait_ge((svr).h, (svr).base + (2 * t + 1))
                vector.tensor_tensor(acc[:, 0:1], acc[:, 0:1], acc[:, 2:3],
                                     ALU.add).then_inc(*(svr).tick(1))
            vector.wait_ge((st1).h, (st1).base + (1))
            vector.tensor_scalar(sval[:, wi:wi + 1], ps_s[:, 0:1],
                                 1.0 / (ntl * 128 * wf), None, ALU.mult).then_inc(*(sv1).tick(1))
            vector.wait_ge((st1).h, (st1).base + (2))
            vector.tensor_copy(sbc[:, 2 + wi:3 + wi], ps_b[:, 0:1]).then_inc(*(sv1).tick(1))
            vector.wait_ge((sv1).h, (sv1).base + (2))
            vector.reciprocal(sbc[:, wi:wi + 1], sbc[:, 2 + wi:3 + wi]).then_inc(*(sv1).tick(2))
            if wi == 1:
                vector.wait_ge(wprep[0]["stt"].h, wprep[0]["stt"].base)
            for t in range(ntl):
                vector.wait_ge(((sld2A, sld2B)[t % nb]).h, ((sld2A, sld2B)[t % nb]).base + (16 * (t // nb + 1)))
                if t > 0:
                    vector.wait_ge((s_tt).h, (s_tt).base + (nblk * t))
                wv = wstg[t % nb][:, 0:wf]
                vector.tensor_scalar(wv, wv, sbc[:, wi:wi + 1], MAGIC,
                                     ALU.mult, ALU.add).then_inc(*(svq).tick(1))
                vector.wait_ge((svq).h, (svq).base + (3 * t + 1))
                vector.tensor_scalar(wv, wv, MAGIC, -1.0,
                                     ALU.subtract, ALU.max).then_inc(*(svq).tick(1))
                vector.wait_ge((svq).h, (svq).base + (3 * t + 2))
                vector.tensor_scalar_min(tern[:, 0:wf], wv, 1.0).then_inc(*(svq).tick(1))
                for g in range(ngrp):
                    gg = t * ngrp + g
                    vector.wait_ge((s_tt).h, (s_tt).base + (nblk * t + 4 * (g + 1)))
                    srcp = ps_t[g % 2][:].rearrange("p (c k) -> p c k", c=4)
                    if res:
                        dstv = w1qT[:, 4 * g:4 * g + 4, 128 * t:128 * t + 128]
                        vector.tensor_copy(dstv, srcp).then_inc(*(sve).tick(1))
                    else:
                        if gg > 0:
                            vector.wait_ge((sdw).h, (sdw).base + (64 * gg))
                        vector.tensor_copy(tstg[:], ps_t[g % 2][:]).then_inc(*(sve).tick(1))

        @blk.tensor
        def _(tensor, wn=wn, ntl=ntl, wf=wf, res=res, svr=svr, st1=st1, sv1=sv1,
              svq=svq, s_tt=s_tt, sve=sve, nblk=nblk, ngrp=ngrp, wi=wi):
            if wi == 1:
                tensor.wait_ge(wprep[0]["sve"].h, wprep[0]["sve"].base)
            tensor.wait_ge((svr).h, (svr).base + (2 * ntl))
            nc.tensor.matmul(ps_s[:, 0:1], acc[:, 0:1], onesf[:],
                             start=True, stop=True, skip_group_check=True).then_inc(*(st1).tick(1))
            tensor.wait_ge((sv1).h, (sv1).base + (1))
            nc.tensor.matmul(ps_b[:, 0:1], onesr[:], sval[:, wi:wi + 1],
                             start=True, stop=True, skip_group_check=True).then_inc(*(st1).tick(1))
            for t in range(ntl):
                tensor.wait_ge((svq).h, (svq).base + (3 * (t + 1)))
                for c in range(nblk):
                    g, j = c // 4, c % 4
                    if t * ngrp + g >= 2:
                        tensor.wait_ge((sve).h, (sve).base + (t * ngrp + g - 1))
                    nc.tensor.matmul(
                        ps_t[g % 2][:, 128 * j:128 * j + 128],
                        tern[:, 128 * c:128 * c + 128], ident_bf[:],
                        start=True, stop=True, skip_group_check=True).then_inc(*(s_tt).tick(1))

    s_vx = p.rs("p2_vx")
    s_tx = p.rs("p2_tx")
    s_vxe = p.rs("p2_vxe")
    s_tm1 = p.rs("p2_tm1")
    s_sg = p.rs("p2_sg")
    s_vh = p.rs("p2_vh")
    s_th = p.rs("p2_th")
    s_vhq = p.rs("p2_vhq")
    s_tm2 = p.rs("p2_tm2")
    s_vr2 = p.rs("p2_vr2")
    s_wb = [p.rs("p2_wbA"), p.rs("p2_wbB")]
    V_X = 9

    @blk.sync
    def _(sync):
        sync.wait_ge((wprep[1]["sdw"]).h, (wprep[1]["sdw"]).base + (64 * 64 * 8 // 8))
        for tt in range(NT):
            for i2 in range(32):
                k = 32 * tt + i2
                if k >= 2:
                    sync.wait_ge((s_tm2).h, (s_tm2).base + (k - 1))
                src2 = p.ap["w2qT_d"][128 * i2:128 * i2 + 128, :]
                sync.dma_start(out=w2b[k % 2][:], in_=src2).then_inc(*(s_wb[k % 2]).tick(16))

    @blk.vector
    def _(vector):
        for tt in range(NT):
            xt = x_res[:, tt, :]
            n0 = V_X * tt

            def step(ins, k=[0]):
                ins.then_inc(*(s_vx).tick(1))
                k[0] += 1
                vector.wait_ge((s_vx).h, (s_vx).base + (n0 + k[0]))

            step(vector.reduce_max(stt[:, 0:1], xt, AX.X, apply_absolute_value=True))
            step(vector.tensor_scalar_max(stt[:, 0:1], stt[:, 0:1], 1e-5))
            step(vector.reciprocal(stt[:, 1:2], stt[:, 0:1]))
            step(vector.tensor_scalar(stt[:, 2:3], stt[:, 1:2], 127.0, None, ALU.mult))
            step(vector.tensor_scalar(stt[:, 3:4], stt[:, 0:1], 1.0 / 127.0, None, ALU.mult))
            step(vector.tensor_tensor(stt[:, 4:5], stt[:, 3:4], sbc[:, 2:3], ALU.mult))
            if tt > 0:
                vector.wait_ge((s_tx).h, (s_tx).base + (8 * tt))
                vector.wait_ge((s_sg).h, (s_sg).base + (8 * tt))
            qt = h_t[:, 0:D]
            step(vector.tensor_scalar(qt, xt, stt[:, 2:3], MAGIC, ALU.mult, ALU.add))
            step(vector.tensor_scalar(qt, qt, MAGIC, -128.0, ALU.subtract, ALU.max))
            step(vector.tensor_scalar_min(xq_tm[:], qt, 127.0))
            for g in range(2):
                vector.wait_ge((s_tx).h, (s_tx).base + (8 * tt + 4 * (g + 1)))
                srcp = ps_t[g % 2][:].rearrange("p (c k) -> p c k", c=4)
                vector.tensor_copy(xq_fm[:, 4 * g:4 * g + 4, :], srcp).then_inc(*(s_vxe).tick(1))
            m0 = 8 * tt

            def hstep(ins, k=[0]):
                ins.then_inc(*(s_vh).tick(1))
                k[0] += 1
                vector.wait_ge((s_vh).h, (s_vh).base + (m0 + k[0]))

            vector.wait_ge((s_sg).h, (s_sg).base + (8 * (tt + 1)))
            hstep(vector.reduce_max(stt[:, 5:6], h_t[:], AX.X, apply_absolute_value=True))
            hstep(vector.tensor_scalar_max(stt[:, 5:6], stt[:, 5:6], 1e-5))
            hstep(vector.reciprocal(stt[:, 6:7], stt[:, 5:6]))
            hstep(vector.tensor_scalar(stt[:, 7:8], stt[:, 6:7], 127.0, None, ALU.mult))
            if tt > 0:
                vector.wait_ge((s_th).h, (s_th).base + (32 * tt))
            hstep(vector.tensor_scalar(h_t[:], h_t[:], stt[:, 7:8], MAGIC, ALU.mult, ALU.add))
            hstep(vector.tensor_scalar(h_t[:], h_t[:], MAGIC, -128.0, ALU.subtract, ALU.max))
            hstep(vector.tensor_scalar_min(hq_tm[:], h_t[:], 127.0))
            hstep(vector.tensor_scalar(stt[:, 5:6], stt[:, 5:6], 1.0 / 127.0, None, ALU.mult))
            vector.wait_ge((s_vh).h, (s_vh).base + (m0 + 8))
            vector.tensor_tensor(stt[:, 6:7], stt[:, 5:6], sbc[:, 3:4], ALU.mult)
            for g in range(8):
                vector.wait_ge((s_th).h, (s_th).base + (32 * tt + 4 * (g + 1)))
                srcp = ps_t[g % 2][:].rearrange("p (c k) -> p c k", c=4)
                vector.tensor_copy(hq_fm[:, 4 * g:4 * g + 4, :], srcp).then_inc(*(s_vhq).tick(1))
            for oh in range(2):
                vector.wait_ge((s_tm2).h, (s_tm2).base + (32 * (tt + 1)))
                vector.scalar_tensor_tensor(
                    x_res[:, tt, 512 * oh:512 * oh + 512],
                    ps_m2[oh][:], stt[:, 6:7],
                    x_res[:, tt, 512 * oh:512 * oh + 512],
                    ALU.mult, ALU.add).then_inc(*(s_vr2).tick(1))

    @blk.scalar
    def _(scalar):
        for tt in range(NT):
            for oc in range(8):
                k = 8 * tt + oc
                scalar.wait_ge((s_tm1).h, (s_tm1).base + (k + 1))
                scalar.activation(
                    out=h_t[:, 512 * oc:512 * oc + 512],
                    in_=ps_m1[oc % 2][:], func=AF.Gelu,
                    scale=stt[:, 4:5]).then_inc(*(s_sg).tick(1))

    @blk.tensor
    def _(tensor):
        tensor.wait_ge((wprep[1]["sve"]).h, (wprep[1]["sve"]).base + (wprep[1]["sve_total"]))
        tensor.wait_ge((wprep[1]["sdw"]).h, (wprep[1]["sdw"]).base + (64 * wprep[1]["sve_total"]))
        for tt in range(NT):
            tensor.wait_ge((s_vx).h, (s_vx).base + (V_X * tt + 9))
            if tt == 0:
                pass
            for c in range(8):
                g, j = c // 4, c % 4
                gg = 2 * tt + g
                if gg >= 2:
                    tensor.wait_ge((s_vxe).h, (s_vxe).base + (gg - 1))
                nc.tensor.matmul(
                    ps_t[g % 2][:, 128 * j:128 * j + 128],
                    xq_tm[:, 128 * c:128 * c + 128], ident_bf[:],
                    start=True, stop=True, skip_group_check=True).then_inc(*(s_tx).tick(1))
            tensor.wait_ge((s_vxe).h, (s_vxe).base + (2 * tt + 2))
            for oc in range(8):
                k = 8 * tt + oc
                if k >= 2:
                    tensor.wait_ge((s_sg).h, (s_sg).base + (k - 1))
                for ci in range(DC):
                    ins = nc.tensor.matmul(
                        ps_m1[oc % 2][:], xq_fm[:, ci, :],
                        w1qT[:, ci, 512 * oc:512 * oc + 512],
                        start=(ci == 0), stop=(ci == DC - 1),
                        skip_group_check=True)
                ins.then_inc(*(s_tm1).tick(1))
            tensor.wait_ge((s_vh).h, (s_vh).base + (8 * tt + 7))
            for c in range(32):
                g, j = c // 4, c % 4
                gg = 8 * tt + g
                if gg >= 2:
                    tensor.wait_ge((s_vhq).h, (s_vhq).base + (gg - 1))
                nc.tensor.matmul(
                    ps_t[g % 2][:, 128 * j:128 * j + 128],
                    hq_tm[:, 128 * c:128 * c + 128], ident_bf[:],
                    start=True, stop=True, skip_group_check=True).then_inc(*(s_th).tick(1))
            tensor.wait_ge((s_vhq).h, (s_vhq).base + (8 * tt + 8))
            if tt > 0:
                tensor.wait_ge((s_vr2).h, (s_vr2).base + (2 * tt))
            for i2 in range(32):
                k = 32 * tt + i2
                tensor.wait_ge((s_wb[k % 2]).h, (s_wb[k % 2]).base + (16 * (k // 2 + 1)))
                for oh in range(2):
                    ins = nc.tensor.matmul(
                        ps_m2[oh][:], hq_fm[:, i2, :],
                        w2b[k % 2][:, 512 * oh:512 * oh + 512],
                        start=(i2 == 0), stop=(i2 == 31),
                        skip_group_check=True)
                ins.then_inc(*(s_tm2).tick(1))

    sc.close()


def _emit_wT(p, blk, srcname, dstname, ntl, C, identt, tag):
    nc = p.nc
    nblk = C // 128
    ngrp = nblk // 4
    sc = contextlib.ExitStack()
    wstg = [p.sb(f"{tag}_wstg{i}", [128, C], F32, sc) for i in range(2)]
    tstg = p.sb(f"{tag}_tstg", [128, 512], F32, sc)
    sldA = p.rs("ga_ldA"); sldB = p.rs("ga_ldB")
    s_tt = p.rs("ga_tt")
    sve = p.rs("ga_ve")
    sdw = p.rs("ga_dw")
    ps = [sc.enter_context(nc.psum_tensor(f"{tag}_ps{i}", [128, 512], F32))
          for i in range(2)]

    @blk.sync
    def _(sync):
        srcw = p.ap[srcname].rearrange("(t p) f -> p t f", p=128)
        for t in range(ntl):
            if t >= 2:
                sync.wait_ge((s_tt).h, (s_tt).base + (nblk * (t - 1)))
            sync.dma_start(out=wstg[t % 2][:], in_=srcw[:, t, :]).then_inc(*((sldA, sldB)[t % 2]).tick(16))

    @blk.tensor
    def _(tensor):
        for t in range(ntl):
            tensor.wait_ge(((sldA, sldB)[t % 2]).h, ((sldA, sldB)[t % 2]).base + (16 * (t // 2 + 1)))
            for c in range(nblk):
                g, j = c // 4, c % 4
                gg = t * ngrp + g
                if gg >= 2:
                    tensor.wait_ge((sve).h, (sve).base + (gg - 1))
                nc.tensor.transpose(
                    ps[g % 2][:, 128 * j:128 * j + 128],
                    wstg[t % 2][:, 128 * c:128 * c + 128], identt[:]
                ).then_inc(*(s_tt).tick(1))

    @blk.vector
    def _(vector):
        for t in range(ntl):
            for g in range(ngrp):
                gg = t * ngrp + g
                vector.wait_ge((s_tt).h, (s_tt).base + (nblk * t + 4 * (g + 1)))
                vector.wait_ge((sdw).h, (sdw).base + (64 * gg))
                vector.tensor_copy(tstg[:], ps[g % 2][:]).then_inc(*(sve).tick(1))

    @blk.scalar
    def _(scalar):
        for t in range(ntl):
            for g in range(ngrp):
                gg = t * ngrp + g
                scalar.wait_ge((sve).h, (sve).base + (gg + 1))
                for j in range(4):
                    c = 4 * g + j
                    dst = p.ap[dstname][128 * c:128 * c + 128, 128 * t:128 * t + 128]
                    scalar.dma_start(out=dst, in_=tstg[:, 128 * j:128 * j + 128]
                                     ).then_inc(*(sdw).tick(16))
    sc.close()
    return sdw, sdw.ctr


def _emit_x2gather(p, blk, x_res, x_fm3, identt, tag="x2g"):
    nc = p.nc
    sc = contextlib.ExitStack()
    fms = [p.sb(f"{tag}_fms{i}", [128, DC, 128], BF16, sc) for i in range(2)]
    stx = p.rs("ga_tx")
    sve = p.rs("ga_ve")
    sdin = [p.rs("ga_din0"), p.rs("ga_din1")]
    sgc = p.rs("ga_gc")
    sdrb = p.rs("ga_drb")
    ps = [sc.enter_context(nc.psum_tensor(f"{tag}_ps{i}", [128, 1024], F32))
          for i in range(2)]

    @blk.tensor
    def _(tensor):
        for tt in range(NT):
            if tt >= 2:
                tensor.wait_ge((sve).h, (sve).base + (tt - 1))
            for c in range(DC):
                ins = nc.tensor.transpose(
                    ps[tt % 2][:, 128 * c:128 * c + 128],
                    x_res[:, tt, 128 * c:128 * c + 128], identt[:])
            ins.then_inc(*(stx).tick(1))

    @blk.vector
    def _(vector):
        for tt in range(NT):
            vector.wait_ge((stx).h, (stx).base + (tt + 1))
            if tt >= 2:
                vector.wait_ge((sdin[tt % 2]).h, (sdin[tt % 2]).base + (128 * (tt // 2)))
            srcp = ps[tt % 2][:].rearrange("p (c k) -> p c k", c=DC)
            vector.tensor_copy(fms[tt % 2][:], srcp).then_inc(*(sve).tick(1))

    @blk.sync
    def _(sync):
        for tt in range(NT):
            sync.wait_ge((sve).h, (sve).base + (tt + 1))
            for c in range(DC):
                dst = p.ap["x2cc_in"][128 * c:128 * c + 128, 128 * tt:128 * tt + 128]
                sync.dma_start(out=dst, in_=fms[tt % 2][:, c, :]).then_inc(*(sdin[tt % 2]).tick(16))

    @blk.gpsimd
    def _(gp):
        gp.wait_ge((sdin[0]).h, (sdin[0]).base + (128 * NT // 2))
        gp.wait_ge((sdin[1]).h, (sdin[1]).base + (128 * NT // 2))
        gp.collective_compute(
            "AllGather", ALU.bypass, replica_groups=RG,
            ins=[p.ap["x2cc_in"][:]], outs=[p.ap["x2cc_out"][:]],
        ).then_inc(*(sgc).tick(1))

    @blk.sync
    def _(sync):
        sync.wait_ge((sgc).h, (sgc).base + (1))
        for rank in range(2):
            for c in range(DC):
                srcd = p.ap["x2cc_out"][rank][128 * c:128 * c + 128, :]
                sync.dma_start(out=x_fm3[:, c, HALF * rank:HALF * rank + HALF],
                               in_=srcd).then_inc(*(sdrb).tick(16))
        sync.wait_ge((sdrb).h, (sdrb).base + (256))
    sc.close()


PW = HALF + 513


def _emit_inner(p, blk, x_res, identt, ident_bf, tri_pt, tril_pt, cst):
    nc = p.nc
    s_x3 = p.rs("in_x3")

    s_lnc = p.rs("ga_v")

    @blk.sync
    def _(sync):
        sync.wait_ge((s_lnc).h, (s_lnc).base + (0))
        x3d = p.ap["x3_d"].rearrange("(t p) d -> p t d", p=128)
        for tt in range(NT):
            sync.dma_start(out=x3d[:, tt, :], in_=x_res[:, tt, :]).then_inc(*(s_x3).tick(16))
        sync.wait_ge((s_x3).h, (s_x3).base + (16 * NT))
    p.barrier(blk)

    for il in range(2):
        _emit_inner_attn(p, blk, x_res, identt, ident_bf, tri_pt, tril_pt, cst, il)
        p.barrier(blk)
        _emit_inner_ffn(p, blk, x_res, identt, ident_bf, cst, il)
        p.barrier(blk)

    sc = contextlib.ExitStack()
    x3s = [p.sb(f"in_x3s{i}", [128, D], F32, sc) for i in range(2)]
    s_xb = [p.rs("in_xbA"), p.rs("in_xbB")]
    s_xv = p.rs("in_xv")

    @blk.sync
    def _(sync):
        x3d = p.ap["x3_d"].rearrange("(t p) d -> p t d", p=128)
        for tt in range(NT):
            if tt >= 2:
                sync.wait_ge((s_xv).h, (s_xv).base + (tt - 1))
            sync.dma_start(out=x3s[tt % 2][:], in_=x3d[:, tt, :]).then_inc(
                *(s_xb[tt % 2]).tick(16))

    s_frc = p.rs("f4_vr")

    @blk.vector
    def _(vector):
        vector.wait_ge((s_frc).h, (s_frc).base + (0))
        for tt in range(NT):
            vector.wait_ge((s_xb[tt % 2]).h, (s_xb[tt % 2]).base + (16 * (tt // 2 + 1)))
            vector.tensor_tensor(x_res[:, tt, :], x_res[:, tt, :], x3s[tt % 2][:],
                                 ALU.add).then_inc(*(s_xv).tick(1))
    sc.close()


def _norm_rot(p, blk, buf, width, toff, ta, tb, sccol, B, ready):
    nc = p.nc
    s_sq = p.rs("nr_sq")
    s_tn = p.rs("nr_tn")
    s_vn = p.rs("nr_vn")
    s_tb = p.rs("nr_tb")
    s_vtr = p.rs("p4_tl2")
    nch = cdiv(width, 512)
    n2, nrb, rt1, rt2, sqs = B["n2"], B["nrb"], B["rt1"], B["rt2"], B["sqs"]
    bo33, hsel, qs8 = B["bo33"], B["hsel"], B["qs8"]
    n2b = B["n2b"]
    ps_n = B["ps_n"]
    ps_b = B["ps_b"]

    @blk.scalar
    def _(scalar):
        scalar.wait_ge(ready[0], ready[1])
        scalar.wait_ge((s_tn).h, (s_tn).base + (0))
        scalar.wait_ge((s_vn).h, (s_vn).base + (0))
        scalar.activation(out=sqs[:, 0:width], in_=buf[:, 0:width],
                          func=AF.Square).then_inc(*(s_sq).tick(1))
        scalar.wait_ge((s_vn).h, (s_vn).base + (nch))
        scalar.activation(out=n2[:, 0:width], in_=n2[:, 0:width],
                          func=AF.Sqrt).then_inc(*(s_sq).tick(1))

    @blk.tensor
    def _(tensor):
        tensor.wait_ge((s_sq).h, (s_sq).base + (1))
        for cch in range(nch):
            w = min(512, width - 512 * cch)
            if cch >= 1:
                tensor.wait_ge((s_vn).h, (s_vn).base + (cch))
            nc.tensor.matmul(ps_n[0:2, 0:w], bo33[:],
                             sqs[:, 512 * cch:512 * cch + w],
                             start=True, stop=True,
                             skip_group_check=True).then_inc(*(s_tn).tick(1))
        tensor.wait_ge((s_vn).h, (s_vn).base + (nch + 2))
        for cch in range(nch):
            w = min(512, width - 512 * cch)
            if cch >= 1:
                tensor.wait_ge((s_vn).h, (s_vn).base + (nch + 2 + cch))
            nc.tensor.matmul(ps_b[:, 0:w], hsel[:],
                             n2b[:, 512 * cch:512 * cch + w],
                             start=True, stop=True,
                             skip_group_check=True).then_inc(*(s_tb).tick(1))
        tensor.wait_ge(p.s_rsw.h, 48)
        base2 = 2 * nch + 2
        for cch in range(nch):
            w = min(512, width - 512 * cch)
            tensor.wait_ge((s_vn).h, (s_vn).base + (base2 + 2 if cch == 0
                                                    else base2 + 3 + cch))
            nc.tensor.matmul(ps_b[:, 0:w], p.rsw[:],
                             buf[:, 512 * cch:512 * cch + w],
                             start=True, stop=True,
                             skip_group_check=True).then_inc(*(s_tb).tick(1))

    @blk.vector
    def _(vector):
        vector.wait_ge((s_vtr).h, (s_vtr).base + (0))
        vector.wait_ge(p.p4_qs_ready[0], p.p4_qs_ready[1])
        for cch in range(nch):
            w = min(512, width - 512 * cch)
            vector.wait_ge((s_tn).h, (s_tn).base + (cch + 1))
            vector.tensor_copy(n2[:, 512 * cch:512 * cch + w],
                               ps_n[0:2, 0:w]).then_inc(*(s_vn).tick(1))
        vector.wait_ge((s_sq).h, (s_sq).base + (2))
        vector.tensor_scalar(n2[:, 0:width], n2[:, 0:width], 1e-12, None,
                             ALU.max).then_inc(*(s_vn).tick(1))
        vector.wait_ge((s_vn).h, (s_vn).base + (nch + 1))
        with nc.allow_low_precision(reason="l2 norms tolerate reduced precision"):
            vector.reciprocal(n2b[:, 0:width], n2[:, 0:width]).then_inc(*(s_vn).tick(1))
        for cch in range(nch):
            w = min(512, width - 512 * cch)
            vector.wait_ge((s_tb).h, (s_tb).base + (cch + 1))
            vector.tensor_copy(nrb[:, 512 * cch:512 * cch + w],
                               ps_b[:, 0:w]).then_inc(*(s_vn).tick(1))
        bufv = buf[:, 0:width]
        base = 2 * nch + 2
        vector.wait_ge((s_vn).h, (s_vn).base + (base))
        vector.tensor_tensor(bufv, bufv, nrb[:, 0:width], ALU.mult).then_inc(*(s_vn).tick(1))
        vector.wait_ge((s_vn).h, (s_vn).base + (base + 1))
        vector.tensor_scalar(bufv, bufv, qs8[:, sccol:sccol + 1], None,
                             ALU.mult).then_inc(*(s_vn).tick(1))
        vector.wait_ge((s_vn).h, (s_vn).base + (base + 2))
        vector.tensor_tensor(rt1[:, 0:width], bufv, ta[:, toff:toff + width],
                             ALU.mult).then_inc(*(s_vn).tick(1))
        for cch in range(nch):
            w = min(512, width - 512 * cch)
            vector.wait_ge((s_tb).h, (s_tb).base + (nch + cch + 1))
            vector.tensor_tensor(rt2[:, 512 * cch:512 * cch + w],
                                 ps_b[:, 0:w],
                                 tb[:, toff + 512 * cch:toff + 512 * cch + w],
                                 ALU.mult).then_inc(*(s_vn).tick(1))
        vector.wait_ge((s_vn).h, (s_vn).base + (base + 3 + nch))
        vector.tensor_tensor(bufv, rt1[:, 0:width], rt2[:, 0:width],
                             ALU.add).then_inc(*(s_vn).tick(1))
    return s_vn




def _emit_inner_attn(p, blk, x_res, identt, ident_bf, tri_pt, tril_pt, cst, il):
    nc = p.nc
    sc = contextlib.ExitStack()

    x_fm4 = p.sb(f"p4{il}_xfm", [128, DC, 2 + PW], BF16, sc)
    q4 = p.sb(f"p4{il}_q", [128, HALF], BF16, sc)
    k4 = p.sb(f"p4{il}_k", [128, WINDOW + HALF], BF16, sc)
    v4 = p.sb(f"p4{il}_v", [128, 20 * 128], BF16, sc)
    wst = p.sb(f"p4{il}_wst", [128, DC, 128], F32, sc)
    wbf = [p.sb(f"p4{il}_wbf{i}", [128, DC, 128], BF16, sc) for i in range(3)]
    tA = {nm: p.sb(f"p4{il}_t{nm}", [128, HALF if nm in ("aq", "bq") else PW],
                   BF16, sc)
          for nm in ("aq", "bq", "ak", "bk")}
    qs8 = p.sb(f"p4{il}_qs8", [128, 2], F32, sc)
    n2 = p.sb(f"p4{il}_n2", [2, PW], F32, sc)
    n2b = p.sb(f"p4{il}_n2b", [2, PW], BF16, sc)
    nrb = p.sb(f"p4{il}_nrb", [128, PW], BF16, sc)
    rt1 = p.sb(f"p4{il}_rt1", [128, PW], BF16, sc)
    rt2 = p.sb(f"p4{il}_rt2", [128, PW], BF16, sc)
    sqs = rt1
    bo33 = p.sb(f"p4{il}_bo33", [128, 2], BF16, sc)
    hsel = p.sb(f"p4{il}_hsel", [2, 128], BF16, sc)
    hself = p.sb(f"p4{il}_hself", [2, 128], F32, sc)
    or64 = p.sb(f"p4{il}_or64", [1, 64], BF16, sc)
    orf = p.sb(f"p4{il}_orf", [1, 64], F32, sc)
    oc1 = p.sb(f"p4{il}_oc1", [128, 1], BF16, sc)
    p_f = p.sb(f"p4{il}_pf", [128, 640], F32, sc)
    p_fb = p.sb(f"p4{il}_pfb", [128, 640], BF16, sc)
    lcol = p.sb(f"p4{il}_lcol", [128, 2], F32, sc)
    p_bf = [p.sb(f"p4{il}_pbf{i}", [128, 5, 128], BF16, sc) for i in range(2)]
    negm = p.sb(f"p4{il}_negm", [128, 2], F32, sc)
    ot4 = p.sb(f"p4{il}_ot4", [128, NHP, HALF], BF16, sc)
    wo4 = p.sb(f"p4{il}_wo4", [128, 4, 512], BF16, sc)

    ps_sa = sc.enter_context(nc.psum_tensor(f"p4a{il}_sa", [128, 1024], F32))
    ps_sb = sc.enter_context(nc.psum_tensor(f"p4a{il}_sb", [128, 1024], F32))
    ps_pt = sc.enter_context(nc.psum_tensor(f"p4a{il}_pt", [128, 640], F32))
    ps_ot = sc.enter_context(nc.psum_tensor(f"p4a{il}_ot", [128, 512], F32))
    ps_l = sc.enter_context(nc.psum_tensor(f"p4a{il}_l", [33, 512], F32))

    B = dict(n2=n2, n2b=n2b, nrb=nrb, rt1=rt1, rt2=rt2, sqs=sqs, bo33=bo33,
             hsel=hsel, hself=hself, qs8=qs8, ps_n=ps_l, ps_b=ps_ot)
    p.p4_qs_ready = None

    s_tx = p.rs("ga_tx"); s_vx = p.rs("ga_vx")

    @blk.tensor
    def _(tensor):
        for role in ("ga_vwo", "ga_v", "f4_vr"):
            rr = p.rs(role)
            tensor.wait_ge(rr.h, rr.base + 0)
        for tt in range(NT):
            if tt >= 2:
                tensor.wait_ge((s_vx).h, (s_vx).base + (tt - 1))
            pst = ps_sa if tt % 2 == 0 else ps_sb
            for c in range(DC):
                ins = nc.tensor.transpose(
                    pst[:, 128 * c:128 * c + 128],
                    x_res[:, tt, 128 * c:128 * c + 128], identt[:])
            ins.then_inc(*(s_tx).tick(1))

    @blk.vector
    def _(vector):
        vector.memset(x_fm4[:, :, 0:2], 0.0)
        vector.memset(bo33[0:64, 0:1], 1.0)
        vector.memset(bo33[64:128, 0:1], 0.0)
        vector.memset(bo33[0:64, 1:2], 0.0)
        vector.memset(bo33[64:128, 1:2], 1.0)
        vector.memset(or64[:], 1.0)
        vector.memset(orf[:], 1.0)
        vector.memset(oc1[:], 1.0)
        for tt in range(NT):
            vector.wait_ge((s_tx).h, (s_tx).base + (tt + 1))
            srcp = (ps_sa if tt % 2 == 0 else ps_sb)[:].rearrange("p (c k) -> p c k", c=DC)
            vector.tensor_copy(x_fm4[:, :, 515 + 128 * tt:515 + 128 * tt + 128],
                               srcp).then_inc(*(s_vx).tick(1))

    s_din = p.rs("ga_din0"); s_gc = p.rs("ga_gc"); s_drb = p.rs("ga_drb")
    s_vh = p.rs("p4_vh"); s_tld = p.rs("p4_tld")
    s_tl2 = p.rs("p4_tl2")
    p.p4_qs_ready = (s_vh.h, s_vh.base + 2)

    @blk.sync
    def _(sync):
        sync.wait_ge((s_vx).h, (s_vx).base + (NT))
        for c in range(DC):
            dst = p.ap["shcc_in"][128 * c:128 * c + 128, :]
            sync.dma_start(out=dst, in_=x_fm4[:, c, 2050:2563]).then_inc(*(s_din).tick(16))

    @blk.gpsimd
    def _(gp):
        gp.wait_ge((s_din).h, (s_din).base + (16 * DC))
        gp.collective_compute(
            "AllGather", ALU.bypass, replica_groups=RG,
            ins=[p.ap["shcc_in"][:]], outs=[p.ap["shcc_out"][:]],
        ).then_inc(*(s_gc).tick(1))

    @blk.sync
    def _(sync):
        sync.wait_ge((s_gc).h, (s_gc).base + (1))
        for arm in sync.switch_core_id(n=128):
            pr = 1 - (arm.logical % 2)
            srcd = p.ap["shcc_out"][pr]
            for c in range(DC):
                sync.dma_start(out=x_fm4[:, c, 2:515],
                               in_=srcd[128 * c:128 * c + 128, :]).then_inc(s_drb.h, 16)
        s_drb.ctr += 16 * DC
        for half128 in range(2):
            b = 64 * half128
            sync.dma_start(out=qs8[b:b + 64, 0:1],
                           in_=p.ap["i_qs"][il][:, None]).then_inc(*(s_tld).tick(16))
            sync.dma_start(out=qs8[b:b + 64, 1:2],
                           in_=p.ap["i_ks"][il][:, None]).then_inc(*(s_tld).tick(16))
        for nm in ("aq", "bq", "ak", "bk"):
            sync.dma_start(out=tA[nm][:], in_=p.ap[f"rot_{nm}"][:]).then_inc(*(s_tl2).tick(16))
        sync.dma_start(out=hsel[:], in_=p.ap["hsel_bf"][:]).then_inc(*(s_tl2).tick(16))
        sync.dma_start(out=hself[:], in_=p.ap["hsel_f"][:]).then_inc(*(s_tl2).tick(16))

    @blk.vector
    def _(vector):
        vector.wait_ge((s_drb).h, (s_drb).base + (16 * DC))
        vector.wait_ge(p.s_hv0.h, 16)
        vector.tensor_scalar(x_fm4[:, :, 2:515], x_fm4[:, :, 2:515],
                             p.hv[:, 0:1], None, ALU.mult).then_inc(*(s_vh).tick(1))
        vector.wait_ge((s_tld).h, (s_tld).base + (64))
        vector.tensor_scalar(qs8[:, 0:1], qs8[:, 0:1], QK_SCALE, None,
                             ALU.mult).then_inc(*(s_vh).tick(1))

    for hp in range(NHP):
        _inner_attn_hp(p, blk, hp, il, x_res, x_fm4, q4, k4, v4, wst, wbf,
                       tA, B, p_f, p_fb, lcol, p_bf, negm, ot4, oc1, or64,
                       tri_pt, tril_pt, cst, identt, ident_bf,
                       ps_sa, ps_sb, ps_pt, ps_ot, ps_l,
                       s_vh)

    s_wld = p.rs("ga_wld"); s_vw = p.rs("ga_vw")
    s_tw = p.rs("ga_tw"); s_vwo = p.rs("ga_vwo")
    s_ta = p.rs("ga_tav"); s_vo = p.rs("ga_vo"); s_vb = p.rs("ga_vb")

    @blk.sync
    def _(sync):
        sync.wait_ge((s_vo).h, (s_vo).base + (0))
        sync.wait_ge((s_vw).h, (s_vw).base + (0))
        srcw = p.ap["i_wo"][il].rearrange("(t p) o -> p t o", p=128)
        for oh in range(2):
            for t4 in range(4):
                e = 4 * oh + t4
                if e > 0:
                    sync.wait_ge((s_vw).h, (s_vw).base + (e))
                for j in range(4):
                    sync.dma_start(
                        out=wst[:, j, :],
                        in_=srcw[:, t4, 512 * oh + 128 * j:512 * oh + 128 * j + 128],
                    ).then_inc(*(s_wld).tick(16))

    @blk.vector
    def _(vector):
        for oh in range(2):
            for t4 in range(4):
                e = 4 * oh + t4
                vector.wait_ge((s_wld).h, (s_wld).base + (64 * (e + 1)))
                if oh > 0:
                    vector.wait_ge((s_tw).h, (s_tw).base + (16 * oh))
                vector.tensor_copy(
                    wo4[:, t4, :],
                    wst[:, 0:4, :].rearrange("p c k -> p (c k)")).then_inc(*(s_vw).tick(1))
            for tt in range(NT):
                g = 16 * oh + tt
                vector.wait_ge((s_tw).h, (s_tw).base + (g + 1))
                xt = x_res[:, tt, 512 * oh:512 * oh + 512]
                pst = (ps_sa if g % 2 == 0 else ps_sb)[:, 0:512]
                vector.tensor_tensor(xt, pst, xt, ALU.add).then_inc(*(s_vwo).tick(1))

    @blk.tensor
    def _(tensor):
        tensor.wait_ge((s_vo).h, (s_vo).base + (0))
        for oh in range(2):
            tensor.wait_ge((s_vw).h, (s_vw).base + (4 * (oh + 1)))
            for tt in range(NT):
                g = 16 * oh + tt
                if g >= 2:
                    tensor.wait_ge((s_vwo).h, (s_vwo).base + (g - 1))
                dst = (ps_sa if g % 2 == 0 else ps_sb)[:, 0:512]
                for c4 in range(4):
                    ins = nc.tensor.matmul(
                        dst, ot4[:, c4, 128 * tt:128 * tt + 128],
                        wo4[:, c4, :],
                        start=(c4 == 0), stop=(c4 == 3), skip_group_check=True)
                ins.then_inc(*(s_tw).tick(1))
    sc.close()


def _inner_attn_hp(p, blk, hp, il, x_res, x_fm4, q4, k4, v4, wst, wbf,
                   tA, B, p_f, p_fb, lcol, p_bf, negm, ot4, oc1, or64,
                   tri_pt, tril_pt, cst, identt, ident_bf,
                   ps_sa, ps_sb, ps_pt, ps_ot, ps_l, s_vh):
    nc = p.nc
    KW = WINDOW + HALF
    hsel2 = B["hsel"]
    s_wld = p.rs("ga_wld"); s_vw = p.rs("ga_vw")
    s_tq = p.rs("ga_tq"); s_vq = p.rs("ga_vq")
    s_tv = p.rs("ga_tv"); s_vv = p.rs("ga_vv")

    @blk.sync
    def _(sync, hp=hp):
        for wi in range(3):
            sync.wait_ge((s_vw).h, (s_vw).base + (wi))
            col = (0, 512, 1024)[wi] + 128 * hp
            srcw = p.ap["i_wqkv"][il].rearrange("(c p) o -> p c o", p=128)
            for c in range(DC):
                sync.dma_start(out=wst[:, c, :],
                               in_=srcw[:, c, col:col + 128]).then_inc(*(s_wld).tick(16))

    @blk.vector
    def _(vector, hp=hp):
        for wi in range(3):
            vector.wait_ge((s_wld).h, (s_wld).base + (128 * (wi + 1)))
            vector.wait_ge((s_tq).h, (s_tq).base + (0))
            vector.tensor_copy(wbf[wi][:], wst[:]).then_inc(*(s_vw).tick(1))
        nq = (cdiv(HALF, 512), cdiv(KW, 512), cdiv(KW, 512))
        e = 0
        for wi in range(3):
            for qb in range(nq[wi]):
                vector.wait_ge((s_tq).h, (s_tq).base + (e + 1))
                w = min(512, (HALF if wi == 0 else KW) - 512 * qb)
                srcps = (ps_sa if e % 2 == 0 else ps_sb)[:, 0:w]
                if wi == 0:
                    vector.tensor_copy(q4[:, 512 * qb:512 * qb + w], srcps).then_inc(*(s_vq).tick(1))
                elif wi == 1:
                    vector.tensor_copy(k4[:, 512 * qb:512 * qb + w], srcps).then_inc(*(s_vq).tick(1))
                else:
                    vector.tensor_copy(v4[:, 512 * qb:512 * qb + w], srcps).then_inc(*(s_vq).tick(1))
                e += 1
        for g in range(5):
            vector.wait_ge((s_tv).h, (s_tv).base + (g + 1))
            n4 = 4
            srcp = (ps_pt if g % 2 == 0 else ps_ot)[:, 0:512]
            vector.tensor_copy(v4[:, 512 * g:512 * g + 512], srcp).then_inc(*(s_vv).tick(1))

    @blk.tensor
    def _(tensor, hp=hp):
        if hp == 0:
            tensor.wait_ge((s_vh).h, (s_vh).base + (1))
        nq = (cdiv(HALF, 512), cdiv(KW, 512), cdiv(KW, 512))
        offs = (515, 3, 3)
        e = 0
        for wi in range(3):
            tensor.wait_ge((s_vw).h, (s_vw).base + (wi + 1))
            for qb in range(nq[wi]):
                if e >= 2:
                    tensor.wait_ge((s_vq).h, (s_vq).base + (e - 1))
                w = min(512, (HALF if wi == 0 else KW) - 512 * qb)
                dst = (ps_sa if e % 2 == 0 else ps_sb)[:, 0:w]
                base = offs[wi] + 512 * qb
                for c in range(DC):
                    off = base if c < 4 else base - 1
                    ins = nc.tensor.matmul(
                        dst, wbf[wi][:, c, :], x_fm4[:, c, off:off + w],
                        start=(c == 0), stop=(c == DC - 1), skip_group_check=True)
                ins.then_inc(*(s_tq).tick(1))
                e += 1
        tensor.wait_ge((s_vq).h, (s_vq).base + (14))
        for g in range(5):
            if g >= 2:
                tensor.wait_ge((s_vv).h, (s_vv).base + (g - 1))
            for t4 in range(4):
                tt = 4 * g + t4
                ins = nc.tensor.matmul(
                    (ps_pt if g % 2 == 0 else ps_ot)[:, 128 * t4:128 * t4 + 128],
                    v4[:, 128 * tt:128 * tt + 128], ident_bf[:],
                    start=True, stop=True, skip_group_check=True)
            ins.then_inc(*(s_tv).tick(1))


    _norm_rot(p, blk, q4, HALF, 0, tA["aq"], tA["bq"], 0, B,
              ready=(s_vq.h, s_vq.base + 4))
    _norm_rot(p, blk, k4, KW, 1, tA["ak"], tA["bk"], 1, B,
              ready=(s_vq.h, s_vq.base + 9))

    s_ts = p.rs("ga_ts"); s_vm = p.rs("ga_vm"); s_se = p.rs("ga_se")
    s_tp = p.rs("p4_tp"); s_vp = p.rs("p4_vp")
    s_ta = p.rs("ga_tav"); s_vo = p.rs("ga_vo")
    s_vnr = p.rs("nr_vn")

    def nmask(t):
        return (1 + max(0, 3 - t) + 1) if t < 4 else 2
    vpc = 0
    mask_done = {}; cast_done = {}; evac_done = {}
    for t in range(NT):
        for h in range(2):
            u = 2 * t + h
            vpc += nmask(t); mask_done[u] = vpc
            vpc += 3; cast_done[u] = vpc
            vpc += 1; evac_done[u] = vpc

    @blk.tensor
    def _(tensor, hp=hp):
        tensor.wait_ge((s_vv).h, (s_vv).base + (5))
        tensor.wait_ge((s_vnr).h, (s_vnr).base + (0))
        for t in range(NT):
            for h in range(2):
                u = 2 * t + h
                pss = ps_sa if u % 2 == 0 else ps_sb
                if u >= 2:
                    tensor.wait_ge((s_se).h, (s_se).base + (u - 1))
                for kk in range(2):
                    w = 512 if kk == 0 else 128
                    ins = nc.tensor.matmul(
                        pss[:, 512 * kk:512 * kk + w],
                        q4[64 * h:64 * h + 64, 128 * t:128 * t + 128],
                        k4[64 * h:64 * h + 64, 128 * t + 512 * kk:128 * t + 512 * kk + w],
                        start=True, stop=True, tile_position=(64 * h, 0),
                        skip_group_check=True)
                ins.then_inc(*(s_ts).tick(1))
                tensor.wait_ge((s_vp).h, (s_vp).base + (cast_done[u]))
                if u >= 1:
                    tensor.wait_ge((s_vp).h, (s_vp).base + (evac_done[u - 1]))
                for j in range(5):
                    ins = nc.tensor.matmul(
                        ps_pt[:, 128 * j:128 * j + 128],
                        p_fb[:, 128 * j:128 * j + 128], ident_bf[:],
                        start=True, stop=True, skip_group_check=True)
                ins.then_inc(*(s_tp).tick(1))
                tensor.wait_ge((s_vp).h, (s_vp).base + (evac_done[u]))
                if t % 4 == 0 and h == 0 and t > 0:
                    tensor.wait_ge((s_vo).h, (s_vo).base + (t // 4))
                for j in range(5):
                    ins = nc.tensor.matmul(
                        ps_ot[64 * h:64 * h + 64, 128 * (t % 4):128 * (t % 4) + 128],
                        v4[:].rearrange("p (n k) -> p n k", k=128)[:, t + j, 64 * h:64 * h + 64],
                        p_bf[u % 2][:, j, :],
                        start=(j == 0), stop=(j == 4), tile_position=(0, 64 * h),
                        skip_group_check=True)
                ins.then_inc(*(s_ta).tick(1))

    @blk.vector
    def _(vector, hp=hp):
        vector.wait_ge(p.s_rsw.h, 48)
        for t in range(NT):
            for h in range(2):
                u = 2 * t + h
                pss = ps_sa if u % 2 == 0 else ps_sb
                vector.wait_ge((s_ts).h, (s_ts).base + (u + 1))
                vector.reduce_max(negm[:, 0:1], pss[:, 0:640], AX.X).then_inc(*(s_vm).tick(1))
                vector.wait_ge((s_vm).h, (s_vm).base + (2 * u + 1))
                if u >= 1:
                    vector.wait_ge((s_se).h, (s_se).base + (u))
                vector.tensor_scalar(negm[:, 1:2], negm[:, 0:1], -1.0, None,
                                     ALU.mult).then_inc(*(s_vm).tick(1))
                vector.wait_ge((s_se).h, (s_se).base + (u + 1))
                k0 = mask_done[u] - nmask(t)
                km = [k0]

                def mtick(ins):
                    ins.then_inc(*(s_vp).tick(1))
                    km[0] += 1
                    vector.wait_ge((s_vp).h, (s_vp).base + (km[0]))

                m0 = p_f[:, 0:128]
                m4 = p_f[:, 512:640]
                if t < 4:
                    mtick(vector.scalar_tensor_tensor(m0, m0, p.hv[:, 0:1],
                                                      p.trif[:], ALU.mult, ALU.mult))
                    for j in range(1, 4 - t):
                        mj = p_f[:, 128 * j:128 * j + 128]
                        mtick(vector.tensor_scalar(mj, mj, p.hv[:, 0:1], None, ALU.mult))
                else:
                    mtick(vector.tensor_tensor(m0, m0, p.trif[:], ALU.mult))
                mtick(vector.tensor_tensor(m4, m4, p.trilf[:], ALU.mult))
                mtick(vector.reduce_sum(lcol[:, 0:1], p_f[:], AX.X))
                mtick(vector.reciprocal(lcol[:, 1:2], lcol[:, 0:1]))
                mtick(vector.tensor_scalar(p_fb[:], p_f[:], lcol[:, 1:2], None,
                                           ALU.mult))
                vector.wait_ge((s_tp).h, (s_tp).base + (u + 1))
                if u >= 2:
                    vector.wait_ge((s_ta).h, (s_ta).base + (u - 1))
                vector.tensor_copy(p_bf[u % 2][:].rearrange("p j k -> p (j k)"),
                                   ps_pt[:, 0:640]).then_inc(*(s_vp).tick(1))
            if t % 4 == 3:
                g = t // 4
                vector.wait_ge((s_ta).h, (s_ta).base + (2 * t + 2))
                vector.tensor_copy(ot4[:, hp, 512 * g:512 * g + 512],
                                   ps_ot[:, 0:512]).then_inc(*(s_vo).tick(1))

    @blk.scalar
    def _(scalar, hp=hp):
        for t in range(NT):
            for h in range(2):
                u = 2 * t + h
                pss = ps_sa if u % 2 == 0 else ps_sb
                scalar.wait_ge((s_vm).h, (s_vm).base + (2 * u + 2))
                if u >= 1:
                    scalar.wait_ge((s_vp).h, (s_vp).base + (cast_done[u - 1]))
                scalar.activation(out=p_f[:], in_=pss[:, 0:640], func=AF.Exp,
                                  bias=negm[:, 1:2]).then_inc(*(s_se).tick(1))


def _emit_inner_ffn(p, blk, x_res, identt, ident_bf, cst, il):
    nc = p.nc
    sc = contextlib.ExitStack()
    NHC = 22

    sh = p.sb(f"f4{il}_sh", [128, D], F32, sc)
    hn = p.sb(f"f4{il}_hn", [128, D], F32, sc)
    hfm = p.sb(f"f4{il}_hfm", [128, DC, HALF], BF16, sc)
    hrow = p.sb(f"f4{il}_hrow", [1, D], F32, sc)
    g_fm = p.sb(f"f4{il}_gfm", [128, NHC, 512], BF16, sc)
    w1stg = [p.sb(f"f4{il}_w1s{i}", [128, DC, 256], F32, sc) for i in range(2)]
    w1bf = [p.sb(f"f4{il}_w1b{i}", [128, DC, 256], BF16, sc) for i in range(2)]
    w2stg = [p.sb(f"f4{il}_w2s{i}", [128, D], F32, sc) for i in range(2)]
    w2bf = [p.sb(f"f4{il}_w2b{i}", [128, D], BF16, sc) for i in range(2)]
    lgro = p.sb(f"f4{il}_lgro", [1, D], F32, sc)
    lbro = p.sb(f"f4{il}_lbro", [1, D], F32, sc)
    b2ro = p.sb(f"f4{il}_b2ro", [1, D], F32, sc)
    lg_b = p.sb(f"f4{il}_lgb", [128, D], F32, sc)
    lb_b = p.sb(f"f4{il}_lbb", [128, D], F32, sc)
    b2_b = p.sb(f"f4{il}_b2b", [128, D], F32, sc)
    b1c = p.sb(f"f4{il}_b1c", [128, 2 * NHC], F32, sc)
    gtmp = p.sb(f"f4{il}_gtmp", [128, 512], F32, sc)
    stt = p.sb(f"f4{il}_stt", [128, 8], F32, sc)
    or128 = p.sb(f"f4{il}_or128", [1, 128], F32, sc)

    ps8 = [sc.enter_context(nc.psum_tensor(f"f4{il}_ps{i}", [128, 512], F32))
           for i in range(8)]
    ps_a, ps_g, ps_t = ps8[0], ps8[1], ps8[2]

    s_sh1 = p.rs("f4_sh1")
    s_gc = p.rs("ga_gc")
    s_hrb = p.rs("f4_hrb")
    s_ld = p.rs("f4_ld")
    s_tbb = p.rs("ga_tb")
    s_vbb = p.rs("ga_vb")

    s_vwoF = p.rs("ga_vwo")

    @blk.sync
    def _(sync):
        sync.wait_ge((s_vwoF).h, (s_vwoF).base + (0))
        sync.dma_start(out=p.ap["sh1_in"][:], in_=x_res[127:128, NT - 1, :]).then_inc(
            *(s_sh1).tick(16))

    @blk.gpsimd
    def _(gp):
        gp.wait_ge((s_sh1).h, (s_sh1).base + (16))
        gp.collective_compute(
            "AllGather", ALU.bypass, replica_groups=RG,
            ins=[p.ap["sh1_in"][:]], outs=[p.ap["sh1_out"][:]],
        ).then_inc(*(s_gc).tick(1))

    @blk.sync
    def _(sync):
        sync.wait_ge((s_gc).h, (s_gc).base + (1))
        for arm in sync.switch_core_id(n=128):
            pr = 1 - (arm.logical % 2)
            sync.dma_start(out=hrow[:], in_=p.ap["sh1_out"][pr]).then_inc(s_hrb.h, 16)
        s_hrb.ctr += 16
        sync.dma_start(out=lgro[:], in_=p.ap["i_lng"][il][None, :]).then_inc(*(s_ld).tick(16))
        sync.dma_start(out=lbro[:], in_=p.ap["i_lnb"][il][None, :]).then_inc(*(s_ld).tick(16))
        sync.dma_start(out=b2ro[:], in_=p.ap["i_b2"][il][None, :]).then_inc(*(s_ld).tick(16))
        for hc in range(NHC):
            n = 128 if hc < NHC - 1 else 42
            for ag in range(2):
                srcb = p.ap["i_b1"][il][256 * hc + n * ag:256 * hc + n * ag + n, None]
                sync.dma_start(out=b1c[0:n, 2 * hc + ag:2 * hc + ag + 1],
                               in_=srcb).then_inc(*(s_ld).tick(16))

    @blk.vector
    def _(vector):
        vector.memset(or128[:], 1.0).then_inc(*(s_vbb).tick(1))
        vector.wait_ge((s_hrb).h, (s_hrb).base + (16))
        vector.tensor_scalar(hrow[:], hrow[:], p.hv[0:1, 0:1], None,
                             ALU.mult).then_inc(*(s_vbb).tick(1))
        for bi in range(3):
            dstb = (lg_b, lb_b, b2_b)[bi]
            for oh in range(2):
                vector.wait_ge((s_tbb).h, (s_tbb).base + (2 * bi + oh + 1))
                vector.tensor_copy(dstb[:, 512 * oh:512 * oh + 512],
                                   ps_t[:, 0:512]).then_inc(*(s_vbb).tick(1))

    @blk.tensor
    def _(tensor):
        tensor.wait_ge((s_ld).h, (s_ld).base + (16 * (3 + 2 * NHC)))
        for bi in range(3):
            rowb = (lgro, lbro, b2ro)[bi]
            for oh in range(2):
                k = 2 * bi + oh
                tensor.wait_ge((s_vbb).h, (s_vbb).base + (1 if k == 0 else k + 2))
                nc.tensor.matmul(ps_t[:, 0:512], or128[:],
                                 rowb[:, 512 * oh:512 * oh + 512],
                                 start=True, stop=True,
                                 skip_group_check=True).then_inc(*(s_tbb).tick(1))

    s_dsh = [p.rs("f4_dshA"), p.rs("f4_dshB")]
    s_vln = p.rs("f4_vln")
    s_sln = p.rs("f4_sln")
    s_th = p.rs("f4_th")
    s_vh2 = p.rs("f4_vh2")

    @blk.scalar
    def _(scalar):
        for tt in range(NT):
            scalar.wait_ge((s_vln).h, (s_vln).base + (12 * tt))
            if tt == 0:
                scalar.wait_ge((s_vbb).h, (s_vbb).base + (2))
            scalar.dma_start(out=sh[1:128, 512:1024],
                             in_=x_res[0:127, tt, 512:1024]).then_inc(*(s_dsh[0]).tick(16))
            if tt > 0:
                scalar.dma_start(out=sh[0:1, 512:1024],
                                 in_=x_res[127:128, tt - 1, 512:1024]).then_inc(
                    *(s_dsh[1]).tick(16))
            else:
                scalar.dma_start(out=sh[0:1, 512:1024],
                                 in_=hrow[:, 512:1024]).then_inc(*(s_dsh[1]).tick(16))
            scalar.wait_ge((s_vln).h, (s_vln).base + (12 * tt + 8))
            scalar.activation(out=stt[:, 6:7], in_=stt[:, 5:6], func=AF.Sqrt,
                              bias=cst[:, 1:2]).then_inc(*(s_sln).tick(1))

    @blk.vector
    def _(vector):
        for tt in range(NT):
            n0 = 12 * tt
            k = [0]

            def step(ins):
                ins.then_inc(*(s_vln).tick(1))
                k[0] += 1
                vector.wait_ge((s_vln).h, (s_vln).base + (n0 + k[0]))

            if tt >= 1:
                vector.wait_ge((s_th).h, (s_th).base + (2 * tt))
            vector.wait_ge((s_dsh[0]).h, (s_dsh[0]).base + (16 * (tt + 1)))
            vector.wait_ge((s_dsh[1]).h, (s_dsh[1]).base + (16 * (tt + 1)))
            if tt == 0:
                vector.wait_ge((s_vbb).h, (s_vbb).base + (8))
            step(vector.tensor_copy(sh[:, 0:512], x_res[:, tt, 0:512]))
            step(vector.tensor_tensor(hn[:], sh[:], sh[:], ALU.mult))
            step(vector.reduce_sum(stt[:, 0:1], sh[:], AX.X))
            step(vector.tensor_scalar(stt[:, 2:3], stt[:, 0:1], 1.0 / D, None, ALU.mult))
            step(vector.reduce_sum(stt[:, 1:2], hn[:], AX.X))
            step(vector.tensor_scalar(stt[:, 3:4], stt[:, 1:2], 1.0 / D, None, ALU.mult))
            step(vector.tensor_tensor(stt[:, 4:5], stt[:, 2:3], stt[:, 2:3], ALU.mult))
            step(vector.tensor_tensor(stt[:, 5:6], stt[:, 3:4], stt[:, 4:5], ALU.subtract))
            vector.wait_ge((s_sln).h, (s_sln).base + (tt + 1))
            step(vector.reciprocal(stt[:, 7:8], stt[:, 6:7]))
            step(vector.tensor_scalar(hn[:], sh[:], stt[:, 2:3], stt[:, 7:8],
                                      ALU.subtract, ALU.mult))
            step(vector.tensor_tensor(hn[:], hn[:], lg_b[:], ALU.mult))
            step(vector.tensor_tensor(hn[:], hn[:], lb_b[:], ALU.add))
            for g in range(2):
                vector.wait_ge((s_th).h, (s_th).base + (2 * tt + g + 1))
                srcp = (ps_t if g == 0 else ps_a)[:, 0:512].rearrange("p (c k) -> p c k", c=4)
                vector.tensor_copy(hfm[:, 4 * g:4 * g + 4, 128 * tt:128 * tt + 128],
                                   srcp).then_inc(*(s_vh2).tick(1))

    @blk.tensor
    def _(tensor):
        for tt in range(NT):
            tensor.wait_ge((s_vln).h, (s_vln).base + (12 * tt + 12))
            for c in range(DC):
                g = c // 4
                if 2 * tt + g >= 2:
                    tensor.wait_ge((s_vh2).h, (s_vh2).base + (2 * tt + g - 1))
                ins = nc.tensor.transpose(
                    (ps_t if g == 0 else ps_a)[:, 128 * (c % 4):128 * (c % 4) + 128],
                    hn[:, 128 * c:128 * c + 128], identt[:])
                if c % 4 == 3:
                    ins.then_inc(*(s_th).tick(1))

    for ts in range(4):
        s_w1l = [p.rs("f4_w1lA"), p.rs("f4_w1lB")]
        s_vw1 = p.rs("f4_vw1")
        s_tm1 = p.rs("f4_tm1")
        s_sg = p.rs("f4_sg")
        s_vg = p.rs("f4_vg")
        s_w2l = [p.rs("f4_w2lA"), p.rs("f4_w2lB")]
        s_vw2 = p.rs("f4_vw2")
        s_tm2 = p.rs("f4_tm2")
        s_vr = p.rs("f4_vr")
        s_vh2b = p.rs("f4_vh2")
        s_vlnF = p.rs("f4_vln")

        @blk.sync
        def _(sync, ts=ts):
            sync.wait_ge((s_vw1).h, (s_vw1).base + (0))
            sync.wait_ge((s_vw2).h, (s_vw2).base + (0))
            for hc in range(NHC):
                if hc >= 2:
                    sync.wait_ge((s_vw1).h, (s_vw1).base + (hc - 1))
                ncol = 256 if hc < NHC - 1 else 84
                srcw = p.ap["i_w1"][il].rearrange("(c p) f -> p c f", p=128)
                for c in range(DC):
                    sync.dma_start(out=w1stg[hc % 2][:, c, 0:ncol],
                                   in_=srcw[:, c, 256 * hc:256 * hc + ncol]).then_inc(
                        *(s_w1l[hc % 2]).tick(16))
            for hc in range(NHC):
                if hc >= 2:
                    sync.wait_ge((s_vw2).h, (s_vw2).base + (hc - 1))
                n = 128 if hc < NHC - 1 else 42
                sync.dma_start(out=w2stg[hc % 2][0:n, :],
                               in_=p.ap["i_w2"][il][128 * hc:128 * hc + n, :]).then_inc(
                    *(s_w2l[hc % 2]).tick(16))

        @blk.vector
        def _(vector, ts=ts):
            vector.wait_ge((s_vlnF).h, (s_vlnF).base + (0))
            vector.wait_ge((s_tm1).h, (s_tm1).base + (0))
            vector.wait_ge((s_tm2).h, (s_tm2).base + (0))
            for hc in range(NHC):
                ncol = 256 if hc < NHC - 1 else 84
                vector.wait_ge((s_w1l[hc % 2]).h, (s_w1l[hc % 2]).base + (128 * (hc // 2 + 1)))
                if hc >= 2:
                    vector.wait_ge((s_tm1).h, (s_tm1).base + (2 * (hc - 1)))
                vector.tensor_copy(w1bf[hc % 2][:, :, 0:ncol],
                                   w1stg[hc % 2][:, :, 0:ncol]).then_inc(*(s_vw1).tick(1))
                n = 128 if hc < NHC - 1 else 42
                vector.wait_ge((s_sg).h, (s_sg).base + (hc + 1))
                vector.scalar_tensor_tensor(
                    g_fm[0:n, hc, :], ps_a[0:n, :], b1c[0:n, 2 * hc:2 * hc + 1],
                    gtmp[0:n, :], ALU.add, ALU.mult).then_inc(*(s_vg).tick(1))
            for hc in range(NHC):
                n = 128 if hc < NHC - 1 else 42
                vector.wait_ge((s_w2l[hc % 2]).h, (s_w2l[hc % 2]).base + (16 * (hc // 2 + 1)))
                if hc >= 2:
                    vector.wait_ge((s_tm2).h, (s_tm2).base + (8 * (hc - 1)))
                vector.tensor_copy(w2bf[hc % 2][0:n, :],
                                   w2stg[hc % 2][0:n, :]).then_inc(*(s_vw2).tick(1))
            for e in range(8):
                t4, oh = e // 2, e % 2
                tt = 4 * ts + t4
                vector.wait_ge((s_tm2).h, (s_tm2).base + (8 * NHC))
                if e >= 1:
                    vector.wait_ge((s_vr).h, (s_vr).base + (2 * e))
                xt = x_res[:, tt, 512 * oh:512 * oh + 512]
                vector.tensor_tensor(gtmp[:], ps8[e][:],
                                     b2_b[:, 512 * oh:512 * oh + 512],
                                     ALU.add).then_inc(*(s_vr).tick(1))
                vector.wait_ge((s_vr).h, (s_vr).base + (2 * e + 1))
                vector.tensor_tensor(xt, gtmp[:], xt, ALU.add).then_inc(*(s_vr).tick(1))

        @blk.scalar
        def _(scalar, ts=ts):
            for hc in range(NHC):
                n = 128 if hc < NHC - 1 else 42
                if hc >= 1:
                    scalar.wait_ge((s_vg).h, (s_vg).base + (hc))
                scalar.wait_ge((s_tm1).h, (s_tm1).base + (2 * hc + 2))
                scalar.activation(out=gtmp[0:n, :], in_=ps_g[0:n, :], func=AF.Gelu,
                                  bias=b1c[0:n, 2 * hc + 1:2 * hc + 2]).then_inc(
                    *(s_sg).tick(1))

        @blk.tensor
        def _(tensor, ts=ts):
            tensor.wait_ge((s_vh2b).h, (s_vh2b).base + (0))
            tensor.wait_ge((s_vr).h, (s_vr).base + (0))
            for hc in range(NHC):
                n = 128 if hc < NHC - 1 else 42
                gcol0 = 128 if hc < NHC - 1 else 42
                tensor.wait_ge((s_vw1).h, (s_vw1).base + (hc + 1))
                if hc >= 1:
                    tensor.wait_ge((s_vg).h, (s_vg).base + (hc))
                if hc >= 2:
                    tensor.wait_ge((s_sg).h, (s_sg).base + (hc - 1))
                for half in range(2):
                    pdst = (ps_a, ps_g)[half]
                    col0 = 0 if half == 0 else gcol0
                    for c in range(DC):
                        ins = nc.tensor.matmul(
                            pdst[0:n, 0:512],
                            w1bf[hc % 2][:, c, col0:col0 + n],
                            hfm[:, c, 512 * ts:512 * ts + 512],
                            start=(c == 0), stop=(c == DC - 1), skip_group_check=True)
                    if half == 1:
                        ins.then_inc(*(s_tm1).tick(2))
            tensor.wait_ge((s_vg).h, (s_vg).base + (NHC))
            for hc in range(NHC):
                n = 128 if hc < NHC - 1 else 42
                tensor.wait_ge((s_vw2).h, (s_vw2).base + (hc + 1))
                for e in range(8):
                    t4, oh = e // 2, e % 2
                    ins = nc.tensor.matmul(
                        ps8[e][:],
                        g_fm[0:n, hc, 128 * t4:128 * t4 + 128],
                        w2bf[hc % 2][0:n, 512 * oh:512 * oh + 512],
                        start=(hc == 0), stop=(hc == NHC - 1), skip_group_check=True)
                    if e == 7:
                        ins.then_inc(*(s_tm2).tick(8))
    sc.close()


_cache = {}


def kernel(**inputs):
    per_core = _stage_inputs(inputs)
    if "prog" not in _cache:
        _cache["prog"] = build_program()
    nc = _cache["prog"]
    res = run_bass_kernel_spmd(nc, per_core, core_ids=list(range(8)))
    shards = [np.asarray(res.results[c]["out"]) for c in range(8)]
    return np.concatenate(shards, axis=0).reshape(B, S, D).astype(np.float32)


# version 06
# speedup vs baseline: 1.0362x; 1.0252x over previous
import contextlib

import numpy as np
import ml_dtypes

import concourse.bass as bass
import concourse.mybir as mybir
from concourse.bass_utils import run_bass_kernel_spmd

F32 = mybir.dt.float32
BF16 = mybir.dt.bfloat16
AF = mybir.ActivationFunctionType
ALU = mybir.AluOpType
AX = mybir.AxisListType
bfloat16 = ml_dtypes.bfloat16

B, S, D = 4, 4096, 1024
HALF = S // 2
NT = HALF // 128
NTB = S // 128
DC = D // 128
NHP = 4
RG = [[0, 1], [2, 3], [4, 5], [6, 7]]
BIGB = 30.0
MAGIC = 12582912.0
N_PHASES = 4

IDH = 64
INNER = 512
WINDOW = 512
HID = 2730
QK_SCALE = 8.0


def cdiv(a, b):
    return (a + b - 1) // b


def _rotary_tables():
    half_idx = np.arange(0, IDH, 2, dtype=np.float64)
    inv_freq = 1.0 / 10000.0 ** (half_idx / IDH)
    t = np.arange(S, dtype=np.float64)
    freqs = np.concatenate([t[:, None] * inv_freq[None, :]] * 2, axis=1)
    base = (half_idx + 0.4 * IDH) / (1.4 * IDH)
    power = (t - S // 2) / float(WINDOW // 2)
    scale = np.concatenate([base[None, :] ** power[:, None]] * 2, axis=1)
    cos, sin = np.cos(freqs), np.sin(freqs)
    return ((cos * scale).T.astype(np.float32), (sin * scale).T.astype(np.float32),
            (cos / scale).T.astype(np.float32), (sin / scale).T.astype(np.float32))


def _bitffn_perm():
    perm = []
    i = 0
    while i < HID:
        j = min(i + 128, HID)
        perm.extend(range(i, j))
        perm.extend(range(HID + i, HID + j))
        i = j
    return np.array(perm, dtype=np.int64)


def _stage_inputs(inp):
    per_core = []
    x = np.ascontiguousarray(np.asarray(inp["L0_attn_x"], dtype=np.float32))
    eye = np.eye(128, dtype=np.float32)
    tri_keep = np.triu(np.ones((128, 128), dtype=np.float32)).astype(bfloat16)
    tril_keep = np.tril(np.ones((128, 128), dtype=np.float32)).astype(bfloat16)
    if N_PHASES >= 4:
        Aq, Bq, Ak, Bk = _rotary_tables()
        w1perm = _bitffn_perm()
        rsw = np.zeros((128, 128), dtype=np.float32)
        for blk0 in (0, 64):
            for i in range(32):
                rsw[blk0 + i + 32, blk0 + i] = 1.0
                rsw[blk0 + i, blk0 + i + 32] = 1.0
        rsw = rsw.astype(bfloat16)

    for c in range(8):
        b, h = c // 2, c % 2
        d = {}
        d["x_own"] = np.ascontiguousarray(x[b, HALF * h:HALF * h + HALF])
        d["xb"] = x[b]
        hs = slice(512 * h, 512 * h + 512)
        d["l0_wq"] = np.ascontiguousarray(np.asarray(inp["L0_attn_WQ"], np.float32)[:, hs])
        d["l0_wk"] = np.ascontiguousarray(np.asarray(inp["L0_attn_WK"], np.float32)[:, hs])
        d["l0_wv"] = np.ascontiguousarray(np.asarray(inp["L0_attn_WV"], np.float32)[:, hs])
        d["l0_wo"] = np.asarray(inp["L0_attn_WO"], np.float32)
        d["ident"] = eye
        d["ident_bf"] = eye.astype(bfloat16)
        d["tri_pt"] = tri_keep
        d["tril_pt"] = tril_keep
        d["own_half"] = np.array([h], dtype=np.int32)
        if N_PHASES >= 2:
            d["l0_w1"] = np.asarray(inp["L0_ffn_w1"], np.float32)
            d["l0_w2"] = np.asarray(inp["L0_ffn_w2"], np.float32)
        if N_PHASES >= 3:
            rs = slice(512 * h, 512 * h + 512)
            d["l1_wq"] = np.ascontiguousarray(np.asarray(inp["L1_attn_W_query"], np.float32)[rs].T)
            d["l1_wk"] = np.ascontiguousarray(np.asarray(inp["L1_attn_W_key"], np.float32)[rs].T)
            d["l1_wv"] = np.ascontiguousarray(np.asarray(inp["L1_attn_W_value"], np.float32)[rs].T)
            d["l1_wp"] = np.ascontiguousarray(np.asarray(inp["L1_attn_W_proj"], np.float32).T)
        if N_PHASES >= 4:
            pos = np.clip(np.arange(HALF * h - 513, HALF * h + HALF), 0, S - 1)
            posq = pos[513:]
            sgn = np.concatenate([-np.ones(32, np.float32), np.ones(32, np.float32)])
            d["rot_aq"] = np.ascontiguousarray(np.tile(Aq[:, posq], (2, 1)).astype(bfloat16))
            d["rot_bq"] = np.ascontiguousarray(np.tile(sgn[:, None] * Bq[:, posq], (2, 1)).astype(bfloat16))
            d["rot_ak"] = np.ascontiguousarray(np.tile(Ak[:, pos], (2, 1)).astype(bfloat16))
            d["rot_bk"] = np.ascontiguousarray(np.tile(sgn[:, None] * Bk[:, pos], (2, 1)).astype(bfloat16))
            d["hv128"] = np.full((128, 1), float(h), dtype=np.float32)
            d["rswap"] = rsw
            d["tri_f"] = np.triu(np.ones((128, 128), dtype=np.float32))
            d["tril_f"] = np.tril(np.ones((128, 128), dtype=np.float32))
            hs = np.zeros((2, 128), dtype=np.float32)
            hs[0, 0:64] = 1.0
            hs[1, 64:128] = 1.0
            d["hsel_f"] = hs
            d["hsel_bf"] = hs.astype(bfloat16)
            d["i_wqkv"] = np.asarray(inp["L1_ffn_wqkv"], np.float32)
            d["i_qs"] = np.asarray(inp["L1_ffn_q_scale"], np.float32)
            d["i_ks"] = np.asarray(inp["L1_ffn_k_scale"], np.float32)
            d["i_wo"] = np.asarray(inp["L1_ffn_wo"], np.float32)
            d["i_lng"] = np.asarray(inp["L1_ffn_ln_g"], np.float32)
            d["i_lnb"] = np.asarray(inp["L1_ffn_ln_b"], np.float32)
            d["i_w1"] = np.ascontiguousarray(np.asarray(inp["L1_ffn_w1"], np.float32)[:, :, w1perm])
            d["i_b1"] = np.ascontiguousarray(np.asarray(inp["L1_ffn_b1"], np.float32)[:, w1perm])
            d["i_w2"] = np.asarray(inp["L1_ffn_w2"], np.float32)
            d["i_b2"] = np.asarray(inp["L1_ffn_b2"], np.float32)
            d["halo_valid"] = np.array([float(h)], dtype=np.float32)
        per_core.append(d)
    return per_core


class RSem:
    __slots__ = ("h", "base", "ctr", "role")

    def __init__(self, h, role):
        self.h, self.base, self.ctr, self.role = h, 0, 0, role

    def tick(self, k=1):
        self.ctr += k
        return (self.h, k)


class Prog:
    def __init__(self):
        self.nc = bass.Bass("TRN2", target_bir_lowering=False)
        self.ap = {}
        self.est = contextlib.ExitStack()
        self._barr = None
        self._barr_n = 0
        self.rsems = {}

    ROLE_ALIAS = {
        "in_xbA": "fmb_ldA", "in_xbB": "fmb_ldB", "in_x3": "ga_dcc",
        "in_xv": "fmb_vx",
        "f4_w1lA": "p2w_ldA", "f4_w1lB": "p2w_ldB",
        "f4_w2lA": "p2w_ld2A", "f4_w2lB": "p2w_ld2B",
        "f4_ld": "p2_wbA", "f4_sh1": "p2_wbB", "f4_hrb": "ga_din1",
        "f4_dshA": "ga_dw", "f4_dshB": "p2w_dw",
        "p4_tld": "ga_ldA", "p4_tl2": "ga_ldB",
    }

    def rs(self, role):
        role = self.ROLE_ALIAS.get(role, role)
        r = self.rsems.get(role)
        if r is None:
            r = RSem(self.est.enter_context(self.nc.semaphore(role)), role)
            self.rsems[role] = r
        r.base = r.ctr
        return r

    def din(self, name, shape, dt=F32):
        self.ap[name] = self.nc.dram_tensor(name, shape, dt, kind="ExternalInput").ap()
        return self.ap[name]

    def dint(self, name, shape, dt=F32):
        self.ap[name] = self.nc.dram_tensor(name, shape, dt, kind="Internal").ap()
        return self.ap[name]

    def sb(self, name, shape, dt, scope=None):
        return (scope or self.est).enter_context(self.nc.sbuf_tensor(name, shape, dt))

    def sem(self, name):
        return self.est.enter_context(self.nc.semaphore(name))

    def barrier(self, blk):
        if self._barr is None:
            self._barr = {e: self.est.enter_context(self.nc.semaphore(f"barr_{e}"))
                          for e in ("sync", "tensor", "vector", "scalar", "gpsimd")}
        self._barr_n += 1
        k = self._barr_n
        sems = self._barr

        def arm(name):
            def emit(eng):
                eng.wait_ge(sems[name], k - 1).then_inc(sems[name], 1)
                for other in sems:
                    if other != name:
                        eng.wait_ge(sems[other], k)
            return emit

        blk.sync(arm("sync"))
        blk.tensor(arm("tensor"))
        blk.vector(arm("vector"))
        blk.scalar(arm("scalar"))
        blk.gpsimd(arm("gpsimd"))


def build_program():
    p = Prog()
    nc = p.nc

    p.din("x_own", [HALF, D])
    p.din("xb", [S, D])
    p.din("l0_wq", [D, 512]); p.din("l0_wk", [D, 512]); p.din("l0_wv", [D, 512])
    p.din("l0_wo", [D, D])
    p.din("ident", [128, 128])
    p.din("ident_bf", [128, 128], BF16)
    p.din("tri_pt", [128, 128], BF16)
    p.din("tril_pt", [128, 128], BF16)
    p.din("own_half", [1], mybir.dt.int32)
    if N_PHASES >= 2:
        p.din("l0_w1", [4096, D]); p.din("l0_w2", [D, 4096])
        p.dint("w1qT_d", [D, 4096], BF16)
        p.dint("w2qT_d", [4096, D], BF16)
    if N_PHASES >= 3:
        p.din("l1_wq", [D, 512]); p.din("l1_wk", [D, 512]); p.din("l1_wv", [D, 512])
        p.din("l1_wp", [D, D])
    if N_PHASES >= 4:
        PW = HALF + 513
        p.din("rot_aq", [128, HALF], BF16); p.din("rot_bq", [128, HALF], BF16)
        p.din("rot_ak", [128, PW], BF16); p.din("rot_bk", [128, PW], BF16)
        p.din("hv128", [128, 1])
        p.din("rswap", [128, 128], BF16)
        p.din("hsel_f", [2, 128], F32)
        p.din("tri_f", [128, 128]); p.din("tril_f", [128, 128])
        p.din("hsel_bf", [2, 128], BF16)
        p.din("i_wqkv", [2, D, 3 * INNER])
        p.din("i_qs", [2, IDH]); p.din("i_ks", [2, IDH])
        p.din("i_wo", [2, INNER, D])
        p.din("i_lng", [2, D]); p.din("i_lnb", [2, D])
        p.din("i_w1", [2, D, 2 * HID]); p.din("i_b1", [2, 2 * HID])
        p.din("i_w2", [2, HID, D]); p.din("i_b2", [2, D])
        p.din("halo_valid", [1])
        p.dint("x3_d", [HALF, D])

    out_ap = nc.dram_tensor("out", [HALF, D], F32, kind="ExternalOutput").ap()

    for tag, nph in (("p1", 1), ("p3", 3)):
        if N_PHASES >= nph:
            for hp in range(NHP):
                for half in range(2):
                    p.dint(f"{tag}cc_in_{hp}_{half}", [128, HALF], BF16)
                    p.dint(f"{tag}cc_out_{hp}_{half}", [2, 128, HALF], BF16)
    if N_PHASES >= 3:
        p.dint("x2cc_in", [D, HALF], BF16)
        p.dint("x2cc_out", [2, D, HALF], BF16)
    if N_PHASES >= 4:
        p.dint("shcc_in", [D, 513], BF16)
        p.dint("shcc_out", [2, D, 513], BF16)
        p.dint("sh1_in", [1, D], F32)
        p.dint("sh1_out", [2, 1, D], F32)

    with p.est:
        _emit_all(p, out_ap)
    return nc


def _emit_all(p, out_ap):
    nc = p.nc
    x_res = p.sb("x_res", [128, NT, D], F32)
    cst = p.sb("cst", [128, 8], F32)
    identt = p.sb("identt", [128, 128], F32)
    ident_bf = p.sb("ident_bf_s", [128, 128], BF16)
    tri_pt = p.sb("tri_pt_s", [128, 128], BF16)
    tril_pt = p.sb("tril_pt_s", [128, 128], BF16)
    if N_PHASES >= 4:
        p.hv = p.sb("hv_s", [128, 1], F32)
        p.rsw = p.sb("rsw_s", [128, 128], BF16)
        p.trif = p.sb("trif_s", [128, 128], F32)
        p.trilf = p.sb("trilf_s", [128, 128], F32)

    with nc.Block() as blk:
        s_ld0 = p.rs("s_ld0")
        s_ldx = p.rs("s_ldx")
        s_cst = p.rs("s_cst")
        p.s_ldx = s_ldx
        p.s_ld0 = s_ld0
        p.s_cst = s_cst
        if N_PHASES >= 4:
            s_hv0 = p.rs("s_hv0")
            p.s_hv0 = s_hv0
            s_rsw = p.rs("s_rsw")
            p.s_rsw = s_rsw

        @blk.sync
        def _(sync):
            sync.dma_start(out=identt[:], in_=p.ap["ident"][:]).then_inc(*(s_ld0).tick(16))
            sync.dma_start(out=ident_bf[:], in_=p.ap["ident_bf"][:]).then_inc(*(s_ld0).tick(16))
            sync.dma_start(out=tri_pt[:], in_=p.ap["tri_pt"][:]).then_inc(*(s_ld0).tick(16))
            sync.dma_start(out=tril_pt[:], in_=p.ap["tril_pt"][:]).then_inc(*(s_ld0).tick(16))
            xo = p.ap["x_own"].rearrange("(t p) d -> p t d", p=128)
            for tt in range(NT):
                sync.dma_start(out=x_res[:, tt, :], in_=xo[:, tt, :]).then_inc(*(s_ldx).tick(16))
            if N_PHASES >= 4:
                sync.dma_start(out=p.hv[:], in_=p.ap["hv128"][:]).then_inc(*(s_hv0).tick(16))
                sync.dma_start(out=p.rsw[:], in_=p.ap["rswap"][:]).then_inc(*(s_rsw).tick(16))
                sync.dma_start(out=p.trif[:], in_=p.ap["tri_f"][:]).then_inc(*(s_rsw).tick(16))
                sync.dma_start(out=p.trilf[:], in_=p.ap["tril_f"][:]).then_inc(*(s_rsw).tick(16))

        @blk.gpsimd
        def _(gp):
            gp.memset(cst[:, 0:1], -BIGB)
            gp.memset(cst[:, 1:2], 1e-5).then_inc(*(s_cst).tick(1))

        with contextlib.ExitStack() as p1scope:
            x_fm = p.sb("p1_x_fm", [128, DC, S], BF16, p1scope)
            p.s_fmdone = _emit_fm_build(p, blk, x_fm, identt, "xb", NTB,
                                        ident_ld=(s_ld0, 64))
            p.barrier(blk)
            _emit_global_attn(p, blk, tag="p1", x_fm=x_fm, x_res=x_res,
                              identt=ident_bf, tri_pt=tri_pt, cst=cst,
                              wq="l0_wq", wk="l0_wk", wv="l0_wv", wo="l0_wo",
                              score_scale=0.125, proj_scale=1.0)
            p.barrier(blk)
        s_lnv = _emit_ln(p, blk, x_res, cst, tag="ln1")
        p.barrier(blk)

        if N_PHASES >= 2:
            _emit_bitffn(p, blk, x_res, ident_bf, cst)
            p.barrier(blk)
            s_lnv = _emit_ln(p, blk, x_res, cst, tag="ln2")
            p.barrier(blk)

        if N_PHASES >= 3:
            with contextlib.ExitStack() as p3scope:
                x_fm3 = p.sb("p3_x_fm", [128, DC, S], BF16, p3scope)
                _emit_x2gather(p, blk, x_res, x_fm3, identt)
                p.barrier(blk)
                _emit_global_attn(p, blk, tag="p3", x_fm=x_fm3, x_res=x_res,
                                  identt=ident_bf, tri_pt=tri_pt, cst=cst,
                                  wq="l1_wq", wk="l1_wk", wv="l1_wv",
                                  wo="l1_wp",
                                  score_scale=1.0 / 8192.0,
                                  proj_scale=1.0 / 1024.0)
                p.barrier(blk)
            s_lnv = _emit_ln(p, blk, x_res, cst, tag="ln3")
            p.barrier(blk)

        if N_PHASES >= 4:
            _emit_inner(p, blk, x_res, identt, ident_bf, tri_pt, tril_pt, cst)
            p.barrier(blk)
            s_lnv = _emit_ln(p, blk, x_res, cst, tag="ln4")
            p.barrier(blk)

        s_st = p.rs("s_st")

        @blk.sync
        def _(sync):
            sync.wait_ge((s_lnv).h, (s_lnv).base + (9 * NT))
            xo = out_ap.rearrange("(t p) d -> p t d", p=128)
            for tt in range(NT):
                sync.dma_start(out=xo[:, tt, :], in_=x_res[:, tt, :]).then_inc(*(s_st).tick(16))
            sync.wait_ge((s_st).h, (s_st).base + (16 * NT))


def _emit_fm_build(p, blk, dst_fm, identt, src_name, ntiles, ident_ld):
    nc = p.nc
    s_ld = [p.rs("fmb_ldA"), p.rs("fmb_ldB")]
    s_tx = p.rs("fmb_tx")
    s_vx = p.rs("fmb_vx")
    with contextlib.ExitStack() as scope:
        xstage = [p.sb(f"fmb_{src_name}_st{i}", [128, D], F32, scope) for i in range(2)]
        ps_tp = [scope.enter_context(nc.psum_tensor(f"fmb_{src_name}_ps{i}", [128, 1024], F32))
                 for i in range(2)]

        @blk.sync
        def _(sync):
            src = p.ap[src_name].rearrange("(t p) d -> p t d", p=128)
            for tt in range(ntiles):
                if tt >= 2:
                    sync.wait_ge((s_tx).h, (s_tx).base + (tt - 1))
                sync.dma_start(out=xstage[tt % 2][:], in_=src[:, tt, :]).then_inc(*(s_ld[tt % 2]).tick(16))

        @blk.tensor
        def _(tensor):
            tensor.wait_ge((ident_ld[0]).h, (ident_ld[0]).base + (ident_ld[1]))
            for tt in range(ntiles):
                tensor.wait_ge((s_ld[tt % 2]).h, (s_ld[tt % 2]).base + (16 * (tt // 2 + 1)))
                if tt >= 2:
                    tensor.wait_ge((s_vx).h, (s_vx).base + (tt - 1))
                pst = ps_tp[tt % 2]
                for c in range(DC):
                    ins = nc.tensor.transpose(
                        pst[:, 128 * c:128 * c + 128],
                        xstage[tt % 2][:, 128 * c:128 * c + 128], identt[:])
                ins.then_inc(*(s_tx).tick(1))

        @blk.vector
        def _(vector):
            for tt in range(ntiles):
                vector.wait_ge((s_tx).h, (s_tx).base + (tt + 1))
                src = ps_tp[tt % 2][:].rearrange("p (c k) -> p c k", c=DC)
                vector.tensor_copy(dst_fm[:, :, 128 * tt:128 * tt + 128], src).then_inc(*(s_vx).tick(1))

        s_done = p.rs("fmb_done")

        @blk.tensor
        def _(tensor):
            tensor.wait_ge((s_vx).h, (s_vx).base + ntiles).then_inc(*(s_done).tick(1))
    return s_done


def _emit_ln(p, blk, x_res, cst, tag, ntiles=NT):
    nc = p.nc
    lscope = contextlib.ExitStack()
    sq_t = p.sb(f"{tag}_sq", [128, D], F32, lscope)
    st_t = p.sb(f"{tag}_st", [128, 8], F32, lscope)
    s_v = p.rs("ga_v")
    s_s = p.rs("ga_s")

    @blk.vector
    def _(vector):
        rr = p.rs("fmb_vx")
        vector.wait_ge(rr.h, rr.base + 0)
        n = [0]

        def step(ins):
            n[0] += 1
            ins.then_inc(*(s_v).tick(1))
            vector.wait_ge((s_v).h, (s_v).base + (n[0]))

        for tt in range(ntiles):
            xt = x_res[:, tt, :]
            step(vector.tensor_tensor(sq_t[:], xt, xt, ALU.mult))
            step(vector.reduce_sum(st_t[:, 0:1], xt, AX.X))
            step(vector.tensor_scalar(st_t[:, 2:3], st_t[:, 0:1], 1.0 / D, None, ALU.mult))
            step(vector.reduce_sum(st_t[:, 1:2], sq_t[:], AX.X))
            step(vector.tensor_scalar(st_t[:, 3:4], st_t[:, 1:2], 1.0 / D, None, ALU.mult))
            step(vector.tensor_tensor(st_t[:, 4:5], st_t[:, 2:3], st_t[:, 2:3], ALU.mult))
            step(vector.tensor_tensor(st_t[:, 5:6], st_t[:, 3:4], st_t[:, 4:5], ALU.subtract))
            vector.wait_ge((s_s).h, (s_s).base + (tt + 1))
            step(vector.reciprocal(st_t[:, 7:8], st_t[:, 6:7]))
            step(vector.tensor_scalar(xt, xt, st_t[:, 2:3], st_t[:, 7:8],
                                      ALU.subtract, ALU.mult))

    @blk.scalar
    def _(scalar):
        scalar.wait_ge((p.s_cst).h, (p.s_cst).base + (1))
        for tt in range(ntiles):
            scalar.wait_ge((s_v).h, (s_v).base + (9 * tt + 7))
            scalar.activation(out=st_t[:, 6:7], in_=st_t[:, 5:6], func=AF.Sqrt,
                              bias=cst[:, 1:2]).then_inc(*(s_s).tick(1))
    lscope.close()
    return s_v


def _emit_global_attn(p, blk, tag, x_fm, x_res, identt, tri_pt, cst,
                      wq, wk, wv, wo, score_scale, proj_scale,
                      w_pre_bf16=False):
    nc = p.nc

    s_wld = p.rs("ga_wld")
    s_vw = p.rs("ga_vw")
    s_tq = p.rs("ga_tq")
    s_vq = p.rs("ga_vq")
    s_tv = p.rs("ga_tv")
    s_vv = p.rs("ga_vv")
    s_ts = p.rs("ga_ts")
    s_se = p.rs("ga_se")
    s_vm = p.rs("ga_vm")
    s_tav = p.rs("ga_tav")
    s_vo = p.rs("ga_vo")
    s_gl = p.rs("ga_gl")
    s_vrc = p.rs("ga_vrc")
    s_vn = p.rs("ga_vn")
    s_dcc = p.rs("ga_dcc")
    s_gc = p.rs("ga_gc")
    s_drb = p.rs("ga_drb")
    s_tw = p.rs("ga_tw")
    s_vwo = p.rs("ga_vwo")

    bscope = contextlib.ExitStack()
    wstage = p.sb(f"{tag}_wstage", [128, DC, 128], F32, bscope)
    w_bf = [p.sb(f"{tag}_wbf{i}", [128, DC, 128], BF16, bscope) for i in range(3)]
    q_fm = p.sb(f"{tag}_qfm", [128, S], BF16, bscope)
    k_fm = p.sb(f"{tag}_kfm", [128, S], BF16, bscope)
    v_bf = p.sb(f"{tag}_vbf", [128, S], BF16, bscope)
    v_aug = p.sb(f"{tag}_vaug", [128, NTB, 130], BF16, bscope)
    pt_buf = [p.sb(f"{tag}_pt{i}", [128, 1536], BF16, bscope) for i in range(2)]
    ot_un = [p.sb(f"{tag}_otun{i}", [65, S], BF16, bscope) for i in range(2)]
    l2_t = [p.sb(f"{tag}_l2{i}", [1, HALF], BF16, bscope) for i in range(2)]
    rlb = p.sb(f"{tag}_rlb", [64, HALF], BF16, bscope)
    ones64 = p.sb(f"{tag}_ones64", [1, 64], BF16, bscope)
    s_tb = p.rs("ga_tb")
    s_vb = p.rs("ga_vb")

    def qb_supers(qb):
        slots = [(kt, h) for kt in range(4 * qb + 4) for h in range(2)]
        return [slots[i:i + 3] for i in range(0, len(slots), 3)]

    SUP = []
    for hp in range(NHP):
        for qb in range(8):
            sups = qb_supers(qb)
            for i, sl in enumerate(sups):
                diag = [(si, kt, h) for si, (kt, h) in enumerate(sl) if kt >= 4 * qb]
                SUP.append(dict(hp=hp, qb=qb, slots=sl, diag=diag, first=(i == 0)))
    n_sup = len(SUP)
    n_sup_hp = n_sup // NHP
    cum_diag = np.cumsum([len(s["diag"]) for s in SUP]).tolist()

    for hp in range(NHP):
        with contextlib.ExitStack() as scope:
            ps_qkv = scope.enter_context(nc.psum_tensor(f"{tag}_psqkv{hp}", [128, 1024], F32))
            ps_vt = [scope.enter_context(nc.psum_tensor(f"{tag}_psvt{hp}{i}", [128, 512], F32))
                     for i in range(2)]

            @blk.sync
            def _(sync, hp=hp):
                if hp == 0:
                    sync.wait_ge((p.s_fmdone).h, (p.s_fmdone).base + (1))
                for wi, wn in enumerate((wq, wk, wv)):
                    idx = hp * 3 + wi
                    if idx > 0:
                        sync.wait_ge((s_vw).h, (s_vw).base + (idx))
                    src = p.ap[wn].rearrange("(c p) o -> p c o", p=128)[:, :, 128 * hp:128 * hp + 128]
                    for c in range(DC):
                        sync.dma_start(out=wstage[:, c, :], in_=src[:, c, :]).then_inc(*(s_wld).tick(16))

            @blk.vector
            def _(vector, hp=hp):
                if hp == 0:
                    vector.wait_ge((p.s_ld0).h, (p.s_ld0).base + (64))
                    va = v_aug[:].rearrange("p t (h c) -> p t h c", c=65)
                    vector.memset(va[:, :, :, 64:65], 1.0)
                    vector.memset(ones64[:], 1.0)
                for wi in range(3):
                    idx = hp * 3 + wi
                    vector.wait_ge((s_wld).h, (s_wld).base + (128 * (idx + 1)))
                    vector.tensor_copy(w_bf[wi][:], wstage[:]).then_inc(*(s_vw).tick(1))
                for wi in range(3):
                    for qb in range(8):
                        e = (hp * 3 + wi) * 8 + qb
                        vector.wait_ge((s_tq).h, (s_tq).base + (e + 1))
                        src = ps_qkv[:, 512 * (qb % 2):512 * (qb % 2) + 512]
                        dst = (q_fm, k_fm, v_bf)[wi][:, 512 * qb:512 * qb + 512]
                        vector.tensor_copy(dst, src).then_inc(*(s_vq).tick(1))
                for g in range(8):
                    vector.wait_ge((s_tv).h, (s_tv).base + (hp * 8 + g + 1))
                    src = ps_vt[g % 2][:].rearrange("p (t h c) -> p t h c", t=4, h=2)
                    dst = v_aug[:, 4 * g:4 * g + 4, :].rearrange(
                        "p t (h c) -> p t h c", c=65)[:, :, :, 0:64]
                    vector.tensor_copy(dst, src).then_inc(*(s_vv).tick(1))

            @blk.tensor
            def _(tensor, hp=hp):
                if hp > 0:
                    tensor.wait_ge((s_se).h, (s_se).base + (hp * n_sup_hp))
                    tensor.wait_ge((s_vo).h, (s_vo).base + (16 * hp))
                    tensor.wait_ge((s_vb).h, (s_vb).base + (16 * hp))
                for wi in range(3):
                    tensor.wait_ge((s_vw).h, (s_vw).base + (hp * 3 + wi + 1))
                    for qb in range(8):
                        e = (hp * 3 + wi) * 8 + qb
                        if e >= 2:
                            tensor.wait_ge((s_vq).h, (s_vq).base + (e - 1))
                        dst = ps_qkv[:, 512 * (qb % 2):512 * (qb % 2) + 512]
                        for c in range(DC):
                            ins = nc.tensor.matmul(
                                dst, w_bf[wi][:, c, :],
                                x_fm[:, c, 512 * qb:512 * qb + 512],
                                start=(c == 0), stop=(c == DC - 1))
                        ins.then_inc(*(s_tq).tick(1))
                tensor.wait_ge((s_vq).h, (s_vq).base + ((hp * 3 + 3) * 8))
                for g in range(8):
                    if g >= 2:
                        tensor.wait_ge((s_vv).h, (s_vv).base + (hp * 8 + g - 1))
                    for t4 in range(4):
                        tt = 4 * g + t4
                        ins = nc.tensor.matmul(
                            ps_vt[g % 2][:, 128 * t4:128 * t4 + 128],
                            v_bf[:, 128 * tt:128 * tt + 128], identt[:],
                            start=True, stop=True, skip_group_check=True)
                    ins.then_inc(*(s_tv).tick(1))

        with contextlib.ExitStack() as scope:
            ps_s = [scope.enter_context(nc.psum_tensor(f"{tag}_pss{hp}{i}", [128, 1536], F32))
                    for i in range(2)]
            ps_o = [scope.enter_context(nc.psum_tensor(f"{tag}_pso{hp}{i}", [65, 512], F32))
                    for i in range(2)]
            sup_lo = hp * n_sup_hp
            sup_hi = (hp + 1) * n_sup_hp

            @blk.tensor
            def _(tensor, hp=hp, sup_lo=sup_lo, sup_hi=sup_hi):
                tensor.wait_ge((s_vv).h, (s_vv).base + (8 * (hp + 1)))

                def emit_S(sg):
                    sup = SUP[sg]
                    qb = sup["qb"]
                    if sg - sup_lo >= 2:
                        tensor.wait_ge((s_se).h, (s_se).base + (sg - 1))
                    for si, (kt, h) in enumerate(sup["slots"]):
                        ins = nc.tensor.matmul(
                            ps_s[sg % 2][:, 512 * si:512 * si + 512],
                            k_fm[64 * h:64 * h + 64, 128 * kt:128 * kt + 128],
                            q_fm[64 * h:64 * h + 64, 512 * qb:512 * qb + 512],
                            start=True, stop=True, tile_position=(64 * h, 0),
                            skip_group_check=True)
                    ins.then_inc(*(s_ts).tick(1))

                def emit_AV(sg):
                    sup = SUP[sg]
                    qb = sup["qb"]
                    tensor.wait_ge((s_se).h, (s_se).base + (sg + 1))
                    tensor.wait_ge((s_vv).h, (s_vv).base + (hp * 8 + sup["slots"][-1][0] // 4 + 1))
                    if cum_diag[sg] > 0:
                        tensor.wait_ge((s_vm).h, (s_vm).base + (cum_diag[sg]))
                    if sup["first"]:
                        tensor.wait_ge((s_vo).h, (s_vo).base + (2 * (hp * 8 + qb)))
                    for si, (kt, h) in enumerate(sup["slots"]):
                        j = kt - 4 * qb
                        off = 128 * j if j > 0 else 0
                        ins = nc.tensor.matmul(
                            ps_o[h][:, off:512],
                            v_aug[:, kt, 65 * h:65 * h + 65],
                            pt_buf[sg % 2][:, 512 * si + off:512 * si + 512],
                            start=(kt == 0), stop=(kt == 4 * qb + 3),
                            skip_group_check=True)
                    ins.then_inc(*(s_tav).tick(1))

                emit_S(sup_lo)
                for sg in range(sup_lo, sup_hi - 1):
                    emit_S(sg + 1)
                    emit_AV(sg)
                emit_AV(sup_hi - 1)
                for half in range(2):
                    tensor.wait_ge((s_vrc).h, (s_vrc).base + (2 * hp + half + 1))
                    for head in range(2):
                        for c4 in range(4):
                            idx = 16 * hp + 8 * half + 4 * head + c4
                            if idx >= 3:
                                tensor.wait_ge((s_vb).h, (s_vb).base + (idx - 2))
                            nc.tensor.matmul(
                                ps_s[0][0:64, 512 * (idx % 3):512 * (idx % 3) + 512],
                                ones64[:], l2_t[head][:, 512 * c4:512 * c4 + 512],
                                start=True, stop=True,
                                skip_group_check=True).then_inc(*(s_tb).tick(1))

            @blk.scalar
            def _(scalar, hp=hp, sup_lo=sup_lo, sup_hi=sup_hi):
                if hp == 0:
                    scalar.wait_ge((p.s_cst).h, (p.s_cst).base + (1))
                for sg in range(sup_lo, sup_hi):
                    n = len(SUP[sg]["slots"])
                    scalar.wait_ge((s_ts).h, (s_ts).base + (sg + 1))
                    if sg - sup_lo >= 2:
                        scalar.wait_ge((s_tav).h, (s_tav).base + (sg - 1))
                    scalar.activation(
                        out=pt_buf[sg % 2][:, 0:512 * n],
                        in_=ps_s[sg % 2][:, 0:512 * n],
                        func=AF.Exp, bias=cst[:, 0:1], scale=score_scale,
                    ).then_inc(*(s_se).tick(1))

            @blk.vector
            def _(vector, hp=hp, sup_lo=sup_lo, sup_hi=sup_hi):
                if hp > 0:
                    vector.wait_ge((s_gl).h, (s_gl).base + (64 * hp))
                    vector.wait_ge((s_dcc).h, (s_dcc).base + (64 * hp))
                for sg in range(sup_lo, sup_hi):
                    sup = SUP[sg]
                    if sup["diag"]:
                        vector.wait_ge((s_se).h, (s_se).base + (sg + 1))
                        for (si, kt, h) in sup["diag"]:
                            j = kt - 4 * sup["qb"]
                            sl = pt_buf[sg % 2][:, 512 * si + 128 * j:512 * si + 128 * j + 128]
                            vector.tensor_tensor(sl, sl, tri_pt[:], ALU.mult).then_inc(*(s_vm).tick(1))
                    if (sg + 1 == sup_hi) or SUP[sg + 1]["first"]:
                        qb = sup["qb"]
                        vector.wait_ge((s_tav).h, (s_tav).base + (sg + 1))
                        for head in range(2):
                            vector.tensor_copy(
                                ot_un[head][0:65, 512 * qb:512 * qb + 512],
                                ps_o[head][0:65, :]).then_inc(*(s_vo).tick(1))
                for half in range(2):
                    vector.wait_ge((s_gl).h, (s_gl).base + (64 * hp + 32 * (half + 1)))
                    with nc.allow_low_precision(reason="softmax denom tolerates bf16"):
                        vector.reciprocal(l2_t[0][:], l2_t[0][:])
                        vector.reciprocal(l2_t[1][:], l2_t[1][:]).then_inc(*(s_vrc).tick(1))
                    for head in range(2):
                        g = 4 * hp + 2 * half + head
                        if g > 0:
                            vector.wait_ge((s_vn).h, (s_vn).base + (g))
                        for c4 in range(4):
                            idx = 16 * hp + 8 * half + 4 * head + c4
                            vector.wait_ge((s_tb).h, (s_tb).base + (idx + 1))
                            vector.tensor_copy(
                                rlb[:, 512 * c4:512 * c4 + 512],
                                ps_s[0][0:64, 512 * (idx % 3):512 * (idx % 3) + 512],
                            ).then_inc(*(s_vb).tick(1))
                        vector.wait_ge((s_vb).h, (s_vb).base + (16 * hp + 8 * half + 4 * head + 4))
                        sl = ot_un[head][0:64, HALF * half:HALF * half + HALF]
                        vector.tensor_tensor(sl, sl, rlb[:, :], ALU.mult).then_inc(*(s_vn).tick(1))

            @blk.sync
            def _(sync, hp=hp):
                sync.wait_ge((s_vo).h, (s_vo).base + (16 * (hp + 1)))
                for half in range(2):
                    if hp + half > 0:
                        sync.wait_ge((s_tb).h, (s_tb).base + (16 * hp + 8 * half))
                    for head in range(2):
                        sync.dma_start(
                            out=l2_t[head][:],
                            in_=ot_un[head][64:65, HALF * half:HALF * half + HALF],
                        ).then_inc(*(s_gl).tick(16))
                sync.wait_ge((s_vn).h, (s_vn).base + (4 * (hp + 1)))
                for half in range(2):
                    ccin = p.ap[f"{tag}cc_in_{hp}_{half}"]
                    for head in range(2):
                        sync.dma_start(
                            out=ccin[64 * head:64 * head + 64, :],
                            in_=ot_un[head][0:64, HALF * half:HALF * half + HALF],
                        ).then_inc(*(s_dcc).tick(16))

            @blk.gpsimd
            def _(gp, hp=hp):
                gp.wait_ge((s_dcc).h, (s_dcc).base + (64 * (hp + 1)))
                for half in range(2):
                    gp.collective_compute(
                        "AllGather", ALU.bypass, replica_groups=RG,
                        ins=[p.ap[f"{tag}cc_in_{hp}_{half}"][:]],
                        outs=[p.ap[f"{tag}cc_out_{hp}_{half}"][:]],
                    ).then_inc(*(s_gc).tick(1))

    bscope.close()
    with contextlib.ExitStack() as scope:
        ps_wo = scope.enter_context(nc.psum_tensor(f"{tag}_pswo", [128, 1024], F32))
        wstage_c = p.sb(f"{tag}_wstc", [128, DC, 128], F32, scope)
        wo_bf = p.sb(f"{tag}_wobf", [128, DC, 512], BF16, scope)

        @blk.sync
        def _(sync):
            sync.wait_ge((s_tq).h, (s_tq).base + (12 * 8))
            for hp in range(NHP):
                sync.wait_ge((s_gc).h, (s_gc).base + (2 * (hp + 1)))
                for arm in sync.switch_core_id(n=128):
                    my_half = arm.logical % 2
                    cout = p.ap[f"{tag}cc_out_{hp}_{my_half}"]
                    for rank in range(2):
                        sync.dma_start(
                            out=x_fm[:, 4 * rank + hp, 0:HALF],
                            in_=cout[rank],
                        ).then_inc(s_drb.h, 16)
                s_drb.ctr += 32
            sync.wait_ge((s_tav).h, (s_tav).base + (n_sup))
            sync.wait_ge((s_se).h, (s_se).base + (n_sup))
            sync.wait_ge((s_vo).h, (s_vo).base + (64))
            sync.wait_ge((s_vb).h, (s_vb).base + (64))
            sync.wait_ge((s_vn).h, (s_vn).base + (16))
            sync.wait_ge((s_gl).h, (s_gl).base + (64 * NHP))
            sync.wait_ge((s_dcc).h, (s_dcc).base + (64 * NHP))
            for qq in range(8):
                if w_pre_bf16 and qq % 2 == 0:
                    oh = qq // 4
                    srcw = p.ap[wo].rearrange("(c p) o -> p c o", p=128)[:, :, 512 * oh:512 * oh + 512]
                    if oh == 1:
                        sync.wait_ge((s_tw).h, (s_tw).base + (NT))
                    for c in range(DC):
                        sync.dma_start(out=wo_bf[:, c, :], in_=srcw[:, c, :]).then_inc(*(s_wld).tick(16))
                elif not w_pre_bf16:
                    sync.wait_ge((s_vw).h, (s_vw).base + (12 + qq))
                    srcw = p.ap[wo].rearrange("(c p) o -> p c o", p=128)[:, :, 128 * qq:128 * qq + 128]
                    for c in range(DC):
                        sync.dma_start(out=wstage_c[:, c, :], in_=srcw[:, c, :]).then_inc(*(s_wld).tick(16))


        @blk.tensor
        def _(tensor):
            tensor.wait_ge((s_drb).h, (s_drb).base + (16 * 8))
            for oh in range(2):
                if not w_pre_bf16:
                    tensor.wait_ge((s_vw).h, (s_vw).base + (12 + 4 * (oh + 1)))
                else:
                    tensor.wait_ge((s_wld).h, (s_wld).base + (128 * (12 + oh + 1)))
                for tt in range(NT):
                    e = NT * oh + tt
                    if e >= 2:
                        tensor.wait_ge((s_vwo).h, (s_vwo).base + (e - 1))
                    dst = ps_wo[:, 512 * (e % 2):512 * (e % 2) + 512]
                    for ci in range(DC):
                        ins = nc.tensor.matmul(
                            dst, x_fm[:, ci, 128 * tt:128 * tt + 128],
                            wo_bf[:, ci, 0:512],
                            start=(ci == 0), stop=(ci == DC - 1))
                    ins.then_inc(*(s_tw).tick(1))

        @blk.vector
        def _(vector):
            vector.wait_ge((p.s_ldx).h, (p.s_ldx).base + (256))
            for oh in range(2):
                if not w_pre_bf16:
                    for q4 in range(4):
                        qq = 4 * oh + q4
                        vector.wait_ge((s_wld).h, (s_wld).base + (128 * (12 + qq + 1)))
                        if oh == 1 and q4 == 0:
                            vector.wait_ge((s_tw).h, (s_tw).base + (NT))
                        vector.tensor_copy(
                            wo_bf[:, :, 128 * q4:128 * q4 + 128],
                            wstage_c[:]).then_inc(*(s_vw).tick(1))
                for tt in range(NT):
                    e = NT * oh + tt
                    vector.wait_ge((s_tw).h, (s_tw).base + (e + 1))
                    xt = x_res[:, tt, 512 * oh:512 * oh + 512]
                    vector.scalar_tensor_tensor(
                        xt, ps_wo[:, 512 * (e % 2):512 * (e % 2) + 512], proj_scale, xt,
                        ALU.mult, ALU.add).then_inc(*(s_vwo).tick(1))




def _emit_bitffn(p, blk, x_res, ident_bf, cst):
    nc = p.nc
    sc = contextlib.ExitStack()

    wstg = [p.sb("p2_wstg0", [128, 4096], F32, sc), p.sb("p2_wstg1", [128, 1024], F32, sc)]
    w1qT = p.sb("p2_w1qT", [128, DC, 4096], BF16, sc)
    tern = p.sb("p2_tern", [128, 4096], BF16, sc)
    tstg = p.sb("p2_tstg", [128, 512], BF16, sc)
    acc = p.sb("p2_acc", [128, 4], F32, sc)
    sbc = p.sb("p2_sbc", [128, 4], F32, sc)
    onesf = p.sb("p2_onesf", [128, 1], F32, sc)
    onesr = p.sb("p2_onesr", [1, 128], F32, sc)
    sval = p.sb("p2_sval", [1, 2], F32, sc)
    h_t = p.sb("p2_h", [128, 4096], F32, sc)
    xq_tm = p.sb("p2_xq", [128, D], BF16, sc)
    xq_fm = p.sb("p2_xqfm", [128, DC, 128], BF16, sc)
    hq_tm = p.sb("p2_hqtm", [128, 4096], BF16, sc)
    hq_fm = p.sb("p2_hqfm", [128, 32, 128], BF16, sc)
    w2b = [p.sb(f"p2_w2b{i}", [128, D], BF16, sc) for i in range(2)]
    stt = p.sb("p2_stt", [128, 8], F32, sc)

    ps_t = [sc.enter_context(nc.psum_tensor(f"p2_pst{i}", [128, 512], F32)) for i in range(2)]
    ps_m1 = [sc.enter_context(nc.psum_tensor(f"p2_psm1{i}", [128, 512], F32)) for i in range(2)]
    ps_m2 = [sc.enter_context(nc.psum_tensor(f"p2_psm2{i}", [128, 512], F32)) for i in range(2)]
    ps_s = sc.enter_context(nc.psum_tensor("p2_pss", [1, 128], F32))
    ps_b = sc.enter_context(nc.psum_tensor("p2_psb", [128, 128], F32))

    WCfg = [("l0_w1", 32, 1024, True), ("l0_w2", 8, 4096, False)]
    wprep = {}

    for wi, (wn, ntl, wf, res) in enumerate(WCfg):
        nblk = wf // 128
        ngrp = nblk // 4
        sldA = p.rs("p2w_ldA"); sldB = p.rs("p2w_ldB")
        svr = p.rs("p2w_vr")
        st1 = p.rs("p2w_t1")
        sv1 = p.rs("p2w_v1")
        sld2A = p.rs("p2w_ld2A"); sld2B = p.rs("p2w_ld2B")
        svq = p.rs("p2w_vq")
        s_tt = p.rs("p2w_tt")
        sve = p.rs("p2w_ve")
        sdw = p.rs("p2w_dw")

        wprep[wi] = dict(sve=sve, sdw=sdw, svq=svq, stt=s_tt,
                         sve_total=ntl * (wf // 512), nq=3 * ntl,
                         ntt=ntl * (wf // 128))

        @blk.sync
        def _(sync, wn=wn, ntl=ntl, wf=wf, res=res, sldA=sldA, sldB=sldB,
              svr=svr, sv1=sv1, svq=svq, sve=sve, sdw=sdw, nblk=nblk, ngrp=ngrp,
              sld2A=sld2A, sld2B=sld2B, wi=wi):
            srcw = p.ap[wn].rearrange("(t p) f -> p t f", p=128)
            nb = 2 if wf <= 1024 else 1
            if wi == 1:
                sync.wait_ge(wprep[0]["svq"].h, wprep[0]["svq"].base)
            for t in range(ntl):
                if t >= nb:
                    sync.wait_ge((svr).h, (svr).base + (2 * (t - nb + 1)))
                sync.dma_start(out=wstg[t % nb][:, 0:wf], in_=srcw[:, t, :]).then_inc(*((sldA, sldB)[t % nb]).tick(16))
            sync.wait_ge((sv1).h, (sv1).base + (4))
            for t in range(ntl):
                if t >= nb:
                    sync.wait_ge((svq).h, (svq).base + (3 * (t - nb + 1)))
                sync.dma_start(out=wstg[t % nb][:, 0:wf], in_=srcw[:, t, :]).then_inc(*((sld2A, sld2B)[t % nb]).tick(16))

        if not res:
            @blk.scalar
            def _(scalar, wn=wn, ntl=ntl, res=res, sve=sve, sdw=sdw, ngrp=ngrp):
                for t in range(ntl):
                    for g in range(ngrp):
                        gg = t * ngrp + g
                        scalar.wait_ge((sve).h, (sve).base + (gg + 1))
                        for j in range(4):
                            c = 4 * g + j
                            dst = p.ap["w2qT_d"][128 * c:128 * c + 128,
                                                 128 * t:128 * t + 128]
                            scalar.dma_start(out=dst, in_=tstg[:, 128 * j:128 * j + 128]
                                             ).then_inc(*(sdw).tick(16))

        @blk.vector
        def _(vector, wn=wn, ntl=ntl, wf=wf, res=res, sldA=sldA, sldB=sldB,
              svr=svr, st1=st1, sv1=sv1, svq=svq, s_tt=s_tt, sve=sve, sdw=sdw,
              nblk=nblk, ngrp=ngrp, sld2A=sld2A, sld2B=sld2B, wi=wi):
            if wi == 0:
                vector.memset(onesf[:], 1.0)
                vector.memset(onesr[:], 1.0)
            vector.memset(acc[:, 0:1], 0.0)
            nb = 2 if wf <= 1024 else 1
            for t in range(ntl):
                vector.wait_ge(((sldA, sldB)[t % nb]).h, ((sldA, sldB)[t % nb]).base + (16 * (t // nb + 1)))
                if t > 0:
                    vector.wait_ge((svr).h, (svr).base + (2 * t))
                vector.reduce_sum(acc[:, 2:3], wstg[t % nb][:, 0:wf], AX.X,
                                  apply_absolute_value=True).then_inc(*(svr).tick(1))
                vector.wait_ge((svr).h, (svr).base + (2 * t + 1))
                vector.tensor_tensor(acc[:, 0:1], acc[:, 0:1], acc[:, 2:3],
                                     ALU.add).then_inc(*(svr).tick(1))
            vector.wait_ge((st1).h, (st1).base + (1))
            vector.tensor_scalar(sval[:, wi:wi + 1], ps_s[:, 0:1],
                                 1.0 / (ntl * 128 * wf), None, ALU.mult).then_inc(*(sv1).tick(1))
            vector.wait_ge((st1).h, (st1).base + (2))
            vector.tensor_copy(sbc[:, 2 + wi:3 + wi], ps_b[:, 0:1]).then_inc(*(sv1).tick(1))
            vector.wait_ge((sv1).h, (sv1).base + (2))
            vector.reciprocal(sbc[:, wi:wi + 1], sbc[:, 2 + wi:3 + wi]).then_inc(*(sv1).tick(2))
            if wi == 1:
                vector.wait_ge(wprep[0]["stt"].h, wprep[0]["stt"].base)
            for t in range(ntl):
                vector.wait_ge(((sld2A, sld2B)[t % nb]).h, ((sld2A, sld2B)[t % nb]).base + (16 * (t // nb + 1)))
                if t > 0:
                    vector.wait_ge((s_tt).h, (s_tt).base + (nblk * t))
                wv = wstg[t % nb][:, 0:wf]
                vector.tensor_scalar(wv, wv, sbc[:, wi:wi + 1], MAGIC,
                                     ALU.mult, ALU.add).then_inc(*(svq).tick(1))
                vector.wait_ge((svq).h, (svq).base + (3 * t + 1))
                vector.tensor_scalar(wv, wv, MAGIC, -1.0,
                                     ALU.subtract, ALU.max).then_inc(*(svq).tick(1))
                vector.wait_ge((svq).h, (svq).base + (3 * t + 2))
                vector.tensor_scalar_min(tern[:, 0:wf], wv, 1.0).then_inc(*(svq).tick(1))
                for g in range(ngrp):
                    gg = t * ngrp + g
                    vector.wait_ge((s_tt).h, (s_tt).base + (nblk * t + 4 * (g + 1)))
                    srcp = ps_t[g % 2][:].rearrange("p (c k) -> p c k", c=4)
                    if res:
                        dstv = w1qT[:, 4 * g:4 * g + 4, 128 * t:128 * t + 128]
                        vector.tensor_copy(dstv, srcp).then_inc(*(sve).tick(1))
                    else:
                        if gg > 0:
                            vector.wait_ge((sdw).h, (sdw).base + (64 * gg))
                        vector.tensor_copy(tstg[:], ps_t[g % 2][:]).then_inc(*(sve).tick(1))

        @blk.tensor
        def _(tensor, wn=wn, ntl=ntl, wf=wf, res=res, svr=svr, st1=st1, sv1=sv1,
              svq=svq, s_tt=s_tt, sve=sve, nblk=nblk, ngrp=ngrp, wi=wi):
            if wi == 1:
                tensor.wait_ge(wprep[0]["sve"].h, wprep[0]["sve"].base)
            tensor.wait_ge((svr).h, (svr).base + (2 * ntl))
            nc.tensor.matmul(ps_s[:, 0:1], acc[:, 0:1], onesf[:],
                             start=True, stop=True, skip_group_check=True).then_inc(*(st1).tick(1))
            tensor.wait_ge((sv1).h, (sv1).base + (1))
            nc.tensor.matmul(ps_b[:, 0:1], onesr[:], sval[:, wi:wi + 1],
                             start=True, stop=True, skip_group_check=True).then_inc(*(st1).tick(1))
            for t in range(ntl):
                tensor.wait_ge((svq).h, (svq).base + (3 * (t + 1)))
                for c in range(nblk):
                    g, j = c // 4, c % 4
                    if t * ngrp + g >= 2:
                        tensor.wait_ge((sve).h, (sve).base + (t * ngrp + g - 1))
                    nc.tensor.matmul(
                        ps_t[g % 2][:, 128 * j:128 * j + 128],
                        tern[:, 128 * c:128 * c + 128], ident_bf[:],
                        start=True, stop=True, skip_group_check=True).then_inc(*(s_tt).tick(1))

    s_vx = p.rs("p2_vx")
    s_tx = p.rs("p2_tx")
    s_vxe = p.rs("p2_vxe")
    s_tm1 = p.rs("p2_tm1")
    s_sg = p.rs("p2_sg")
    s_vh = p.rs("p2_vh")
    s_th = p.rs("p2_th")
    s_vhq = p.rs("p2_vhq")
    s_tm2 = p.rs("p2_tm2")
    s_vr2 = p.rs("p2_vr2")
    s_wb = [p.rs("p2_wbA"), p.rs("p2_wbB")]
    V_X = 9

    @blk.sync
    def _(sync):
        sync.wait_ge((wprep[1]["sdw"]).h, (wprep[1]["sdw"]).base + (64 * 64 * 8 // 8))
        for tt in range(NT):
            for i2 in range(32):
                k = 32 * tt + i2
                if k >= 2:
                    sync.wait_ge((s_tm2).h, (s_tm2).base + (k - 1))
                src2 = p.ap["w2qT_d"][128 * i2:128 * i2 + 128, :]
                sync.dma_start(out=w2b[k % 2][:], in_=src2).then_inc(*(s_wb[k % 2]).tick(16))

    @blk.vector
    def _(vector):
        for tt in range(NT):
            xt = x_res[:, tt, :]
            n0 = V_X * tt

            def step(ins, k=[0]):
                ins.then_inc(*(s_vx).tick(1))
                k[0] += 1
                vector.wait_ge((s_vx).h, (s_vx).base + (n0 + k[0]))

            step(vector.reduce_max(stt[:, 0:1], xt, AX.X, apply_absolute_value=True))
            step(vector.tensor_scalar_max(stt[:, 0:1], stt[:, 0:1], 1e-5))
            step(vector.reciprocal(stt[:, 1:2], stt[:, 0:1]))
            step(vector.tensor_scalar(stt[:, 2:3], stt[:, 1:2], 127.0, None, ALU.mult))
            step(vector.tensor_scalar(stt[:, 3:4], stt[:, 0:1], 1.0 / 127.0, None, ALU.mult))
            step(vector.tensor_tensor(stt[:, 4:5], stt[:, 3:4], sbc[:, 2:3], ALU.mult))
            if tt > 0:
                vector.wait_ge((s_tx).h, (s_tx).base + (8 * tt))
                vector.wait_ge((s_sg).h, (s_sg).base + (8 * tt))
            qt = h_t[:, 0:D]
            step(vector.tensor_scalar(qt, xt, stt[:, 2:3], MAGIC, ALU.mult, ALU.add))
            step(vector.tensor_scalar(qt, qt, MAGIC, -128.0, ALU.subtract, ALU.max))
            step(vector.tensor_scalar_min(xq_tm[:], qt, 127.0))
            for g in range(2):
                vector.wait_ge((s_tx).h, (s_tx).base + (8 * tt + 4 * (g + 1)))
                srcp = ps_t[g % 2][:].rearrange("p (c k) -> p c k", c=4)
                vector.tensor_copy(xq_fm[:, 4 * g:4 * g + 4, :], srcp).then_inc(*(s_vxe).tick(1))
            m0 = 8 * tt

            def hstep(ins, k=[0]):
                ins.then_inc(*(s_vh).tick(1))
                k[0] += 1
                vector.wait_ge((s_vh).h, (s_vh).base + (m0 + k[0]))

            vector.wait_ge((s_sg).h, (s_sg).base + (8 * (tt + 1)))
            hstep(vector.reduce_max(stt[:, 5:6], h_t[:], AX.X, apply_absolute_value=True))
            hstep(vector.tensor_scalar_max(stt[:, 5:6], stt[:, 5:6], 1e-5))
            hstep(vector.reciprocal(stt[:, 6:7], stt[:, 5:6]))
            hstep(vector.tensor_scalar(stt[:, 7:8], stt[:, 6:7], 127.0, None, ALU.mult))
            if tt > 0:
                vector.wait_ge((s_th).h, (s_th).base + (32 * tt))
            hstep(vector.tensor_scalar(h_t[:], h_t[:], stt[:, 7:8], MAGIC, ALU.mult, ALU.add))
            hstep(vector.tensor_scalar(h_t[:], h_t[:], MAGIC, -128.0, ALU.subtract, ALU.max))
            hstep(vector.tensor_scalar_min(hq_tm[:], h_t[:], 127.0))
            hstep(vector.tensor_scalar(stt[:, 5:6], stt[:, 5:6], 1.0 / 127.0, None, ALU.mult))
            vector.wait_ge((s_vh).h, (s_vh).base + (m0 + 8))
            vector.tensor_tensor(stt[:, 6:7], stt[:, 5:6], sbc[:, 3:4], ALU.mult)
            for g in range(8):
                vector.wait_ge((s_th).h, (s_th).base + (32 * tt + 4 * (g + 1)))
                srcp = ps_t[g % 2][:].rearrange("p (c k) -> p c k", c=4)
                vector.tensor_copy(hq_fm[:, 4 * g:4 * g + 4, :], srcp).then_inc(*(s_vhq).tick(1))
            for oh in range(2):
                vector.wait_ge((s_tm2).h, (s_tm2).base + (32 * (tt + 1)))
                vector.scalar_tensor_tensor(
                    x_res[:, tt, 512 * oh:512 * oh + 512],
                    ps_m2[oh][:], stt[:, 6:7],
                    x_res[:, tt, 512 * oh:512 * oh + 512],
                    ALU.mult, ALU.add).then_inc(*(s_vr2).tick(1))

    @blk.scalar
    def _(scalar):
        for tt in range(NT):
            for oc in range(8):
                k = 8 * tt + oc
                scalar.wait_ge((s_tm1).h, (s_tm1).base + (k + 1))
                scalar.activation(
                    out=h_t[:, 512 * oc:512 * oc + 512],
                    in_=ps_m1[oc % 2][:], func=AF.Gelu,
                    scale=stt[:, 4:5]).then_inc(*(s_sg).tick(1))

    @blk.tensor
    def _(tensor):
        tensor.wait_ge((wprep[1]["sve"]).h, (wprep[1]["sve"]).base + (wprep[1]["sve_total"]))
        tensor.wait_ge((wprep[1]["sdw"]).h, (wprep[1]["sdw"]).base + (64 * wprep[1]["sve_total"]))
        for tt in range(NT):
            tensor.wait_ge((s_vx).h, (s_vx).base + (V_X * tt + 9))
            if tt == 0:
                pass
            for c in range(8):
                g, j = c // 4, c % 4
                gg = 2 * tt + g
                if gg >= 2:
                    tensor.wait_ge((s_vxe).h, (s_vxe).base + (gg - 1))
                nc.tensor.matmul(
                    ps_t[g % 2][:, 128 * j:128 * j + 128],
                    xq_tm[:, 128 * c:128 * c + 128], ident_bf[:],
                    start=True, stop=True, skip_group_check=True).then_inc(*(s_tx).tick(1))
            tensor.wait_ge((s_vxe).h, (s_vxe).base + (2 * tt + 2))
            for oc in range(8):
                k = 8 * tt + oc
                if k >= 2:
                    tensor.wait_ge((s_sg).h, (s_sg).base + (k - 1))
                for ci in range(DC):
                    ins = nc.tensor.matmul(
                        ps_m1[oc % 2][:], xq_fm[:, ci, :],
                        w1qT[:, ci, 512 * oc:512 * oc + 512],
                        start=(ci == 0), stop=(ci == DC - 1),
                        skip_group_check=True)
                ins.then_inc(*(s_tm1).tick(1))
            tensor.wait_ge((s_vh).h, (s_vh).base + (8 * tt + 7))
            for c in range(32):
                g, j = c // 4, c % 4
                gg = 8 * tt + g
                if gg >= 2:
                    tensor.wait_ge((s_vhq).h, (s_vhq).base + (gg - 1))
                nc.tensor.matmul(
                    ps_t[g % 2][:, 128 * j:128 * j + 128],
                    hq_tm[:, 128 * c:128 * c + 128], ident_bf[:],
                    start=True, stop=True, skip_group_check=True).then_inc(*(s_th).tick(1))
            tensor.wait_ge((s_vhq).h, (s_vhq).base + (8 * tt + 8))
            if tt > 0:
                tensor.wait_ge((s_vr2).h, (s_vr2).base + (2 * tt))
            for i2 in range(32):
                k = 32 * tt + i2
                tensor.wait_ge((s_wb[k % 2]).h, (s_wb[k % 2]).base + (16 * (k // 2 + 1)))
                for oh in range(2):
                    ins = nc.tensor.matmul(
                        ps_m2[oh][:], hq_fm[:, i2, :],
                        w2b[k % 2][:, 512 * oh:512 * oh + 512],
                        start=(i2 == 0), stop=(i2 == 31),
                        skip_group_check=True)
                ins.then_inc(*(s_tm2).tick(1))

    sc.close()


def _emit_wT(p, blk, srcname, dstname, ntl, C, identt, tag):
    nc = p.nc
    nblk = C // 128
    ngrp = nblk // 4
    sc = contextlib.ExitStack()
    wstg = [p.sb(f"{tag}_wstg{i}", [128, C], F32, sc) for i in range(2)]
    tstg = p.sb(f"{tag}_tstg", [128, 512], F32, sc)
    sldA = p.rs("ga_ldA"); sldB = p.rs("ga_ldB")
    s_tt = p.rs("ga_tt")
    sve = p.rs("ga_ve")
    sdw = p.rs("ga_dw")
    ps = [sc.enter_context(nc.psum_tensor(f"{tag}_ps{i}", [128, 512], F32))
          for i in range(2)]

    @blk.sync
    def _(sync):
        srcw = p.ap[srcname].rearrange("(t p) f -> p t f", p=128)
        for t in range(ntl):
            if t >= 2:
                sync.wait_ge((s_tt).h, (s_tt).base + (nblk * (t - 1)))
            sync.dma_start(out=wstg[t % 2][:], in_=srcw[:, t, :]).then_inc(*((sldA, sldB)[t % 2]).tick(16))

    @blk.tensor
    def _(tensor):
        for t in range(ntl):
            tensor.wait_ge(((sldA, sldB)[t % 2]).h, ((sldA, sldB)[t % 2]).base + (16 * (t // 2 + 1)))
            for c in range(nblk):
                g, j = c // 4, c % 4
                gg = t * ngrp + g
                if gg >= 2:
                    tensor.wait_ge((sve).h, (sve).base + (gg - 1))
                nc.tensor.transpose(
                    ps[g % 2][:, 128 * j:128 * j + 128],
                    wstg[t % 2][:, 128 * c:128 * c + 128], identt[:]
                ).then_inc(*(s_tt).tick(1))

    @blk.vector
    def _(vector):
        for t in range(ntl):
            for g in range(ngrp):
                gg = t * ngrp + g
                vector.wait_ge((s_tt).h, (s_tt).base + (nblk * t + 4 * (g + 1)))
                vector.wait_ge((sdw).h, (sdw).base + (64 * gg))
                vector.tensor_copy(tstg[:], ps[g % 2][:]).then_inc(*(sve).tick(1))

    @blk.scalar
    def _(scalar):
        for t in range(ntl):
            for g in range(ngrp):
                gg = t * ngrp + g
                scalar.wait_ge((sve).h, (sve).base + (gg + 1))
                for j in range(4):
                    c = 4 * g + j
                    dst = p.ap[dstname][128 * c:128 * c + 128, 128 * t:128 * t + 128]
                    scalar.dma_start(out=dst, in_=tstg[:, 128 * j:128 * j + 128]
                                     ).then_inc(*(sdw).tick(16))
    sc.close()
    return sdw, sdw.ctr


def _emit_x2gather(p, blk, x_res, x_fm3, identt, tag="x2g"):
    nc = p.nc
    sc = contextlib.ExitStack()
    fms = [p.sb(f"{tag}_fms{i}", [128, DC, 128], BF16, sc) for i in range(2)]
    stx = p.rs("ga_tx")
    sve = p.rs("ga_ve")
    sdin = [p.rs("ga_din0"), p.rs("ga_din1")]
    sgc = p.rs("ga_gc")
    sdrb = p.rs("ga_drb")
    ps = [sc.enter_context(nc.psum_tensor(f"{tag}_ps{i}", [128, 1024], F32))
          for i in range(2)]

    @blk.tensor
    def _(tensor):
        for tt in range(NT):
            if tt >= 2:
                tensor.wait_ge((sve).h, (sve).base + (tt - 1))
            for c in range(DC):
                ins = nc.tensor.transpose(
                    ps[tt % 2][:, 128 * c:128 * c + 128],
                    x_res[:, tt, 128 * c:128 * c + 128], identt[:])
            ins.then_inc(*(stx).tick(1))

    @blk.vector
    def _(vector):
        for tt in range(NT):
            vector.wait_ge((stx).h, (stx).base + (tt + 1))
            if tt >= 2:
                vector.wait_ge((sdin[tt % 2]).h, (sdin[tt % 2]).base + (128 * (tt // 2)))
            srcp = ps[tt % 2][:].rearrange("p (c k) -> p c k", c=DC)
            vector.tensor_copy(fms[tt % 2][:], srcp).then_inc(*(sve).tick(1))

    @blk.sync
    def _(sync):
        for tt in range(NT):
            sync.wait_ge((sve).h, (sve).base + (tt + 1))
            for c in range(DC):
                dst = p.ap["x2cc_in"][128 * c:128 * c + 128, 128 * tt:128 * tt + 128]
                sync.dma_start(out=dst, in_=fms[tt % 2][:, c, :]).then_inc(*(sdin[tt % 2]).tick(16))

    @blk.gpsimd
    def _(gp):
        gp.wait_ge((sdin[0]).h, (sdin[0]).base + (128 * NT // 2))
        gp.wait_ge((sdin[1]).h, (sdin[1]).base + (128 * NT // 2))
        gp.collective_compute(
            "AllGather", ALU.bypass, replica_groups=RG,
            ins=[p.ap["x2cc_in"][:]], outs=[p.ap["x2cc_out"][:]],
        ).then_inc(*(sgc).tick(1))

    @blk.sync
    def _(sync):
        sync.wait_ge((sgc).h, (sgc).base + (1))
        for rank in range(2):
            for c in range(DC):
                srcd = p.ap["x2cc_out"][rank][128 * c:128 * c + 128, :]
                sync.dma_start(out=x_fm3[:, c, HALF * rank:HALF * rank + HALF],
                               in_=srcd).then_inc(*(sdrb).tick(16))
        sync.wait_ge((sdrb).h, (sdrb).base + (256))
    sc.close()


PW = HALF + 513


def _emit_inner(p, blk, x_res, identt, ident_bf, tri_pt, tril_pt, cst):
    nc = p.nc
    s_x3 = p.rs("in_x3")

    s_lnc = p.rs("ga_v")

    @blk.sync
    def _(sync):
        sync.wait_ge((s_lnc).h, (s_lnc).base + (0))
        x3d = p.ap["x3_d"].rearrange("(t p) d -> p t d", p=128)
        for tt in range(NT):
            sync.dma_start(out=x3d[:, tt, :], in_=x_res[:, tt, :]).then_inc(*(s_x3).tick(16))
        sync.wait_ge((s_x3).h, (s_x3).base + (16 * NT))
    p.barrier(blk)

    for il in range(2):
        _emit_inner_attn(p, blk, x_res, identt, ident_bf, tri_pt, tril_pt, cst, il)
        p.barrier(blk)
        _emit_inner_ffn(p, blk, x_res, identt, ident_bf, cst, il)
        p.barrier(blk)

    sc = contextlib.ExitStack()
    x3s = [p.sb(f"in_x3s{i}", [128, D], F32, sc) for i in range(2)]
    s_xb = [p.rs("in_xbA"), p.rs("in_xbB")]
    s_xv = p.rs("in_xv")

    @blk.sync
    def _(sync):
        x3d = p.ap["x3_d"].rearrange("(t p) d -> p t d", p=128)
        for tt in range(NT):
            if tt >= 2:
                sync.wait_ge((s_xv).h, (s_xv).base + (tt - 1))
            sync.dma_start(out=x3s[tt % 2][:], in_=x3d[:, tt, :]).then_inc(
                *(s_xb[tt % 2]).tick(16))

    s_frc = p.rs("f4_vr")

    @blk.vector
    def _(vector):
        vector.wait_ge((s_frc).h, (s_frc).base + (0))
        for tt in range(NT):
            vector.wait_ge((s_xb[tt % 2]).h, (s_xb[tt % 2]).base + (16 * (tt // 2 + 1)))
            vector.tensor_tensor(x_res[:, tt, :], x_res[:, tt, :], x3s[tt % 2][:],
                                 ALU.add).then_inc(*(s_xv).tick(1))
    sc.close()


def _norm_rot(p, blk, buf, width, toff, ta, tb, sccol, B, ready):
    nc = p.nc
    s_sq = p.rs("nr_sq")
    s_tn = p.rs("nr_tn")
    s_vn = p.rs("nr_vn")
    s_tb = p.rs("nr_tb")
    s_vtr = p.rs("p4_tl2")
    nch = cdiv(width, 512)
    n2, nrb, rt1, rt2, sqs = B["n2"], B["nrb"], B["rt1"], B["rt2"], B["sqs"]
    bo33, hsel, qs8 = B["bo33"], B["hsel"], B["qs8"]
    n2b = B["n2b"]
    ps_n = B["ps_n"]
    ps_b = B["ps_b"]

    @blk.scalar
    def _(scalar):
        scalar.wait_ge(ready[0], ready[1])
        scalar.wait_ge((s_tn).h, (s_tn).base + (0))
        scalar.wait_ge((s_vn).h, (s_vn).base + (0))
        scalar.activation(out=sqs[:, 0:width], in_=buf[:, 0:width],
                          func=AF.Square).then_inc(*(s_sq).tick(1))
        scalar.wait_ge((s_vn).h, (s_vn).base + (nch))
        scalar.activation(out=n2[:, 0:width], in_=n2[:, 0:width],
                          func=AF.Sqrt).then_inc(*(s_sq).tick(1))

    @blk.tensor
    def _(tensor):
        tensor.wait_ge((s_sq).h, (s_sq).base + (1))
        for cch in range(nch):
            w = min(512, width - 512 * cch)
            if cch >= 1:
                tensor.wait_ge((s_vn).h, (s_vn).base + (cch))
            nc.tensor.matmul(ps_n[0:2, 0:w], bo33[:],
                             sqs[:, 512 * cch:512 * cch + w],
                             start=True, stop=True,
                             skip_group_check=True).then_inc(*(s_tn).tick(1))
        tensor.wait_ge((s_vn).h, (s_vn).base + (nch + 2))
        for cch in range(nch):
            w = min(512, width - 512 * cch)
            if cch >= 1:
                tensor.wait_ge((s_vn).h, (s_vn).base + (nch + 2 + cch))
            nc.tensor.matmul(ps_b[:, 0:w], hsel[:],
                             n2b[:, 512 * cch:512 * cch + w],
                             start=True, stop=True,
                             skip_group_check=True).then_inc(*(s_tb).tick(1))
        tensor.wait_ge(p.s_rsw.h, 48)
        base2 = 2 * nch + 2
        for cch in range(nch):
            w = min(512, width - 512 * cch)
            tensor.wait_ge((s_vn).h, (s_vn).base + (base2 + 2 if cch == 0
                                                    else base2 + 3 + cch))
            nc.tensor.matmul(ps_b[:, 0:w], p.rsw[:],
                             buf[:, 512 * cch:512 * cch + w],
                             start=True, stop=True,
                             skip_group_check=True).then_inc(*(s_tb).tick(1))

    @blk.vector
    def _(vector):
        vector.wait_ge((s_vtr).h, (s_vtr).base + (0))
        vector.wait_ge(p.p4_qs_ready[0], p.p4_qs_ready[1])
        for cch in range(nch):
            w = min(512, width - 512 * cch)
            vector.wait_ge((s_tn).h, (s_tn).base + (cch + 1))
            vector.tensor_copy(n2[:, 512 * cch:512 * cch + w],
                               ps_n[0:2, 0:w]).then_inc(*(s_vn).tick(1))
        vector.wait_ge((s_sq).h, (s_sq).base + (2))
        vector.tensor_scalar(n2[:, 0:width], n2[:, 0:width], 1e-12, None,
                             ALU.max).then_inc(*(s_vn).tick(1))
        vector.wait_ge((s_vn).h, (s_vn).base + (nch + 1))
        with nc.allow_low_precision(reason="l2 norms tolerate reduced precision"):
            vector.reciprocal(n2b[:, 0:width], n2[:, 0:width]).then_inc(*(s_vn).tick(1))
        for cch in range(nch):
            w = min(512, width - 512 * cch)
            vector.wait_ge((s_tb).h, (s_tb).base + (cch + 1))
            vector.tensor_copy(nrb[:, 512 * cch:512 * cch + w],
                               ps_b[:, 0:w]).then_inc(*(s_vn).tick(1))
        bufv = buf[:, 0:width]
        base = 2 * nch + 2
        vector.wait_ge((s_vn).h, (s_vn).base + (base))
        vector.tensor_tensor(bufv, bufv, nrb[:, 0:width], ALU.mult).then_inc(*(s_vn).tick(1))
        vector.wait_ge((s_vn).h, (s_vn).base + (base + 1))
        vector.tensor_scalar(bufv, bufv, qs8[:, sccol:sccol + 1], None,
                             ALU.mult).then_inc(*(s_vn).tick(1))
        vector.wait_ge((s_vn).h, (s_vn).base + (base + 2))
        vector.tensor_tensor(rt1[:, 0:width], bufv, ta[:, toff:toff + width],
                             ALU.mult).then_inc(*(s_vn).tick(1))
        for cch in range(nch):
            w = min(512, width - 512 * cch)
            vector.wait_ge((s_tb).h, (s_tb).base + (nch + cch + 1))
            vector.tensor_tensor(rt2[:, 512 * cch:512 * cch + w],
                                 ps_b[:, 0:w],
                                 tb[:, toff + 512 * cch:toff + 512 * cch + w],
                                 ALU.mult).then_inc(*(s_vn).tick(1))
        vector.wait_ge((s_vn).h, (s_vn).base + (base + 3 + nch))
        vector.tensor_tensor(bufv, rt1[:, 0:width], rt2[:, 0:width],
                             ALU.add).then_inc(*(s_vn).tick(1))
    return s_vn




def _emit_inner_attn(p, blk, x_res, identt, ident_bf, tri_pt, tril_pt, cst, il):
    nc = p.nc
    sc = contextlib.ExitStack()

    x_fm4 = p.sb(f"p4{il}_xfm", [128, DC, 2 + PW], BF16, sc)
    q4 = p.sb(f"p4{il}_q", [128, HALF], BF16, sc)
    k4 = p.sb(f"p4{il}_k", [128, WINDOW + HALF], BF16, sc)
    v4 = p.sb(f"p4{il}_v", [128, 20 * 128], BF16, sc)
    wst = p.sb(f"p4{il}_wst", [128, DC, 128], F32, sc)
    wbf = [p.sb(f"p4{il}_wbf{i}", [128, DC, 128], BF16, sc) for i in range(3)]
    tA = {nm: p.sb(f"p4{il}_t{nm}", [128, HALF if nm in ("aq", "bq") else PW],
                   BF16, sc)
          for nm in ("aq", "bq", "ak", "bk")}
    qs8 = p.sb(f"p4{il}_qs8", [128, 2], F32, sc)
    n2 = p.sb(f"p4{il}_n2", [2, PW], F32, sc)
    n2b = p.sb(f"p4{il}_n2b", [2, PW], BF16, sc)
    nrb = p.sb(f"p4{il}_nrb", [128, PW], BF16, sc)
    rt1 = p.sb(f"p4{il}_rt1", [128, PW], BF16, sc)
    rt2 = p.sb(f"p4{il}_rt2", [128, PW], BF16, sc)
    sqs = rt1
    bo33 = p.sb(f"p4{il}_bo33", [128, 2], BF16, sc)
    hsel = p.sb(f"p4{il}_hsel", [2, 128], BF16, sc)
    hself = p.sb(f"p4{il}_hself", [2, 128], F32, sc)
    or64 = p.sb(f"p4{il}_or64", [1, 64], BF16, sc)
    orf = p.sb(f"p4{il}_orf", [1, 64], F32, sc)
    oc1 = p.sb(f"p4{il}_oc1", [128, 1], BF16, sc)
    p_f = p.sb(f"p4{il}_pf", [128, 640], F32, sc)
    p_fb = p.sb(f"p4{il}_pfb", [128, 640], BF16, sc)
    lcol = p.sb(f"p4{il}_lcol", [128, 2], F32, sc)
    p_bf = [p.sb(f"p4{il}_pbf{i}", [128, 5, 128], BF16, sc) for i in range(2)]
    negm = p.sb(f"p4{il}_negm", [128, 2], F32, sc)
    ot4 = p.sb(f"p4{il}_ot4", [128, NHP, HALF], BF16, sc)
    wo4 = p.sb(f"p4{il}_wo4", [128, 4, 512], BF16, sc)

    ps_sa = sc.enter_context(nc.psum_tensor(f"p4a{il}_sa", [128, 1024], F32))
    ps_sb = sc.enter_context(nc.psum_tensor(f"p4a{il}_sb", [128, 1024], F32))
    ps_pt = sc.enter_context(nc.psum_tensor(f"p4a{il}_pt", [128, 640], F32))
    ps_ot = sc.enter_context(nc.psum_tensor(f"p4a{il}_ot", [128, 512], F32))
    ps_l = sc.enter_context(nc.psum_tensor(f"p4a{il}_l", [33, 512], F32))

    B = dict(n2=n2, n2b=n2b, nrb=nrb, rt1=rt1, rt2=rt2, sqs=sqs, bo33=bo33,
             hsel=hsel, hself=hself, qs8=qs8, ps_n=ps_l, ps_b=ps_ot)
    p.p4_qs_ready = None

    s_tx = p.rs("ga_tx"); s_vx = p.rs("ga_vx")

    @blk.tensor
    def _(tensor):
        for role in ("ga_vwo", "ga_v", "f4_vr"):
            rr = p.rs(role)
            tensor.wait_ge(rr.h, rr.base + 0)
        for tt in range(NT):
            if tt >= 2:
                tensor.wait_ge((s_vx).h, (s_vx).base + (tt - 1))
            pst = ps_sa if tt % 2 == 0 else ps_sb
            for c in range(DC):
                ins = nc.tensor.transpose(
                    pst[:, 128 * c:128 * c + 128],
                    x_res[:, tt, 128 * c:128 * c + 128], identt[:])
            ins.then_inc(*(s_tx).tick(1))

    @blk.vector
    def _(vector):
        vector.memset(x_fm4[:, :, 0:2], 0.0)
        vector.memset(bo33[0:64, 0:1], 1.0)
        vector.memset(bo33[64:128, 0:1], 0.0)
        vector.memset(bo33[0:64, 1:2], 0.0)
        vector.memset(bo33[64:128, 1:2], 1.0)
        vector.memset(or64[:], 1.0)
        vector.memset(orf[:], 1.0)
        vector.memset(oc1[:], 1.0)
        for tt in range(NT):
            vector.wait_ge((s_tx).h, (s_tx).base + (tt + 1))
            srcp = (ps_sa if tt % 2 == 0 else ps_sb)[:].rearrange("p (c k) -> p c k", c=DC)
            vector.tensor_copy(x_fm4[:, :, 515 + 128 * tt:515 + 128 * tt + 128],
                               srcp).then_inc(*(s_vx).tick(1))

    s_din = p.rs("ga_din0"); s_gc = p.rs("ga_gc"); s_drb = p.rs("ga_drb")
    s_vh = p.rs("p4_vh"); s_tld = p.rs("p4_tld")
    s_tl2 = p.rs("p4_tl2")
    p.p4_qs_ready = (s_vh.h, s_vh.base + 2)

    @blk.sync
    def _(sync):
        sync.wait_ge((s_vx).h, (s_vx).base + (NT))
        for c in range(DC):
            dst = p.ap["shcc_in"][128 * c:128 * c + 128, :]
            sync.dma_start(out=dst, in_=x_fm4[:, c, 2050:2563]).then_inc(*(s_din).tick(16))

    @blk.gpsimd
    def _(gp):
        gp.wait_ge((s_din).h, (s_din).base + (16 * DC))
        gp.collective_compute(
            "AllGather", ALU.bypass, replica_groups=RG,
            ins=[p.ap["shcc_in"][:]], outs=[p.ap["shcc_out"][:]],
        ).then_inc(*(s_gc).tick(1))

    @blk.sync
    def _(sync):
        sync.wait_ge((s_gc).h, (s_gc).base + (1))
        for arm in sync.switch_core_id(n=128):
            pr = 1 - (arm.logical % 2)
            srcd = p.ap["shcc_out"][pr]
            for c in range(DC):
                sync.dma_start(out=x_fm4[:, c, 2:515],
                               in_=srcd[128 * c:128 * c + 128, :]).then_inc(s_drb.h, 16)
        s_drb.ctr += 16 * DC
        for half128 in range(2):
            b = 64 * half128
            sync.dma_start(out=qs8[b:b + 64, 0:1],
                           in_=p.ap["i_qs"][il][:, None]).then_inc(*(s_tld).tick(16))
            sync.dma_start(out=qs8[b:b + 64, 1:2],
                           in_=p.ap["i_ks"][il][:, None]).then_inc(*(s_tld).tick(16))
        for nm in ("aq", "bq", "ak", "bk"):
            sync.dma_start(out=tA[nm][:], in_=p.ap[f"rot_{nm}"][:]).then_inc(*(s_tl2).tick(16))
        sync.dma_start(out=hsel[:], in_=p.ap["hsel_bf"][:]).then_inc(*(s_tl2).tick(16))
        sync.dma_start(out=hself[:], in_=p.ap["hsel_f"][:]).then_inc(*(s_tl2).tick(16))

    @blk.vector
    def _(vector):
        vector.wait_ge((s_drb).h, (s_drb).base + (16 * DC))
        vector.wait_ge(p.s_hv0.h, 16)
        vector.tensor_scalar(x_fm4[:, :, 2:515], x_fm4[:, :, 2:515],
                             p.hv[:, 0:1], None, ALU.mult).then_inc(*(s_vh).tick(1))
        vector.wait_ge((s_tld).h, (s_tld).base + (64))
        vector.tensor_scalar(qs8[:, 0:1], qs8[:, 0:1], QK_SCALE, None,
                             ALU.mult).then_inc(*(s_vh).tick(1))

    for hp in range(NHP):
        _inner_attn_hp(p, blk, hp, il, x_res, x_fm4, q4, k4, v4, wst, wbf,
                       tA, B, p_f, p_fb, lcol, p_bf, negm, ot4, oc1, or64,
                       tri_pt, tril_pt, cst, identt, ident_bf,
                       ps_sa, ps_sb, ps_pt, ps_ot, ps_l,
                       s_vh)

    s_wld = p.rs("ga_wld"); s_vw = p.rs("ga_vw")
    s_tw = p.rs("ga_tw"); s_vwo = p.rs("ga_vwo")
    s_ta = p.rs("ga_tav"); s_vo = p.rs("ga_vo"); s_vb = p.rs("ga_vb")

    @blk.sync
    def _(sync):
        sync.wait_ge((s_vo).h, (s_vo).base + (0))
        sync.wait_ge((s_vw).h, (s_vw).base + (0))
        srcw = p.ap["i_wo"][il].rearrange("(t p) o -> p t o", p=128)
        for oh in range(2):
            for t4 in range(4):
                e = 4 * oh + t4
                if e > 0:
                    sync.wait_ge((s_vw).h, (s_vw).base + (e))
                for j in range(4):
                    sync.dma_start(
                        out=wst[:, j, :],
                        in_=srcw[:, t4, 512 * oh + 128 * j:512 * oh + 128 * j + 128],
                    ).then_inc(*(s_wld).tick(16))

    @blk.vector
    def _(vector):
        for oh in range(2):
            for t4 in range(4):
                e = 4 * oh + t4
                vector.wait_ge((s_wld).h, (s_wld).base + (64 * (e + 1)))
                if oh > 0:
                    vector.wait_ge((s_tw).h, (s_tw).base + (16 * oh))
                vector.tensor_copy(
                    wo4[:, t4, :],
                    wst[:, 0:4, :].rearrange("p c k -> p (c k)")).then_inc(*(s_vw).tick(1))
            for tt in range(NT):
                g = 16 * oh + tt
                vector.wait_ge((s_tw).h, (s_tw).base + (g + 1))
                xt = x_res[:, tt, 512 * oh:512 * oh + 512]
                pst = (ps_sa if g % 2 == 0 else ps_sb)[:, 0:512]
                vector.tensor_tensor(xt, pst, xt, ALU.add).then_inc(*(s_vwo).tick(1))

    @blk.tensor
    def _(tensor):
        tensor.wait_ge((s_vo).h, (s_vo).base + (0))
        for oh in range(2):
            tensor.wait_ge((s_vw).h, (s_vw).base + (4 * (oh + 1)))
            for tt in range(NT):
                g = 16 * oh + tt
                if g >= 2:
                    tensor.wait_ge((s_vwo).h, (s_vwo).base + (g - 1))
                dst = (ps_sa if g % 2 == 0 else ps_sb)[:, 0:512]
                for c4 in range(4):
                    ins = nc.tensor.matmul(
                        dst, ot4[:, c4, 128 * tt:128 * tt + 128],
                        wo4[:, c4, :],
                        start=(c4 == 0), stop=(c4 == 3), skip_group_check=True)
                ins.then_inc(*(s_tw).tick(1))
    sc.close()


def _inner_attn_hp(p, blk, hp, il, x_res, x_fm4, q4, k4, v4, wst, wbf,
                   tA, B, p_f, p_fb, lcol, p_bf, negm, ot4, oc1, or64,
                   tri_pt, tril_pt, cst, identt, ident_bf,
                   ps_sa, ps_sb, ps_pt, ps_ot, ps_l, s_vh):
    nc = p.nc
    KW = WINDOW + HALF
    hsel2 = B["hsel"]
    s_wld = p.rs("ga_wld"); s_vw = p.rs("ga_vw")
    s_tq = p.rs("ga_tq"); s_vq = p.rs("ga_vq")
    s_tv = p.rs("ga_tv"); s_vv = p.rs("ga_vv")

    @blk.sync
    def _(sync, hp=hp):
        for wi in range(3):
            sync.wait_ge((s_vw).h, (s_vw).base + (wi))
            col = (0, 512, 1024)[wi] + 128 * hp
            srcw = p.ap["i_wqkv"][il].rearrange("(c p) o -> p c o", p=128)
            for c in range(DC):
                sync.dma_start(out=wst[:, c, :],
                               in_=srcw[:, c, col:col + 128]).then_inc(*(s_wld).tick(16))

    @blk.vector
    def _(vector, hp=hp):
        for wi in range(3):
            vector.wait_ge((s_wld).h, (s_wld).base + (128 * (wi + 1)))
            vector.wait_ge((s_tq).h, (s_tq).base + (0))
            vector.tensor_copy(wbf[wi][:], wst[:]).then_inc(*(s_vw).tick(1))
        nq = (cdiv(HALF, 512), cdiv(KW, 512), cdiv(KW, 512))
        e = 0
        for wi in range(3):
            for qb in range(nq[wi]):
                vector.wait_ge((s_tq).h, (s_tq).base + (e + 1))
                w = min(512, (HALF if wi == 0 else KW) - 512 * qb)
                srcps = (ps_sa if e % 2 == 0 else ps_sb)[:, 0:w]
                if wi == 0:
                    vector.tensor_copy(q4[:, 512 * qb:512 * qb + w], srcps).then_inc(*(s_vq).tick(1))
                elif wi == 1:
                    vector.tensor_copy(k4[:, 512 * qb:512 * qb + w], srcps).then_inc(*(s_vq).tick(1))
                else:
                    vector.tensor_copy(v4[:, 512 * qb:512 * qb + w], srcps).then_inc(*(s_vq).tick(1))
                e += 1
        for g in range(5):
            vector.wait_ge((s_tv).h, (s_tv).base + (g + 1))
            n4 = 4
            srcp = (ps_pt if g % 2 == 0 else ps_ot)[:, 0:512]
            vector.tensor_copy(v4[:, 512 * g:512 * g + 512], srcp).then_inc(*(s_vv).tick(1))

    @blk.tensor
    def _(tensor, hp=hp):
        if hp == 0:
            tensor.wait_ge((s_vh).h, (s_vh).base + (1))
        nq = (cdiv(HALF, 512), cdiv(KW, 512), cdiv(KW, 512))
        offs = (515, 3, 3)
        e = 0
        for wi in range(3):
            tensor.wait_ge((s_vw).h, (s_vw).base + (wi + 1))
            for qb in range(nq[wi]):
                if e >= 2:
                    tensor.wait_ge((s_vq).h, (s_vq).base + (e - 1))
                w = min(512, (HALF if wi == 0 else KW) - 512 * qb)
                dst = (ps_sa if e % 2 == 0 else ps_sb)[:, 0:w]
                base = offs[wi] + 512 * qb
                for c in range(DC):
                    off = base if c < 4 else base - 1
                    ins = nc.tensor.matmul(
                        dst, wbf[wi][:, c, :], x_fm4[:, c, off:off + w],
                        start=(c == 0), stop=(c == DC - 1), skip_group_check=True)
                ins.then_inc(*(s_tq).tick(1))
                e += 1
        tensor.wait_ge((s_vq).h, (s_vq).base + (14))
        for g in range(5):
            if g >= 2:
                tensor.wait_ge((s_vv).h, (s_vv).base + (g - 1))
            for t4 in range(4):
                tt = 4 * g + t4
                ins = nc.tensor.matmul(
                    (ps_pt if g % 2 == 0 else ps_ot)[:, 128 * t4:128 * t4 + 128],
                    v4[:, 128 * tt:128 * tt + 128], ident_bf[:],
                    start=True, stop=True, skip_group_check=True)
            ins.then_inc(*(s_tv).tick(1))


    _norm_rot(p, blk, q4, HALF, 0, tA["aq"], tA["bq"], 0, B,
              ready=(s_vq.h, s_vq.base + 4))
    _norm_rot(p, blk, k4, KW, 1, tA["ak"], tA["bk"], 1, B,
              ready=(s_vq.h, s_vq.base + 9))

    s_ts = p.rs("ga_ts"); s_vm = p.rs("ga_vm"); s_se = p.rs("ga_se")
    s_tp = p.rs("p4_tp"); s_vp = p.rs("p4_vp")
    s_ta = p.rs("ga_tav"); s_vo = p.rs("ga_vo")
    s_vnr = p.rs("nr_vn")

    def nmask(t):
        return (1 + max(0, 3 - t) + 1) if t < 4 else 2
    vpc = 0
    mask_done = {}; cast_done = {}; evac_done = {}
    for t in range(NT):
        for h in range(2):
            u = 2 * t + h
            vpc += nmask(t); mask_done[u] = vpc
            vpc += 3; cast_done[u] = vpc
            vpc += 1; evac_done[u] = vpc

    @blk.tensor
    def _(tensor, hp=hp):
        tensor.wait_ge((s_vv).h, (s_vv).base + (5))
        tensor.wait_ge((s_vnr).h, (s_vnr).base + (0))
        for t in range(NT):
            for h in range(2):
                u = 2 * t + h
                pss = ps_sa if u % 2 == 0 else ps_sb
                if u >= 2:
                    tensor.wait_ge((s_se).h, (s_se).base + (u - 1))
                for kk in range(2):
                    w = 512 if kk == 0 else 128
                    ins = nc.tensor.matmul(
                        pss[:, 512 * kk:512 * kk + w],
                        q4[64 * h:64 * h + 64, 128 * t:128 * t + 128],
                        k4[64 * h:64 * h + 64, 128 * t + 512 * kk:128 * t + 512 * kk + w],
                        start=True, stop=True, tile_position=(64 * h, 0),
                        skip_group_check=True)
                ins.then_inc(*(s_ts).tick(1))
                tensor.wait_ge((s_vp).h, (s_vp).base + (cast_done[u]))
                if u >= 1:
                    tensor.wait_ge((s_vp).h, (s_vp).base + (evac_done[u - 1]))
                for j in range(5):
                    ins = nc.tensor.matmul(
                        ps_pt[:, 128 * j:128 * j + 128],
                        p_fb[:, 128 * j:128 * j + 128], ident_bf[:],
                        start=True, stop=True, skip_group_check=True)
                ins.then_inc(*(s_tp).tick(1))
                tensor.wait_ge((s_vp).h, (s_vp).base + (evac_done[u]))
                if t % 4 == 0 and h == 0 and t > 0:
                    tensor.wait_ge((s_vo).h, (s_vo).base + (t // 4))
                for j in range(5):
                    ins = nc.tensor.matmul(
                        ps_ot[64 * h:64 * h + 64, 128 * (t % 4):128 * (t % 4) + 128],
                        v4[:].rearrange("p (n k) -> p n k", k=128)[:, t + j, 64 * h:64 * h + 64],
                        p_bf[u % 2][:, j, :],
                        start=(j == 0), stop=(j == 4), tile_position=(0, 64 * h),
                        skip_group_check=True)
                ins.then_inc(*(s_ta).tick(1))

    @blk.vector
    def _(vector, hp=hp):
        vector.wait_ge(p.s_rsw.h, 48)
        for t in range(NT):
            for h in range(2):
                u = 2 * t + h
                pss = ps_sa if u % 2 == 0 else ps_sb
                vector.wait_ge((s_ts).h, (s_ts).base + (u + 1))
                vector.reduce_max(negm[:, 0:1], pss[:, 0:640], AX.X).then_inc(*(s_vm).tick(1))
                vector.wait_ge((s_vm).h, (s_vm).base + (2 * u + 1))
                if u >= 1:
                    vector.wait_ge((s_se).h, (s_se).base + (u))
                vector.tensor_scalar(negm[:, 1:2], negm[:, 0:1], -1.0, None,
                                     ALU.mult).then_inc(*(s_vm).tick(1))
                vector.wait_ge((s_se).h, (s_se).base + (u + 1))
                k0 = mask_done[u] - nmask(t)
                km = [k0]

                def mtick(ins):
                    ins.then_inc(*(s_vp).tick(1))
                    km[0] += 1
                    vector.wait_ge((s_vp).h, (s_vp).base + (km[0]))

                m0 = p_f[:, 0:128]
                m4 = p_f[:, 512:640]
                if t < 4:
                    mtick(vector.scalar_tensor_tensor(m0, m0, p.hv[:, 0:1],
                                                      p.trif[:], ALU.mult, ALU.mult))
                    for j in range(1, 4 - t):
                        mj = p_f[:, 128 * j:128 * j + 128]
                        mtick(vector.tensor_scalar(mj, mj, p.hv[:, 0:1], None, ALU.mult))
                else:
                    mtick(vector.tensor_tensor(m0, m0, p.trif[:], ALU.mult))
                mtick(vector.tensor_tensor(m4, m4, p.trilf[:], ALU.mult))
                mtick(vector.reduce_sum(lcol[:, 0:1], p_f[:], AX.X))
                mtick(vector.reciprocal(lcol[:, 1:2], lcol[:, 0:1]))
                mtick(vector.tensor_scalar(p_fb[:], p_f[:], lcol[:, 1:2], None,
                                           ALU.mult))
                vector.wait_ge((s_tp).h, (s_tp).base + (u + 1))
                if u >= 2:
                    vector.wait_ge((s_ta).h, (s_ta).base + (u - 1))
                vector.tensor_copy(p_bf[u % 2][:].rearrange("p j k -> p (j k)"),
                                   ps_pt[:, 0:640]).then_inc(*(s_vp).tick(1))
            if t % 4 == 3:
                g = t // 4
                vector.wait_ge((s_ta).h, (s_ta).base + (2 * t + 2))
                vector.tensor_copy(ot4[:, hp, 512 * g:512 * g + 512],
                                   ps_ot[:, 0:512]).then_inc(*(s_vo).tick(1))

    @blk.scalar
    def _(scalar, hp=hp):
        for t in range(NT):
            for h in range(2):
                u = 2 * t + h
                pss = ps_sa if u % 2 == 0 else ps_sb
                scalar.wait_ge((s_vm).h, (s_vm).base + (2 * u + 2))
                if u >= 1:
                    scalar.wait_ge((s_vp).h, (s_vp).base + (cast_done[u - 1]))
                scalar.activation(out=p_f[:], in_=pss[:, 0:640], func=AF.Exp,
                                  bias=negm[:, 1:2]).then_inc(*(s_se).tick(1))


def _emit_inner_ffn(p, blk, x_res, identt, ident_bf, cst, il):
    nc = p.nc
    sc = contextlib.ExitStack()
    NHC = 22

    sh = p.sb(f"f4{il}_sh", [128, D], F32, sc)
    hn = p.sb(f"f4{il}_hn", [128, D], F32, sc)
    hfm = p.sb(f"f4{il}_hfm", [128, DC, HALF], BF16, sc)
    hrow = p.sb(f"f4{il}_hrow", [1, D], F32, sc)
    g_fm = p.sb(f"f4{il}_gfm", [128, NHC, 512], BF16, sc)
    w1stg = [p.sb(f"f4{il}_w1s{i}", [128, DC, 256], F32, sc) for i in range(2)]
    w1bf = [p.sb(f"f4{il}_w1b{i}", [128, DC, 256], BF16, sc) for i in range(2)]
    w2stg = [p.sb(f"f4{il}_w2s{i}", [128, D], F32, sc) for i in range(2)]
    w2bf = [p.sb(f"f4{il}_w2b{i}", [128, D], BF16, sc) for i in range(2)]
    lgro = p.sb(f"f4{il}_lgro", [1, D], F32, sc)
    lbro = p.sb(f"f4{il}_lbro", [1, D], F32, sc)
    b2ro = p.sb(f"f4{il}_b2ro", [1, D], F32, sc)
    lg_b = p.sb(f"f4{il}_lgb", [128, D], F32, sc)
    lb_b = p.sb(f"f4{il}_lbb", [128, D], F32, sc)
    b2_b = p.sb(f"f4{il}_b2b", [128, D], F32, sc)
    b1c = p.sb(f"f4{il}_b1c", [128, 2 * NHC], F32, sc)
    gtmp = p.sb(f"f4{il}_gtmp", [128, 512], F32, sc)
    stt = p.sb(f"f4{il}_stt", [128, 8], F32, sc)
    or128 = p.sb(f"f4{il}_or128", [1, 128], F32, sc)

    ps8 = [sc.enter_context(nc.psum_tensor(f"f4{il}_ps{i}", [128, 512], F32))
           for i in range(8)]
    ps_a, ps_g, ps_t = ps8[0], ps8[1], ps8[2]

    s_sh1 = p.rs("f4_sh1")
    s_gc = p.rs("ga_gc")
    s_hrb = p.rs("f4_hrb")
    s_ld = p.rs("f4_ld")
    s_tbb = p.rs("ga_tb")
    s_vbb = p.rs("ga_vb")

    s_vwoF = p.rs("ga_vwo")

    @blk.sync
    def _(sync):
        sync.wait_ge((s_vwoF).h, (s_vwoF).base + (0))
        sync.dma_start(out=p.ap["sh1_in"][:], in_=x_res[127:128, NT - 1, :]).then_inc(
            *(s_sh1).tick(16))

    @blk.gpsimd
    def _(gp):
        gp.wait_ge((s_sh1).h, (s_sh1).base + (16))
        gp.collective_compute(
            "AllGather", ALU.bypass, replica_groups=RG,
            ins=[p.ap["sh1_in"][:]], outs=[p.ap["sh1_out"][:]],
        ).then_inc(*(s_gc).tick(1))

    @blk.sync
    def _(sync):
        sync.wait_ge((s_gc).h, (s_gc).base + (1))
        for arm in sync.switch_core_id(n=128):
            pr = 1 - (arm.logical % 2)
            sync.dma_start(out=hrow[:], in_=p.ap["sh1_out"][pr]).then_inc(s_hrb.h, 16)
        s_hrb.ctr += 16
        sync.dma_start(out=lgro[:], in_=p.ap["i_lng"][il][None, :]).then_inc(*(s_ld).tick(16))
        sync.dma_start(out=lbro[:], in_=p.ap["i_lnb"][il][None, :]).then_inc(*(s_ld).tick(16))
        sync.dma_start(out=b2ro[:], in_=p.ap["i_b2"][il][None, :]).then_inc(*(s_ld).tick(16))
        for hc in range(NHC):
            n = 128 if hc < NHC - 1 else 42
            for ag in range(2):
                srcb = p.ap["i_b1"][il][256 * hc + n * ag:256 * hc + n * ag + n, None]
                sync.dma_start(out=b1c[0:n, 2 * hc + ag:2 * hc + ag + 1],
                               in_=srcb).then_inc(*(s_ld).tick(16))

    @blk.vector
    def _(vector):
        vector.memset(or128[:], 1.0).then_inc(*(s_vbb).tick(1))
        vector.wait_ge((s_hrb).h, (s_hrb).base + (16))
        vector.tensor_scalar(hrow[:], hrow[:], p.hv[0:1, 0:1], None,
                             ALU.mult).then_inc(*(s_vbb).tick(1))
        for bi in range(3):
            dstb = (lg_b, lb_b, b2_b)[bi]
            for oh in range(2):
                vector.wait_ge((s_tbb).h, (s_tbb).base + (2 * bi + oh + 1))
                vector.tensor_copy(dstb[:, 512 * oh:512 * oh + 512],
                                   ps_t[:, 0:512]).then_inc(*(s_vbb).tick(1))

    @blk.tensor
    def _(tensor):
        tensor.wait_ge((s_ld).h, (s_ld).base + (16 * (3 + 2 * NHC)))
        for bi in range(3):
            rowb = (lgro, lbro, b2ro)[bi]
            for oh in range(2):
                k = 2 * bi + oh
                tensor.wait_ge((s_vbb).h, (s_vbb).base + (1 if k == 0 else k + 2))
                nc.tensor.matmul(ps_t[:, 0:512], or128[:],
                                 rowb[:, 512 * oh:512 * oh + 512],
                                 start=True, stop=True,
                                 skip_group_check=True).then_inc(*(s_tbb).tick(1))

    s_dsh = [p.rs("f4_dshA"), p.rs("f4_dshB")]
    s_vln = p.rs("f4_vln")
    s_sln = p.rs("f4_sln")
    s_th = p.rs("f4_th")
    s_vh2 = p.rs("f4_vh2")
    vh2_abs0 = s_vh2.base

    @blk.scalar
    def _(scalar):
        for tt in range(NT):
            scalar.wait_ge((s_vln).h, (s_vln).base + (12 * tt))
            if tt == 0:
                scalar.wait_ge((s_vbb).h, (s_vbb).base + (2))
            scalar.dma_start(out=sh[1:128, 512:1024],
                             in_=x_res[0:127, tt, 512:1024]).then_inc(*(s_dsh[0]).tick(16))
            if tt > 0:
                scalar.dma_start(out=sh[0:1, 512:1024],
                                 in_=x_res[127:128, tt - 1, 512:1024]).then_inc(
                    *(s_dsh[1]).tick(16))
            else:
                scalar.dma_start(out=sh[0:1, 512:1024],
                                 in_=hrow[:, 512:1024]).then_inc(*(s_dsh[1]).tick(16))
            scalar.wait_ge((s_vln).h, (s_vln).base + (12 * tt + 8))
            scalar.activation(out=stt[:, 6:7], in_=stt[:, 5:6], func=AF.Sqrt,
                              bias=cst[:, 1:2]).then_inc(*(s_sln).tick(1))

    @blk.vector
    def _(vector):
        for tt in range(NT):
            n0 = 12 * tt
            k = [0]

            def step(ins):
                ins.then_inc(*(s_vln).tick(1))
                k[0] += 1
                vector.wait_ge((s_vln).h, (s_vln).base + (n0 + k[0]))

            if tt >= 1:
                vector.wait_ge((s_th).h, (s_th).base + (2 * tt))
            vector.wait_ge((s_dsh[0]).h, (s_dsh[0]).base + (16 * (tt + 1)))
            vector.wait_ge((s_dsh[1]).h, (s_dsh[1]).base + (16 * (tt + 1)))
            if tt == 0:
                vector.wait_ge((s_vbb).h, (s_vbb).base + (8))
            step(vector.tensor_copy(sh[:, 0:512], x_res[:, tt, 0:512]))
            step(vector.tensor_tensor(hn[:], sh[:], sh[:], ALU.mult))
            step(vector.reduce_sum(stt[:, 0:1], sh[:], AX.X))
            step(vector.tensor_scalar(stt[:, 2:3], stt[:, 0:1], 1.0 / D, None, ALU.mult))
            step(vector.reduce_sum(stt[:, 1:2], hn[:], AX.X))
            step(vector.tensor_scalar(stt[:, 3:4], stt[:, 1:2], 1.0 / D, None, ALU.mult))
            step(vector.tensor_tensor(stt[:, 4:5], stt[:, 2:3], stt[:, 2:3], ALU.mult))
            step(vector.tensor_tensor(stt[:, 5:6], stt[:, 3:4], stt[:, 4:5], ALU.subtract))
            vector.wait_ge((s_sln).h, (s_sln).base + (tt + 1))
            step(vector.reciprocal(stt[:, 7:8], stt[:, 6:7]))
            step(vector.tensor_scalar(hn[:], sh[:], stt[:, 2:3], stt[:, 7:8],
                                      ALU.subtract, ALU.mult))
            step(vector.tensor_tensor(hn[:], hn[:], lg_b[:], ALU.mult))
            step(vector.tensor_tensor(hn[:], hn[:], lb_b[:], ALU.add))
            for g in range(2):
                vector.wait_ge((s_th).h, (s_th).base + (2 * tt + g + 1))
                srcp = (ps_t if g == 0 else ps_a)[:, 0:512].rearrange("p (c k) -> p c k", c=4)
                vector.tensor_copy(hfm[:, 4 * g:4 * g + 4, 128 * tt:128 * tt + 128],
                                   srcp).then_inc(*(s_vh2).tick(1))

    @blk.tensor
    def _(tensor):
        for tt in range(NT):
            tensor.wait_ge((s_vln).h, (s_vln).base + (12 * tt + 12))
            for c in range(DC):
                g = c // 4
                if 2 * tt + g >= 2:
                    tensor.wait_ge((s_vh2).h, (s_vh2).base + (2 * tt + g - 1))
                ins = nc.tensor.transpose(
                    (ps_t if g == 0 else ps_a)[:, 128 * (c % 4):128 * (c % 4) + 128],
                    hn[:, 128 * c:128 * c + 128], identt[:])
                if c % 4 == 3:
                    ins.then_inc(*(s_th).tick(1))

    for ts in range(4):
        s_w1l = [p.rs("f4_w1lA"), p.rs("f4_w1lB")]
        s_vw1 = p.rs("f4_vw1")
        s_tm1 = p.rs("f4_tm1")
        s_sg = p.rs("f4_sg")
        s_vg = p.rs("f4_vg")
        s_w2l = [p.rs("f4_w2lA"), p.rs("f4_w2lB")]
        s_vw2 = p.rs("f4_vw2")
        s_tm2 = p.rs("f4_tm2")
        s_vr = p.rs("f4_vr")
        s_vh2b = p.rs("f4_vh2")
        s_vlnF = p.rs("f4_vln")

        @blk.sync
        def _(sync, ts=ts):
            sync.wait_ge((s_vw1).h, (s_vw1).base + (0))
            sync.wait_ge((s_vw2).h, (s_vw2).base + (0))
            for hc in range(NHC):
                if hc >= 2:
                    sync.wait_ge((s_vw1).h, (s_vw1).base + (hc - 1))
                ncol = 256 if hc < NHC - 1 else 84
                srcw = p.ap["i_w1"][il].rearrange("(c p) f -> p c f", p=128)
                for c in range(DC):
                    sync.dma_start(out=w1stg[hc % 2][:, c, 0:ncol],
                                   in_=srcw[:, c, 256 * hc:256 * hc + ncol]).then_inc(
                        *(s_w1l[hc % 2]).tick(16))
            for hc in range(NHC):
                if hc >= 2:
                    sync.wait_ge((s_vw2).h, (s_vw2).base + (hc - 1))
                n = 128 if hc < NHC - 1 else 42
                sync.dma_start(out=w2stg[hc % 2][0:n, :],
                               in_=p.ap["i_w2"][il][128 * hc:128 * hc + n, :]).then_inc(
                    *(s_w2l[hc % 2]).tick(16))

        @blk.vector
        def _(vector, ts=ts):
            vector.wait_ge((s_vlnF).h, (s_vlnF).base + (0))
            vector.wait_ge((s_tm1).h, (s_tm1).base + (0))
            vector.wait_ge((s_tm2).h, (s_tm2).base + (0))
            for hc in range(NHC):
                ncol = 256 if hc < NHC - 1 else 84
                vector.wait_ge((s_w1l[hc % 2]).h, (s_w1l[hc % 2]).base + (128 * (hc // 2 + 1)))
                if hc >= 2:
                    vector.wait_ge((s_tm1).h, (s_tm1).base + (2 * (hc - 1)))
                vector.tensor_copy(w1bf[hc % 2][:, :, 0:ncol],
                                   w1stg[hc % 2][:, :, 0:ncol]).then_inc(*(s_vw1).tick(1))
                n = 128 if hc < NHC - 1 else 42
                vector.wait_ge((s_sg).h, (s_sg).base + (hc + 1))
                vector.scalar_tensor_tensor(
                    g_fm[0:n, hc, :], ps_a[0:n, :], b1c[0:n, 2 * hc:2 * hc + 1],
                    gtmp[0:n, :], ALU.add, ALU.mult).then_inc(*(s_vg).tick(1))
            for hc in range(NHC):
                n = 128 if hc < NHC - 1 else 42
                vector.wait_ge((s_w2l[hc % 2]).h, (s_w2l[hc % 2]).base + (16 * (hc // 2 + 1)))
                if hc >= 2:
                    vector.wait_ge((s_tm2).h, (s_tm2).base + (8 * (hc - 1)))
                vector.tensor_copy(w2bf[hc % 2][0:n, :],
                                   w2stg[hc % 2][0:n, :]).then_inc(*(s_vw2).tick(1))
            for e in range(8):
                t4, oh = e // 2, e % 2
                tt = 4 * ts + t4
                vector.wait_ge((s_tm2).h, (s_tm2).base + (8 * NHC))
                if e >= 1:
                    vector.wait_ge((s_vr).h, (s_vr).base + (2 * e))
                xt = x_res[:, tt, 512 * oh:512 * oh + 512]
                vector.tensor_tensor(gtmp[:], ps8[e][:],
                                     b2_b[:, 512 * oh:512 * oh + 512],
                                     ALU.add).then_inc(*(s_vr).tick(1))
                vector.wait_ge((s_vr).h, (s_vr).base + (2 * e + 1))
                vector.tensor_tensor(xt, gtmp[:], xt, ALU.add).then_inc(*(s_vr).tick(1))

        @blk.scalar
        def _(scalar, ts=ts):
            for hc in range(NHC):
                n = 128 if hc < NHC - 1 else 42
                if hc >= 1:
                    scalar.wait_ge((s_vg).h, (s_vg).base + (hc))
                scalar.wait_ge((s_tm1).h, (s_tm1).base + (2 * hc + 2))
                scalar.activation(out=gtmp[0:n, :], in_=ps_g[0:n, :], func=AF.Gelu,
                                  bias=b1c[0:n, 2 * hc + 1:2 * hc + 2]).then_inc(
                    *(s_sg).tick(1))

        @blk.tensor
        def _(tensor, ts=ts):
            tensor.wait_ge((s_vh2b).h, vh2_abs0 + 2 * (4 * ts + 4))
            tensor.wait_ge((s_vr).h, (s_vr).base + (0))
            for hc in range(NHC):
                n = 128 if hc < NHC - 1 else 42
                gcol0 = 128 if hc < NHC - 1 else 42
                tensor.wait_ge((s_vw1).h, (s_vw1).base + (hc + 1))
                if hc >= 1:
                    tensor.wait_ge((s_vg).h, (s_vg).base + (hc))
                if hc >= 2:
                    tensor.wait_ge((s_sg).h, (s_sg).base + (hc - 1))
                for half in range(2):
                    pdst = (ps_a, ps_g)[half]
                    col0 = 0 if half == 0 else gcol0
                    for c in range(DC):
                        ins = nc.tensor.matmul(
                            pdst[0:n, 0:512],
                            w1bf[hc % 2][:, c, col0:col0 + n],
                            hfm[:, c, 512 * ts:512 * ts + 512],
                            start=(c == 0), stop=(c == DC - 1), skip_group_check=True)
                    if half == 1:
                        ins.then_inc(*(s_tm1).tick(2))
            tensor.wait_ge((s_vg).h, (s_vg).base + (NHC))
            for hc in range(NHC):
                n = 128 if hc < NHC - 1 else 42
                tensor.wait_ge((s_vw2).h, (s_vw2).base + (hc + 1))
                for e in range(8):
                    t4, oh = e // 2, e % 2
                    ins = nc.tensor.matmul(
                        ps8[e][:],
                        g_fm[0:n, hc, 128 * t4:128 * t4 + 128],
                        w2bf[hc % 2][0:n, 512 * oh:512 * oh + 512],
                        start=(hc == 0), stop=(hc == NHC - 1), skip_group_check=True)
                    if e == 7:
                        ins.then_inc(*(s_tm2).tick(8))
    sc.close()


_cache = {}


def kernel(**inputs):
    per_core = _stage_inputs(inputs)
    if "prog" not in _cache:
        _cache["prog"] = build_program()
    nc = _cache["prog"]
    res = run_bass_kernel_spmd(nc, per_core, core_ids=list(range(8)))
    shards = [np.asarray(res.results[c]["out"]) for c in range(8)]
    return np.concatenate(shards, axis=0).reshape(B, S, D).astype(np.float32)


# version 07 (final)
# speedup vs baseline: 1.0591x; 1.0221x over previous
import contextlib

import numpy as np
import ml_dtypes

import concourse.bass as bass
import concourse.mybir as mybir
from concourse.bass_utils import run_bass_kernel_spmd

F32 = mybir.dt.float32
BF16 = mybir.dt.bfloat16
AF = mybir.ActivationFunctionType
ALU = mybir.AluOpType
AX = mybir.AxisListType
bfloat16 = ml_dtypes.bfloat16

B, S, D = 4, 4096, 1024
HALF = S // 2
NT = HALF // 128
NTB = S // 128
DC = D // 128
NHP = 4
RG = [[0, 1], [2, 3], [4, 5], [6, 7]]
BIGB = 30.0
MAGIC = 12582912.0
N_PHASES = 4

IDH = 64
INNER = 512
WINDOW = 512
HID = 2730
QK_SCALE = 8.0


def cdiv(a, b):
    return (a + b - 1) // b


def _rotary_tables():
    half_idx = np.arange(0, IDH, 2, dtype=np.float64)
    inv_freq = 1.0 / 10000.0 ** (half_idx / IDH)
    t = np.arange(S, dtype=np.float64)
    freqs = np.concatenate([t[:, None] * inv_freq[None, :]] * 2, axis=1)
    base = (half_idx + 0.4 * IDH) / (1.4 * IDH)
    power = (t - S // 2) / float(WINDOW // 2)
    scale = np.concatenate([base[None, :] ** power[:, None]] * 2, axis=1)
    cos, sin = np.cos(freqs), np.sin(freqs)
    return ((cos * scale).T.astype(np.float32), (sin * scale).T.astype(np.float32),
            (cos / scale).T.astype(np.float32), (sin / scale).T.astype(np.float32))


def _bitffn_perm():
    perm = []
    i = 0
    while i < HID:
        j = min(i + 128, HID)
        perm.extend(range(i, j))
        perm.extend(range(HID + i, HID + j))
        i = j
    return np.array(perm, dtype=np.int64)


def _stage_inputs(inp):
    per_core = []
    x = np.ascontiguousarray(np.asarray(inp["L0_attn_x"], dtype=np.float32))
    eye = np.eye(128, dtype=np.float32)
    tri_keep = np.triu(np.ones((128, 128), dtype=np.float32)).astype(bfloat16)
    tril_keep = np.tril(np.ones((128, 128), dtype=np.float32)).astype(bfloat16)
    if N_PHASES >= 4:
        Aq, Bq, Ak, Bk = _rotary_tables()
        w1perm = _bitffn_perm()
        rsw = np.zeros((128, 128), dtype=np.float32)
        for blk0 in (0, 64):
            for i in range(32):
                rsw[blk0 + i + 32, blk0 + i] = 1.0
                rsw[blk0 + i, blk0 + i + 32] = 1.0
        rsw = rsw.astype(bfloat16)

    for c in range(8):
        b, h = c // 2, c % 2
        d = {}
        d["x_own"] = np.ascontiguousarray(x[b, HALF * h:HALF * h + HALF])
        d["xb"] = x[b]
        hs = slice(512 * h, 512 * h + 512)
        d["l0_wq"] = np.ascontiguousarray(np.asarray(inp["L0_attn_WQ"], np.float32)[:, hs])
        d["l0_wk"] = np.ascontiguousarray(np.asarray(inp["L0_attn_WK"], np.float32)[:, hs])
        d["l0_wv"] = np.ascontiguousarray(np.asarray(inp["L0_attn_WV"], np.float32)[:, hs])
        d["l0_wo"] = np.asarray(inp["L0_attn_WO"], np.float32)
        d["ident"] = eye
        d["ident_bf"] = eye.astype(bfloat16)
        d["tri_pt"] = tri_keep
        d["tril_pt"] = tril_keep
        d["own_half"] = np.array([h], dtype=np.int32)
        if N_PHASES >= 2:
            d["l0_w1"] = np.ascontiguousarray(np.asarray(inp["L0_ffn_w1"], np.float32).T)
            d["l0_w2"] = np.ascontiguousarray(np.asarray(inp["L0_ffn_w2"], np.float32).T)
        if N_PHASES >= 3:
            rs = slice(512 * h, 512 * h + 512)
            d["l1_wq"] = np.ascontiguousarray(np.asarray(inp["L1_attn_W_query"], np.float32)[rs].T)
            d["l1_wk"] = np.ascontiguousarray(np.asarray(inp["L1_attn_W_key"], np.float32)[rs].T)
            d["l1_wv"] = np.ascontiguousarray(np.asarray(inp["L1_attn_W_value"], np.float32)[rs].T)
            d["l1_wp"] = np.ascontiguousarray(np.asarray(inp["L1_attn_W_proj"], np.float32).T)
        if N_PHASES >= 4:
            pos = np.clip(np.arange(HALF * h - 513, HALF * h + HALF), 0, S - 1)
            posq = pos[513:]
            sgn = np.concatenate([-np.ones(32, np.float32), np.ones(32, np.float32)])
            d["rot_aq"] = np.ascontiguousarray(np.tile(Aq[:, posq], (2, 1)).astype(bfloat16))
            d["rot_bq"] = np.ascontiguousarray(np.tile(sgn[:, None] * Bq[:, posq], (2, 1)).astype(bfloat16))
            d["rot_ak"] = np.ascontiguousarray(np.tile(Ak[:, pos], (2, 1)).astype(bfloat16))
            d["rot_bk"] = np.ascontiguousarray(np.tile(sgn[:, None] * Bk[:, pos], (2, 1)).astype(bfloat16))
            d["hv128"] = np.full((128, 1), float(h), dtype=np.float32)
            d["rswap"] = rsw
            d["tri_f"] = np.triu(np.ones((128, 128), dtype=np.float32))
            d["tril_f"] = np.tril(np.ones((128, 128), dtype=np.float32))
            hs = np.zeros((2, 128), dtype=np.float32)
            hs[0, 0:64] = 1.0
            hs[1, 64:128] = 1.0
            d["hsel_f"] = hs
            d["hsel_bf"] = hs.astype(bfloat16)
            d["i_wqkv"] = np.asarray(inp["L1_ffn_wqkv"], np.float32)
            d["i_qs"] = np.asarray(inp["L1_ffn_q_scale"], np.float32)
            d["i_ks"] = np.asarray(inp["L1_ffn_k_scale"], np.float32)
            d["i_wo"] = np.asarray(inp["L1_ffn_wo"], np.float32)
            d["i_lng"] = np.asarray(inp["L1_ffn_ln_g"], np.float32)
            d["i_lnb"] = np.asarray(inp["L1_ffn_ln_b"], np.float32)
            d["i_w1"] = np.ascontiguousarray(np.asarray(inp["L1_ffn_w1"], np.float32)[:, :, w1perm])
            d["i_b1"] = np.ascontiguousarray(np.asarray(inp["L1_ffn_b1"], np.float32)[:, w1perm])
            d["i_w2"] = np.asarray(inp["L1_ffn_w2"], np.float32)
            d["i_b2"] = np.asarray(inp["L1_ffn_b2"], np.float32)
            d["halo_valid"] = np.array([float(h)], dtype=np.float32)
        per_core.append(d)
    return per_core


class RSem:
    __slots__ = ("h", "base", "ctr", "role")

    def __init__(self, h, role):
        self.h, self.base, self.ctr, self.role = h, 0, 0, role

    def tick(self, k=1):
        self.ctr += k
        return (self.h, k)


class Prog:
    def __init__(self):
        self.nc = bass.Bass("TRN2", target_bir_lowering=False)
        self.ap = {}
        self.est = contextlib.ExitStack()
        self._barr = None
        self._barr_n = 0
        self.rsems = {}

    ROLE_ALIAS = {
        "in_xbA": "fmb_ldA", "in_xbB": "fmb_ldB", "in_x3": "ga_dcc",
        "in_xv": "fmb_vx",
        "f4_w1lA": "p2w_ldA", "f4_w1lB": "p2w_ldB",
        "f4_w2lA": "p2w_ld2A", "f4_w2lB": "p2w_ld2B",
        "f4_ld": "p2_wbA", "f4_sh1": "p2_wbB", "f4_hrb": "ga_din1",
        "f4_dshA": "ga_dw", "f4_dshB": "p2w_dw",
        "p4_tld": "ga_ldA", "p4_tl2": "ga_ldB",
    }

    def rs(self, role):
        role = self.ROLE_ALIAS.get(role, role)
        r = self.rsems.get(role)
        if r is None:
            r = RSem(self.est.enter_context(self.nc.semaphore(role)), role)
            self.rsems[role] = r
        r.base = r.ctr
        return r

    def din(self, name, shape, dt=F32):
        self.ap[name] = self.nc.dram_tensor(name, shape, dt, kind="ExternalInput").ap()
        return self.ap[name]

    def dint(self, name, shape, dt=F32):
        self.ap[name] = self.nc.dram_tensor(name, shape, dt, kind="Internal").ap()
        return self.ap[name]

    def sb(self, name, shape, dt, scope=None):
        return (scope or self.est).enter_context(self.nc.sbuf_tensor(name, shape, dt))

    def sem(self, name):
        return self.est.enter_context(self.nc.semaphore(name))

    def barrier(self, blk):
        if self._barr is None:
            self._barr = {e: self.est.enter_context(self.nc.semaphore(f"barr_{e}"))
                          for e in ("sync", "tensor", "vector", "scalar", "gpsimd")}
        self._barr_n += 1
        k = self._barr_n
        sems = self._barr

        def arm(name):
            def emit(eng):
                eng.wait_ge(sems[name], k - 1).then_inc(sems[name], 1)
                for other in sems:
                    if other != name:
                        eng.wait_ge(sems[other], k)
            return emit

        blk.sync(arm("sync"))
        blk.tensor(arm("tensor"))
        blk.vector(arm("vector"))
        blk.scalar(arm("scalar"))
        blk.gpsimd(arm("gpsimd"))


def build_program():
    p = Prog()
    nc = p.nc

    p.din("x_own", [HALF, D])
    p.din("xb", [S, D])
    p.din("l0_wq", [D, 512]); p.din("l0_wk", [D, 512]); p.din("l0_wv", [D, 512])
    p.din("l0_wo", [D, D])
    p.din("ident", [128, 128])
    p.din("ident_bf", [128, 128], BF16)
    p.din("tri_pt", [128, 128], BF16)
    p.din("tril_pt", [128, 128], BF16)
    p.din("own_half", [1], mybir.dt.int32)
    if N_PHASES >= 2:
        p.din("l0_w1", [D, 4096]); p.din("l0_w2", [4096, D])
        p.dint("w1qT_d", [D, 4096], BF16)
        p.dint("w2qT_d", [4096, D], BF16)
    if N_PHASES >= 3:
        p.din("l1_wq", [D, 512]); p.din("l1_wk", [D, 512]); p.din("l1_wv", [D, 512])
        p.din("l1_wp", [D, D])
    if N_PHASES >= 4:
        PW = HALF + 513
        p.din("rot_aq", [128, HALF], BF16); p.din("rot_bq", [128, HALF], BF16)
        p.din("rot_ak", [128, PW], BF16); p.din("rot_bk", [128, PW], BF16)
        p.din("hv128", [128, 1])
        p.din("rswap", [128, 128], BF16)
        p.din("hsel_f", [2, 128], F32)
        p.din("tri_f", [128, 128]); p.din("tril_f", [128, 128])
        p.din("hsel_bf", [2, 128], BF16)
        p.din("i_wqkv", [2, D, 3 * INNER])
        p.din("i_qs", [2, IDH]); p.din("i_ks", [2, IDH])
        p.din("i_wo", [2, INNER, D])
        p.din("i_lng", [2, D]); p.din("i_lnb", [2, D])
        p.din("i_w1", [2, D, 2 * HID]); p.din("i_b1", [2, 2 * HID])
        p.din("i_w2", [2, HID, D]); p.din("i_b2", [2, D])
        p.din("halo_valid", [1])
        p.dint("x3_d", [HALF, D])

    out_ap = nc.dram_tensor("out", [HALF, D], F32, kind="ExternalOutput").ap()

    for tag, nph in (("p1", 1), ("p3", 3)):
        if N_PHASES >= nph:
            for hp in range(NHP):
                for half in range(2):
                    p.dint(f"{tag}cc_in_{hp}_{half}", [128, HALF], BF16)
                    p.dint(f"{tag}cc_out_{hp}_{half}", [2, 128, HALF], BF16)
    if N_PHASES >= 3:
        p.dint("x2cc_in", [D, HALF], BF16)
        p.dint("x2cc_out", [2, D, HALF], BF16)
    if N_PHASES >= 4:
        p.dint("shcc_in", [D, 513], BF16)
        p.dint("shcc_out", [2, D, 513], BF16)
        p.dint("sh1_in", [1, D], F32)
        p.dint("sh1_out", [2, 1, D], F32)

    with p.est:
        _emit_all(p, out_ap)
    return nc


def _emit_all(p, out_ap):
    nc = p.nc
    x_res = p.sb("x_res", [128, NT, D], F32)
    cst = p.sb("cst", [128, 8], F32)
    identt = p.sb("identt", [128, 128], F32)
    ident_bf = p.sb("ident_bf_s", [128, 128], BF16)
    tri_pt = p.sb("tri_pt_s", [128, 128], BF16)
    tril_pt = p.sb("tril_pt_s", [128, 128], BF16)
    if N_PHASES >= 4:
        p.hv = p.sb("hv_s", [128, 1], F32)
        p.rsw = p.sb("rsw_s", [128, 128], BF16)
        p.trif = p.sb("trif_s", [128, 128], F32)
        p.trilf = p.sb("trilf_s", [128, 128], F32)

    with nc.Block() as blk:
        s_ld0 = p.rs("s_ld0")
        s_ldx = p.rs("s_ldx")
        s_cst = p.rs("s_cst")
        p.s_ldx = s_ldx
        p.s_ld0 = s_ld0
        p.s_cst = s_cst
        if N_PHASES >= 4:
            s_hv0 = p.rs("s_hv0")
            p.s_hv0 = s_hv0
            s_rsw = p.rs("s_rsw")
            p.s_rsw = s_rsw

        @blk.sync
        def _(sync):
            sync.dma_start(out=identt[:], in_=p.ap["ident"][:]).then_inc(*(s_ld0).tick(16))
            sync.dma_start(out=ident_bf[:], in_=p.ap["ident_bf"][:]).then_inc(*(s_ld0).tick(16))
            sync.dma_start(out=tri_pt[:], in_=p.ap["tri_pt"][:]).then_inc(*(s_ld0).tick(16))
            sync.dma_start(out=tril_pt[:], in_=p.ap["tril_pt"][:]).then_inc(*(s_ld0).tick(16))
            xo = p.ap["x_own"].rearrange("(t p) d -> p t d", p=128)
            for tt in range(NT):
                sync.dma_start(out=x_res[:, tt, :], in_=xo[:, tt, :]).then_inc(*(s_ldx).tick(16))
            if N_PHASES >= 4:
                sync.dma_start(out=p.hv[:], in_=p.ap["hv128"][:]).then_inc(*(s_hv0).tick(16))
                sync.dma_start(out=p.rsw[:], in_=p.ap["rswap"][:]).then_inc(*(s_rsw).tick(16))
                sync.dma_start(out=p.trif[:], in_=p.ap["tri_f"][:]).then_inc(*(s_rsw).tick(16))
                sync.dma_start(out=p.trilf[:], in_=p.ap["tril_f"][:]).then_inc(*(s_rsw).tick(16))

        @blk.gpsimd
        def _(gp):
            gp.memset(cst[:, 0:1], -BIGB)
            gp.memset(cst[:, 1:2], 1e-5).then_inc(*(s_cst).tick(1))

        with contextlib.ExitStack() as p1scope:
            x_fm = p.sb("p1_x_fm", [128, DC, S], BF16, p1scope)
            p.s_fmdone = _emit_fm_build(p, blk, x_fm, identt, "xb", NTB,
                                        ident_ld=(s_ld0, 64))
            p.barrier(blk)
            _emit_global_attn(p, blk, tag="p1", x_fm=x_fm, x_res=x_res,
                              identt=ident_bf, tri_pt=tri_pt, cst=cst,
                              wq="l0_wq", wk="l0_wk", wv="l0_wv", wo="l0_wo",
                              score_scale=0.125, proj_scale=1.0)
            p.barrier(blk)
        s_lnv = _emit_ln(p, blk, x_res, cst, tag="ln1")
        p.barrier(blk)

        if N_PHASES >= 2:
            _emit_bitffn(p, blk, x_res, ident_bf, cst)
            p.barrier(blk)
            s_lnv = _emit_ln(p, blk, x_res, cst, tag="ln2")
            p.barrier(blk)

        if N_PHASES >= 3:
            with contextlib.ExitStack() as p3scope:
                x_fm3 = p.sb("p3_x_fm", [128, DC, S], BF16, p3scope)
                _emit_x2gather(p, blk, x_res, x_fm3, identt)
                p.barrier(blk)
                _emit_global_attn(p, blk, tag="p3", x_fm=x_fm3, x_res=x_res,
                                  identt=ident_bf, tri_pt=tri_pt, cst=cst,
                                  wq="l1_wq", wk="l1_wk", wv="l1_wv",
                                  wo="l1_wp",
                                  score_scale=1.0 / 8192.0,
                                  proj_scale=1.0 / 1024.0)
                p.barrier(blk)
            s_lnv = _emit_ln(p, blk, x_res, cst, tag="ln3")
            p.barrier(blk)

        if N_PHASES >= 4:
            _emit_inner(p, blk, x_res, identt, ident_bf, tri_pt, tril_pt, cst)
            p.barrier(blk)
            s_lnv = _emit_ln(p, blk, x_res, cst, tag="ln4")
            p.barrier(blk)

        s_st = p.rs("s_st")

        @blk.sync
        def _(sync):
            sync.wait_ge((s_lnv).h, (s_lnv).base + (9 * NT))
            xo = out_ap.rearrange("(t p) d -> p t d", p=128)
            for tt in range(NT):
                sync.dma_start(out=xo[:, tt, :], in_=x_res[:, tt, :]).then_inc(*(s_st).tick(16))
            sync.wait_ge((s_st).h, (s_st).base + (16 * NT))


def _emit_fm_build(p, blk, dst_fm, identt, src_name, ntiles, ident_ld):
    nc = p.nc
    s_ld = [p.rs("fmb_ldA"), p.rs("fmb_ldB")]
    s_tx = p.rs("fmb_tx")
    s_vx = p.rs("fmb_vx")
    with contextlib.ExitStack() as scope:
        xstage = [p.sb(f"fmb_{src_name}_st{i}", [128, D], F32, scope) for i in range(2)]
        ps_tp = [scope.enter_context(nc.psum_tensor(f"fmb_{src_name}_ps{i}", [128, 1024], F32))
                 for i in range(2)]

        @blk.sync
        def _(sync):
            src = p.ap[src_name].rearrange("(t p) d -> p t d", p=128)
            for tt in range(ntiles):
                if tt >= 2:
                    sync.wait_ge((s_tx).h, (s_tx).base + (tt - 1))
                sync.dma_start(out=xstage[tt % 2][:], in_=src[:, tt, :]).then_inc(*(s_ld[tt % 2]).tick(16))

        @blk.tensor
        def _(tensor):
            tensor.wait_ge((ident_ld[0]).h, (ident_ld[0]).base + (ident_ld[1]))
            for tt in range(ntiles):
                tensor.wait_ge((s_ld[tt % 2]).h, (s_ld[tt % 2]).base + (16 * (tt // 2 + 1)))
                if tt >= 2:
                    tensor.wait_ge((s_vx).h, (s_vx).base + (tt - 1))
                pst = ps_tp[tt % 2]
                for c in range(DC):
                    ins = nc.tensor.transpose(
                        pst[:, 128 * c:128 * c + 128],
                        xstage[tt % 2][:, 128 * c:128 * c + 128], identt[:])
                ins.then_inc(*(s_tx).tick(1))

        @blk.vector
        def _(vector):
            for tt in range(ntiles):
                vector.wait_ge((s_tx).h, (s_tx).base + (tt + 1))
                src = ps_tp[tt % 2][:].rearrange("p (c k) -> p c k", c=DC)
                vector.tensor_copy(dst_fm[:, :, 128 * tt:128 * tt + 128], src).then_inc(*(s_vx).tick(1))

        s_done = p.rs("fmb_done")

        @blk.tensor
        def _(tensor):
            tensor.wait_ge((s_vx).h, (s_vx).base + ntiles).then_inc(*(s_done).tick(1))
    return s_done


def _emit_ln(p, blk, x_res, cst, tag, ntiles=NT):
    nc = p.nc
    lscope = contextlib.ExitStack()
    sq_t = p.sb(f"{tag}_sq", [128, D], F32, lscope)
    st_t = p.sb(f"{tag}_st", [128, 8], F32, lscope)
    s_v = p.rs("ga_v")
    s_s = p.rs("ga_s")

    @blk.vector
    def _(vector):
        rr = p.rs("fmb_vx")
        vector.wait_ge(rr.h, rr.base + 0)
        n = [0]

        def step(ins):
            n[0] += 1
            ins.then_inc(*(s_v).tick(1))
            vector.wait_ge((s_v).h, (s_v).base + (n[0]))

        for tt in range(ntiles):
            xt = x_res[:, tt, :]
            step(vector.tensor_tensor(sq_t[:], xt, xt, ALU.mult))
            step(vector.reduce_sum(st_t[:, 0:1], xt, AX.X))
            step(vector.tensor_scalar(st_t[:, 2:3], st_t[:, 0:1], 1.0 / D, None, ALU.mult))
            step(vector.reduce_sum(st_t[:, 1:2], sq_t[:], AX.X))
            step(vector.tensor_scalar(st_t[:, 3:4], st_t[:, 1:2], 1.0 / D, None, ALU.mult))
            step(vector.tensor_tensor(st_t[:, 4:5], st_t[:, 2:3], st_t[:, 2:3], ALU.mult))
            step(vector.tensor_tensor(st_t[:, 5:6], st_t[:, 3:4], st_t[:, 4:5], ALU.subtract))
            vector.wait_ge((s_s).h, (s_s).base + (tt + 1))
            step(vector.reciprocal(st_t[:, 7:8], st_t[:, 6:7]))
            step(vector.tensor_scalar(xt, xt, st_t[:, 2:3], st_t[:, 7:8],
                                      ALU.subtract, ALU.mult))

    @blk.scalar
    def _(scalar):
        scalar.wait_ge((p.s_cst).h, (p.s_cst).base + (1))
        for tt in range(ntiles):
            scalar.wait_ge((s_v).h, (s_v).base + (9 * tt + 7))
            scalar.activation(out=st_t[:, 6:7], in_=st_t[:, 5:6], func=AF.Sqrt,
                              bias=cst[:, 1:2]).then_inc(*(s_s).tick(1))
    lscope.close()
    return s_v


def _emit_global_attn(p, blk, tag, x_fm, x_res, identt, tri_pt, cst,
                      wq, wk, wv, wo, score_scale, proj_scale,
                      w_pre_bf16=False):
    nc = p.nc

    s_wld = p.rs("ga_wld")
    s_vw = p.rs("ga_vw")
    s_tq = p.rs("ga_tq")
    s_vq = p.rs("ga_vq")
    s_tv = p.rs("ga_tv")
    s_vv = p.rs("ga_vv")
    s_ts = p.rs("ga_ts")
    s_se = p.rs("ga_se")
    s_vm = p.rs("ga_vm")
    s_tav = p.rs("ga_tav")
    s_vo = p.rs("ga_vo")
    s_gl = p.rs("ga_gl")
    s_vrc = p.rs("ga_vrc")
    s_vn = p.rs("ga_vn")
    s_dcc = p.rs("ga_dcc")
    s_gc = p.rs("ga_gc")
    s_drb = p.rs("ga_drb")
    s_tw = p.rs("ga_tw")
    s_vwo = p.rs("ga_vwo")

    bscope = contextlib.ExitStack()
    wstage = p.sb(f"{tag}_wstage", [128, DC, 128], F32, bscope)
    w_bf = [p.sb(f"{tag}_wbf{i}", [128, DC, 128], BF16, bscope) for i in range(3)]
    q_fm = p.sb(f"{tag}_qfm", [128, S], BF16, bscope)
    k_fm = p.sb(f"{tag}_kfm", [128, S], BF16, bscope)
    v_bf = p.sb(f"{tag}_vbf", [128, S], BF16, bscope)
    v_aug = p.sb(f"{tag}_vaug", [128, NTB, 130], BF16, bscope)
    pt_buf = [p.sb(f"{tag}_pt{i}", [128, 1536], BF16, bscope) for i in range(2)]
    ot_un = [p.sb(f"{tag}_otun{i}", [65, S], BF16, bscope) for i in range(2)]
    l2_t = [p.sb(f"{tag}_l2{i}", [1, HALF], BF16, bscope) for i in range(2)]
    rlb = p.sb(f"{tag}_rlb", [64, HALF], BF16, bscope)
    ones64 = p.sb(f"{tag}_ones64", [1, 64], BF16, bscope)
    s_tb = p.rs("ga_tb")
    s_vb = p.rs("ga_vb")

    def qb_supers(qb):
        slots = [(kt, h) for kt in range(4 * qb + 4) for h in range(2)]
        return [slots[i:i + 3] for i in range(0, len(slots), 3)]

    SUP = []
    for hp in range(NHP):
        for qb in range(8):
            sups = qb_supers(qb)
            for i, sl in enumerate(sups):
                diag = [(si, kt, h) for si, (kt, h) in enumerate(sl) if kt >= 4 * qb]
                SUP.append(dict(hp=hp, qb=qb, slots=sl, diag=diag, first=(i == 0)))
    n_sup = len(SUP)
    n_sup_hp = n_sup // NHP
    cum_diag = np.cumsum([len(s["diag"]) for s in SUP]).tolist()

    for hp in range(NHP):
        with contextlib.ExitStack() as scope:
            ps_qkv = scope.enter_context(nc.psum_tensor(f"{tag}_psqkv{hp}", [128, 1024], F32))
            ps_vt = [scope.enter_context(nc.psum_tensor(f"{tag}_psvt{hp}{i}", [128, 512], F32))
                     for i in range(2)]

            @blk.sync
            def _(sync, hp=hp):
                if hp == 0:
                    sync.wait_ge((p.s_fmdone).h, (p.s_fmdone).base + (1))
                for wi, wn in enumerate((wq, wk, wv)):
                    idx = hp * 3 + wi
                    if idx > 0:
                        sync.wait_ge((s_vw).h, (s_vw).base + (idx))
                    src = p.ap[wn].rearrange("(c p) o -> p c o", p=128)[:, :, 128 * hp:128 * hp + 128]
                    for c in range(DC):
                        sync.dma_start(out=wstage[:, c, :], in_=src[:, c, :]).then_inc(*(s_wld).tick(16))

            @blk.vector
            def _(vector, hp=hp):
                if hp == 0:
                    vector.wait_ge((p.s_ld0).h, (p.s_ld0).base + (64))
                    va = v_aug[:].rearrange("p t (h c) -> p t h c", c=65)
                    vector.memset(va[:, :, :, 64:65], 1.0)
                    vector.memset(ones64[:], 1.0)
                for wi in range(3):
                    idx = hp * 3 + wi
                    vector.wait_ge((s_wld).h, (s_wld).base + (128 * (idx + 1)))
                    vector.tensor_copy(w_bf[wi][:], wstage[:]).then_inc(*(s_vw).tick(1))
                for wi in range(3):
                    for qb in range(8):
                        e = (hp * 3 + wi) * 8 + qb
                        vector.wait_ge((s_tq).h, (s_tq).base + (e + 1))
                        src = ps_qkv[:, 512 * (qb % 2):512 * (qb % 2) + 512]
                        dst = (q_fm, k_fm, v_bf)[wi][:, 512 * qb:512 * qb + 512]
                        vector.tensor_copy(dst, src).then_inc(*(s_vq).tick(1))
                for g in range(8):
                    vector.wait_ge((s_tv).h, (s_tv).base + (hp * 8 + g + 1))
                    src = ps_vt[g % 2][:].rearrange("p (t h c) -> p t h c", t=4, h=2)
                    dst = v_aug[:, 4 * g:4 * g + 4, :].rearrange(
                        "p t (h c) -> p t h c", c=65)[:, :, :, 0:64]
                    vector.tensor_copy(dst, src).then_inc(*(s_vv).tick(1))

            @blk.tensor
            def _(tensor, hp=hp):
                if hp > 0:
                    tensor.wait_ge((s_se).h, (s_se).base + (hp * n_sup_hp))
                    tensor.wait_ge((s_vo).h, (s_vo).base + (16 * hp))
                    tensor.wait_ge((s_vb).h, (s_vb).base + (16 * hp))
                for wi in range(3):
                    tensor.wait_ge((s_vw).h, (s_vw).base + (hp * 3 + wi + 1))
                    for qb in range(8):
                        e = (hp * 3 + wi) * 8 + qb
                        if e >= 2:
                            tensor.wait_ge((s_vq).h, (s_vq).base + (e - 1))
                        dst = ps_qkv[:, 512 * (qb % 2):512 * (qb % 2) + 512]
                        for c in range(DC):
                            ins = nc.tensor.matmul(
                                dst, w_bf[wi][:, c, :],
                                x_fm[:, c, 512 * qb:512 * qb + 512],
                                start=(c == 0), stop=(c == DC - 1))
                        ins.then_inc(*(s_tq).tick(1))
                tensor.wait_ge((s_vq).h, (s_vq).base + ((hp * 3 + 3) * 8))
                for g in range(8):
                    if g >= 2:
                        tensor.wait_ge((s_vv).h, (s_vv).base + (hp * 8 + g - 1))
                    for t4 in range(4):
                        tt = 4 * g + t4
                        ins = nc.tensor.matmul(
                            ps_vt[g % 2][:, 128 * t4:128 * t4 + 128],
                            v_bf[:, 128 * tt:128 * tt + 128], identt[:],
                            start=True, stop=True, skip_group_check=True)
                    ins.then_inc(*(s_tv).tick(1))

        with contextlib.ExitStack() as scope:
            ps_s = [scope.enter_context(nc.psum_tensor(f"{tag}_pss{hp}{i}", [128, 1536], F32))
                    for i in range(2)]
            ps_o = [scope.enter_context(nc.psum_tensor(f"{tag}_pso{hp}{i}", [65, 512], F32))
                    for i in range(2)]
            sup_lo = hp * n_sup_hp
            sup_hi = (hp + 1) * n_sup_hp

            @blk.tensor
            def _(tensor, hp=hp, sup_lo=sup_lo, sup_hi=sup_hi):
                tensor.wait_ge((s_vv).h, (s_vv).base + (8 * (hp + 1)))

                def emit_S(sg):
                    sup = SUP[sg]
                    qb = sup["qb"]
                    if sg - sup_lo >= 2:
                        tensor.wait_ge((s_se).h, (s_se).base + (sg - 1))
                    for si, (kt, h) in enumerate(sup["slots"]):
                        ins = nc.tensor.matmul(
                            ps_s[sg % 2][:, 512 * si:512 * si + 512],
                            k_fm[64 * h:64 * h + 64, 128 * kt:128 * kt + 128],
                            q_fm[64 * h:64 * h + 64, 512 * qb:512 * qb + 512],
                            start=True, stop=True, tile_position=(64 * h, 0),
                            skip_group_check=True)
                    ins.then_inc(*(s_ts).tick(1))

                def emit_AV(sg):
                    sup = SUP[sg]
                    qb = sup["qb"]
                    tensor.wait_ge((s_se).h, (s_se).base + (sg + 1))
                    tensor.wait_ge((s_vv).h, (s_vv).base + (hp * 8 + sup["slots"][-1][0] // 4 + 1))
                    if cum_diag[sg] > 0:
                        tensor.wait_ge((s_vm).h, (s_vm).base + (cum_diag[sg]))
                    if sup["first"]:
                        tensor.wait_ge((s_vo).h, (s_vo).base + (2 * (hp * 8 + qb)))
                    for si, (kt, h) in enumerate(sup["slots"]):
                        j = kt - 4 * qb
                        off = 128 * j if j > 0 else 0
                        ins = nc.tensor.matmul(
                            ps_o[h][:, off:512],
                            v_aug[:, kt, 65 * h:65 * h + 65],
                            pt_buf[sg % 2][:, 512 * si + off:512 * si + 512],
                            start=(kt == 0), stop=(kt == 4 * qb + 3),
                            skip_group_check=True)
                    ins.then_inc(*(s_tav).tick(1))

                emit_S(sup_lo)
                for sg in range(sup_lo, sup_hi - 1):
                    emit_S(sg + 1)
                    emit_AV(sg)
                emit_AV(sup_hi - 1)
                for half in range(2):
                    tensor.wait_ge((s_vrc).h, (s_vrc).base + (2 * hp + half + 1))
                    for head in range(2):
                        for c4 in range(4):
                            idx = 16 * hp + 8 * half + 4 * head + c4
                            if idx >= 3:
                                tensor.wait_ge((s_vb).h, (s_vb).base + (idx - 2))
                            nc.tensor.matmul(
                                ps_s[0][0:64, 512 * (idx % 3):512 * (idx % 3) + 512],
                                ones64[:], l2_t[head][:, 512 * c4:512 * c4 + 512],
                                start=True, stop=True,
                                skip_group_check=True).then_inc(*(s_tb).tick(1))

            @blk.scalar
            def _(scalar, hp=hp, sup_lo=sup_lo, sup_hi=sup_hi):
                if hp == 0:
                    scalar.wait_ge((p.s_cst).h, (p.s_cst).base + (1))
                for sg in range(sup_lo, sup_hi):
                    n = len(SUP[sg]["slots"])
                    scalar.wait_ge((s_ts).h, (s_ts).base + (sg + 1))
                    if sg - sup_lo >= 2:
                        scalar.wait_ge((s_tav).h, (s_tav).base + (sg - 1))
                    scalar.activation(
                        out=pt_buf[sg % 2][:, 0:512 * n],
                        in_=ps_s[sg % 2][:, 0:512 * n],
                        func=AF.Exp, bias=cst[:, 0:1], scale=score_scale,
                    ).then_inc(*(s_se).tick(1))

            @blk.vector
            def _(vector, hp=hp, sup_lo=sup_lo, sup_hi=sup_hi):
                if hp > 0:
                    vector.wait_ge((s_gl).h, (s_gl).base + (64 * hp))
                    vector.wait_ge((s_dcc).h, (s_dcc).base + (64 * hp))
                for sg in range(sup_lo, sup_hi):
                    sup = SUP[sg]
                    if sup["diag"]:
                        vector.wait_ge((s_se).h, (s_se).base + (sg + 1))
                        for (si, kt, h) in sup["diag"]:
                            j = kt - 4 * sup["qb"]
                            sl = pt_buf[sg % 2][:, 512 * si + 128 * j:512 * si + 128 * j + 128]
                            vector.tensor_tensor(sl, sl, tri_pt[:], ALU.mult).then_inc(*(s_vm).tick(1))
                    if (sg + 1 == sup_hi) or SUP[sg + 1]["first"]:
                        qb = sup["qb"]
                        vector.wait_ge((s_tav).h, (s_tav).base + (sg + 1))
                        for head in range(2):
                            vector.tensor_copy(
                                ot_un[head][0:65, 512 * qb:512 * qb + 512],
                                ps_o[head][0:65, :]).then_inc(*(s_vo).tick(1))
                for half in range(2):
                    vector.wait_ge((s_gl).h, (s_gl).base + (64 * hp + 32 * (half + 1)))
                    with nc.allow_low_precision(reason="softmax denom tolerates bf16"):
                        vector.reciprocal(l2_t[0][:], l2_t[0][:])
                        vector.reciprocal(l2_t[1][:], l2_t[1][:]).then_inc(*(s_vrc).tick(1))
                    for head in range(2):
                        g = 4 * hp + 2 * half + head
                        if g > 0:
                            vector.wait_ge((s_vn).h, (s_vn).base + (g))
                        for c4 in range(4):
                            idx = 16 * hp + 8 * half + 4 * head + c4
                            vector.wait_ge((s_tb).h, (s_tb).base + (idx + 1))
                            vector.tensor_copy(
                                rlb[:, 512 * c4:512 * c4 + 512],
                                ps_s[0][0:64, 512 * (idx % 3):512 * (idx % 3) + 512],
                            ).then_inc(*(s_vb).tick(1))
                        vector.wait_ge((s_vb).h, (s_vb).base + (16 * hp + 8 * half + 4 * head + 4))
                        sl = ot_un[head][0:64, HALF * half:HALF * half + HALF]
                        vector.tensor_tensor(sl, sl, rlb[:, :], ALU.mult).then_inc(*(s_vn).tick(1))

            @blk.sync
            def _(sync, hp=hp):
                sync.wait_ge((s_vo).h, (s_vo).base + (16 * (hp + 1)))
                for half in range(2):
                    if hp + half > 0:
                        sync.wait_ge((s_tb).h, (s_tb).base + (16 * hp + 8 * half))
                    for head in range(2):
                        sync.dma_start(
                            out=l2_t[head][:],
                            in_=ot_un[head][64:65, HALF * half:HALF * half + HALF],
                        ).then_inc(*(s_gl).tick(16))
                sync.wait_ge((s_vn).h, (s_vn).base + (4 * (hp + 1)))
                for half in range(2):
                    ccin = p.ap[f"{tag}cc_in_{hp}_{half}"]
                    for head in range(2):
                        sync.dma_start(
                            out=ccin[64 * head:64 * head + 64, :],
                            in_=ot_un[head][0:64, HALF * half:HALF * half + HALF],
                        ).then_inc(*(s_dcc).tick(16))

            @blk.gpsimd
            def _(gp, hp=hp):
                gp.wait_ge((s_dcc).h, (s_dcc).base + (64 * (hp + 1)))
                for half in range(2):
                    gp.collective_compute(
                        "AllGather", ALU.bypass, replica_groups=RG,
                        ins=[p.ap[f"{tag}cc_in_{hp}_{half}"][:]],
                        outs=[p.ap[f"{tag}cc_out_{hp}_{half}"][:]],
                    ).then_inc(*(s_gc).tick(1))

    bscope.close()
    with contextlib.ExitStack() as scope:
        ps_wo = scope.enter_context(nc.psum_tensor(f"{tag}_pswo", [128, 1024], F32))
        wstage_c = p.sb(f"{tag}_wstc", [128, DC, 128], F32, scope)
        wo_bf = p.sb(f"{tag}_wobf", [128, DC, 512], BF16, scope)

        @blk.sync
        def _(sync):
            sync.wait_ge((s_tq).h, (s_tq).base + (12 * 8))
            for hp in range(NHP):
                sync.wait_ge((s_gc).h, (s_gc).base + (2 * (hp + 1)))
                for arm in sync.switch_core_id(n=128):
                    my_half = arm.logical % 2
                    cout = p.ap[f"{tag}cc_out_{hp}_{my_half}"]
                    for rank in range(2):
                        sync.dma_start(
                            out=x_fm[:, 4 * rank + hp, 0:HALF],
                            in_=cout[rank],
                        ).then_inc(s_drb.h, 16)
                s_drb.ctr += 32
            sync.wait_ge((s_tav).h, (s_tav).base + (n_sup))
            sync.wait_ge((s_se).h, (s_se).base + (n_sup))
            sync.wait_ge((s_vo).h, (s_vo).base + (64))
            sync.wait_ge((s_vb).h, (s_vb).base + (64))
            sync.wait_ge((s_vn).h, (s_vn).base + (16))
            sync.wait_ge((s_gl).h, (s_gl).base + (64 * NHP))
            sync.wait_ge((s_dcc).h, (s_dcc).base + (64 * NHP))
            for qq in range(8):
                if w_pre_bf16 and qq % 2 == 0:
                    oh = qq // 4
                    srcw = p.ap[wo].rearrange("(c p) o -> p c o", p=128)[:, :, 512 * oh:512 * oh + 512]
                    if oh == 1:
                        sync.wait_ge((s_tw).h, (s_tw).base + (NT))
                    for c in range(DC):
                        sync.dma_start(out=wo_bf[:, c, :], in_=srcw[:, c, :]).then_inc(*(s_wld).tick(16))
                elif not w_pre_bf16:
                    sync.wait_ge((s_vw).h, (s_vw).base + (12 + qq))
                    srcw = p.ap[wo].rearrange("(c p) o -> p c o", p=128)[:, :, 128 * qq:128 * qq + 128]
                    for c in range(DC):
                        sync.dma_start(out=wstage_c[:, c, :], in_=srcw[:, c, :]).then_inc(*(s_wld).tick(16))


        @blk.tensor
        def _(tensor):
            tensor.wait_ge((s_drb).h, (s_drb).base + (16 * 8))
            for oh in range(2):
                if not w_pre_bf16:
                    tensor.wait_ge((s_vw).h, (s_vw).base + (12 + 4 * (oh + 1)))
                else:
                    tensor.wait_ge((s_wld).h, (s_wld).base + (128 * (12 + oh + 1)))
                for tt in range(NT):
                    e = NT * oh + tt
                    if e >= 2:
                        tensor.wait_ge((s_vwo).h, (s_vwo).base + (e - 1))
                    dst = ps_wo[:, 512 * (e % 2):512 * (e % 2) + 512]
                    for ci in range(DC):
                        ins = nc.tensor.matmul(
                            dst, x_fm[:, ci, 128 * tt:128 * tt + 128],
                            wo_bf[:, ci, 0:512],
                            start=(ci == 0), stop=(ci == DC - 1))
                    ins.then_inc(*(s_tw).tick(1))

        @blk.vector
        def _(vector):
            vector.wait_ge((p.s_ldx).h, (p.s_ldx).base + (256))
            for oh in range(2):
                if not w_pre_bf16:
                    for q4 in range(4):
                        qq = 4 * oh + q4
                        vector.wait_ge((s_wld).h, (s_wld).base + (128 * (12 + qq + 1)))
                        if oh == 1 and q4 == 0:
                            vector.wait_ge((s_tw).h, (s_tw).base + (NT))
                        vector.tensor_copy(
                            wo_bf[:, :, 128 * q4:128 * q4 + 128],
                            wstage_c[:]).then_inc(*(s_vw).tick(1))
                for tt in range(NT):
                    e = NT * oh + tt
                    vector.wait_ge((s_tw).h, (s_tw).base + (e + 1))
                    xt = x_res[:, tt, 512 * oh:512 * oh + 512]
                    vector.scalar_tensor_tensor(
                        xt, ps_wo[:, 512 * (e % 2):512 * (e % 2) + 512], proj_scale, xt,
                        ALU.mult, ALU.add).then_inc(*(s_vwo).tick(1))




def _emit_bitffn(p, blk, x_res, ident_bf, cst):
    nc = p.nc
    sc = contextlib.ExitStack()

    wstg = [p.sb("p2_wstg0", [128, 4096], F32, sc), p.sb("p2_wstg1", [128, 1024], F32, sc)]
    w1qT = p.sb("p2_w1qT", [128, DC, 4096], BF16, sc)
    tern = p.sb("p2_tern", [128, 4096], BF16, sc)
    tstg = p.sb("p2_tstg", [128, 512], BF16, sc)
    acc = p.sb("p2_acc", [128, 4], F32, sc)
    sbc = p.sb("p2_sbc", [128, 4], F32, sc)
    onesf = p.sb("p2_onesf", [128, 1], F32, sc)
    onesr = p.sb("p2_onesr", [1, 128], F32, sc)
    sval = p.sb("p2_sval", [1, 2], F32, sc)
    h_t = p.sb("p2_h", [128, 4096], F32, sc)
    xq_tm = p.sb("p2_xq", [128, D], BF16, sc)
    xq_fm = p.sb("p2_xqfm", [128, DC, 128], BF16, sc)
    hq_tm = p.sb("p2_hqtm", [128, 4096], BF16, sc)
    hq_fm = p.sb("p2_hqfm", [128, 32, 128], BF16, sc)
    w2b = [p.sb(f"p2_w2b{i}", [128, D], BF16, sc) for i in range(2)]
    stt = p.sb("p2_stt", [128, 8], F32, sc)

    ps_t = [sc.enter_context(nc.psum_tensor(f"p2_pst{i}", [128, 512], F32)) for i in range(2)]
    ps_m1 = [sc.enter_context(nc.psum_tensor(f"p2_psm1{i}", [128, 512], F32)) for i in range(2)]
    ps_m2 = [sc.enter_context(nc.psum_tensor(f"p2_psm2{i}", [128, 512], F32)) for i in range(2)]
    ps_s = sc.enter_context(nc.psum_tensor("p2_pss", [1, 128], F32))
    ps_b = sc.enter_context(nc.psum_tensor("p2_psb", [128, 128], F32))

    WCfg = [("l0_w1", 8, 4096, True), ("l0_w2", 32, 1024, False)]
    wprep = {}

    for wi, (wn, ntl, wf, res) in enumerate(WCfg):
        nblk = wf // 128
        ngrp = nblk // 4
        sldA = p.rs("p2w_ldA"); sldB = p.rs("p2w_ldB")
        svr = p.rs("p2w_vr")
        st1 = p.rs("p2w_t1")
        sv1 = p.rs("p2w_v1")
        sld2A = p.rs("p2w_ld2A"); sld2B = p.rs("p2w_ld2B")
        svq = p.rs("p2w_vq")
        s_tt = p.rs("p2w_tt")
        sve = p.rs("p2w_ve")
        sdw = p.rs("p2w_dw")

        wprep[wi] = dict(sve=sve, sdw=sdw, svq=svq, stt=s_tt,
                         sve_total=ntl * (wf // 512), nq=3 * ntl,
                         ntt=ntl * (wf // 128))

        @blk.sync
        def _(sync, wn=wn, ntl=ntl, wf=wf, res=res, sldA=sldA, sldB=sldB,
              svr=svr, sv1=sv1, svq=svq, sve=sve, sdw=sdw, nblk=nblk, ngrp=ngrp,
              sld2A=sld2A, sld2B=sld2B, wi=wi):
            srcw = p.ap[wn].rearrange("(t p) f -> p t f", p=128)
            nb = 2 if wf <= 1024 else 1
            if wi == 1:
                sync.wait_ge(wprep[0]["svq"].h, wprep[0]["svq"].base)
            for t in range(ntl):
                if t >= nb:
                    sync.wait_ge((svr).h, (svr).base + (2 * (t - nb + 1)))
                sync.dma_start(out=wstg[t % nb][:, 0:wf], in_=srcw[:, t, :]).then_inc(*((sldA, sldB)[t % nb]).tick(16))
            sync.wait_ge((sv1).h, (sv1).base + (4))
            for t in range(ntl):
                if t >= nb:
                    sync.wait_ge((svq).h, (svq).base + (3 * (t - nb + 1)))
                sync.dma_start(out=wstg[t % nb][:, 0:wf], in_=srcw[:, t, :]).then_inc(*((sld2A, sld2B)[t % nb]).tick(16))

        if not res:
            @blk.scalar
            def _(scalar, wn=wn, ntl=ntl, res=res, svq=svq, sdw=sdw):
                for t in range(ntl):
                    scalar.wait_ge((svq).h, (svq).base + (3 * t + 3))
                    if t >= 1:
                        scalar.wait_ge((sdw).h, (sdw).base + (16 * t))
                    dst = p.ap["w2qT_d"][128 * t:128 * t + 128, :]
                    scalar.dma_start(out=dst, in_=tern[:, 1024 * (t % 2):1024 * (t % 2) + 1024]
                                     ).then_inc(*(sdw).tick(16))

        @blk.vector
        def _(vector, wn=wn, ntl=ntl, wf=wf, res=res, sldA=sldA, sldB=sldB,
              svr=svr, st1=st1, sv1=sv1, svq=svq, s_tt=s_tt, sve=sve, sdw=sdw,
              nblk=nblk, ngrp=ngrp, sld2A=sld2A, sld2B=sld2B, wi=wi):
            if wi == 0:
                vector.memset(onesf[:], 1.0)
                vector.memset(onesr[:], 1.0)
            vector.memset(acc[:, 0:1], 0.0)
            nb = 2 if wf <= 1024 else 1
            for t in range(ntl):
                vector.wait_ge(((sldA, sldB)[t % nb]).h, ((sldA, sldB)[t % nb]).base + (16 * (t // nb + 1)))
                if t > 0:
                    vector.wait_ge((svr).h, (svr).base + (2 * t))
                vector.reduce_sum(acc[:, 2:3], wstg[t % nb][:, 0:wf], AX.X,
                                  apply_absolute_value=True).then_inc(*(svr).tick(1))
                vector.wait_ge((svr).h, (svr).base + (2 * t + 1))
                vector.tensor_tensor(acc[:, 0:1], acc[:, 0:1], acc[:, 2:3],
                                     ALU.add).then_inc(*(svr).tick(1))
            vector.wait_ge((st1).h, (st1).base + (1))
            vector.tensor_scalar(sval[:, wi:wi + 1], ps_s[:, 0:1],
                                 1.0 / (ntl * 128 * wf), None, ALU.mult).then_inc(*(sv1).tick(1))
            vector.wait_ge((st1).h, (st1).base + (2))
            vector.tensor_copy(sbc[:, 2 + wi:3 + wi], ps_b[:, 0:1]).then_inc(*(sv1).tick(1))
            vector.wait_ge((sv1).h, (sv1).base + (2))
            vector.reciprocal(sbc[:, wi:wi + 1], sbc[:, 2 + wi:3 + wi]).then_inc(*(sv1).tick(2))
            if wi == 1:
                vector.wait_ge(wprep[0]["svq"].h, wprep[0]["svq"].base)
            for t in range(ntl):
                vector.wait_ge(((sld2A, sld2B)[t % nb]).h, ((sld2A, sld2B)[t % nb]).base + (16 * (t // nb + 1)))
                if (not res) and t >= 2:
                    vector.wait_ge((sdw).h, (sdw).base + (16 * (t - 1)))
                wv = wstg[t % nb][:, 0:wf]
                vector.tensor_scalar(wv, wv, sbc[:, wi:wi + 1], MAGIC,
                                     ALU.mult, ALU.add).then_inc(*(svq).tick(1))
                vector.wait_ge((svq).h, (svq).base + (3 * t + 1))
                vector.tensor_scalar(wv, wv, MAGIC, -1.0,
                                     ALU.subtract, ALU.max).then_inc(*(svq).tick(1))
                vector.wait_ge((svq).h, (svq).base + (3 * t + 2))
                if res:
                    vector.tensor_scalar_min(w1qT[:, t, :], wv, 1.0).then_inc(*(svq).tick(1))
                else:
                    vector.tensor_scalar_min(tern[:, 1024 * (t % 2):1024 * (t % 2) + 1024],
                                             wv, 1.0).then_inc(*(svq).tick(1))

        @blk.tensor
        def _(tensor, wn=wn, ntl=ntl, wf=wf, res=res, svr=svr, st1=st1, sv1=sv1,
              svq=svq, s_tt=s_tt, sve=sve, nblk=nblk, ngrp=ngrp, wi=wi):
            if wi == 1:
                tensor.wait_ge((sv1).h, (sv1).base + (0))
            tensor.wait_ge((svr).h, (svr).base + (2 * ntl))
            nc.tensor.matmul(ps_s[:, 0:1], acc[:, 0:1], onesf[:],
                             start=True, stop=True, skip_group_check=True).then_inc(*(st1).tick(1))
            tensor.wait_ge((sv1).h, (sv1).base + (1))
            nc.tensor.matmul(ps_b[:, 0:1], onesr[:], sval[:, wi:wi + 1],
                             start=True, stop=True, skip_group_check=True).then_inc(*(st1).tick(1))

        wprep[wi]["svq_abs"] = svq.ctr
        wprep[wi]["sdw_base"] = sdw.base

    s_vx = p.rs("p2_vx")
    s_tx = p.rs("p2_tx")
    s_vxe = p.rs("p2_vxe")
    s_tm1 = p.rs("p2_tm1")
    s_sg = p.rs("p2_sg")
    s_vh = p.rs("p2_vh")
    s_th = p.rs("p2_th")
    s_vhq = p.rs("p2_vhq")
    s_tm2 = p.rs("p2_tm2")
    s_vr2 = p.rs("p2_vr2")
    s_wb = [p.rs("p2_wbA"), p.rs("p2_wbB")]
    V_X = 9

    @blk.sync
    def _(sync):
        for tt in range(NT):
            for i2 in range(32):
                k = 32 * tt + i2
                if tt == 0:
                    sync.wait_ge((wprep[1]["sdw"]).h,
                                 wprep[1]["sdw_base"] + 16 * (i2 + 1))
                if k >= 2:
                    sync.wait_ge((s_tm2).h, (s_tm2).base + (k - 1))
                src2 = p.ap["w2qT_d"][128 * i2:128 * i2 + 128, :]
                sync.dma_start(out=w2b[k % 2][:], in_=src2).then_inc(*(s_wb[k % 2]).tick(16))

    @blk.vector
    def _(vector):
        for tt in range(NT):
            xt = x_res[:, tt, :]
            n0 = V_X * tt

            def step(ins, k=[0]):
                ins.then_inc(*(s_vx).tick(1))
                k[0] += 1
                vector.wait_ge((s_vx).h, (s_vx).base + (n0 + k[0]))

            step(vector.reduce_max(stt[:, 0:1], xt, AX.X, apply_absolute_value=True))
            step(vector.tensor_scalar_max(stt[:, 0:1], stt[:, 0:1], 1e-5))
            step(vector.reciprocal(stt[:, 1:2], stt[:, 0:1]))
            step(vector.tensor_scalar(stt[:, 2:3], stt[:, 1:2], 127.0, None, ALU.mult))
            step(vector.tensor_scalar(stt[:, 3:4], stt[:, 0:1], 1.0 / 127.0, None, ALU.mult))
            step(vector.tensor_tensor(stt[:, 4:5], stt[:, 3:4], sbc[:, 2:3], ALU.mult))
            if tt > 0:
                vector.wait_ge((s_tx).h, (s_tx).base + (8 * tt))
                vector.wait_ge((s_sg).h, (s_sg).base + (8 * tt))
            qt = h_t[:, 0:D]
            step(vector.tensor_scalar(qt, xt, stt[:, 2:3], MAGIC, ALU.mult, ALU.add))
            step(vector.tensor_scalar(qt, qt, MAGIC, -128.0, ALU.subtract, ALU.max))
            step(vector.tensor_scalar_min(xq_tm[:], qt, 127.0))
            for g in range(2):
                vector.wait_ge((s_tx).h, (s_tx).base + (8 * tt + 4 * (g + 1)))
                srcp = ps_t[g % 2][:].rearrange("p (c k) -> p c k", c=4)
                vector.tensor_copy(xq_fm[:, 4 * g:4 * g + 4, :], srcp).then_inc(*(s_vxe).tick(1))
            m0 = 8 * tt

            def hstep(ins, k=[0]):
                ins.then_inc(*(s_vh).tick(1))
                k[0] += 1
                vector.wait_ge((s_vh).h, (s_vh).base + (m0 + k[0]))

            vector.wait_ge((s_sg).h, (s_sg).base + (8 * (tt + 1)))
            hstep(vector.reduce_max(stt[:, 5:6], h_t[:], AX.X, apply_absolute_value=True))
            hstep(vector.tensor_scalar_max(stt[:, 5:6], stt[:, 5:6], 1e-5))
            hstep(vector.reciprocal(stt[:, 6:7], stt[:, 5:6]))
            hstep(vector.tensor_scalar(stt[:, 7:8], stt[:, 6:7], 127.0, None, ALU.mult))
            if tt > 0:
                vector.wait_ge((s_th).h, (s_th).base + (32 * tt))
            hstep(vector.tensor_scalar(h_t[:], h_t[:], stt[:, 7:8], MAGIC, ALU.mult, ALU.add))
            hstep(vector.tensor_scalar(h_t[:], h_t[:], MAGIC, -128.0, ALU.subtract, ALU.max))
            hstep(vector.tensor_scalar_min(hq_tm[:], h_t[:], 127.0))
            hstep(vector.tensor_scalar(stt[:, 5:6], stt[:, 5:6], 1.0 / 127.0, None, ALU.mult))
            vector.wait_ge((s_vh).h, (s_vh).base + (m0 + 8))
            vector.tensor_tensor(stt[:, 6:7], stt[:, 5:6], sbc[:, 3:4], ALU.mult)
            for g in range(8):
                vector.wait_ge((s_th).h, (s_th).base + (32 * tt + 4 * (g + 1)))
                srcp = ps_t[g % 2][:].rearrange("p (c k) -> p c k", c=4)
                vector.tensor_copy(hq_fm[:, 4 * g:4 * g + 4, :], srcp).then_inc(*(s_vhq).tick(1))
            for oh in range(2):
                vector.wait_ge((s_tm2).h, (s_tm2).base + (32 * (tt + 1)))
                vector.scalar_tensor_tensor(
                    x_res[:, tt, 512 * oh:512 * oh + 512],
                    ps_m2[oh][:], stt[:, 6:7],
                    x_res[:, tt, 512 * oh:512 * oh + 512],
                    ALU.mult, ALU.add).then_inc(*(s_vr2).tick(1))

    @blk.scalar
    def _(scalar):
        for tt in range(NT):
            for oc in range(8):
                k = 8 * tt + oc
                scalar.wait_ge((s_tm1).h, (s_tm1).base + (k + 1))
                scalar.activation(
                    out=h_t[:, 512 * oc:512 * oc + 512],
                    in_=ps_m1[oc % 2][:], func=AF.Gelu,
                    scale=stt[:, 4:5]).then_inc(*(s_sg).tick(1))

    @blk.tensor
    def _(tensor):
        tensor.wait_ge((wprep[0]["svq"]).h, wprep[0]["svq_abs"])
        for tt in range(NT):
            tensor.wait_ge((s_vx).h, (s_vx).base + (V_X * tt + 9))
            if tt == 0:
                pass
            for c in range(8):
                g, j = c // 4, c % 4
                gg = 2 * tt + g
                if gg >= 2:
                    tensor.wait_ge((s_vxe).h, (s_vxe).base + (gg - 1))
                nc.tensor.matmul(
                    ps_t[g % 2][:, 128 * j:128 * j + 128],
                    xq_tm[:, 128 * c:128 * c + 128], ident_bf[:],
                    start=True, stop=True, skip_group_check=True).then_inc(*(s_tx).tick(1))
            tensor.wait_ge((s_vxe).h, (s_vxe).base + (2 * tt + 2))
            for oc in range(8):
                k = 8 * tt + oc
                if k >= 2:
                    tensor.wait_ge((s_sg).h, (s_sg).base + (k - 1))
                for ci in range(DC):
                    ins = nc.tensor.matmul(
                        ps_m1[oc % 2][:], xq_fm[:, ci, :],
                        w1qT[:, ci, 512 * oc:512 * oc + 512],
                        start=(ci == 0), stop=(ci == DC - 1),
                        skip_group_check=True)
                ins.then_inc(*(s_tm1).tick(1))
            tensor.wait_ge((s_vh).h, (s_vh).base + (8 * tt + 7))
            for c in range(32):
                g, j = c // 4, c % 4
                gg = 8 * tt + g
                if gg >= 2:
                    tensor.wait_ge((s_vhq).h, (s_vhq).base + (gg - 1))
                nc.tensor.matmul(
                    ps_t[g % 2][:, 128 * j:128 * j + 128],
                    hq_tm[:, 128 * c:128 * c + 128], ident_bf[:],
                    start=True, stop=True, skip_group_check=True).then_inc(*(s_th).tick(1))
            tensor.wait_ge((s_vhq).h, (s_vhq).base + (8 * tt + 8))
            if tt > 0:
                tensor.wait_ge((s_vr2).h, (s_vr2).base + (2 * tt))
            for i2 in range(32):
                k = 32 * tt + i2
                tensor.wait_ge((s_wb[k % 2]).h, (s_wb[k % 2]).base + (16 * (k // 2 + 1)))
                for oh in range(2):
                    ins = nc.tensor.matmul(
                        ps_m2[oh][:], hq_fm[:, i2, :],
                        w2b[k % 2][:, 512 * oh:512 * oh + 512],
                        start=(i2 == 0), stop=(i2 == 31),
                        skip_group_check=True)
                ins.then_inc(*(s_tm2).tick(1))

    sc.close()


def _emit_wT(p, blk, srcname, dstname, ntl, C, identt, tag):
    nc = p.nc
    nblk = C // 128
    ngrp = nblk // 4
    sc = contextlib.ExitStack()
    wstg = [p.sb(f"{tag}_wstg{i}", [128, C], F32, sc) for i in range(2)]
    tstg = p.sb(f"{tag}_tstg", [128, 512], F32, sc)
    sldA = p.rs("ga_ldA"); sldB = p.rs("ga_ldB")
    s_tt = p.rs("ga_tt")
    sve = p.rs("ga_ve")
    sdw = p.rs("ga_dw")
    ps = [sc.enter_context(nc.psum_tensor(f"{tag}_ps{i}", [128, 512], F32))
          for i in range(2)]

    @blk.sync
    def _(sync):
        srcw = p.ap[srcname].rearrange("(t p) f -> p t f", p=128)
        for t in range(ntl):
            if t >= 2:
                sync.wait_ge((s_tt).h, (s_tt).base + (nblk * (t - 1)))
            sync.dma_start(out=wstg[t % 2][:], in_=srcw[:, t, :]).then_inc(*((sldA, sldB)[t % 2]).tick(16))

    @blk.tensor
    def _(tensor):
        for t in range(ntl):
            tensor.wait_ge(((sldA, sldB)[t % 2]).h, ((sldA, sldB)[t % 2]).base + (16 * (t // 2 + 1)))
            for c in range(nblk):
                g, j = c // 4, c % 4
                gg = t * ngrp + g
                if gg >= 2:
                    tensor.wait_ge((sve).h, (sve).base + (gg - 1))
                nc.tensor.transpose(
                    ps[g % 2][:, 128 * j:128 * j + 128],
                    wstg[t % 2][:, 128 * c:128 * c + 128], identt[:]
                ).then_inc(*(s_tt).tick(1))

    @blk.vector
    def _(vector):
        for t in range(ntl):
            for g in range(ngrp):
                gg = t * ngrp + g
                vector.wait_ge((s_tt).h, (s_tt).base + (nblk * t + 4 * (g + 1)))
                vector.wait_ge((sdw).h, (sdw).base + (64 * gg))
                vector.tensor_copy(tstg[:], ps[g % 2][:]).then_inc(*(sve).tick(1))

    @blk.scalar
    def _(scalar):
        for t in range(ntl):
            for g in range(ngrp):
                gg = t * ngrp + g
                scalar.wait_ge((sve).h, (sve).base + (gg + 1))
                for j in range(4):
                    c = 4 * g + j
                    dst = p.ap[dstname][128 * c:128 * c + 128, 128 * t:128 * t + 128]
                    scalar.dma_start(out=dst, in_=tstg[:, 128 * j:128 * j + 128]
                                     ).then_inc(*(sdw).tick(16))
    sc.close()
    return sdw, sdw.ctr


def _emit_x2gather(p, blk, x_res, x_fm3, identt, tag="x2g"):
    nc = p.nc
    sc = contextlib.ExitStack()
    fms = [p.sb(f"{tag}_fms{i}", [128, DC, 128], BF16, sc) for i in range(2)]
    stx = p.rs("ga_tx")
    sve = p.rs("ga_ve")
    sdin = [p.rs("ga_din0"), p.rs("ga_din1")]
    sgc = p.rs("ga_gc")
    sdrb = p.rs("ga_drb")
    ps = [sc.enter_context(nc.psum_tensor(f"{tag}_ps{i}", [128, 1024], F32))
          for i in range(2)]

    @blk.tensor
    def _(tensor):
        for tt in range(NT):
            if tt >= 2:
                tensor.wait_ge((sve).h, (sve).base + (tt - 1))
            for c in range(DC):
                ins = nc.tensor.transpose(
                    ps[tt % 2][:, 128 * c:128 * c + 128],
                    x_res[:, tt, 128 * c:128 * c + 128], identt[:])
            ins.then_inc(*(stx).tick(1))

    @blk.vector
    def _(vector):
        for tt in range(NT):
            vector.wait_ge((stx).h, (stx).base + (tt + 1))
            if tt >= 2:
                vector.wait_ge((sdin[tt % 2]).h, (sdin[tt % 2]).base + (128 * (tt // 2)))
            srcp = ps[tt % 2][:].rearrange("p (c k) -> p c k", c=DC)
            vector.tensor_copy(fms[tt % 2][:], srcp).then_inc(*(sve).tick(1))

    @blk.sync
    def _(sync):
        for tt in range(NT):
            sync.wait_ge((sve).h, (sve).base + (tt + 1))
            for c in range(DC):
                dst = p.ap["x2cc_in"][128 * c:128 * c + 128, 128 * tt:128 * tt + 128]
                sync.dma_start(out=dst, in_=fms[tt % 2][:, c, :]).then_inc(*(sdin[tt % 2]).tick(16))

    @blk.gpsimd
    def _(gp):
        gp.wait_ge((sdin[0]).h, (sdin[0]).base + (128 * NT // 2))
        gp.wait_ge((sdin[1]).h, (sdin[1]).base + (128 * NT // 2))
        gp.collective_compute(
            "AllGather", ALU.bypass, replica_groups=RG,
            ins=[p.ap["x2cc_in"][:]], outs=[p.ap["x2cc_out"][:]],
        ).then_inc(*(sgc).tick(1))

    @blk.sync
    def _(sync):
        sync.wait_ge((sgc).h, (sgc).base + (1))
        for rank in range(2):
            for c in range(DC):
                srcd = p.ap["x2cc_out"][rank][128 * c:128 * c + 128, :]
                sync.dma_start(out=x_fm3[:, c, HALF * rank:HALF * rank + HALF],
                               in_=srcd).then_inc(*(sdrb).tick(16))
        sync.wait_ge((sdrb).h, (sdrb).base + (256))
    sc.close()


PW = HALF + 513


def _emit_inner(p, blk, x_res, identt, ident_bf, tri_pt, tril_pt, cst):
    nc = p.nc
    s_x3 = p.rs("in_x3")

    s_lnc = p.rs("ga_v")

    @blk.sync
    def _(sync):
        sync.wait_ge((s_lnc).h, (s_lnc).base + (0))
        x3d = p.ap["x3_d"].rearrange("(t p) d -> p t d", p=128)
        for tt in range(NT):
            sync.dma_start(out=x3d[:, tt, :], in_=x_res[:, tt, :]).then_inc(*(s_x3).tick(16))
        sync.wait_ge((s_x3).h, (s_x3).base + (16 * NT))
    p.barrier(blk)

    for il in range(2):
        _emit_inner_attn(p, blk, x_res, identt, ident_bf, tri_pt, tril_pt, cst, il)
        p.barrier(blk)
        _emit_inner_ffn(p, blk, x_res, identt, ident_bf, cst, il)
        p.barrier(blk)

    sc = contextlib.ExitStack()
    x3s = [p.sb(f"in_x3s{i}", [128, D], F32, sc) for i in range(2)]
    s_xb = [p.rs("in_xbA"), p.rs("in_xbB")]
    s_xv = p.rs("in_xv")

    @blk.sync
    def _(sync):
        x3d = p.ap["x3_d"].rearrange("(t p) d -> p t d", p=128)
        for tt in range(NT):
            if tt >= 2:
                sync.wait_ge((s_xv).h, (s_xv).base + (tt - 1))
            sync.dma_start(out=x3s[tt % 2][:], in_=x3d[:, tt, :]).then_inc(
                *(s_xb[tt % 2]).tick(16))

    s_frc = p.rs("f4_vr")

    @blk.vector
    def _(vector):
        vector.wait_ge((s_frc).h, (s_frc).base + (0))
        for tt in range(NT):
            vector.wait_ge((s_xb[tt % 2]).h, (s_xb[tt % 2]).base + (16 * (tt // 2 + 1)))
            vector.tensor_tensor(x_res[:, tt, :], x_res[:, tt, :], x3s[tt % 2][:],
                                 ALU.add).then_inc(*(s_xv).tick(1))
    sc.close()


def _norm_rot(p, blk, buf, width, toff, ta, tb, sccol, B, ready):
    nc = p.nc
    s_sq = p.rs("nr_sq")
    s_tn = p.rs("nr_tn")
    s_vn = p.rs("nr_vn")
    s_tb = p.rs("nr_tb")
    s_vtr = p.rs("p4_tl2")
    nch = cdiv(width, 512)
    n2, nrb, rt1, rt2, sqs = B["n2"], B["nrb"], B["rt1"], B["rt2"], B["sqs"]
    bo33, hsel, qs8 = B["bo33"], B["hsel"], B["qs8"]
    n2b = B["n2b"]
    ps_n = B["ps_n"]
    ps_b = B["ps_b"]

    @blk.scalar
    def _(scalar):
        scalar.wait_ge(ready[0], ready[1])
        scalar.wait_ge((s_tn).h, (s_tn).base + (0))
        scalar.wait_ge((s_vn).h, (s_vn).base + (0))
        scalar.activation(out=sqs[:, 0:width], in_=buf[:, 0:width],
                          func=AF.Square).then_inc(*(s_sq).tick(1))
        scalar.wait_ge((s_vn).h, (s_vn).base + (nch))
        scalar.activation(out=n2[:, 0:width], in_=n2[:, 0:width],
                          func=AF.Sqrt).then_inc(*(s_sq).tick(1))

    @blk.tensor
    def _(tensor):
        tensor.wait_ge((s_sq).h, (s_sq).base + (1))
        for cch in range(nch):
            w = min(512, width - 512 * cch)
            if cch >= 1:
                tensor.wait_ge((s_vn).h, (s_vn).base + (cch))
            nc.tensor.matmul(ps_n[0:2, 0:w], bo33[:],
                             sqs[:, 512 * cch:512 * cch + w],
                             start=True, stop=True,
                             skip_group_check=True).then_inc(*(s_tn).tick(1))
        tensor.wait_ge((s_vn).h, (s_vn).base + (nch + 2))
        for cch in range(nch):
            w = min(512, width - 512 * cch)
            if cch >= 1:
                tensor.wait_ge((s_vn).h, (s_vn).base + (nch + 2 + cch))
            nc.tensor.matmul(ps_b[:, 0:w], hsel[:],
                             n2b[:, 512 * cch:512 * cch + w],
                             start=True, stop=True,
                             skip_group_check=True).then_inc(*(s_tb).tick(1))
        tensor.wait_ge(p.s_rsw.h, 48)
        base2 = 2 * nch + 2
        for cch in range(nch):
            w = min(512, width - 512 * cch)
            tensor.wait_ge((s_vn).h, (s_vn).base + (base2 + 2 if cch == 0
                                                    else base2 + 3 + cch))
            nc.tensor.matmul(ps_b[:, 0:w], p.rsw[:],
                             buf[:, 512 * cch:512 * cch + w],
                             start=True, stop=True,
                             skip_group_check=True).then_inc(*(s_tb).tick(1))

    @blk.vector
    def _(vector):
        vector.wait_ge((s_vtr).h, (s_vtr).base + (0))
        vector.wait_ge(p.p4_qs_ready[0], p.p4_qs_ready[1])
        for cch in range(nch):
            w = min(512, width - 512 * cch)
            vector.wait_ge((s_tn).h, (s_tn).base + (cch + 1))
            vector.tensor_copy(n2[:, 512 * cch:512 * cch + w],
                               ps_n[0:2, 0:w]).then_inc(*(s_vn).tick(1))
        vector.wait_ge((s_sq).h, (s_sq).base + (2))
        vector.tensor_scalar(n2[:, 0:width], n2[:, 0:width], 1e-12, None,
                             ALU.max).then_inc(*(s_vn).tick(1))
        vector.wait_ge((s_vn).h, (s_vn).base + (nch + 1))
        with nc.allow_low_precision(reason="l2 norms tolerate reduced precision"):
            vector.reciprocal(n2b[:, 0:width], n2[:, 0:width]).then_inc(*(s_vn).tick(1))
        for cch in range(nch):
            w = min(512, width - 512 * cch)
            vector.wait_ge((s_tb).h, (s_tb).base + (cch + 1))
            vector.tensor_copy(nrb[:, 512 * cch:512 * cch + w],
                               ps_b[:, 0:w]).then_inc(*(s_vn).tick(1))
        bufv = buf[:, 0:width]
        base = 2 * nch + 2
        vector.wait_ge((s_vn).h, (s_vn).base + (base))
        vector.tensor_tensor(bufv, bufv, nrb[:, 0:width], ALU.mult).then_inc(*(s_vn).tick(1))
        vector.wait_ge((s_vn).h, (s_vn).base + (base + 1))
        vector.tensor_scalar(bufv, bufv, qs8[:, sccol:sccol + 1], None,
                             ALU.mult).then_inc(*(s_vn).tick(1))
        vector.wait_ge((s_vn).h, (s_vn).base + (base + 2))
        vector.tensor_tensor(rt1[:, 0:width], bufv, ta[:, toff:toff + width],
                             ALU.mult).then_inc(*(s_vn).tick(1))
        for cch in range(nch):
            w = min(512, width - 512 * cch)
            vector.wait_ge((s_tb).h, (s_tb).base + (nch + cch + 1))
            vector.tensor_tensor(rt2[:, 512 * cch:512 * cch + w],
                                 ps_b[:, 0:w],
                                 tb[:, toff + 512 * cch:toff + 512 * cch + w],
                                 ALU.mult).then_inc(*(s_vn).tick(1))
        vector.wait_ge((s_vn).h, (s_vn).base + (base + 3 + nch))
        vector.tensor_tensor(bufv, rt1[:, 0:width], rt2[:, 0:width],
                             ALU.add).then_inc(*(s_vn).tick(1))
    return s_vn




def _emit_inner_attn(p, blk, x_res, identt, ident_bf, tri_pt, tril_pt, cst, il):
    nc = p.nc
    sc = contextlib.ExitStack()

    x_fm4 = p.sb(f"p4{il}_xfm", [128, DC, 2 + PW], BF16, sc)
    q4 = p.sb(f"p4{il}_q", [128, HALF], BF16, sc)
    k4 = p.sb(f"p4{il}_k", [128, WINDOW + HALF], BF16, sc)
    v4 = p.sb(f"p4{il}_v", [128, 20 * 128], BF16, sc)
    wst = p.sb(f"p4{il}_wst", [128, DC, 128], F32, sc)
    wbf = [p.sb(f"p4{il}_wbf{i}", [128, DC, 128], BF16, sc) for i in range(3)]
    tA = {nm: p.sb(f"p4{il}_t{nm}", [128, HALF if nm in ("aq", "bq") else PW],
                   BF16, sc)
          for nm in ("aq", "bq", "ak", "bk")}
    qs8 = p.sb(f"p4{il}_qs8", [128, 2], F32, sc)
    n2 = p.sb(f"p4{il}_n2", [2, PW], F32, sc)
    n2b = p.sb(f"p4{il}_n2b", [2, PW], BF16, sc)
    nrb = p.sb(f"p4{il}_nrb", [128, PW], BF16, sc)
    rt1 = p.sb(f"p4{il}_rt1", [128, PW], BF16, sc)
    rt2 = p.sb(f"p4{il}_rt2", [128, PW], BF16, sc)
    sqs = rt1
    bo33 = p.sb(f"p4{il}_bo33", [128, 2], BF16, sc)
    hsel = p.sb(f"p4{il}_hsel", [2, 128], BF16, sc)
    hself = p.sb(f"p4{il}_hself", [2, 128], F32, sc)
    or64 = p.sb(f"p4{il}_or64", [1, 64], BF16, sc)
    orf = p.sb(f"p4{il}_orf", [1, 64], F32, sc)
    oc1 = p.sb(f"p4{il}_oc1", [128, 1], BF16, sc)
    p_f = p.sb(f"p4{il}_pf", [128, 640], F32, sc)
    p_fb = p.sb(f"p4{il}_pfb", [128, 640], BF16, sc)
    lcol = p.sb(f"p4{il}_lcol", [128, 2], F32, sc)
    p_bf = [p.sb(f"p4{il}_pbf{i}", [128, 5, 128], BF16, sc) for i in range(2)]
    negm = p.sb(f"p4{il}_negm", [128, 2], F32, sc)
    ot4 = p.sb(f"p4{il}_ot4", [128, NHP, HALF], BF16, sc)
    wo4 = p.sb(f"p4{il}_wo4", [128, 4, 512], BF16, sc)

    ps_sa = sc.enter_context(nc.psum_tensor(f"p4a{il}_sa", [128, 1024], F32))
    ps_sb = sc.enter_context(nc.psum_tensor(f"p4a{il}_sb", [128, 1024], F32))
    ps_pt = sc.enter_context(nc.psum_tensor(f"p4a{il}_pt", [128, 640], F32))
    ps_ot = sc.enter_context(nc.psum_tensor(f"p4a{il}_ot", [128, 512], F32))
    ps_l = sc.enter_context(nc.psum_tensor(f"p4a{il}_l", [33, 512], F32))

    B = dict(n2=n2, n2b=n2b, nrb=nrb, rt1=rt1, rt2=rt2, sqs=sqs, bo33=bo33,
             hsel=hsel, hself=hself, qs8=qs8, ps_n=ps_l, ps_b=ps_ot)
    p.p4_qs_ready = None

    s_tx = p.rs("ga_tx"); s_vx = p.rs("ga_vx")

    @blk.tensor
    def _(tensor):
        for role in ("ga_vwo", "ga_v", "f4_vr"):
            rr = p.rs(role)
            tensor.wait_ge(rr.h, rr.base + 0)
        for tt in range(NT):
            if tt >= 2:
                tensor.wait_ge((s_vx).h, (s_vx).base + (tt - 1))
            pst = ps_sa if tt % 2 == 0 else ps_sb
            for c in range(DC):
                ins = nc.tensor.transpose(
                    pst[:, 128 * c:128 * c + 128],
                    x_res[:, tt, 128 * c:128 * c + 128], identt[:])
            ins.then_inc(*(s_tx).tick(1))

    @blk.vector
    def _(vector):
        vector.memset(x_fm4[:, :, 0:2], 0.0)
        vector.memset(bo33[0:64, 0:1], 1.0)
        vector.memset(bo33[64:128, 0:1], 0.0)
        vector.memset(bo33[0:64, 1:2], 0.0)
        vector.memset(bo33[64:128, 1:2], 1.0)
        vector.memset(or64[:], 1.0)
        vector.memset(orf[:], 1.0)
        vector.memset(oc1[:], 1.0)
        for tt in range(NT):
            vector.wait_ge((s_tx).h, (s_tx).base + (tt + 1))
            srcp = (ps_sa if tt % 2 == 0 else ps_sb)[:].rearrange("p (c k) -> p c k", c=DC)
            vector.tensor_copy(x_fm4[:, :, 515 + 128 * tt:515 + 128 * tt + 128],
                               srcp).then_inc(*(s_vx).tick(1))

    s_din = p.rs("ga_din0"); s_gc = p.rs("ga_gc"); s_drb = p.rs("ga_drb")
    s_vh = p.rs("p4_vh"); s_tld = p.rs("p4_tld")
    s_tl2 = p.rs("p4_tl2")
    p.p4_qs_ready = (s_vh.h, s_vh.base + 2)

    @blk.sync
    def _(sync):
        sync.wait_ge((s_vx).h, (s_vx).base + (NT))
        for c in range(DC):
            dst = p.ap["shcc_in"][128 * c:128 * c + 128, :]
            sync.dma_start(out=dst, in_=x_fm4[:, c, 2050:2563]).then_inc(*(s_din).tick(16))

    @blk.gpsimd
    def _(gp):
        gp.wait_ge((s_din).h, (s_din).base + (16 * DC))
        gp.collective_compute(
            "AllGather", ALU.bypass, replica_groups=RG,
            ins=[p.ap["shcc_in"][:]], outs=[p.ap["shcc_out"][:]],
        ).then_inc(*(s_gc).tick(1))

    @blk.sync
    def _(sync):
        sync.wait_ge((s_gc).h, (s_gc).base + (1))
        for arm in sync.switch_core_id(n=128):
            pr = 1 - (arm.logical % 2)
            srcd = p.ap["shcc_out"][pr]
            for c in range(DC):
                sync.dma_start(out=x_fm4[:, c, 2:515],
                               in_=srcd[128 * c:128 * c + 128, :]).then_inc(s_drb.h, 16)
        s_drb.ctr += 16 * DC
        for half128 in range(2):
            b = 64 * half128
            sync.dma_start(out=qs8[b:b + 64, 0:1],
                           in_=p.ap["i_qs"][il][:, None]).then_inc(*(s_tld).tick(16))
            sync.dma_start(out=qs8[b:b + 64, 1:2],
                           in_=p.ap["i_ks"][il][:, None]).then_inc(*(s_tld).tick(16))
        for nm in ("aq", "bq", "ak", "bk"):
            sync.dma_start(out=tA[nm][:], in_=p.ap[f"rot_{nm}"][:]).then_inc(*(s_tl2).tick(16))
        sync.dma_start(out=hsel[:], in_=p.ap["hsel_bf"][:]).then_inc(*(s_tl2).tick(16))
        sync.dma_start(out=hself[:], in_=p.ap["hsel_f"][:]).then_inc(*(s_tl2).tick(16))

    @blk.vector
    def _(vector):
        vector.wait_ge((s_drb).h, (s_drb).base + (16 * DC))
        vector.wait_ge(p.s_hv0.h, 16)
        vector.tensor_scalar(x_fm4[:, :, 2:515], x_fm4[:, :, 2:515],
                             p.hv[:, 0:1], None, ALU.mult).then_inc(*(s_vh).tick(1))
        vector.wait_ge((s_tld).h, (s_tld).base + (64))
        vector.tensor_scalar(qs8[:, 0:1], qs8[:, 0:1], QK_SCALE, None,
                             ALU.mult).then_inc(*(s_vh).tick(1))

    for hp in range(NHP):
        _inner_attn_hp(p, blk, hp, il, x_res, x_fm4, q4, k4, v4, wst, wbf,
                       tA, B, p_f, p_fb, lcol, p_bf, negm, ot4, oc1, or64,
                       tri_pt, tril_pt, cst, identt, ident_bf,
                       ps_sa, ps_sb, ps_pt, ps_ot, ps_l,
                       s_vh)

    s_wld = p.rs("ga_wld"); s_vw = p.rs("ga_vw")
    s_tw = p.rs("ga_tw"); s_vwo = p.rs("ga_vwo")
    s_ta = p.rs("ga_tav"); s_vo = p.rs("ga_vo"); s_vb = p.rs("ga_vb")

    @blk.sync
    def _(sync):
        sync.wait_ge((s_vo).h, (s_vo).base + (0))
        sync.wait_ge((s_vw).h, (s_vw).base + (0))
        srcw = p.ap["i_wo"][il].rearrange("(t p) o -> p t o", p=128)
        for oh in range(2):
            for t4 in range(4):
                e = 4 * oh + t4
                if e > 0:
                    sync.wait_ge((s_vw).h, (s_vw).base + (e))
                for j in range(4):
                    sync.dma_start(
                        out=wst[:, j, :],
                        in_=srcw[:, t4, 512 * oh + 128 * j:512 * oh + 128 * j + 128],
                    ).then_inc(*(s_wld).tick(16))

    @blk.vector
    def _(vector):
        for oh in range(2):
            for t4 in range(4):
                e = 4 * oh + t4
                vector.wait_ge((s_wld).h, (s_wld).base + (64 * (e + 1)))
                if oh > 0:
                    vector.wait_ge((s_tw).h, (s_tw).base + (16 * oh))
                vector.tensor_copy(
                    wo4[:, t4, :],
                    wst[:, 0:4, :].rearrange("p c k -> p (c k)")).then_inc(*(s_vw).tick(1))
            for tt in range(NT):
                g = 16 * oh + tt
                vector.wait_ge((s_tw).h, (s_tw).base + (g + 1))
                xt = x_res[:, tt, 512 * oh:512 * oh + 512]
                pst = (ps_sa if g % 2 == 0 else ps_sb)[:, 0:512]
                vector.tensor_tensor(xt, pst, xt, ALU.add).then_inc(*(s_vwo).tick(1))

    @blk.tensor
    def _(tensor):
        tensor.wait_ge((s_vo).h, (s_vo).base + (0))
        for oh in range(2):
            tensor.wait_ge((s_vw).h, (s_vw).base + (4 * (oh + 1)))
            for tt in range(NT):
                g = 16 * oh + tt
                if g >= 2:
                    tensor.wait_ge((s_vwo).h, (s_vwo).base + (g - 1))
                dst = (ps_sa if g % 2 == 0 else ps_sb)[:, 0:512]
                for c4 in range(4):
                    ins = nc.tensor.matmul(
                        dst, ot4[:, c4, 128 * tt:128 * tt + 128],
                        wo4[:, c4, :],
                        start=(c4 == 0), stop=(c4 == 3), skip_group_check=True)
                ins.then_inc(*(s_tw).tick(1))
    sc.close()


def _inner_attn_hp(p, blk, hp, il, x_res, x_fm4, q4, k4, v4, wst, wbf,
                   tA, B, p_f, p_fb, lcol, p_bf, negm, ot4, oc1, or64,
                   tri_pt, tril_pt, cst, identt, ident_bf,
                   ps_sa, ps_sb, ps_pt, ps_ot, ps_l, s_vh):
    nc = p.nc
    KW = WINDOW + HALF
    hsel2 = B["hsel"]
    s_wld = p.rs("ga_wld"); s_vw = p.rs("ga_vw")
    s_tq = p.rs("ga_tq"); s_vq = p.rs("ga_vq")
    s_tv = p.rs("ga_tv"); s_vv = p.rs("ga_vv")

    @blk.sync
    def _(sync, hp=hp):
        for wi in range(3):
            sync.wait_ge((s_vw).h, (s_vw).base + (wi))
            col = (0, 512, 1024)[wi] + 128 * hp
            srcw = p.ap["i_wqkv"][il].rearrange("(c p) o -> p c o", p=128)
            for c in range(DC):
                sync.dma_start(out=wst[:, c, :],
                               in_=srcw[:, c, col:col + 128]).then_inc(*(s_wld).tick(16))

    @blk.vector
    def _(vector, hp=hp):
        for wi in range(3):
            vector.wait_ge((s_wld).h, (s_wld).base + (128 * (wi + 1)))
            vector.wait_ge((s_tq).h, (s_tq).base + (0))
            vector.tensor_copy(wbf[wi][:], wst[:]).then_inc(*(s_vw).tick(1))
        nq = (cdiv(HALF, 512), cdiv(KW, 512), cdiv(KW, 512))
        e = 0
        for wi in range(3):
            for qb in range(nq[wi]):
                vector.wait_ge((s_tq).h, (s_tq).base + (e + 1))
                w = min(512, (HALF if wi == 0 else KW) - 512 * qb)
                srcps = (ps_sa if e % 2 == 0 else ps_sb)[:, 0:w]
                if wi == 0:
                    vector.tensor_copy(q4[:, 512 * qb:512 * qb + w], srcps).then_inc(*(s_vq).tick(1))
                elif wi == 1:
                    vector.tensor_copy(k4[:, 512 * qb:512 * qb + w], srcps).then_inc(*(s_vq).tick(1))
                else:
                    vector.tensor_copy(v4[:, 512 * qb:512 * qb + w], srcps).then_inc(*(s_vq).tick(1))
                e += 1
        for g in range(5):
            vector.wait_ge((s_tv).h, (s_tv).base + (g + 1))
            n4 = 4
            srcp = (ps_pt if g % 2 == 0 else ps_ot)[:, 0:512]
            vector.tensor_copy(v4[:, 512 * g:512 * g + 512], srcp).then_inc(*(s_vv).tick(1))

    @blk.tensor
    def _(tensor, hp=hp):
        if hp == 0:
            tensor.wait_ge((s_vh).h, (s_vh).base + (1))
        nq = (cdiv(HALF, 512), cdiv(KW, 512), cdiv(KW, 512))
        offs = (515, 3, 3)
        e = 0
        for wi in range(3):
            tensor.wait_ge((s_vw).h, (s_vw).base + (wi + 1))
            for qb in range(nq[wi]):
                if e >= 2:
                    tensor.wait_ge((s_vq).h, (s_vq).base + (e - 1))
                w = min(512, (HALF if wi == 0 else KW) - 512 * qb)
                dst = (ps_sa if e % 2 == 0 else ps_sb)[:, 0:w]
                base = offs[wi] + 512 * qb
                for c in range(DC):
                    off = base if c < 4 else base - 1
                    ins = nc.tensor.matmul(
                        dst, wbf[wi][:, c, :], x_fm4[:, c, off:off + w],
                        start=(c == 0), stop=(c == DC - 1), skip_group_check=True)
                ins.then_inc(*(s_tq).tick(1))
                e += 1
        tensor.wait_ge((s_vq).h, (s_vq).base + (14))
        for g in range(5):
            if g >= 2:
                tensor.wait_ge((s_vv).h, (s_vv).base + (g - 1))
            for t4 in range(4):
                tt = 4 * g + t4
                ins = nc.tensor.matmul(
                    (ps_pt if g % 2 == 0 else ps_ot)[:, 128 * t4:128 * t4 + 128],
                    v4[:, 128 * tt:128 * tt + 128], ident_bf[:],
                    start=True, stop=True, skip_group_check=True)
            ins.then_inc(*(s_tv).tick(1))


    _norm_rot(p, blk, q4, HALF, 0, tA["aq"], tA["bq"], 0, B,
              ready=(s_vq.h, s_vq.base + 4))
    _norm_rot(p, blk, k4, KW, 1, tA["ak"], tA["bk"], 1, B,
              ready=(s_vq.h, s_vq.base + 9))

    s_ts = p.rs("ga_ts"); s_vm = p.rs("ga_vm"); s_se = p.rs("ga_se")
    s_tp = p.rs("p4_tp"); s_vp = p.rs("p4_vp")
    s_ta = p.rs("ga_tav"); s_vo = p.rs("ga_vo")
    s_vnr = p.rs("nr_vn")

    def nmask(t):
        return (1 + max(0, 3 - t) + 1) if t < 4 else 2
    vpc = 0
    mask_done = {}; cast_done = {}; evac_done = {}
    for t in range(NT):
        for h in range(2):
            u = 2 * t + h
            vpc += nmask(t); mask_done[u] = vpc
            vpc += 3; cast_done[u] = vpc
            vpc += 1; evac_done[u] = vpc

    @blk.tensor
    def _(tensor, hp=hp):
        tensor.wait_ge((s_vv).h, (s_vv).base + (5))
        tensor.wait_ge((s_vnr).h, (s_vnr).base + (0))
        for t in range(NT):
            for h in range(2):
                u = 2 * t + h
                pss = ps_sa if u % 2 == 0 else ps_sb
                if u >= 2:
                    tensor.wait_ge((s_se).h, (s_se).base + (u - 1))
                for kk in range(2):
                    w = 512 if kk == 0 else 128
                    ins = nc.tensor.matmul(
                        pss[:, 512 * kk:512 * kk + w],
                        q4[64 * h:64 * h + 64, 128 * t:128 * t + 128],
                        k4[64 * h:64 * h + 64, 128 * t + 512 * kk:128 * t + 512 * kk + w],
                        start=True, stop=True, tile_position=(64 * h, 0),
                        skip_group_check=True)
                ins.then_inc(*(s_ts).tick(1))
                tensor.wait_ge((s_vp).h, (s_vp).base + (cast_done[u]))
                if u >= 1:
                    tensor.wait_ge((s_vp).h, (s_vp).base + (evac_done[u - 1]))
                for j in range(5):
                    ins = nc.tensor.matmul(
                        ps_pt[:, 128 * j:128 * j + 128],
                        p_fb[:, 128 * j:128 * j + 128], ident_bf[:],
                        start=True, stop=True, skip_group_check=True)
                ins.then_inc(*(s_tp).tick(1))
                tensor.wait_ge((s_vp).h, (s_vp).base + (evac_done[u]))
                if t % 4 == 0 and h == 0 and t > 0:
                    tensor.wait_ge((s_vo).h, (s_vo).base + (t // 4))
                for j in range(5):
                    ins = nc.tensor.matmul(
                        ps_ot[64 * h:64 * h + 64, 128 * (t % 4):128 * (t % 4) + 128],
                        v4[:].rearrange("p (n k) -> p n k", k=128)[:, t + j, 64 * h:64 * h + 64],
                        p_bf[u % 2][:, j, :],
                        start=(j == 0), stop=(j == 4), tile_position=(0, 64 * h),
                        skip_group_check=True)
                ins.then_inc(*(s_ta).tick(1))

    @blk.vector
    def _(vector, hp=hp):
        vector.wait_ge(p.s_rsw.h, 48)
        for t in range(NT):
            for h in range(2):
                u = 2 * t + h
                pss = ps_sa if u % 2 == 0 else ps_sb
                vector.wait_ge((s_ts).h, (s_ts).base + (u + 1))
                vector.reduce_max(negm[:, 0:1], pss[:, 0:640], AX.X).then_inc(*(s_vm).tick(1))
                vector.wait_ge((s_vm).h, (s_vm).base + (2 * u + 1))
                if u >= 1:
                    vector.wait_ge((s_se).h, (s_se).base + (u))
                vector.tensor_scalar(negm[:, 1:2], negm[:, 0:1], -1.0, None,
                                     ALU.mult).then_inc(*(s_vm).tick(1))
                vector.wait_ge((s_se).h, (s_se).base + (u + 1))
                k0 = mask_done[u] - nmask(t)
                km = [k0]

                def mtick(ins):
                    ins.then_inc(*(s_vp).tick(1))
                    km[0] += 1
                    vector.wait_ge((s_vp).h, (s_vp).base + (km[0]))

                m0 = p_f[:, 0:128]
                m4 = p_f[:, 512:640]
                if t < 4:
                    mtick(vector.scalar_tensor_tensor(m0, m0, p.hv[:, 0:1],
                                                      p.trif[:], ALU.mult, ALU.mult))
                    for j in range(1, 4 - t):
                        mj = p_f[:, 128 * j:128 * j + 128]
                        mtick(vector.tensor_scalar(mj, mj, p.hv[:, 0:1], None, ALU.mult))
                else:
                    mtick(vector.tensor_tensor(m0, m0, p.trif[:], ALU.mult))
                mtick(vector.tensor_tensor(m4, m4, p.trilf[:], ALU.mult))
                mtick(vector.reduce_sum(lcol[:, 0:1], p_f[:], AX.X))
                mtick(vector.reciprocal(lcol[:, 1:2], lcol[:, 0:1]))
                mtick(vector.tensor_scalar(p_fb[:], p_f[:], lcol[:, 1:2], None,
                                           ALU.mult))
                vector.wait_ge((s_tp).h, (s_tp).base + (u + 1))
                if u >= 2:
                    vector.wait_ge((s_ta).h, (s_ta).base + (u - 1))
                vector.tensor_copy(p_bf[u % 2][:].rearrange("p j k -> p (j k)"),
                                   ps_pt[:, 0:640]).then_inc(*(s_vp).tick(1))
            if t % 4 == 3:
                g = t // 4
                vector.wait_ge((s_ta).h, (s_ta).base + (2 * t + 2))
                vector.tensor_copy(ot4[:, hp, 512 * g:512 * g + 512],
                                   ps_ot[:, 0:512]).then_inc(*(s_vo).tick(1))

    @blk.scalar
    def _(scalar, hp=hp):
        for t in range(NT):
            for h in range(2):
                u = 2 * t + h
                pss = ps_sa if u % 2 == 0 else ps_sb
                scalar.wait_ge((s_vm).h, (s_vm).base + (2 * u + 2))
                if u >= 1:
                    scalar.wait_ge((s_vp).h, (s_vp).base + (cast_done[u - 1]))
                scalar.activation(out=p_f[:], in_=pss[:, 0:640], func=AF.Exp,
                                  bias=negm[:, 1:2]).then_inc(*(s_se).tick(1))


def _emit_inner_ffn(p, blk, x_res, identt, ident_bf, cst, il):
    nc = p.nc
    sc = contextlib.ExitStack()
    NHC = 22

    sh = p.sb(f"f4{il}_sh", [128, D], F32, sc)
    hn = p.sb(f"f4{il}_hn", [128, D], F32, sc)
    hfm = p.sb(f"f4{il}_hfm", [128, DC, HALF], BF16, sc)
    hrow = p.sb(f"f4{il}_hrow", [1, D], F32, sc)
    g_fm = p.sb(f"f4{il}_gfm", [128, NHC, 512], BF16, sc)
    w1stg = [p.sb(f"f4{il}_w1s{i}", [128, DC, 256], F32, sc) for i in range(2)]
    w1bf = [p.sb(f"f4{il}_w1b{i}", [128, DC, 256], BF16, sc) for i in range(2)]
    w2stg = [p.sb(f"f4{il}_w2s{i}", [128, D], F32, sc) for i in range(2)]
    w2bf = [p.sb(f"f4{il}_w2b{i}", [128, D], BF16, sc) for i in range(2)]
    lgro = p.sb(f"f4{il}_lgro", [1, D], F32, sc)
    lbro = p.sb(f"f4{il}_lbro", [1, D], F32, sc)
    b2ro = p.sb(f"f4{il}_b2ro", [1, D], F32, sc)
    lg_b = p.sb(f"f4{il}_lgb", [128, D], F32, sc)
    lb_b = p.sb(f"f4{il}_lbb", [128, D], F32, sc)
    b2_b = p.sb(f"f4{il}_b2b", [128, D], F32, sc)
    b1c = p.sb(f"f4{il}_b1c", [128, 2 * NHC], F32, sc)
    gtmp = p.sb(f"f4{il}_gtmp", [128, 512], F32, sc)
    stt = p.sb(f"f4{il}_stt", [128, 8], F32, sc)
    or128 = p.sb(f"f4{il}_or128", [1, 128], F32, sc)

    ps8 = [sc.enter_context(nc.psum_tensor(f"f4{il}_ps{i}", [128, 512], F32))
           for i in range(8)]
    ps_a, ps_g, ps_t = ps8[0], ps8[1], ps8[2]

    s_sh1 = p.rs("f4_sh1")
    s_gc = p.rs("ga_gc")
    s_hrb = p.rs("f4_hrb")
    s_ld = p.rs("f4_ld")
    s_tbb = p.rs("ga_tb")
    s_vbb = p.rs("ga_vb")

    s_vwoF = p.rs("ga_vwo")

    @blk.sync
    def _(sync):
        sync.wait_ge((s_vwoF).h, (s_vwoF).base + (0))
        sync.dma_start(out=p.ap["sh1_in"][:], in_=x_res[127:128, NT - 1, :]).then_inc(
            *(s_sh1).tick(16))

    @blk.gpsimd
    def _(gp):
        gp.wait_ge((s_sh1).h, (s_sh1).base + (16))
        gp.collective_compute(
            "AllGather", ALU.bypass, replica_groups=RG,
            ins=[p.ap["sh1_in"][:]], outs=[p.ap["sh1_out"][:]],
        ).then_inc(*(s_gc).tick(1))

    @blk.sync
    def _(sync):
        sync.wait_ge((s_gc).h, (s_gc).base + (1))
        for arm in sync.switch_core_id(n=128):
            pr = 1 - (arm.logical % 2)
            sync.dma_start(out=hrow[:], in_=p.ap["sh1_out"][pr]).then_inc(s_hrb.h, 16)
        s_hrb.ctr += 16
        sync.dma_start(out=lgro[:], in_=p.ap["i_lng"][il][None, :]).then_inc(*(s_ld).tick(16))
        sync.dma_start(out=lbro[:], in_=p.ap["i_lnb"][il][None, :]).then_inc(*(s_ld).tick(16))
        sync.dma_start(out=b2ro[:], in_=p.ap["i_b2"][il][None, :]).then_inc(*(s_ld).tick(16))
        for hc in range(NHC):
            n = 128 if hc < NHC - 1 else 42
            for ag in range(2):
                srcb = p.ap["i_b1"][il][256 * hc + n * ag:256 * hc + n * ag + n, None]
                sync.dma_start(out=b1c[0:n, 2 * hc + ag:2 * hc + ag + 1],
                               in_=srcb).then_inc(*(s_ld).tick(16))

    @blk.vector
    def _(vector):
        vector.memset(or128[:], 1.0).then_inc(*(s_vbb).tick(1))
        vector.wait_ge((s_hrb).h, (s_hrb).base + (16))
        vector.tensor_scalar(hrow[:], hrow[:], p.hv[0:1, 0:1], None,
                             ALU.mult).then_inc(*(s_vbb).tick(1))
        for bi in range(3):
            dstb = (lg_b, lb_b, b2_b)[bi]
            for oh in range(2):
                vector.wait_ge((s_tbb).h, (s_tbb).base + (2 * bi + oh + 1))
                vector.tensor_copy(dstb[:, 512 * oh:512 * oh + 512],
                                   ps_t[:, 0:512]).then_inc(*(s_vbb).tick(1))

    @blk.tensor
    def _(tensor):
        tensor.wait_ge((s_ld).h, (s_ld).base + (16 * (3 + 2 * NHC)))
        for bi in range(3):
            rowb = (lgro, lbro, b2ro)[bi]
            for oh in range(2):
                k = 2 * bi + oh
                tensor.wait_ge((s_vbb).h, (s_vbb).base + (1 if k == 0 else k + 2))
                nc.tensor.matmul(ps_t[:, 0:512], or128[:],
                                 rowb[:, 512 * oh:512 * oh + 512],
                                 start=True, stop=True,
                                 skip_group_check=True).then_inc(*(s_tbb).tick(1))

    s_dsh = [p.rs("f4_dshA"), p.rs("f4_dshB")]
    s_vln = p.rs("f4_vln")
    s_sln = p.rs("f4_sln")
    s_th = p.rs("f4_th")
    s_vh2 = p.rs("f4_vh2")
    vh2_abs0 = s_vh2.base

    @blk.scalar
    def _(scalar):
        for tt in range(NT):
            scalar.wait_ge((s_vln).h, (s_vln).base + (12 * tt))
            if tt == 0:
                scalar.wait_ge((s_vbb).h, (s_vbb).base + (2))
            scalar.dma_start(out=sh[1:128, 512:1024],
                             in_=x_res[0:127, tt, 512:1024]).then_inc(*(s_dsh[0]).tick(16))
            if tt > 0:
                scalar.dma_start(out=sh[0:1, 512:1024],
                                 in_=x_res[127:128, tt - 1, 512:1024]).then_inc(
                    *(s_dsh[1]).tick(16))
            else:
                scalar.dma_start(out=sh[0:1, 512:1024],
                                 in_=hrow[:, 512:1024]).then_inc(*(s_dsh[1]).tick(16))
            scalar.wait_ge((s_vln).h, (s_vln).base + (12 * tt + 8))
            scalar.activation(out=stt[:, 6:7], in_=stt[:, 5:6], func=AF.Sqrt,
                              bias=cst[:, 1:2]).then_inc(*(s_sln).tick(1))

    @blk.vector
    def _(vector):
        for tt in range(NT):
            n0 = 12 * tt
            k = [0]

            def step(ins):
                ins.then_inc(*(s_vln).tick(1))
                k[0] += 1
                vector.wait_ge((s_vln).h, (s_vln).base + (n0 + k[0]))

            if tt >= 1:
                vector.wait_ge((s_th).h, (s_th).base + (2 * tt))
            vector.wait_ge((s_dsh[0]).h, (s_dsh[0]).base + (16 * (tt + 1)))
            vector.wait_ge((s_dsh[1]).h, (s_dsh[1]).base + (16 * (tt + 1)))
            if tt == 0:
                vector.wait_ge((s_vbb).h, (s_vbb).base + (8))
            step(vector.tensor_copy(sh[:, 0:512], x_res[:, tt, 0:512]))
            step(vector.tensor_tensor(hn[:], sh[:], sh[:], ALU.mult))
            step(vector.reduce_sum(stt[:, 0:1], sh[:], AX.X))
            step(vector.tensor_scalar(stt[:, 2:3], stt[:, 0:1], 1.0 / D, None, ALU.mult))
            step(vector.reduce_sum(stt[:, 1:2], hn[:], AX.X))
            step(vector.tensor_scalar(stt[:, 3:4], stt[:, 1:2], 1.0 / D, None, ALU.mult))
            step(vector.tensor_tensor(stt[:, 4:5], stt[:, 2:3], stt[:, 2:3], ALU.mult))
            step(vector.tensor_tensor(stt[:, 5:6], stt[:, 3:4], stt[:, 4:5], ALU.subtract))
            vector.wait_ge((s_sln).h, (s_sln).base + (tt + 1))
            step(vector.reciprocal(stt[:, 7:8], stt[:, 6:7]))
            step(vector.tensor_scalar(hn[:], sh[:], stt[:, 2:3], stt[:, 7:8],
                                      ALU.subtract, ALU.mult))
            step(vector.tensor_tensor(hn[:], hn[:], lg_b[:], ALU.mult))
            step(vector.tensor_tensor(hn[:], hn[:], lb_b[:], ALU.add))
            for g in range(2):
                vector.wait_ge((s_th).h, (s_th).base + (2 * tt + g + 1))
                srcp = (ps_t if g == 0 else ps_a)[:, 0:512].rearrange("p (c k) -> p c k", c=4)
                vector.tensor_copy(hfm[:, 4 * g:4 * g + 4, 128 * tt:128 * tt + 128],
                                   srcp).then_inc(*(s_vh2).tick(1))

    @blk.tensor
    def _(tensor):
        for tt in range(NT):
            tensor.wait_ge((s_vln).h, (s_vln).base + (12 * tt + 12))
            for c in range(DC):
                g = c // 4
                if 2 * tt + g >= 2:
                    tensor.wait_ge((s_vh2).h, (s_vh2).base + (2 * tt + g - 1))
                ins = nc.tensor.transpose(
                    (ps_t if g == 0 else ps_a)[:, 128 * (c % 4):128 * (c % 4) + 128],
                    hn[:, 128 * c:128 * c + 128], identt[:])
                if c % 4 == 3:
                    ins.then_inc(*(s_th).tick(1))

    for ts in range(4):
        s_w1l = [p.rs("f4_w1lA"), p.rs("f4_w1lB")]
        s_vw1 = p.rs("f4_vw1")
        s_tm1 = p.rs("f4_tm1")
        s_sg = p.rs("f4_sg")
        s_vg = p.rs("f4_vg")
        s_w2l = [p.rs("f4_w2lA"), p.rs("f4_w2lB")]
        s_vw2 = p.rs("f4_vw2")
        s_tm2 = p.rs("f4_tm2")
        s_vr = p.rs("f4_vr")
        s_vh2b = p.rs("f4_vh2")
        s_vlnF = p.rs("f4_vln")

        @blk.sync
        def _(sync, ts=ts):
            sync.wait_ge((s_vw1).h, (s_vw1).base + (0))
            sync.wait_ge((s_vw2).h, (s_vw2).base + (0))
            for hc in range(NHC):
                if hc >= 2:
                    sync.wait_ge((s_vw1).h, (s_vw1).base + (hc - 1))
                ncol = 256 if hc < NHC - 1 else 84
                srcw = p.ap["i_w1"][il].rearrange("(c p) f -> p c f", p=128)
                for c in range(DC):
                    sync.dma_start(out=w1stg[hc % 2][:, c, 0:ncol],
                                   in_=srcw[:, c, 256 * hc:256 * hc + ncol]).then_inc(
                        *(s_w1l[hc % 2]).tick(16))
            for hc in range(NHC):
                if hc >= 2:
                    sync.wait_ge((s_vw2).h, (s_vw2).base + (hc - 1))
                n = 128 if hc < NHC - 1 else 42
                sync.dma_start(out=w2stg[hc % 2][0:n, :],
                               in_=p.ap["i_w2"][il][128 * hc:128 * hc + n, :]).then_inc(
                    *(s_w2l[hc % 2]).tick(16))

        @blk.vector
        def _(vector, ts=ts):
            vector.wait_ge((s_vlnF).h, (s_vlnF).base + (0))
            vector.wait_ge((s_tm1).h, (s_tm1).base + (0))
            vector.wait_ge((s_tm2).h, (s_tm2).base + (0))
            for hc in range(NHC):
                ncol = 256 if hc < NHC - 1 else 84
                vector.wait_ge((s_w1l[hc % 2]).h, (s_w1l[hc % 2]).base + (128 * (hc // 2 + 1)))
                if hc >= 2:
                    vector.wait_ge((s_tm1).h, (s_tm1).base + (2 * (hc - 1)))
                vector.tensor_copy(w1bf[hc % 2][:, :, 0:ncol],
                                   w1stg[hc % 2][:, :, 0:ncol]).then_inc(*(s_vw1).tick(1))
                n = 128 if hc < NHC - 1 else 42
                vector.wait_ge((s_sg).h, (s_sg).base + (hc + 1))
                vector.scalar_tensor_tensor(
                    g_fm[0:n, hc, :], ps_a[0:n, :], b1c[0:n, 2 * hc:2 * hc + 1],
                    gtmp[0:n, :], ALU.add, ALU.mult).then_inc(*(s_vg).tick(1))
            for hc in range(NHC):
                n = 128 if hc < NHC - 1 else 42
                vector.wait_ge((s_w2l[hc % 2]).h, (s_w2l[hc % 2]).base + (16 * (hc // 2 + 1)))
                if hc >= 2:
                    vector.wait_ge((s_tm2).h, (s_tm2).base + (8 * (hc - 1)))
                vector.tensor_copy(w2bf[hc % 2][0:n, :],
                                   w2stg[hc % 2][0:n, :]).then_inc(*(s_vw2).tick(1))
            for e in range(8):
                t4, oh = e // 2, e % 2
                tt = 4 * ts + t4
                vector.wait_ge((s_tm2).h, (s_tm2).base + (8 * NHC))
                if e >= 1:
                    vector.wait_ge((s_vr).h, (s_vr).base + (2 * e))
                xt = x_res[:, tt, 512 * oh:512 * oh + 512]
                vector.tensor_tensor(gtmp[:], ps8[e][:],
                                     b2_b[:, 512 * oh:512 * oh + 512],
                                     ALU.add).then_inc(*(s_vr).tick(1))
                vector.wait_ge((s_vr).h, (s_vr).base + (2 * e + 1))
                vector.tensor_tensor(xt, gtmp[:], xt, ALU.add).then_inc(*(s_vr).tick(1))

        @blk.scalar
        def _(scalar, ts=ts):
            for hc in range(NHC):
                n = 128 if hc < NHC - 1 else 42
                if hc >= 1:
                    scalar.wait_ge((s_vg).h, (s_vg).base + (hc))
                scalar.wait_ge((s_tm1).h, (s_tm1).base + (2 * hc + 2))
                scalar.activation(out=gtmp[0:n, :], in_=ps_g[0:n, :], func=AF.Gelu,
                                  bias=b1c[0:n, 2 * hc + 1:2 * hc + 2]).then_inc(
                    *(s_sg).tick(1))

        @blk.tensor
        def _(tensor, ts=ts):
            tensor.wait_ge((s_vh2b).h, vh2_abs0 + 2 * (4 * ts + 4))
            tensor.wait_ge((s_vr).h, (s_vr).base + (0))
            for hc in range(NHC):
                n = 128 if hc < NHC - 1 else 42
                gcol0 = 128 if hc < NHC - 1 else 42
                tensor.wait_ge((s_vw1).h, (s_vw1).base + (hc + 1))
                if hc >= 1:
                    tensor.wait_ge((s_vg).h, (s_vg).base + (hc))
                if hc >= 2:
                    tensor.wait_ge((s_sg).h, (s_sg).base + (hc - 1))
                for half in range(2):
                    pdst = (ps_a, ps_g)[half]
                    col0 = 0 if half == 0 else gcol0
                    for c in range(DC):
                        ins = nc.tensor.matmul(
                            pdst[0:n, 0:512],
                            w1bf[hc % 2][:, c, col0:col0 + n],
                            hfm[:, c, 512 * ts:512 * ts + 512],
                            start=(c == 0), stop=(c == DC - 1), skip_group_check=True)
                    if half == 1:
                        ins.then_inc(*(s_tm1).tick(2))
            tensor.wait_ge((s_vg).h, (s_vg).base + (NHC))
            for hc in range(NHC):
                n = 128 if hc < NHC - 1 else 42
                tensor.wait_ge((s_vw2).h, (s_vw2).base + (hc + 1))
                for e in range(8):
                    t4, oh = e // 2, e % 2
                    ins = nc.tensor.matmul(
                        ps8[e][:],
                        g_fm[0:n, hc, 128 * t4:128 * t4 + 128],
                        w2bf[hc % 2][0:n, 512 * oh:512 * oh + 512],
                        start=(hc == 0), stop=(hc == NHC - 1), skip_group_check=True)
                    if e == 7:
                        ins.then_inc(*(s_tm2).tick(8))
    sc.close()


_cache = {}


def kernel(**inputs):
    per_core = _stage_inputs(inputs)
    if "prog" not in _cache:
        _cache["prog"] = build_program()
    nc = _cache["prog"]
    res = run_bass_kernel_spmd(nc, per_core, core_ids=list(range(8)))
    shards = [np.asarray(res.results[c]["out"]) for c in range(8)]
    return np.concatenate(shards, axis=0).reshape(B, S, D).astype(np.float32)
